# version 06
# speedup vs baseline: 1.7969x; 1.1390x over previous
import contextlib
import numpy as np

import concourse.bass as bass
import concourse.mybir as mybir
from concourse.bass_utils import run_bass_kernel_spmd

F32 = mybir.dt.float32
BF16 = mybir.dt.bfloat16
AF = mybir.ActivationFunctionType
ALU = mybir.AluOpType
AX = mybir.AxisListType

NCORES = 8
D = 1024
KCH = 8
TOK = 2048
SUB = 1024
LN_EPS = 1e-5



class Pr:
    ENGINES = ("tensor", "vector", "scalar", "gpsimd", "sync")

    def __init__(self, num_devices=None):
        self.nc = bass.Bass("TRN2", target_bir_lowering=False, num_devices=num_devices)
        self._stack = contextlib.ExitStack()
        self._scopes = []
        self.sems = {}
        self.cnt = {}
        self.hw = {}
        self.block = None
        self.blockcm = None
        self.sb = _SbMap(self)
        self.ps = []
        self._scope_id = 0
        self.regs = {}

    def sbuf(self, name, shape, dtype):
        if self._scopes:
            st, d = self._scopes[-1]
            h = st.enter_context(self.nc.sbuf_tensor(
                f"s{self._scope_id}_{name}", list(shape), dtype))
            d[name] = h
        else:
            h = self._stack.enter_context(self.nc.sbuf_tensor(name, list(shape), dtype))
            self.sb._root[name] = h
        return h

    def push_scope(self):
        self._scope_id += 1
        self._scopes.append((contextlib.ExitStack(), {}))

    def pop_scope(self):
        st, _ = self._scopes.pop()
        st.close()

    def const(self, data, name):
        self._scope_id += 1
        return self.nc.inline_tensor(np.ascontiguousarray(data),
                                     name=f"{name}_{self._scope_id}")

    def dram(self, name, shape, dtype, kind="Internal", shared=False):
        return self.nc.dram_tensor(name, list(shape), dtype, kind=kind,
                                   addr_space="Shared" if shared else "Local")

    def start(self, npsum=8):
        for i in range(npsum):
            self.ps.append(self._stack.enter_context(
                self.nc.psum_tensor(f"P{i}", [128, 512], F32)))
        for e in self.ENGINES:
            self.sems[e] = self._stack.enter_context(self.nc.semaphore("sp_" + e))
            self.cnt[e] = 0
        for dname in ("dS", "dG"):
            self.sems[dname] = self._stack.enter_context(self.nc.semaphore("sm_" + dname))
            self.cnt[dname] = 0
        self.blockcm = self.nc.Block()
        self.block = self.blockcm.__enter__()
        self._mbn = 0
        self._engobj = {"tensor": self.nc.tensor, "vector": self.nc.vector,
                        "scalar": self.nc.scalar, "gpsimd": self.nc.gpsimd,
                        "sync": self.nc.sync}
        self._alleng = [mybir.EngineType.PE, mybir.EngineType.DVE,
                        mybir.EngineType.Activation, mybir.EngineType.Pool,
                        mybir.EngineType.SP]
        self._open_master()

    def _open_master(self):
        self._mb = self.nc.body(f"master{self._mbn}", valid_engines=self._alleng)
        self._mb.__enter__()
        self._mbn += 1

    def _close_master(self):
        self._mb.__exit__(None, None, None)

    def init_parity(self, engines=("sync", "gpsimd")):
        for en in engines:
            def f(E, en=en):
                eng = E.e
                r_par = eng.alloc_register("par_" + en)
                r_xpar = eng.alloc_register("xpar_" + en)
                r_tmp = eng.alloc_register("tmp_" + en)
                pid = eng.partition_id()
                eng.reg_alu(r_par, pid, 1, ALU.bitwise_and)
                eng.reg_alu(r_xpar, r_par, 1, ALU.bitwise_xor)
                self.regs[en] = (r_par, r_xpar, r_tmp)
            self.sec(en, f)

    def sec(self, ename, body):
        body(_Emit(self, ename, self._engobj[ename]))

    def now(self, ename):
        return (ename, self.cnt[ename])

    def fence(self):
        toks = [self.now(e) for e in self.ENGINES] + [self.now("dS"), self.now("dG")]

        for e in self.ENGINES:
            def f(E, toks=toks):
                E.w(*toks)
            self.sec(e, f)

    def pswitch(self, fn, nbody=128):
        self._close_master()
        self.blockcm.__exit__(None, None, None)
        counts = []

        class _DD:
            def __init__(s, pr):
                s.n = 0
                s.pr = pr

            def __call__(s, out, in_):
                i = s.pr.nc.sync.dma_start(out=out, in_=in_)
                i.then_inc(s.pr.sems["dS"], 16)
                s.n += 1

            def wait(s, *toks):
                for t in toks:
                    if t is None:
                        continue
                    sname, val = t
                    if val > 0:
                        s.pr.nc.sync.wait_ge(s.pr.sems[sname], val)

        for arm in self.nc.switch_core_id(engines=[self.nc.sync], n=nbody):
            dd = _DD(self)
            fn(int(arm) & 1 if not hasattr(arm, "value") else arm.value & 1, dd)
            counts.append(dd.n)
        assert len(set(counts)) == 1, counts
        self.cnt["dS"] += 16 * counts[0]
        self.blockcm = self.nc.Block()
        self.block = self.blockcm.__enter__()
        self._open_master()
        return ("dS", self.cnt["dS"])

    def core_barrier(self):
        self.fence()
        self._close_master()
        self.blockcm.__exit__(None, None, None)
        self.nc.all_core_barrier()
        self.blockcm = self.nc.Block()
        self.block = self.blockcm.__enter__()
        self._open_master()

    def finish(self):
        def f(E):
            E.w(self.now("dS"), self.now("dG"))
        self.sec("sync", f)
        self._close_master()
        self.blockcm.__exit__(None, None, None)
        return self.nc


class _SbMap:
    def __init__(self, pr):
        self._pr = pr
        self._root = {}

    def __setitem__(self, k, v):
        self._root[k] = v

    def __contains__(self, k):
        try:
            self[k]
            return True
        except KeyError:
            return False

    def __getitem__(self, k):
        for _, d in reversed(self._pr._scopes):
            if k in d:
                return d[k]
        return self._root[k]


class _Emit:
    def __init__(self, pr, ename, eng):
        self.pr = pr
        self.en = ename
        self.e = eng

    def dyn(self, handle, which, slot_elems, base_elems, dims):
        r_par, r_xpar, r_tmp = self.pr.regs[self.en]
        sel = r_par if which == "own" else r_xpar
        self.e.reg_mul(r_tmp, sel, int(slot_elems))
        if base_elems:
            self.e.reg_add(r_tmp, r_tmp, int(base_elems))
        return bass.AP(handle, r_tmp, [list(d) for d in dims])

    def w(self, *tokens):
        for t in tokens:
            if t is None:
                continue
            sname, val = t
            if val <= 0:
                continue
            if sname == self.en and val <= self.pr.cnt[sname]:
                pass
            key = (self.en, sname)
            if self.pr.hw.get(key, 0) < val:
                self.e.wait_ge(self.pr.sems[sname], val)
                self.pr.hw[key] = val

    def t(self, instr):
        instr.then_inc(self.pr.sems[self.en], 1)
        self.pr.cnt[self.en] += 1
        return (self.en, self.pr.cnt[self.en])

    def dma(self, out, in_, sem=None, **kw):
        sem = sem or ("dG" if self.en == "gpsimd" else "dS")
        i = self.e.dma_start(out=out, in_=in_, **kw)
        i.then_inc(self.pr.sems[sem], 16)
        self.pr.cnt[sem] += 16
        return (sem, self.pr.cnt[sem])


def rearr_kp(ap):
    return ap.rearrange("(k p) n -> p k n", p=128)


def sb3(tile_ap, k):
    return tile_ap.rearrange("p (k n) -> p k n", k=k)



def emit_consts(pr):
    eye_c = pr.const(np.eye(128, dtype=np.float32), "eye_c")
    pr.sbuf("ones128", [128, 1], F32)
    pr.sbuf("ones128b", [128, 1], BF16)
    pr.sbuf("ones1r", [1, 128], F32)
    pr.sbuf("eye", [128, 128], F32)
    pr.sbuf("eyeb", [128, 128], BF16)
    tok = [None]

    def f(E):
        t1 = E.dma(pr.sb["eye"][:, :], eye_c.ap(), sem="dG")
        t1b = E.dma(pr.sb["eyeb"][:, :], eye_c.ap(), sem="dG")
        t2 = E.t(E.e.memset(pr.sb["ones128"][:, :], 1.0))
        t2b = E.t(E.e.memset(pr.sb["ones128b"][:, :], 1.0))
        t3 = E.t(E.e.memset(pr.sb["ones1r"][:, :], 1.0))
        tok[0] = t3
    pr.sec("gpsimd", f)
    return [tok[0], ("dG", pr.cnt["dG"])]


def emit_ln(pr, xT, xln, n_tok, dep_toks, name, gamma_beta=None):
    nt = n_tok // 512
    ones = pr.sb["ones128"]
    onesr = pr.sb["ones1r"]
    sq = pr.sb[name + "_sq"]
    rows = pr.sb[name + "_rows"]
    tmp = pr.sb[name + "_tmp"]
    P = pr.ps
    out_toks = []

    for t in range(nt):
        col = lambda k, t=t: slice(k * n_tok + t * 512, k * n_tok + (t + 1) * 512)
        sq_tok = [None] * KCH
        qmm = [None] * KCH
        smm = [None]
        rd = [None]
        bc = [None]

        def f_smm(E, t=t, smm=smm):
            E.w(*dep_toks)
            for k in range(KCH):
                i = E.e.matmul(P[0][0:1, :], ones[:, :], xT[:, col(k)],
                               start=(k == 0), stop=(k == KCH - 1))
            smm[0] = E.t(i)
        pr.sec("tensor", f_smm)

        for k in range(KCH):
            def f_sqk(E, t=t, k=k, sq_tok=sq_tok, qmm=qmm):
                E.w(*dep_toks)
                if k >= 2:
                    E.w(qmm[k - 2])
                half = sq[:, (k % 2) * 512:(k % 2) * 512 + 512]
                sq_tok[k] = E.t(E.e.activation(half, xT[:, col(k)], AF.Square))
            pr.sec("scalar", f_sqk)

            def f_qmmk(E, t=t, k=k, sq_tok=sq_tok, qmm=qmm):
                E.w(sq_tok[k])
                half = sq[:, (k % 2) * 512:(k % 2) * 512 + 512]
                i = E.e.matmul(P[1][0:1, :], pr.sb["ones128b"][:, :], half,
                               start=(k == 0), stop=(k == KCH - 1))
                qmm[k] = E.t(i)
            pr.sec("tensor", f_qmmk)

        def f_r1(E, t=t, qmm=qmm, smm=smm, rd=rd):
            m = rows[:, 0:512]; v = rows[:, 512:1024]; b = rows[:, 1024:1536]
            E.w(smm[0], qmm[KCH - 1])
            t1 = E.t(E.e.tensor_scalar_mul(m, P[0][0:1, :], 1.0 / D))
            t2 = E.t(E.e.tensor_scalar_mul(v, P[1][0:1, :], 1.0 / D))
            E.w(t1, t2)
            t3 = E.t(E.e.tensor_mul(b, m, m))
            E.w(t3)
            t4 = E.t(E.e.tensor_tensor(v, v, b, ALU.subtract))
            E.w(t4)
            t5 = E.t(E.e.tensor_scalar(v, v, LN_EPS, None, ALU.add))
            rd[0] = t5
        pr.sec("vector", f_r1)

        def f_r2(E, t=t, rd=rd):
            v = rows[:, 512:1024]
            E.w(rd[0])
            rd[0] = E.t(E.e.activation(v, v, AF.Sqrt))
        pr.sec("scalar", f_r2)

        def f_r3(E, t=t, rd=rd):
            m = rows[:, 0:512]; v = rows[:, 512:1024]; b = rows[:, 1024:1536]
            E.w(rd[0])
            t1 = E.t(E.e.reciprocal(v, v))
            E.w(t1)
            t2 = E.t(E.e.tensor_mul(b, m, v))
            E.w(t2)
            t3 = E.t(E.e.tensor_scalar_mul(b, b, -1.0))
            rd[0] = t3
        pr.sec("vector", f_r3)

        def f_bc(E, t=t, rd=rd, bc=bc):
            a = rows[:, 512:1024]; b = rows[:, 1024:1536]
            E.w(rd[0])
            if out_toks:
                E.w(out_toks[-1])
            E.e.matmul(P[2][:, :], onesr[:, :], a, start=True, stop=True)
            i = E.e.matmul(P[3][:, :], onesr[:, :], b, start=True, stop=True)
            bc[0] = E.t(i)
        pr.sec("tensor", f_bc)

        def f_ap(E, t=t, bc=bc):
            E.w(bc[0])
            last = None
            for k in range(KCH):
                t1 = E.t(E.e.tensor_tensor(tmp[:, :], xT[:, col(k)], P[2][:, :], ALU.mult))
                E.w(t1)
                if gamma_beta is None:
                    last = E.t(E.e.tensor_tensor(xln[:, col(k)], tmp[:, :], P[3][:, :], ALU.add))
                else:
                    last = E.t(E.e.tensor_tensor(tmp[:, :], tmp[:, :], P[3][:, :], ALU.add))
                    E.w(last)
                    g_sb, b_sb = gamma_beta
                    last = E.t(E.e.tensor_scalar(xln[:, col(k)], tmp[:, :],
                                                 g_sb[:, k:k + 1], b_sb[:, k:k + 1],
                                                 ALU.mult, ALU.add))
                E.w(last)
            out_toks.append(last)
        pr.sec("vector", f_ap)

    return out_toks


def alloc_ln(pr, name):
    pr.sbuf(name + "_sq", [128, 1024], BF16)
    pr.sbuf(name + "_rows", [1, 1536], F32)
    pr.sbuf(name + "_tmp", [128, 512], F32)


NKV = 4096
HEADS = 16
PM = 256
DI = 512
GH = 8
SB16 = 16
RH = 16



FFN0_KEYS = ("xT", "gwT", "w1T", "w2T", "w3T", "b1r", "b2r", "b3r",
             "sw1T", "sw2T", "sw3T", "yT")


def build_ffn0():
    pr = Pr()
    nc = pr.nc
    io = {}
    io["xT"] = nc.dram_tensor("xT", [D, TOK], F32, kind="ExternalInput").ap()
    io["gwT"] = nc.dram_tensor("gwT", [D, 8], F32, kind="ExternalInput").ap()
    io["w1T"] = nc.dram_tensor("w1T", [8, D, D], F32, kind="ExternalInput").ap()
    io["w2T"] = nc.dram_tensor("w2T", [8, D, D], F32, kind="ExternalInput").ap()
    io["w3T"] = nc.dram_tensor("w3T", [8, D, D], F32, kind="ExternalInput").ap()
    io["b1r"] = nc.dram_tensor("b1r", [128, 64], F32, kind="ExternalInput").ap()
    io["b2r"] = nc.dram_tensor("b2r", [128, 64], F32, kind="ExternalInput").ap()
    io["b3r"] = nc.dram_tensor("b3r", [128, 64], F32, kind="ExternalInput").ap()
    io["sw1T"] = nc.dram_tensor("sw1T", [D, 2048], F32, kind="ExternalInput").ap()
    io["sw2T"] = nc.dram_tensor("sw2T", [D, 2048], F32, kind="ExternalInput").ap()
    io["sw3T"] = nc.dram_tensor("sw3T", [2048, D], F32, kind="ExternalInput").ap()
    io["yT"] = nc.dram_tensor("yT", [D, TOK], F32, kind="ExternalOutput").ap()
    pr.start()
    ctoks = emit_consts(pr)
    phase_ffn0(pr, io, ctoks)
    return pr.finish()


def phase_ffn0(pr, io, ctoks0):
    pr.push_scope()
    ctoks = list(ctoks0)
    xT_d = io["xT"]; gwT_d = io["gwT"]
    w1T_d = io["w1T"]; w2T_d = io["w2T"]; w3T_d = io["w3T"]
    b1r_d = io["b1r"]; b2r_d = io["b2r"]; b3r_d = io["b3r"]
    sw1T_d = io["sw1T"]; sw2T_d = io["sw2T"]; sw3T_d = io["sw3T"]
    yT_d = io["yT"]

    sel = np.zeros((8, 8 * 128), dtype=np.float32)
    for e in range(8):
        sel[e, e * 128:(e + 1) * 128] = 1.0
    sel_c = pr.const(sel, "sel8_c")

    xT = pr.sbuf("sxT", [128, KCH * SUB], F32)
    xln = pr.sbuf("xln", [128, KCH * SUB], BF16)
    Y = pr.sbuf("Y", [128, KCH * SUB], F32)
    wpool = pr.sbuf("wpool", [128, 16 * D], BF16)
    xs = pr.sbuf("xs", [128, KCH * SUB], BF16)
    hb = pr.sbuf("hb", [128, KCH * SUB], BF16)
    x3sb = pr.sbuf("x3sb", [128, KCH * SUB], BF16)
    pr.sbuf("sel8", [8, 1024], BF16)
    pr.sbuf("gw", [128, KCH * 8], BF16)
    pr.sbuf("b1", [128, 64], F32)
    pr.sbuf("b2", [128, 64], F32)
    pr.sbuf("b3", [128, 64], F32)
    pr.sbuf("WEr", [8, SUB], BF16)
    pr.sbuf("rt", [128, 48], F32)
    pr.sbuf("rt2", [128, 8], F32)
    pr.sbuf("webc", [128, 2 * 512], BF16)
    pr.sbuf("mebc", [128, 2 * 512], BF16)
    pr.sbuf("x2", [128, 512], BF16)
    pr.sbuf("prod", [128, 512], BF16)
    alloc_ln(pr, "ffn")

    def f_init(E):
        t1 = E.dma(pr.sb["sel8"][:, :], sel_c.ap(), sem="dG")
        t2 = E.dma(sb3(pr.sb["gw"][:, :], KCH), rearr_kp(gwT_d), sem="dG")
    pr.sec("gpsimd", f_init)

    def f_init2(E):
        E.dma(pr.sb["b1"][:, :], b1r_d)
        E.dma(pr.sb["b2"][:, :], b2r_d)
        E.dma(pr.sb["b3"][:, :], b3r_d)
    pr.sec("sync", f_init2)
    ctoks += [("dG", pr.cnt["dG"]), ("dS", pr.cnt["dS"])]

    for sub in range(TOK // SUB):
        _ffn0_sub(pr, sub, xT_d, yT_d, w1T_d, w2T_d, w3T_d,
                  sw1T_d, sw2T_d, sw3T_d, ctoks)
    pr.fence()
    pr.pop_scope()


def _ffn0_sub(pr, sub, xT_d, yT_d, w1T_d, w2T_d, w3T_d, sw1T_d, sw2T_d, sw3T_d, ctoks):
    P = pr.ps
    xT = pr.sb["sxT"]; xln = pr.sb["xln"]; Y = pr.sb["Y"]
    wpool = pr.sb["wpool"]; xs = pr.sb["xs"]; hb = pr.sb["hb"]; x3sb = pr.sb["x3sb"]
    WEr = pr.sb["WEr"]; rt = pr.sb["rt"]; rt2 = pr.sb["rt2"]
    webc = pr.sb["webc"]; mebc = pr.sb["mebc"]; x2 = pr.sb["x2"]; prod = pr.sb["prod"]
    b1 = pr.sb["b1"]; b2 = pr.sb["b2"]; b3 = pr.sb["b3"]
    tok0 = sub * SUB
    nt = SUB // 512

    pr.fence()

    ld = [None]

    def f_ldx(E):
        src = rearr_kp(xT_d)
        ld[0] = E.dma(sb3(xT[:, :], KCH)[:, :, 0:SUB] if tok0 == 0
                      else sb3(xT[:, :], KCH)[:, :, 0:SUB],
                      src[:, :, tok0:tok0 + SUB])
    pr.sec("sync", f_ldx)

    def f_yinit(E):
        E.w(ld[0])
        for k in range(KCH):
            E.t(E.e.tensor_copy(Y[:, k * SUB:(k + 1) * SUB], xT[:, k * SUB:(k + 1) * SUB]))
    pr.sec("gpsimd", f_yinit)

    ln_toks = emit_ln(pr, xT, xln, SUB, [ld[0]] + ctoks, "ffn")

    tr_last = [None]
    for tt in range(SUB // 128):
        tile_ln = ln_toks[(tt * 128) // 512]
        st = [None]

        def f_rmm(E, tt=tt, tile_ln=tile_ln, st=st):
            E.w(tile_ln, *ctoks)
            E.w(self_tok(pr, "vector"))
            for k in range(KCH):
                i = E.e.matmul(P[0][:, 0:8],
                               xln[:, k * SUB + tt * 128: k * SUB + (tt + 1) * 128],
                               pr.sb["gw"][:, k * 8:(k + 1) * 8],
                               start=(k == 0), stop=(k == KCH - 1))
            st[0] = E.t(i)
        pr.sec("tensor", f_rmm)

        def f_rv(E, tt=tt, st=st):
            m = rt2[:, 0:1]; nm = rt2[:, 1:2]
            E.w(st[0])
            t1 = E.t(E.e.reduce_max(m, P[0][:, 0:8], axis=AX.X))
            E.w(t1)
            t2 = E.t(E.e.tensor_scalar_mul(nm, m, -1.0))
            E.w(t2)
            st[0] = t2
        pr.sec("vector", f_rv)

        def f_rs(E, tt=tt, st=st):
            ex = rt[:, 8:16]; nm = rt2[:, 1:2]
            E.w(st[0])
            st[0] = E.t(E.e.activation(ex, P[0][:, 0:8], AF.Exp, bias=nm))
        pr.sec("scalar", f_rs)

        def f_rv2(E, tt=tt, st=st):
            s = rt[:, 0:8]; ex = rt[:, 8:16]; ge = rt[:, 16:24]
            we = rt[:, 24:32]; msk = rt[:, 32:40]
            ssum = rt2[:, 2:3]; rsum = rt2[:, 3:4]; m1 = rt2[:, 4:5]; m2 = rt2[:, 5:6]
            E.w(st[0])
            t1 = E.t(E.e.reduce_sum(ssum, ex, axis=AX.X))
            E.w(t1)
            t2 = E.t(E.e.reciprocal(rsum, ssum))
            E.w(t2)
            t3 = E.t(E.e.tensor_scalar_mul(s, ex, rsum))
            E.w(t3)
            t4 = E.t(E.e.reduce_max(m1, s, axis=AX.X))
            E.w(t4)
            t5 = E.t(E.e.tensor_single_scalar(ge, s, m1, ALU.is_ge))
            E.w(t5)
            t6 = E.t(E.e.scalar_tensor_tensor(msk, ge, -1e9, s, ALU.mult, ALU.add))
            E.w(t6)
            t7 = E.t(E.e.reduce_max(m2, msk, axis=AX.X))
            E.w(t7)
            t8 = E.t(E.e.tensor_single_scalar(ge, s, m2, ALU.is_ge))
            E.w(t8)
            t9 = E.t(E.e.tensor_mul(we, s, ge))
            st[0] = t9
        pr.sec("vector", f_rv2)

        def f_rtr(E, tt=tt, st=st):
            E.w(st[0])
            if tr_last[0] is not None:
                E.w(tr_last[0])
            i = E.e.transpose(P[1][0:8, 0:128], rt[:, 24:32], pr.sb["eye"][:, :])
            st[0] = E.t(i)
        pr.sec("tensor", f_rtr)

        def f_rcp(E, tt=tt, st=st):
            E.w(st[0])
            tr_last[0] = E.t(E.e.tensor_copy(WEr[:, tt * 128:(tt + 1) * 128], P[1][0:8, 0:128]))
        pr.sec("vector", f_rcp)

    pr.fence()

    wst = {"slot": 0, "lastc": [None, None]}
    for e in range(8):
        _ffn0_expert(pr, e, w1T_d, w2T_d, w3T_d, wst)

    pr.fence()

    _ffn0_swiglu(pr, sw1T_d, sw2T_d, sw3T_d)

    pr.fence()

    def f_st(E):
        dst = rearr_kp(yT_d)
        E.dma(dst[:, :, tok0:tok0 + SUB], sb3(Y[:, :], KCH))
    pr.sec("sync", f_st)


def self_tok(pr, ename):
    return (ename, pr.cnt[ename])


def _ffn0_expert(pr, e, w1T_d, w2T_d, w3T_d, wst):
    P = pr.ps
    nt = SUB // 512
    xln = pr.sb["xln"]; Y = pr.sb["Y"]; wpool = pr.sb["wpool"]
    xs = pr.sb["xs"]; hb = pr.sb["hb"]; x3sb = pr.sb["x3sb"]
    WEr = pr.sb["WEr"]; webc = pr.sb["webc"]; mebc = pr.sb["mebc"]
    x2 = pr.sb["x2"]; prod = pr.sb["prod"]
    b1 = pr.sb["b1"]; b2 = pr.sb["b2"]; b3 = pr.sb["b3"]

    WHALF = KCH * D

    def wslice(slot, k, o):
        return wpool[:, slot * WHALF + k * D + o * 128: slot * WHALF + k * D + o * 128 + 128]

    wtok = [None]
    s1 = wst["slot"]

    def f_ldw1(E, e=e, s1=s1):
        E.w(wst["lastc"][s1])
        wtok[0] = E.dma(sb3(wpool[:, s1 * WHALF:(s1 + 1) * WHALF], KCH),
                        w1T_d[e, :, :].rearrange("(k p) h -> p k h", p=128), sem="dG")
    pr.sec("gpsimd", f_ldw1)
    wst["slot"] ^= 1

    for t in range(nt):
        bt = [None]

        def f_bc(E, e=e, t=t, bt=bt):
            E.w(self_tok(pr, "vector"))
            i = E.e.matmul(P[2][:, :], pr.sb["sel8"][:, e * 128:(e + 1) * 128],
                           WEr[:, t * 512:(t + 1) * 512], start=True, stop=True)
            bt[0] = E.t(i)
        pr.sec("tensor", f_bc)

        def f_bcc(E, t=t, bt=bt):
            wc = webc[:, t * 512:(t + 1) * 512]
            mc = mebc[:, t * 512:(t + 1) * 512]
            E.w(bt[0])
            t1 = E.t(E.e.tensor_copy(wc, P[2][:, :]))
            E.w(t1)
            t2 = E.t(E.e.tensor_single_scalar(mc, wc, 0.0, ALU.is_gt))
            E.w(t2)
            for k in range(KCH):
                E.t(E.e.tensor_mul(xs[:, k * SUB + t * 512: k * SUB + (t + 1) * 512],
                                   xln[:, k * SUB + t * 512: k * SUB + (t + 1) * 512], wc))
        pr.sec("vector", f_bcc)

    xs_done = self_tok(pr, "vector")

    silu_toks = {}
    for t in range(nt):
        for o in range(KCH):
            h1 = [None]

            def f_h1(E, t=t, o=o, h1=h1):
                E.w(wtok[0], xs_done)
                if o >= 2:
                    E.w(silu_toks[(t, o - 2)])
                elif t >= 1:
                    E.w(silu_toks[(t - 1, KCH - 2 + o)])
                for k in range(KCH):
                    i = E.e.matmul(P[4 + o % 2][:, :],
                                   wslice(s1, k, o),
                                   xs[:, k * SUB + t * 512: k * SUB + (t + 1) * 512],
                                   start=(k == 0), stop=(k == KCH - 1))
                h1[0] = E.t(i)
                wst["lastc"][s1] = h1[0]
            pr.sec("tensor", f_h1)

            def f_silu(E, e=e, t=t, o=o, h1=h1):
                E.w(h1[0])
                silu_toks[(t, o)] = E.t(E.e.activation(
                    hb[:, o * SUB + t * 512: o * SUB + (t + 1) * 512],
                    P[4 + o % 2][:, :], AF.Silu, bias=b1[:, e * 8 + o: e * 8 + o + 1]))
            pr.sec("scalar", f_silu)

    wtok3 = [None]
    s3 = wst["slot"]

    def f_ldw3(E, e=e, s3=s3):
        E.w(wst["lastc"][s3])
        wtok3[0] = E.dma(sb3(wpool[:, s3 * WHALF:(s3 + 1) * WHALF], KCH),
                         w3T_d[e, :, :].rearrange("(k p) h -> p k h", p=128), sem="dG")
    pr.sec("gpsimd", f_ldw3)
    wst["slot"] ^= 1

    x3_toks = {}
    for t in range(nt):
        for o in range(KCH):
            h3 = [None]

            def f_h3(E, t=t, o=o, h3=h3):
                E.w(wtok3[0])
                if o >= 2:
                    E.w(x3_toks[(t, o - 2)])
                elif t >= 1:
                    E.w(x3_toks[(t - 1, KCH - 2 + o)])
                for k in range(KCH):
                    i = E.e.matmul(P[4 + o % 2][:, :],
                                   wslice(s3, k, o),
                                   xs[:, k * SUB + t * 512: k * SUB + (t + 1) * 512],
                                   start=(k == 0), stop=(k == KCH - 1))
                h3[0] = E.t(i)
                wst["lastc"][s3] = h3[0]
            pr.sec("tensor", f_h3)

            def f_x3(E, e=e, t=t, o=o, h3=h3):
                mc = mebc[:, t * 512:(t + 1) * 512]
                E.w(h3[0])
                x3_toks[(t, o)] = E.t(E.e.scalar_tensor_tensor(
                    x3sb[:, o * SUB + t * 512: o * SUB + (t + 1) * 512],
                    mc, b3[:, e * 8 + o: e * 8 + o + 1], P[4 + o % 2][:, :],
                    ALU.mult, ALU.add))
            pr.sec("vector", f_x3)

    wtok2 = [None]
    s2 = wst["slot"]

    def f_ldw2(E, e=e, s2=s2):
        E.w(wst["lastc"][s2])
        wtok2[0] = E.dma(sb3(wpool[:, s2 * WHALF:(s2 + 1) * WHALF], KCH),
                         w2T_d[e, :, :].rearrange("(k p) h -> p k h", p=128), sem="dG")
    pr.sec("gpsimd", f_ldw2)
    wst["slot"] ^= 1

    for t in range(nt):
        for o in range(KCH):
            h2 = [None]
            x2t = [None]
            pt = [None]

            def f_h2(E, t=t, o=o, h2=h2):
                E.w(wtok2[0])
                E.w(self_tok(pr, "scalar"))
                E.w(self_tok(pr, "gpsimd"))
                for k in range(KCH):
                    i = E.e.matmul(P[4 + o % 2][:, :],
                                   wslice(s2, k, o),
                                   hb[:, k * SUB + t * 512: k * SUB + (t + 1) * 512],
                                   start=(k == 0), stop=(k == KCH - 1))
                h2[0] = E.t(i)
                wst["lastc"][s2] = h2[0]
            pr.sec("tensor", f_h2)

            def f_x2(E, e=e, t=t, o=o, h2=h2, x2t=x2t):
                E.w(h2[0])
                x2t[0] = E.t(E.e.activation(x2[:, :], P[4 + o % 2][:, :], AF.Identity,
                                            bias=b2[:, e * 8 + o: e * 8 + o + 1]))
            pr.sec("scalar", f_x2)

            def f_pr(E, t=t, o=o, x2t=x2t, pt=pt):
                E.w(x2t[0], x3_toks[(t, o)])
                t1 = E.t(E.e.tensor_mul(prod[:, :], x2[:, :],
                                        x3sb[:, o * SUB + t * 512: o * SUB + (t + 1) * 512]))
                E.w(t1)
                pt[0] = E.t(E.e.tensor_tensor(
                    Y[:, o * SUB + t * 512: o * SUB + (t + 1) * 512],
                    Y[:, o * SUB + t * 512: o * SUB + (t + 1) * 512],
                    prod[:, :], ALU.add))
            pr.sec("gpsimd", f_pr)


def _ffn0_swiglu(pr, sw1T_d, sw2T_d, sw3T_d):
    P = pr.ps
    nt = SUB // 512
    xln = pr.sb["xln"]; Y = pr.sb["Y"]; wpool = pr.sb["wpool"]
    G = pr.sb["sxT"].ap().bitcast(BF16)
    xs = pr.sb["xs"]; hb = pr.sb["hb"]

    wtok = [None]

    def f_ld1(E):
        E.w(self_tok(pr, "tensor"))
        wtok[0] = E.dma(sb3(wpool[:, :], KCH), rearr_kp(sw1T_d), sem="dG")
    pr.sec("gpsimd", f_ld1)

    g_toks = {}
    for t in range(nt):
        for o in range(16):
            gm = [None]

            def f_gmm(E, t=t, o=o, gm=gm):
                E.w(wtok[0])
                if o >= 2:
                    E.w(g_toks[(t, o - 2)])
                elif t >= 1:
                    E.w(g_toks[(t - 1, 14 + o)])
                for k in range(KCH):
                    i = E.e.matmul(P[4 + o % 2][:, :],
                                   wpool[:, k * 2048 + o * 128: k * 2048 + o * 128 + 128],
                                   xln[:, k * SUB + t * 512: k * SUB + (t + 1) * 512],
                                   start=(k == 0), stop=(k == KCH - 1))
                gm[0] = E.t(i)
            pr.sec("tensor", f_gmm)

            def f_gsilu(E, t=t, o=o, gm=gm):
                E.w(gm[0])
                g_toks[(t, o)] = E.t(E.e.activation(
                    G[:, o * SUB + t * 512: o * SUB + (t + 1) * 512],
                    P[4 + o % 2][:, :], AF.Silu))
            pr.sec("scalar", f_gsilu)

    def f_ld2(E):
        E.w(self_tok(pr, "tensor"))
        wtok[0] = E.dma(sb3(wpool[:, :], KCH), rearr_kp(sw2T_d), sem="dG")
    pr.sec("gpsimd", f_ld2)

    p_toks = {}
    for t in range(nt):
        for o in range(16):
            um = [None]

            def f_umm(E, t=t, o=o, um=um):
                E.w(wtok[0], self_tok(pr, "scalar"))
                if o >= 2:
                    E.w(p_toks[(t, o - 2)])
                elif t >= 1:
                    E.w(p_toks[(t - 1, 14 + o)])
                for k in range(KCH):
                    i = E.e.matmul(P[4 + o % 2][:, :],
                                   wpool[:, k * 2048 + o * 128: k * 2048 + o * 128 + 128],
                                   xln[:, k * SUB + t * 512: k * SUB + (t + 1) * 512],
                                   start=(k == 0), stop=(k == KCH - 1))
                um[0] = E.t(i)
            pr.sec("tensor", f_umm)

            def f_up(E, t=t, o=o, um=um):
                E.w(um[0])
                dst = xs if o < 8 else hb
                oo = o % 8
                p_toks[(t, o)] = E.t(E.e.tensor_tensor(
                    dst[:, oo * SUB + t * 512: oo * SUB + (t + 1) * 512],
                    G[:, o * SUB + t * 512: o * SUB + (t + 1) * 512],
                    P[4 + o % 2][:, :], ALU.mult))
            pr.sec("vector", f_up)

    def f_ld3(E):
        E.w(self_tok(pr, "tensor"))
        wtok[0] = E.dma(sb3(wpool[:, :], 16), rearr_kp(sw3T_d), sem="dG")
    pr.sec("gpsimd", f_ld3)

    zdone = {}
    for t in range(nt):
        for o in range(KCH):
            zm = [None]

            def f_zmm(E, t=t, o=o, zm=zm):
                E.w(wtok[0], self_tok(pr, "vector"))
                if o >= 2:
                    E.w(zdone[(t, o - 2)])
                elif t >= 1:
                    E.w(zdone[(t - 1, KCH - 2 + o)])
                for k in range(16):
                    srcb = xs if k < 8 else hb
                    kk = k % 8
                    i = E.e.matmul(P[4 + o % 2][:, :],
                                   wpool[:, k * D + o * 128: k * D + o * 128 + 128],
                                   srcb[:, kk * SUB + t * 512: kk * SUB + (t + 1) * 512],
                                   start=(k == 0), stop=(k == 15))
                zm[0] = E.t(i)
            pr.sec("tensor", f_zmm)

            def f_zacc(E, t=t, o=o, zm=zm):
                E.w(zm[0])
                zdone[(t, o)] = E.t(E.e.tensor_tensor(
                    Y[:, o * SUB + t * 512: o * SUB + (t + 1) * 512],
                    Y[:, o * SUB + t * 512: o * SUB + (t + 1) * 512],
                    P[4 + o % 2][:, :], ALU.add))
            pr.sec("vector", f_zacc)



_BUILD_CACHE = {}


def _get(name, builder):
    if name not in _BUILD_CACHE:
        _BUILD_CACHE[name] = builder()
    return _BUILD_CACHE[name]


def _shard_T(x_flat):
    out = []
    for c in range(NCORES):
        sh = x_flat[c * TOK:(c + 1) * TOK]
        out.append(np.ascontiguousarray(sh.T))
    return out


def _unshard_T(shards):
    return np.concatenate([s.T for s in shards], axis=0)


def _col_bias(b):
    return np.ascontiguousarray(b.reshape(8, 8, 128).transpose(2, 0, 1).reshape(128, 64))



def phase_atta(pr, io, ctoks0):
    pr.push_scope()
    ctoks = list(ctoks0)
    nc = pr.nc
    sid = pr._scope_id
    xT_d = io["xT"]
    wqT_d = io["wqT"]; wkT_d = io["wkT"]; wvT_d = io["wvT"]; woT_d = io["woT"]
    mrow_d = io["mrow"]
    yT_d = io["yT"]
    scrk_d = pr.dram(f"scrk{sid}", [HEADS, 64, NKV], BF16).ap()
    xsh_h = pr.dram(f"xsh{sid}", [2, D, TOK], BF16, shared=True)
    xloc_d = pr.dram(f"xloc{sid}", [D, TOK], BF16).ap()
    xkv_d = pr.dram(f"xkv{sid}", [D, NKV], BF16).ap()

    trim = np.zeros((4, 128, 512), dtype=np.float32)
    for j in range(4):
        for p in range(128):
            trim[j, p, j * 128 + p:] = 1.0
    tri_c = pr.const(trim.reshape(4 * 128, 512), "tri_c")

    xT = pr.sbuf("sxT", [128, KCH * 512], F32)
    xln = pr.sbuf("xln", [128, 2 * KCH * 512], BF16)
    wbuf = pr.sbuf("wbuf", [128, KCH * D], BF16)
    vres = pr.sbuf("vres", [128, 32 * HEADS * 65], BF16)
    qtb = pr.sbuf("qtb", [128, KCH * TOK], BF16)
    qst = pr.sbuf("qst", [66, 2 * 512], BF16)
    psb = pr.sbuf("psb", [128, 4 * 512], BF16)
    tri = pr.sbuf("tri", [128, 4 * 512], BF16)
    oall = pr.sbuf("oall", [128, KCH * TOK], BF16)
    osb = pr.sbuf("osb", [64, 512], BF16)
    kevac = pr.sbuf("kevac", [128, 2 * 512], BF16)
    mcol = pr.sbuf("mcol", [128, 8], F32)
    lrow = pr.sbuf("lrow", [1, 512], F32)
    outsb = pr.sbuf("outsb", [128, 512], F32)
    alloc_ln(pr, "att")

    def f_init(E):
        ctoks.append(E.dma(sb3(tri[:, :], 4), tri_c.ap().rearrange("(j p) c -> p j c", p=128), sem="dG"))
        ctoks.append(E.t(E.e.memset(vres[:, :], 1.0)))
    pr.sec("gpsimd", f_init)

    P = pr.ps
    for r in range(4):
        ldt = [None]

        def f_ldo(E, r=r):
            E.w(("tensor", pr.cnt["tensor"]), ("scalar", pr.cnt["scalar"]),
                ("vector", pr.cnt["vector"]), ("dS", pr.cnt["dS"]))
            ldt[0] = E.dma(sb3(xT[:, :], KCH), rearr_kp(xT_d)[:, :, r * 512:(r + 1) * 512])
        pr.sec("sync", f_ldo)

        lt = emit_ln(pr, xT, xln[:, 0:KCH * 512], 512, [ldt[0]] + ctoks, "att")

        def f_wsh(E, r=r, lt=lt):
            E.w(lt[-1])
            E.dma(rearr_kp(xloc_d)[:, :, r * 512:(r + 1) * 512], sb3(xln[:, 0:KCH * 512], KCH))
        pr.sec("sync", f_wsh)

    pr.fence()

    def f_xout(par, dd):
        dd.wait(("dS", pr.cnt["dS"]))
        dd(xsh_h.ap()[par, :, :], xloc_d)
    pr.pswitch(f_xout)
    pr.core_barrier()

    def f_xin(par, dd):
        dd(xkv_d[:, 0:TOK], xsh_h.ap()[1 - par, :, :])
        dd(xkv_d[:, TOK:NKV], xsh_h.ap()[par, :, :])
    gtok = pr.pswitch(f_xin)

    _atta_proj(pr, xkv_d, gtok, wqT_d, wkT_d, wvT_d, scrk_d, ctoks)

    pr.fence()

    kstv = pr.sb["wbuf"][0:66, 0:NKV]

    def f_init2(E):
        ctoks.append(E.dma(kstv[65:66, :], mrow_d))
    pr.sec("sync", f_init2)

    def f_init3(E):
        E.w(*[c for c in ctoks if c])
        ctoks.append(E.t(E.e.memset(kstv[64:65, :], -1.0)))
    pr.sec("gpsimd", f_init3)

    for h in range(HEADS):
        _atta_head(pr, h, scrk_d, ctoks)

    pr.fence()

    wo_tok = [None]

    def f_ldwo(E):
        wo_tok[0] = E.dma(sb3(wbuf[:, :], KCH), rearr_kp(woT_d), sem="dG")
    pr.sec("gpsimd", f_ldwo)

    for g in range(4):
        xg = [None]

        def f_ldxg(E, g=g):
            E.w(self_tok(pr, "vector"))
            src = rearr_kp(xT_d)
            xg[0] = E.dma(sb3(xT[:, :], KCH), src[:, :, g * 512:(g + 1) * 512])
        pr.sec("sync", f_ldxg)

        for o in range(KCH):
            om = [None]

            def f_omm(E, g=g, o=o, om=om):
                E.w(wo_tok[0], self_tok(pr, "vector"))
                for k in range(KCH):
                    i = E.e.matmul(P[4 + o % 2][:, :],
                                   wbuf[:, k * D + o * 128: k * D + o * 128 + 128],
                                   pr.sb["oall"][:, k * TOK + g * 512: k * TOK + (g + 1) * 512],
                                   start=(k == 0), stop=(k == KCH - 1))
                om[0] = E.t(i)
            pr.sec("tensor", f_omm)

            def f_oadd(E, g=g, o=o, om=om):
                E.w(om[0], xg[0])
                E.w(("dS", pr.cnt["dS"]))
                t1 = E.t(E.e.tensor_tensor(outsb[:, :], xT[:, o * 512:(o + 1) * 512],
                                           P[4 + o % 2][:, :], ALU.add))
                E.w(t1)
            pr.sec("vector", f_oadd)

            def f_ost(E, g=g, o=o):
                E.w(self_tok(pr, "vector"))
                dst = rearr_kp(yT_d)
                E.dma(dst[:, o, g * 512:(g + 1) * 512], outsb[:, :])
            pr.sec("sync", f_ost)

    pr.fence()
    pr.pop_scope()


def _atta_proj(pr, xkv_d, gtok, wqT_d, wkT_d, wvT_d, scrk_d, ctoks):
    P = pr.ps
    xln = pr.sb["xln"]; wbuf = pr.sb["wbuf"]
    vres = pr.sb["vres"]; qtb = pr.sb["qtb"]; kevac = pr.sb["kevac"]
    XW = KCH * 512
    pr.fence()
    wt = [None]
    xt = {}

    def f_ldwk(E):
        wt[0] = E.dma(sb3(wbuf[:, :], KCH), rearr_kp(wkT_d), sem="dG")
    pr.sec("gpsimd", f_ldwk)

    def ldx(r, dep):
        sl = r % 2

        def f_ldx(E, r=r, sl=sl, dep=dep):
            E.w(gtok, dep)
            xt[r] = E.dma(sb3(xln[:, sl * XW:(sl + 1) * XW], KCH),
                          rearr_kp(xkv_d)[:, :, r * 512:(r + 1) * 512])
        pr.sec("sync", f_ldx)

    kmmt = {}; kevt = {}; kstt = {}
    for r in range(8):
        ldx(r, kmmt.get(r * 8 - 9))
        sl = r % 2
        for o in range(KCH):
            n = r * 8 + o

            def f_kmm(E, r=r, o=o, n=n, sl=sl):
                E.w(wt[0], xt[r], kevt.get(n - 2))
                for k in range(KCH):
                    i = E.e.matmul(P[4 + o % 2][:, :],
                                   wbuf[:, k * D + o * 128: k * D + o * 128 + 128],
                                   xln[:, sl * XW + k * 512:sl * XW + (k + 1) * 512],
                                   start=(k == 0), stop=(k == KCH - 1))
                kmmt[n] = E.t(i)
            pr.sec("tensor", f_kmm)

            def f_kev(E, o=o, n=n):
                E.w(kmmt[n], kstt.get(n - 2))
                kevt[n] = E.t(E.e.tensor_copy(kevac[:, (o % 2) * 512:(o % 2) * 512 + 512],
                                              P[4 + o % 2][:, :]))
            pr.sec("vector", f_kev)

            def f_kst(E, r=r, o=o, n=n):
                E.w(kevt[n])
                for hh in range(2):
                    h = o * 2 + hh
                    t = E.dma(scrk_d[h, :, r * 512:(r + 1) * 512],
                              kevac[hh * 64:(hh + 1) * 64, (o % 2) * 512:(o % 2) * 512 + 512])
                kstt[n] = t
            pr.sec("sync", f_kst)

    wv = [None]

    def f_ldwv(E):
        E.w(kmmt[63])
        wv[0] = E.dma(sb3(wbuf[:, :], KCH), rearr_kp(wvT_d), sem="dG")
    pr.sec("gpsimd", f_ldwv)

    vmmt = {}; vevt = {}
    for r in range(8):
        ldx(r, kmmt.get(55 + 8 * r) if r < 2 else vmmt.get(r * 8 - 9))
        sl = r % 2
        for ts in range(4):
            for o in range(2):
                m = r * 8 + ts * 2 + o

                def f_vmm(E, r=r, ts=ts, o=o, m=m, sl=sl):
                    E.w(wv[0], xt[r], vevt.get(m - 2), *ctoks)
                    for k in range(KCH):
                        i = E.e.matmul(P[4 + m % 2][:, :],
                                       xln[:, sl * XW + k * 512 + ts * 128:sl * XW + k * 512 + (ts + 1) * 128],
                                       wbuf[:, k * D + o * 512: k * D + (o + 1) * 512],
                                       start=(k == 0), stop=(k == KCH - 1))
                    vmmt[m] = E.t(i)
                pr.sec("tensor", f_vmm)

                def f_vev(E, r=r, ts=ts, o=o, m=m):
                    E.w(vmmt[m])
                    rt = r * 4 + ts
                    for hh in range(8):
                        h = o * 8 + hh
                        t = E.t(E.e.tensor_copy(
                            vres[:, rt * HEADS * 65 + h * 65: rt * HEADS * 65 + h * 65 + 64],
                            P[4 + m % 2][:, hh * 64:(hh + 1) * 64]))
                    vevt[m] = t
                pr.sec("vector", f_vev)

    wqq = [None]

    def f_ldwq(E):
        E.w(vmmt[63])
        wqq[0] = E.dma(sb3(wbuf[:, :], KCH), rearr_kp(wqT_d), sem="dG")
    pr.sec("gpsimd", f_ldwq)

    qmmt = {}; qevt = {}
    for idx, r in enumerate([6, 7, 4, 5]):
        if idx >= 2:
            ldx(r, qmmt.get((idx - 2) * 8 + 7))
        sl = r % 2
        for o in range(KCH):
            p = idx * 8 + o

            def f_qmm(E, r=r, o=o, p=p, sl=sl):
                E.w(wqq[0], xt[r], qevt.get(p - 2))
                for k in range(KCH):
                    i = E.e.matmul(P[4 + p % 2][:, :],
                                   wbuf[:, k * D + o * 128: k * D + o * 128 + 128],
                                   xln[:, sl * XW + k * 512:sl * XW + (k + 1) * 512],
                                   start=(k == 0), stop=(k == KCH - 1))
                qmmt[p] = E.t(i)
            pr.sec("tensor", f_qmm)

            def f_qev(E, r=r, o=o, p=p):
                E.w(qmmt[p])
                qevt[p] = E.t(E.e.activation(qtb[:, o * TOK + (r - 4) * 512: o * TOK + (r - 3) * 512],
                                             P[4 + p % 2][:, :], AF.Copy, scale=0.125))
            pr.sec("scalar", f_qev)


def _atta_head(pr, h, scrk_d, ctoks):
    P = pr.ps
    kst = pr.sb["wbuf"][0:66, 0:NKV]
    qst = pr.sb["qst"]; qtb = pr.sb["qtb"]
    vres = pr.sb["vres"]; psb = pr.sb["psb"]; tri = pr.sb["tri"]
    oall = pr.sb["oall"]; osb = pr.sb["osb"]; mcol = pr.sb["mcol"]; lrow = pr.sb["lrow"]
    atmp = pr.sb["att_tmp"]
    ch, hh = h // 2, h % 2

    kld = [None]

    def f_ldk(E, h=h):
        E.w(self_tok(pr, "tensor"))
        kld[0] = E.dma(kst[0:64, :], scrk_d[h, :, :])
    pr.sec("sync", f_ldk)

    stg_fin = {}
    pvlast = {}
    normv_tok = {}
    normbc_tok = {}
    normap_tok = {}
    mtr_tok = {}
    banklast = {0: None, 1: None}

    def mk_units_A(g):
        gp = g % 2
        q0 = g * 512
        c0 = gp * 512
        units = []
        stgq = [None]

        def f_stgq(E, g=g, gp=gp, q0=q0, c0=c0, stgq=stgq):
            E.w(pvlast.get(g - 2))
            stgq[0] = E.t(E.e.tensor_copy(qst[0:64, c0:c0 + 512],
                                          qtb[hh * 64:hh * 64 + 64,
                                              ch * TOK + q0:ch * TOK + q0 + 512]))
        units.append(lambda f=f_stgq: pr.sec("vector", f))

        mt = [None] * 4
        for qs in range(4):
            for ks in range(5 + g):
                sm = [None]
                bk = ks % 2

                def f_smm(E, g=g, gp=gp, c0=c0, qs=qs, ks=ks, sm=sm, bk=bk, stgq=stgq):
                    E.w(kld[0], *ctoks)
                    E.w(stgq[0], stg_fin.get(g - 1))
                    E.w(banklast[bk])
                    i = E.e.matmul(P[bk][:, :],
                                   qst[0:64, c0 + qs * 128:c0 + (qs + 1) * 128],
                                   kst[0:64, ks * 512:(ks + 1) * 512],
                                   start=True, stop=True)
                    sm[0] = E.t(i)

                def f_red(E, g=g, gp=gp, qs=qs, ks=ks, sm=sm, bk=bk, mt=mt):
                    E.w(sm[0])
                    if ks == 0 and qs == 0:
                        E.w(mtr_tok.get(g - 2))
                    if ks == 0:
                        t = E.t(E.e.reduce_max(mcol[:, gp * 4 + qs:gp * 4 + qs + 1],
                                               P[bk][:, :], axis=AX.X))
                    else:
                        t1 = E.t(E.e.reduce_max(atmp[:, gp:gp + 1], P[bk][:, :], axis=AX.X))
                        E.w(t1)
                        t = E.t(E.e.tensor_max(mcol[:, gp * 4 + qs:gp * 4 + qs + 1],
                                               mcol[:, gp * 4 + qs:gp * 4 + qs + 1],
                                               atmp[:, gp:gp + 1]))
                    banklast[bk] = t
                    if ks == 4 + g:
                        mt[qs] = t

                units.append(lambda f=f_smm, f2=f_red: (pr.sec("tensor", f),
                                                        pr.sec("vector", f2)))

        stg = [None]

        def f_mtr(E, g=g, gp=gp, mt=mt, stg=stg):
            E.w(*mt)
            E.w(stg_fin.get(g - 1))
            for qs in range(4):
                i = E.e.transpose(P[1][0:1, qs * 128:(qs + 1) * 128],
                                  mcol[:, gp * 4 + qs:gp * 4 + qs + 1], pr.sb["eye"][:, :])
            stg[0] = E.t(i)
            mtr_tok[g] = stg[0]

        def f_stg(E, g=g, c0=c0, stg=stg):
            E.w(stg[0])
            t0 = E.t(E.e.memset(qst[64:66, c0:c0 + 512], 1.0))
            E.w(t0)
            t2 = E.t(E.e.tensor_copy(qst[64:65, c0:c0 + 512], P[1][0:1, :]))
            E.w(t2)
            stg_fin[g] = t2

        units.append(lambda f=f_mtr, f2=f_stg: (pr.sec("tensor", f),
                                                pr.sec("vector", f2)))
        return units

    def mk_units_B(g):
        gp = g % 2
        nkt = 20 + 4 * g
        q0 = g * 512
        c0 = gp * 512
        units = []
        pv_tok = {}
        exp_tok = {}
        pend_pv = []

        for kt in range(nkt):
            et = [None]
            stb = 2 + kt % 4
            pslot = kt % 4

            def f_stmm(E, g=g, c0=c0, kt=kt, et=et, stb=stb):
                E.w(stg_fin[g])
                E.w(exp_tok.get(kt - 4))
                if kt == 0:
                    E.w(normap_tok.get(g - 1))
                i = E.e.matmul(P[stb][:, :],
                               kst[:, kt * 128:(kt + 1) * 128],
                               qst[:, c0:c0 + 512], start=True, stop=True)
                et[0] = E.t(i)

            def f_exp(E, g=g, kt=kt, et=et, stb=stb, pslot=pslot):
                E.w(et[0])
                E.w(pv_tok.get(kt - 4))
                t = E.t(E.e.activation(psb[:, pslot * 512:(pslot + 1) * 512],
                                       P[stb][:, :], AF.Exp))
                exp_tok[kt] = t
                et[0] = t

            if kt >= nkt - 4:
                j = kt - (nkt - 4)

                def f_tri(E, g=g, kt=kt, j=j, et=et, pslot=pslot):
                    E.w(et[0])
                    et[0] = E.t(E.e.tensor_mul(psb[:, pslot * 512:(pslot + 1) * 512],
                                               psb[:, pslot * 512:(pslot + 1) * 512],
                                               tri[:, j * 512:(j + 1) * 512]))
            else:
                f_tri = None

            def f_pv(E, g=g, kt=kt, nkt=nkt, et=et, pslot=pslot):
                E.w(et[0])
                if kt == 0:
                    E.w(normv_tok.get(g - 1))
                i = E.e.matmul(P[6][0:65, :],
                               vres[:, kt * HEADS * 65 + h * 65: kt * HEADS * 65 + h * 65 + 65],
                               psb[:, pslot * 512:(pslot + 1) * 512],
                               start=(kt == 0), stop=(kt == nkt - 1))
                pv_tok[kt] = E.t(i)
                if kt == nkt - 1:
                    pvlast[g] = pv_tok[kt]

            def emit(f1=f_stmm, f2=f_exp, f3=f_tri, f4s=pend_pv, f4=f_pv, kt=kt):
                pr.sec("tensor", f1)
                pr.sec("scalar", f2)
                if f3 is not None:
                    pr.sec("vector", f3)
                f4s.append(f4)
                if kt >= 2:
                    pr.sec("tensor", f4s.pop(0))
            units.append(emit)

        def flush(f4s=pend_pv):
            while f4s:
                pr.sec("tensor", f4s.pop(0))
        units.append(flush)
        return units, pv_tok, nkt

    def emit_norm(g, pv_tok, nkt):
        gp = g % 2
        q0 = g * 512
        nt_ = [None]

        def f_norm_v(E, g=g, nt_=nt_):
            E.w(pv_tok[nkt - 1])
            E.w(normbc_tok.get(g - 1))
            t1 = E.t(E.e.reciprocal(lrow[:, :], P[6][64:65, :]))
            E.w(t1)
            t2 = E.t(E.e.tensor_copy(osb[:, :], P[6][0:64, :]))
            nt_[0] = t2
            normv_tok[g] = t2
        pr.sec("vector", f_norm_v)

        def f_norm_bc(E, g=g, nt_=nt_):
            E.w(nt_[0])
            i = E.e.matmul(P[2][0:64, :], pr.sb["ones1r"][:, 0:64], lrow[:, :],
                           start=True, stop=True)
            nt_[0] = E.t(i)
            normbc_tok[g] = nt_[0]
        pr.sec("tensor", f_norm_bc)

        def f_norm_ap(E, g=g, q0=q0, nt_=nt_):
            E.w(nt_[0])
            normap_tok[g] = E.t(E.e.tensor_mul(oall[hh * 64:hh * 64 + 64,
                                                    ch * TOK + q0:ch * TOK + q0 + 512],
                                               osb[:, :], P[2][0:64, :]))
        pr.sec("vector", f_norm_ap)

    for u in mk_units_A(0):
        u()
    for g in range(4):
        unitsB, pv_tok, nkt = mk_units_B(g)
        unitsA = mk_units_A(g + 1) if g < 3 else []
        na, nb = len(unitsA), len(unitsB)
        ia = 0
        for ib, ub in enumerate(unitsB):
            ub()
            want = (ib + 1) * na // nb
            while ia < want:
                unitsA[ia]()
                ia += 1
        while ia < na:
            unitsA[ia]()
            ia += 1
        emit_norm(g, pv_tok, nkt)


def phase_perf(pr, io, ctoks0):
    pr.push_scope()
    ctoks = list(ctoks0)
    nc = pr.nc
    sid = pr._scope_id
    xT_d = io["xT"]
    wqT_d = io["wqT"]; wkT_d = io["wkT"]; wvT_d = io["wvT"]; woT_d = io["woT"]
    projT_d = io["projT"]; bor_d = io["bor"]
    yT_d = io["yT"]
    scrk_d = pr.dram(f"scrk{sid}", [HEADS, 64, NKV], BF16).ap()
    xsh_h = pr.dram(f"xsh{sid}", [2, D, TOK], BF16, shared=True)
    xloc_d = pr.dram(f"xloc{sid}", [D, TOK], BF16).ap()
    xkv_d = pr.dram(f"xkv{sid}", [D, NKV], BF16).ap()

    xT = pr.sbuf("sxT", [128, KCH * 512], F32)
    xln = pr.sbuf("xln", [128, 2 * KCH * 512], BF16)
    wbuf = pr.sbuf("wbuf", [128, KCH * D], BF16)
    vres = pr.sbuf("vres", [128, 32 * HEADS * 65], BF16)
    qtb = pr.sbuf("qtb", [128, KCH * TOK], BF16)
    prj = pr.sbuf("prj", [64, HEADS * PM], BF16)
    ones64b = pr.sbuf("ones64b", [64, 1], BF16)
    bias4 = pr.sbuf("bias4", [128, 24], F32)
    phik = pr.sbuf("phik", [128, 2 * 512], BF16)
    phiqt = pr.sbuf("phiqt", [128, 2 * 512], BF16)
    kvsb = pr.sbuf("kvsb", [128, 2 * 65], BF16)
    oall = pr.sbuf("oall", [128, KCH * TOK], BF16)
    outsb = pr.sbuf("outsb", [128, 512], F32)
    bo = pr.sbuf("bo", [128, KCH], F32)
    pr.sbuf("kevb", [128, 2 * 512], BF16)
    alloc_ln(pr, "att")

    def f_init(E):
        ctoks.append(E.dma(prj[:, :].rearrange("d (h m) -> d h m", h=HEADS),
                           projT_d.rearrange("h d m -> d h m"), sem="dG"))
        ctoks.append(E.t(E.e.memset(ones64b[:, :], 1.0)))
        ctoks.append(E.t(E.e.memset(vres[:, :], 1.0)))
    pr.sec("gpsimd", f_init)

    def f_init2(E):
        ctoks.append(E.dma(bo[:, :], bor_d))
    pr.sec("sync", f_init2)

    P = pr.ps
    for r in range(4):
        ldt = [None]

        def f_ldo(E, r=r):
            E.w(("tensor", pr.cnt["tensor"]), ("scalar", pr.cnt["scalar"]),
                ("vector", pr.cnt["vector"]), ("dS", pr.cnt["dS"]))
            ldt[0] = E.dma(sb3(xT[:, :], KCH), rearr_kp(xT_d)[:, :, r * 512:(r + 1) * 512])
        pr.sec("sync", f_ldo)

        lt = emit_ln(pr, xT, xln[:, 0:KCH * 512], 512, [ldt[0]] + ctoks, "att")

        def f_wsh(E, r=r, lt=lt):
            E.w(lt[-1])
            E.dma(rearr_kp(xloc_d)[:, :, r * 512:(r + 1) * 512], sb3(xln[:, 0:KCH * 512], KCH))
        pr.sec("sync", f_wsh)

    pr.fence()

    def f_xout(par, dd):
        dd.wait(("dS", pr.cnt["dS"]))
        dd(xsh_h.ap()[par, :, :], xloc_d)
    pr.pswitch(f_xout)
    pr.core_barrier()

    def f_xin(par, dd):
        dd(xkv_d[:, 0:TOK], xsh_h.ap()[1 - par, :, :])
        dd(xkv_d[:, TOK:NKV], xsh_h.ap()[par, :, :])
    gtok = pr.pswitch(f_xin)

    _perf_proj(pr, xkv_d, gtok, wqT_d, wkT_d, wvT_d, scrk_d, ctoks)

    pr.fence()

    for h in range(HEADS):
        _perf_head(pr, h, scrk_d, ctoks)

    pr.fence()

    wo_tok = [None]

    def f_ldwo(E):
        wo_tok[0] = E.dma(sb3(wbuf[:, :], KCH), rearr_kp(woT_d), sem="dG")
    pr.sec("gpsimd", f_ldwo)

    for g in range(4):
        xg = [None]

        def f_ldxg(E, g=g):
            E.w(("dS", pr.cnt["dS"]))
            src = rearr_kp(xT_d)
            xg[0] = E.dma(sb3(xT[:, :], KCH), src[:, :, g * 512:(g + 1) * 512])
        pr.sec("sync", f_ldxg)

        for o in range(KCH):
            om = [None]
            ev = [None]

            def f_omm(E, g=g, o=o, om=om):
                E.w(wo_tok[0], self_tok(pr, "vector"))
                for k in range(KCH):
                    i = E.e.matmul(P[4 + o % 2][:, :],
                                   wbuf[:, k * D + o * 128: k * D + o * 128 + 128],
                                   pr.sb["oall"][:, k * TOK + g * 512: k * TOK + (g + 1) * 512],
                                   start=(k == 0), stop=(k == KCH - 1))
                om[0] = E.t(i)
            pr.sec("tensor", f_omm)

            def f_obias(E, g=g, o=o, om=om, ev=ev):
                E.w(om[0])
                E.w(("dS", pr.cnt["dS"]))
                ev[0] = E.t(E.e.activation(outsb[:, :], P[4 + o % 2][:, :], AF.Identity,
                                           bias=bo[:, o:o + 1]))
            pr.sec("scalar", f_obias)

            def f_oadd(E, g=g, o=o, ev=ev):
                E.w(ev[0], xg[0])
                E.t(E.e.tensor_tensor(outsb[:, :], outsb[:, :],
                                      xT[:, o * 512:(o + 1) * 512], ALU.add))
            pr.sec("vector", f_oadd)

            def f_ost(E, g=g, o=o):
                E.w(self_tok(pr, "vector"))
                dst = rearr_kp(yT_d)
                E.dma(dst[:, o, g * 512:(g + 1) * 512], outsb[:, :])
            pr.sec("sync", f_ost)

    pr.fence()
    pr.pop_scope()


def _perf_proj(pr, xkv_d, gtok, wqT_d, wkT_d, wvT_d, scrk_d, ctoks):
    P = pr.ps
    xln = pr.sb["xln"]; wbuf = pr.sb["wbuf"]
    vres = pr.sb["vres"]; qtb = pr.sb["qtb"]; kevb = pr.sb["kevb"]
    XW = KCH * 512
    pr.fence()
    wt = [None]
    xt = {}

    def f_ldwk(E):
        wt[0] = E.dma(sb3(wbuf[:, :], KCH), rearr_kp(wkT_d), sem="dG")
    pr.sec("gpsimd", f_ldwk)

    def ldx(r, dep):
        sl = r % 2

        def f_ldx(E, r=r, sl=sl, dep=dep):
            E.w(gtok, dep)
            xt[r] = E.dma(sb3(xln[:, sl * XW:(sl + 1) * XW], KCH),
                          rearr_kp(xkv_d)[:, :, r * 512:(r + 1) * 512])
        pr.sec("sync", f_ldx)

    kmmt = {}; kevt = {}; kstt = {}
    for r in range(8):
        ldx(r, kmmt.get(r * 8 - 9))
        sl = r % 2
        for o in range(KCH):
            n = r * 8 + o

            def f_kmm(E, r=r, o=o, n=n, sl=sl):
                E.w(wt[0], xt[r], kevt.get(n - 2))
                for k in range(KCH):
                    i = E.e.matmul(P[4 + o % 2][:, :],
                                   wbuf[:, k * D + o * 128: k * D + o * 128 + 128],
                                   xln[:, sl * XW + k * 512:sl * XW + (k + 1) * 512],
                                   start=(k == 0), stop=(k == KCH - 1))
                kmmt[n] = E.t(i)
            pr.sec("tensor", f_kmm)

            def f_kev(E, o=o, n=n):
                E.w(kmmt[n], kstt.get(n - 2))
                kevt[n] = E.t(E.e.activation(kevb[:, (o % 2) * 512:(o % 2) * 512 + 512],
                                             P[4 + o % 2][:, :], AF.Copy, scale=0.125))
            pr.sec("scalar", f_kev)

            def f_kst(E, r=r, o=o, n=n):
                E.w(kevt[n])
                for hh in range(2):
                    h = o * 2 + hh
                    t = E.dma(scrk_d[h, :, r * 512:(r + 1) * 512],
                              kevb[hh * 64:(hh + 1) * 64, (o % 2) * 512:(o % 2) * 512 + 512])
                kstt[n] = t
            pr.sec("sync", f_kst)

    wv = [None]

    def f_ldwv(E):
        E.w(kmmt[63])
        wv[0] = E.dma(sb3(wbuf[:, :], KCH), rearr_kp(wvT_d), sem="dG")
    pr.sec("gpsimd", f_ldwv)

    vmmt = {}; vevt = {}
    for r in range(8):
        ldx(r, kmmt.get(55 + 8 * r) if r < 2 else vmmt.get(r * 8 - 9))
        sl = r % 2
        for ts in range(4):
            for o in range(2):
                m = r * 8 + ts * 2 + o

                def f_vmm(E, r=r, ts=ts, o=o, m=m, sl=sl):
                    E.w(wv[0], xt[r], vevt.get(m - 2), *ctoks)
                    for k in range(KCH):
                        i = E.e.matmul(P[4 + m % 2][:, :],
                                       xln[:, sl * XW + k * 512 + ts * 128:sl * XW + k * 512 + (ts + 1) * 128],
                                       wbuf[:, k * D + o * 512: k * D + (o + 1) * 512],
                                       start=(k == 0), stop=(k == KCH - 1))
                    vmmt[m] = E.t(i)
                pr.sec("tensor", f_vmm)

                def f_vev(E, r=r, ts=ts, o=o, m=m):
                    E.w(vmmt[m])
                    rt = r * 4 + ts
                    for hh in range(8):
                        h = o * 8 + hh
                        t = E.t(E.e.tensor_copy(
                            vres[:, rt * HEADS * 65 + h * 65: rt * HEADS * 65 + h * 65 + 64],
                            P[4 + m % 2][:, hh * 64:(hh + 1) * 64]))
                    vevt[m] = t
                pr.sec("vector", f_vev)

    wqq = [None]

    def f_ldwq(E):
        E.w(vmmt[63])
        wqq[0] = E.dma(sb3(wbuf[:, :], KCH), rearr_kp(wqT_d), sem="dG")
    pr.sec("gpsimd", f_ldwq)

    qmmt = {}; qevt = {}
    for idx, r in enumerate([6, 7, 4, 5]):
        if idx >= 2:
            ldx(r, qmmt.get((idx - 2) * 8 + 7))
        sl = r % 2
        for o in range(KCH):
            p = idx * 8 + o

            def f_qmm(E, r=r, o=o, p=p, sl=sl):
                E.w(wqq[0], xt[r], qevt.get(p - 2))
                for k in range(KCH):
                    i = E.e.matmul(P[4 + p % 2][:, :],
                                   wbuf[:, k * D + o * 128: k * D + o * 128 + 128],
                                   xln[:, sl * XW + k * 512:sl * XW + (k + 1) * 512],
                                   start=(k == 0), stop=(k == KCH - 1))
                qmmt[p] = E.t(i)
            pr.sec("tensor", f_qmm)

            def f_qev(E, r=r, o=o, p=p):
                E.w(qmmt[p])
                qevt[p] = E.t(E.e.activation(qtb[:, o * TOK + (r - 4) * 512: o * TOK + (r - 3) * 512],
                                             P[4 + p % 2][:, :], AF.Copy, scale=0.125))
            pr.sec("scalar", f_qev)


def _perf_head(pr, h, scrk_d, ctoks):
    P = pr.ps
    kst2 = pr.sb["wbuf"][0:64, 1024:1024 + NKV]
    qhst = pr.sb["wbuf"][0:64, 1024 + NKV:1024 + NKV + TOK]
    qtb = pr.sb["qtb"]
    vres = pr.sb["vres"]; phik = pr.sb["phik"]
    phiqt = pr.sb["phiqt"]; kvsb = pr.sb["kvsb"]; oall = pr.sb["oall"]
    drow = pr.sb["att_rows"]
    numsb = pr.sb["wbuf"][0:64, 512:1024]
    prj = pr.sb["prj"]; bias4 = pr.sb["bias4"]
    sqs = pr.sb["wbuf"][0:64, 0:512]
    ones64b = pr.sb["ones64b"]
    ch, hh = h // 2, h % 2
    kld = [None]

    def f_ldk(E, h=h):
        E.w(self_tok(pr, "tensor"))
        kld[0] = E.dma(kst2[:, :], scrk_d[h, :, :])
    pr.sec("sync", f_ldk)

    def f_stgq(E, h=h):
        E.w(self_tok(pr, "tensor"), kld[0])
        E.t(E.e.tensor_copy(qhst[:, :], qtb[hh * 64:hh * 64 + 64,
                                            ch * TOK:(ch + 1) * TOK]))
    pr.sec("vector", f_stgq)
    stgq_tok = self_tok(pr, "vector")

    xw_t = {}; sq_t = {}; nq_t = {}; max_t = {}; exp_t = {}; eps_t = {}
    kv_t = {}

    def phi_pair(j, src, xwb, nqb, cb, qphase):
        p = j % 2

        def f_xw(E, j=j, p=p):
            E.w(kld[0] if not qphase else stgq_tok, *ctoks)
            E.w(exp_t.get(j - 2))
            if j < 2 and qphase:
                E.w(self_tok(pr, "vector"))
            if j < 2 and not qphase:
                E.w(self_tok(pr, "scalar"))
            for s in range(2):
                i = E.e.matmul(P[xwb + p][:, s * 256:(s + 1) * 256],
                               src[0:64, (j * 2 + s) * 128:(j * 2 + s + 1) * 128],
                               prj[:, h * PM:(h + 1) * PM], start=True, stop=True)
            xw_t[j] = E.t(i)
        pr.sec("tensor", f_xw)

        def f_sq(E, j=j, p=p):
            E.w(kld[0] if not qphase else stgq_tok)
            E.w(nq_t.get(j - 2))
            sq_t[j] = E.t(E.e.activation(sqs[:, p * 256:(p + 1) * 256],
                                         src[0:64, j * 256:(j + 1) * 256], AF.Square))
        pr.sec("scalar", f_sq)

        def f_nq(E, j=j, p=p):
            E.w(sq_t[j], max_t.get(j - 2))
            for s in range(2):
                i = E.e.matmul(P[nqb + p][:, s:s + 1],
                               sqs[:, p * 256 + s * 128:p * 256 + (s + 1) * 128],
                               ones64b[:, :], start=True, stop=True)
            nq_t[j] = E.t(i)
        pr.sec("tensor", f_nq)

        def f_max(E, j=j, p=p):
            E.w(nq_t[j], exp_t.get(j - 2))
            for s in range(2):
                c = cb + p * 6 + s * 3
                t1 = E.t(E.e.reduce_max(bias4[:, c:c + 1],
                                        P[xwb + p][:, s * 256:(s + 1) * 256], axis=AX.X))
                E.w(t1)
                t2 = E.t(E.e.tensor_copy(bias4[:, c + 1:c + 2], P[nqb + p][:, s:s + 1]))
                E.w(t2)
                t3 = E.t(E.e.scalar_tensor_tensor(bias4[:, c + 2:c + 3],
                                                  bias4[:, c + 1:c + 2], -0.5,
                                                  bias4[:, c:c + 1],
                                                  ALU.mult, ALU.subtract))
            max_t[j] = t3
        pr.sec("vector", f_max)

        def f_exp(E, j=j, p=p):
            E.w(max_t[j])
            E.w((kv_t if not qphase else tr_t).get(j - 2))
            for s in range(2):
                c = cb + p * 6 + s * 3
                t = E.t(E.e.activation(phik[:, p * 512 + s * 256:p * 512 + (s + 1) * 256],
                                       P[xwb + p][:, s * 256:(s + 1) * 256], AF.Exp,
                                       bias=bias4[:, c + 2:c + 3]))
            exp_t[j] = t
        pr.sec("scalar", f_exp)

        def f_eps(E, j=j, p=p):
            E.w(exp_t[j])
            eps_t[j] = E.t(E.e.tensor_scalar(phik[:, p * 512:(p + 1) * 512],
                                             phik[:, p * 512:(p + 1) * 512],
                                             1e-6, None, ALU.add))
        pr.sec("vector", f_eps)

    pend = []
    for j in range(16):
        p = j % 2
        phi_pair(j, kst2, 4, 2, 0, False)

        def f_kv(E, j=j, p=p):
            E.w(eps_t[j], *ctoks)
            if j == 0:
                E.w(self_tok(pr, "vector"))
            for s in range(2):
                rt = j * 2 + s
                for mc in range(2):
                    i = E.e.matmul(P[6 + mc][:, 0:65],
                                   phik[:, p * 512 + s * 256 + mc * 128:
                                        p * 512 + s * 256 + (mc + 1) * 128],
                                   vres[:, rt * HEADS * 65 + h * 65: rt * HEADS * 65 + (h + 1) * 65],
                                   start=(j == 0 and s == 0), stop=(j == 15 and s == 1))
            kv_t[j] = E.t(i)
        pend.append(f_kv)
        if j >= 1:
            pr.sec("tensor", pend.pop(0))
    while pend:
        pr.sec("tensor", pend.pop(0))

    def f_kvev(E, h=h):
        E.w(kv_t[15])
        E.w(self_tok(pr, "tensor"))
        t1 = E.t(E.e.tensor_copy(kvsb[:, 0:65], P[6][:, 0:65]))
        E.w(t1)
        E.t(E.e.tensor_copy(kvsb[:, 65:130], P[7][:, 0:65]))
    pr.sec("vector", f_kvev)
    kvev_tok = self_tok(pr, "vector")

    xw_t = {}; sq_t = {}; nq_t = {}; max_t = {}; exp_t = {}; eps_t = {}
    tr_t = {}; trc_t = {}; num_t = {}; den_t = {}; dbc_t = {}; oap_t = {}

    pendq = []
    for j in range(8):
        p = j % 2
        phi_pair(j, qhst, 2, 0, 12, True)

        def f_tr(E, j=j, p=p):
            E.w(eps_t[j], trc_t.get(j - 2))
            for s in range(2):
                for mc in range(2):
                    i = E.e.transpose(
                        P[6].ap().bitcast(BF16)[0:128,
                                                (p * 2 + s) * 256 + mc * 128:
                                                (p * 2 + s) * 256 + (mc + 1) * 128],
                        phik[:, p * 512 + s * 256 + mc * 128:
                             p * 512 + s * 256 + (mc + 1) * 128],
                        pr.sb["eyeb"][:, :])
            tr_t[j] = E.t(i)
        pendq.append(("tensor", f_tr))

        def f_trc(E, j=j, p=p):
            E.w(tr_t[j], num_t.get(j - 2))
            trc_t[j] = E.t(E.e.tensor_copy(
                phiqt[:, p * 512:(p + 1) * 512],
                P[6].ap().bitcast(BF16)[0:128, p * 512:(p + 1) * 512]))
        pendq.append(("vector", f_trc))

        def f_num(E, j=j, p=p):
            E.w(trc_t[j], den_t.get(j - 2), kvev_tok)
            for s in range(2):
                for mc in range(2):
                    i = E.e.matmul(P[7][0:65, (p * 2 + s) * 128:(p * 2 + s + 1) * 128],
                                   kvsb[:, mc * 65:(mc + 1) * 65],
                                   phiqt[:, p * 512 + s * 256 + mc * 128:
                                         p * 512 + s * 256 + (mc + 1) * 128],
                                   start=(mc == 0), stop=(mc == 1))
            num_t[j] = E.t(i)
        pendq.append(("tensor", f_num))

        def f_den(E, j=j, p=p):
            E.w(num_t[j], dbc_t.get(j - 2))
            for s in range(2):
                c = (p * 2 + s) * 128
                t1 = E.t(E.e.tensor_scalar(drow[:, c:c + 128],
                                           P[7][64:65, c:c + 128], 1e-6, None, ALU.add))
                E.w(t1)
                t2 = E.t(E.e.reciprocal(drow[:, c:c + 128], drow[:, c:c + 128]))
                E.w(t2)
                t3 = E.t(E.e.tensor_copy(numsb[:, c:c + 128], P[7][0:64, c:c + 128]))
            den_t[j] = t3
        pendq.append(("vector", f_den))

        def f_dbc(E, j=j, p=p):
            E.w(den_t[j], oap_t.get(j - 2))
            for s in range(2):
                c = (p * 2 + s) * 128
                i = E.e.matmul(P[0 + p][0:64, 128 + s * 128:256 + s * 128],
                               pr.sb["ones1r"][:, 0:64], drow[:, c:c + 128],
                               start=True, stop=True)
            dbc_t[j] = E.t(i)
        pendq.append(("tensor", f_dbc))

        def f_oap(E, j=j, p=p):
            E.w(dbc_t[j])
            for s in range(2):
                c = (p * 2 + s) * 128
                rt = j * 2 + s
                t = E.t(E.e.tensor_mul(oall[hh * 64:hh * 64 + 64,
                                            ch * TOK + rt * 128: ch * TOK + (rt + 1) * 128],
                                       numsb[:, c:c + 128],
                                       P[0 + p][0:64, 128 + s * 128:256 + s * 128]))
            oap_t[j] = t
        pendq.append(("vector", f_oap))
        if True:
            for _ in range(6):
                en, fn = pendq.pop(0)
                pr.sec(en, fn)
    while pendq:
        en, fn = pendq.pop(0)
        pr.sec(en, fn)


def phase_gla(pr, io, ctoks0):
    pr.push_scope()
    ctoks = list(ctoks0)
    nc = pr.nc
    sid = pr._scope_id
    xT_d = io["xT"]
    wqn_d = io["wqn"]; wkn_d = io["wkn"]; wvn_d = io["wvn"]; wan_d = io["wan"]
    bar_d = io["bar"]; gr_d = io["gr"]; br_d = io["br"]
    res_d = pr.dram(f"glares{sid}", [D, TOK], BF16).ap()
    qsh_h = pr.dram(f"glaq{sid}", [GH, 64, NKV], BF16, shared=True)
    ksh_h = pr.dram(f"glak{sid}", [GH, 64, NKV], BF16, shared=True)
    lash_h = pr.dram(f"glala{sid}", [GH, 64, NKV], F32, shared=True)
    ktsh_h = pr.dram(f"glakt{sid}", [GH, NKV, 64], BF16, shared=True)
    vtsh_h = pr.dram(f"glavt{sid}", [GH, NKV, 64], BF16, shared=True)
    osh_h = pr.dram(f"glao{sid}", [GH, 64, NKV], BF16, shared=True)
    qloc = pr.dram(f"glaql{sid}", [GH, 64, TOK], BF16).ap()
    kloc = pr.dram(f"glakl{sid}", [GH, 64, TOK], BF16).ap()
    laloc = pr.dram(f"glalal{sid}", [GH, 64, TOK], F32).ap()
    ktloc = pr.dram(f"glaktl{sid}", [GH, TOK, 64], BF16).ap()
    vtloc = pr.dram(f"glavtl{sid}", [GH, TOK, 64], BF16).ap()
    qun = pr.dram(f"glaqu{sid}", [4, 64, NKV], BF16).ap()
    kun = pr.dram(f"glaku{sid}", [4, 64, NKV], BF16).ap()
    laun = pr.dram(f"glalau{sid}", [4, 64, NKV], F32).ap()
    ktun = pr.dram(f"glaktu{sid}", [4, NKV, 64], BF16).ap()
    vtun = pr.dram(f"glavtu{sid}", [4, NKV, 64], BF16).ap()
    oun = pr.dram(f"glaou{sid}", [4, 64, NKV], BF16).ap()
    oloc = pr.dram(f"glaol{sid}", [GH, 64, TOK], BF16).ap()

    xT = pr.sbuf("sxT", [128, KCH * 512], F32)
    ln1 = pr.sbuf("ln1", [128, KCH * 512], F32)
    ln1b = pr.sbuf("ln1b", [128, KCH * 512], BF16)
    xn = pr.sbuf("xn", [128, KCH * 512], BF16)
    wbuf = pr.sbuf("wbuf", [128, KCH * DI], BF16)
    gam = pr.sbuf("gam", [128, KCH], F32)
    bet = pr.sbuf("bet", [128, KCH], F32)
    bacol = pr.sbuf("bacol", [128, 4], F32)
    qf = pr.sbuf("qf", [128, 4 * 512], BF16)
    kf = pr.sbuf("kf", [128, 4 * 512], BF16)
    vf = pr.sbuf("vf", [128, 4 * 512], BF16)
    af = pr.sbuf("af", [128, 4 * 512], F32)
    ksq = pr.sbuf("ksq", [128, 512], BF16)
    nrow = pr.sbuf("nrow", [1, 2 * 512], F32)
    ktn = pr.sbuf("ktn", [128, 512], BF16)
    alloc_ln(pr, "att")

    def f_i2(E):
        ctoks.append(E.dma(gam[:, :], gr_d))
        ctoks.append(E.dma(bet[:, :], br_d))
        ctoks.append(E.dma(bacol[:, :], bar_d))
    pr.sec("sync", f_i2)

    P = pr.ps
    for r in range(TOK // 512):
        _glaq_tile(pr, r, xT_d, wqn_d, wkn_d, wvn_d, wan_d,
                   res_d, qloc, kloc, laloc, ktloc, vtloc, ctoks)

    pr.fence()

    def f_out(par, dd):
        dd.wait(("dS", pr.cnt["dS"]))
        c0 = par * TOK
        dd(qsh_h.ap()[:, :, c0:c0 + TOK], qloc)
        dd(ksh_h.ap()[:, :, c0:c0 + TOK], kloc)
        dd(lash_h.ap()[:, :, c0:c0 + TOK], laloc)
        dd(ktsh_h.ap()[:, c0:c0 + TOK, :], ktloc)
        dd(vtsh_h.ap()[:, c0:c0 + TOK, :], vtloc)
    pr.pswitch(f_out)
    pr.core_barrier()

    def f_in(par, dd):
        h0 = par * 4
        dd(qun, qsh_h.ap()[h0:h0 + 4, :, :])
        dd(kun, ksh_h.ap()[h0:h0 + 4, :, :])
        dd(laun, lash_h.ap()[h0:h0 + 4, :, :])
        dd(ktun, ktsh_h.ap()[h0:h0 + 4, :, :])
        dd(vtun, vtsh_h.ap()[h0:h0 + 4, :, :])
    gtok = pr.pswitch(f_in)

    pr.pop_scope()
    pr.push_scope()
    _gla_scan_phase(pr, qun, kun, laun, ktun, vtun, oun, gtok, ctoks)
    pr.fence()

    def f_oout(par, dd):
        dd.wait(("dS", pr.cnt["dS"]))
        dd(osh_h.ap()[par * 4:par * 4 + 4, :, :], oun)
    pr.pswitch(f_oout)
    pr.core_barrier()

    def f_oin(par, dd):
        c0 = par * TOK
        dd(oloc, osh_h.ap()[:, :, c0:c0 + TOK])
    gtok2 = pr.pswitch(f_oin)

    pr.pop_scope()
    pr.push_scope()
    xT2 = pr.sbuf("sxT", [128, KCH * 512], F32)
    _gla_out_phase(pr, io, res_d, oloc, gtok2, ctoks)
    pr.fence()
    pr.pop_scope()


def _glaq_tile(pr, r, xT_d, wqn_d, wkn_d, wvn_d, wan_d,
               res_d, qloc, kloc, laloc, ktloc, vtloc, ctoks):
    P = pr.ps
    xT = pr.sb["sxT"]; ln1 = pr.sb["ln1"]; ln1b = pr.sb["ln1b"]; xn = pr.sb["xn"]
    wbuf = pr.sb["wbuf"]; qf = pr.sb["qf"]; kf = pr.sb["kf"]; vf = pr.sb["vf"]
    af = pr.sb["af"]; ksq = pr.sb["ksq"]; nrow = pr.sb["nrow"]; ktn = pr.sb["ktn"]
    pr.fence()
    ld = [None]

    def f_ld(E, r=r):
        src = rearr_kp(xT_d)
        ld[0] = E.dma(sb3(xT[:, :], KCH), src[:, :, r * 512:(r + 1) * 512])
    pr.sec("sync", f_ld)

    l1 = emit_ln(pr, xT, ln1, 512, [ld[0]] + ctoks, "att")
    def f_cp(E, r=r):
        E.w(l1[-1])
        for k in range(KCH):
            E.t(E.e.tensor_copy(ln1b[:, k * 512:(k + 1) * 512], ln1[:, k * 512:(k + 1) * 512]))
    pr.sec("gpsimd", f_cp)

    def f_str(E, r=r):
        E.w(self_tok(pr, "gpsimd"))
        dst = rearr_kp(res_d)
        E.dma(dst[:, :, r * 512:(r + 1) * 512], sb3(ln1b[:, :], KCH))
    pr.sec("sync", f_str)

    l2 = emit_ln(pr, ln1, xn, 512, [l1[-1]], "att",
                 gamma_beta=(pr.sb["gam"], pr.sb["bet"]))

    for (wd, dstb, isa) in ((wqn_d, qf, 0), (wkn_d, kf, 0), (wvn_d, vf, 0), (wan_d, af, 1)):
        wt = [None]

        def f_ldw(E, wd=wd):
            E.w(self_tok(pr, "tensor"))
            wt[0] = E.dma(sb3(wbuf[:, 0:KCH * DI], KCH),
                          wd.rearrange("(k p) n -> p k n", p=128), sem="dG")
        pr.sec("gpsimd", f_ldw)

        for o in range(4):
            pm = [None]

            def f_mm(E, r=r, o=o, wt=wt, pm=pm):
                E.w(wt[0], l2[-1])
                E.w(self_tok(pr, "scalar"), self_tok(pr, "vector"))
                for k in range(KCH):
                    i = E.e.matmul(P[4 + o % 2][:, :],
                                   wbuf[:, k * DI + o * 128: k * DI + o * 128 + 128],
                                   xn[:, k * 512:(k + 1) * 512],
                                   start=(k == 0), stop=(k == KCH - 1))
                pm[0] = E.t(i)
            pr.sec("tensor", f_mm)

            if isa == 0:
                def f_ev(E, o=o, pm=pm, dstb=dstb):
                    E.w(pm[0])
                    E.t(E.e.activation(dstb[:, o * 512:(o + 1) * 512],
                                       P[4 + o % 2][:, :], AF.Copy))
                pr.sec("scalar", f_ev)
            else:
                sg = [None]

                def f_sig(E, o=o, pm=pm, sg=sg):
                    E.w(pm[0])
                    t1 = E.t(E.e.activation(pr.sb["att_tmp"][:, :], P[4 + o % 2][:, :],
                                            AF.Sigmoid, bias=pr.sb["bacol"][:, o:o + 1]))
                    E.w(t1)
                    sg[0] = E.t(E.e.activation(af[:, o * 512:(o + 1) * 512],
                                               pr.sb["att_tmp"][:, :], AF.Ln))
                pr.sec("scalar", f_sig)

                def f_cl(E, o=o, sg=sg):
                    E.w(sg[0])
                    E.t(E.e.tensor_scalar_max(af[:, o * 512:(o + 1) * 512],
                                              af[:, o * 512:(o + 1) * 512], -5.0))
                pr.sec("vector", f_cl)

    for o in range(4):
        sqt = [None]
        nr = [None]

        def f_ksq(E, r=r, o=o, sqt=sqt):
            E.w(self_tok(pr, "tensor"), self_tok(pr, "vector"))
            sqt[0] = E.t(E.e.activation(ksq[:, :], kf[:, o * 512:(o + 1) * 512], AF.Square))
        pr.sec("scalar", f_ksq)

        def f_nmm(E, o=o, sqt=sqt, nr=nr):
            E.w(sqt[0])
            E.w(self_tok(pr, "vector"))
            i = E.e.matmul(P[0][0:1, :], pr.sb["ones128b"][0:64, :], ksq[0:64, :],
                           start=True, stop=True)
            i = E.e.matmul(P[1][0:1, :], pr.sb["ones128b"][64:128, :], ksq[64:128, :],
                           start=True, stop=True)
            nr[0] = E.t(i)
        pr.sec("tensor", f_nmm)

        def f_nr(E, o=o, nr=nr):
            E.w(nr[0])
            t1 = E.t(E.e.tensor_copy(nrow[:, 0:512], P[0][0:1, :]))
            t2 = E.t(E.e.tensor_copy(nrow[:, 512:1024], P[1][0:1, :]))
            E.w(t1, t2)
            nr[0] = t2
        pr.sec("vector", f_nr)

        def f_ns(E, o=o, nr=nr):
            E.w(nr[0])
            nr[0] = E.t(E.e.activation(nrow[:, :], nrow[:, :], AF.Sqrt))
        pr.sec("scalar", f_ns)

        def f_nrec(E, o=o, nr=nr):
            E.w(nr[0])
            t1 = E.t(E.e.tensor_scalar_max(nrow[:, :], nrow[:, :], 1e-12))
            E.w(t1)
            t2 = E.t(E.e.reciprocal(nrow[:, 0:512], nrow[:, 0:512]))
            E.w(t2)
            t3 = E.t(E.e.reciprocal(nrow[:, 512:1024], nrow[:, 512:1024]))
            nr[0] = t3
        pr.sec("vector", f_nrec)

        def f_nbc(E, o=o, nr=nr):
            E.w(nr[0])
            E.e.matmul(P[2][0:64, :], pr.sb["ones1r"][:, 0:64], nrow[:, 0:512],
                       start=True, stop=True)
            i = E.e.matmul(P[2][64:128, :], pr.sb["ones1r"][:, 0:64], nrow[:, 512:1024],
                           start=True, stop=True)
            nr[0] = E.t(i)
        pr.sec("tensor", f_nbc)

        def f_nap(E, o=o, nr=nr):
            E.w(nr[0])
            E.t(E.e.tensor_mul(kf[:, o * 512:(o + 1) * 512],
                               kf[:, o * 512:(o + 1) * 512], P[2][:, :]))
        pr.sec("vector", f_nap)

    pr.fence()

    def f_stq(E, r=r):
        for o in range(4):
            for hh in range(2):
                h = o * 2 + hh
                E.dma(qloc[h, :, r * 512:(r + 1) * 512],
                      qf[hh * 64:(hh + 1) * 64, o * 512:(o + 1) * 512])
                E.dma(kloc[h, :, r * 512:(r + 1) * 512],
                      kf[hh * 64:(hh + 1) * 64, o * 512:(o + 1) * 512])
                E.dma(laloc[h, :, r * 512:(r + 1) * 512],
                      af[hh * 64:(hh + 1) * 64, o * 512:(o + 1) * 512])
    pr.sec("sync", f_stq)

    for src_b, dst_h in ((pr.sb["kf"], ktloc), (pr.sb["vf"], vtloc)):
        for o in range(4):
            for ts in range(4):
                tt = [None]

                def f_tr(E, o=o, ts=ts, src_b=src_b, tt=tt):
                    E.w(self_tok(pr, "vector"), self_tok(pr, "sync"))
                    E.w(("dS", pr.cnt["dS"]))
                    i = E.e.transpose(P[3].ap().bitcast(BF16)[0:128, 0:128],
                                      src_b[:, o * 512 + ts * 128: o * 512 + (ts + 1) * 128],
                                      pr.sb["eyeb"][:, :])
                    tt[0] = E.t(i)
                pr.sec("tensor", f_tr)

                def f_trc(E, o=o, ts=ts, tt=tt):
                    E.w(tt[0])
                    tt[0] = E.t(E.e.tensor_copy(ktn[:, 0:128],
                                                P[3].ap().bitcast(BF16)[0:128, 0:128]))
                pr.sec("vector", f_trc)

                def f_st2(E, r=r, o=o, ts=ts, dst_h=dst_h, tt=tt):
                    E.w(tt[0])
                    for hh in range(2):
                        h = o * 2 + hh
                        E.dma(dst_h[h, r * 512 + ts * 128: r * 512 + (ts + 1) * 128, :],
                              ktn[:, hh * 64:(hh + 1) * 64])
                pr.sec("sync", f_st2)


def _gla_scan_phase(pr, qun, kun, laun, ktun, vtun, oun, gtok, ctoks0):
    nc = pr.nc
    ctoks = list(ctoks0) + [gtok]

    bd = np.zeros((128, 128), dtype=np.float32)
    for s in range(128):
        sub = s // 16
        bd[s, s:(sub + 1) * 16] = 1.0
    tri_c = pr.const(bd, "bd128_c")

    qT4 = pr.sbuf("qT4", [64, 4 * NKV], BF16)
    kT4 = pr.sbuf("kT4", [64, 4 * NKV], BF16)
    kt4 = pr.sbuf("kt4", [128, 4 * 32 * 64], BF16)
    vt4 = pr.sbuf("vt4", [128, 4 * 32 * 64], BF16)
    oacc = pr.sbuf("oacc", [64, 4 * NKV], BF16)
    lab = pr.sbuf("lab", [64, 2 * 4 * 128], F32)
    LS = pr.sbuf("LS", [64, 4 * 128], F32)
    nLS = pr.sbuf("nLS", [64, 4 * 128], F32)
    reb = pr.sbuf("reb", [64, 4 * 128], F32)
    dif = pr.sbuf("dif", [64, 4 * 128], F32)
    BT = pr.sbuf("BT", [64, 4 * 128], F32)
    EIV = pr.sbuf("EIV", [64, 4 * 128], F32)
    qb4 = pr.sbuf("qb4", [64, 4 * 128], BF16)
    kiv4 = pr.sbuf("kiv4", [64, 4 * 128], BF16)
    eivt = pr.sbuf("eivt", [128, 4 * 64], BF16)
    kivt = pr.sbuf("kivt", [128, 4 * 64], BF16)
    S4 = pr.sbuf("S4", [64, 4 * 64], BF16)
    St = pr.sbuf("St", [64, 4 * 64], BF16)
    PTm = pr.sbuf("PTm", [128, 128], BF16)
    tri = pr.sbuf("tri16", [128, 128], BF16)
    zc = pr.sbuf("zc", [64, 1], F32)
    subm = pr.sbuf("subm", [128, 8], F32)
    kms = pr.sbuf("kms", [128, 64], BF16)

    sm = np.zeros((128, 8), dtype=np.float32)
    for s in range(8):
        sm[s * 16:(s + 1) * 16, s] = 1.0
    sm_c = pr.const(sm, "subm_c")

    def f_i(E):
        ctoks.append(E.dma(tri[:, :], tri_c.ap(), sem="dG"))
        ctoks.append(E.dma(subm[:, :], sm_c.ap(), sem="dG"))
        ctoks.append(E.t(E.e.memset(zc[:, :], 0.0)))
        ctoks.append(E.t(E.e.memset(pr.sb["S4"][:, :], 0.0)))
    pr.sec("gpsimd", f_i)

    ldt = [None]
    SLOT = 4 * 64 * NKV
    SLOTV = 4 * NKV * 64

    def f_ld0(E):
        E.w(gtok)
        for u in range(4):
            E.dma(qT4[:, u * NKV:(u + 1) * NKV], qun[u, :, :])
            E.dma(kT4[:, u * NKV:(u + 1) * NKV], kun[u, :, :])
            E.dma(sb3(kt4[:, u * 2048:(u + 1) * 2048], 32),
                  ktun[u, :, :].rearrange("(r p) d -> p r d", p=128))
            ldt[0] = E.dma(sb3(vt4[:, u * 2048:(u + 1) * 2048], 32),
                           vtun[u, :, :].rearrange("(r p) d -> p r d", p=128))
    pr.sec("sync", f_ld0)
    ctoks.append(ldt[0])

    P = pr.ps
    la_tok = [None, None]

    def f_lal(E, j=0):
        E.w(gtok)
        for u in range(4):
            la_tok[0] = E.dma(lab[:, u * 128:(u + 1) * 128], laun[u, :, 0:128])
    pr.sec("sync", f_lal)

    for j in range(32):
        _gla_block(pr, j, laun, la_tok, ctoks)

    pr.fence()

    def f_st(E):
        for u in range(4):
            E.dma(oun[u, :, :], oacc[:, u * NKV:(u + 1) * NKV])
    pr.sec("sync", f_st)


def _gla_block(pr, j, laun, la_tok, ctoks):
    P = pr.ps
    sb = pr.sb
    buf = j % 2
    nbuf = (j + 1) % 2
    lab = sb["lab"]; LS = sb["LS"]; nLS = sb["nLS"]; reb = sb["reb"]; dif = sb["dif"]
    BT = sb["BT"]; EIV = sb["EIV"]; qb4 = sb["qb4"]; kiv4 = sb["kiv4"]
    eivt = sb["eivt"]; kivt = sb["kivt"]; S4 = sb["S4"]; St = sb["St"]
    PTm = sb["PTm"]; tri = sb["tri16"]; zc = sb["zc"]; subm = sb["subm"]; kms = sb["kms"]
    qT4 = sb["qT4"]; kT4 = sb["kT4"]; kt4 = sb["kt4"]; vt4 = sb["vt4"]; oacc = sb["oacc"]
    lslice = lambda u: slice(buf * 512 + u * 128, buf * 512 + (u + 1) * 128)

    if j < 31:
        def f_lal(E, j=j):
            E.w(self_tok(pr, "vector"))
            for u in range(4):
                la_tok[0] = E.dma(lab[:, nbuf * 512 + u * 128: nbuf * 512 + (u + 1) * 128],
                                  laun[u, :, (j + 1) * 128:(j + 2) * 128])
        pr.sec("sync", f_lal)

    cu = [None]

    def f_scan(E, j=j, cu=cu):
        E.w(("dS", pr.cnt["dS"]), *ctoks)
        t = None
        for u in range(4):
            t = E.t(E.e.tensor_tensor_scan(LS[:, u * 128:(u + 1) * 128],
                                           lab[:, lslice(u)], lab[:, lslice(u)],
                                           0.0, ALU.add, ALU.bypass))
        E.w(t)
        tz = None
        for u in range(4):
            for s in range(8):
                srccol = zc[:, 0:1] if s == 0 else LS[:, u * 128 + s * 16 - 1: u * 128 + s * 16]
                tz = E.t(E.e.tensor_copy(
                    reb[:, u * 128 + s * 16: u * 128 + (s + 1) * 16],
                    srccol.to_broadcast([64, 16])))
        E.w(tz)
        t3 = E.t(E.e.tensor_sub(dif[:, :], LS[:, :], reb[:, :]))
        E.w(t3)
        cu[0] = t3
    pr.sec("vector", f_scan)

    ex = [None, None]

    def f_exp(E, j=j, cu=cu, ex=ex):
        E.w(cu[0])
        ex[0] = E.t(E.e.activation(BT[:, :], dif[:, :], AF.Exp))
        ex[1] = E.t(E.e.activation(EIV[:, :], dif[:, :], AF.Exp, scale=-1.0))
    pr.sec("scalar", f_exp)

    mu = [None]

    def f_mul2(E, j=j, ex=ex, mu=mu):
        E.w(ex[0], ex[1])
        E.w(self_tok(pr, "tensor"))
        t = None
        for u in range(4):
            t = E.t(E.e.tensor_mul(qb4[:, u * 128:(u + 1) * 128],
                                   qT4[:, u * NKV + j * 128: u * NKV + (j + 1) * 128],
                                   BT[:, u * 128:(u + 1) * 128]))
            t = E.t(E.e.tensor_mul(kiv4[:, u * 128:(u + 1) * 128],
                                   kT4[:, u * NKV + j * 128: u * NKV + (j + 1) * 128],
                                   EIV[:, u * 128:(u + 1) * 128]))
        mu[0] = t
    pr.sec("vector", f_mul2)

    ett = [None]

    def f_etr(E, j=j, mu=mu, ett=ett):
        E.w(mu[0])
        for u in range(4):
            i = E.e.transpose(P[0][:, u * 64:(u + 1) * 64],
                              EIV[:, u * 128:(u + 1) * 128],
                              pr.sb["eye"][0:64, 0:64])
        ett[0] = E.t(i)
    pr.sec("tensor", f_etr)

    def f_etc(E, j=j, ett=ett):
        E.w(ett[0])
        t = None
        for u in range(4):
            t = E.t(E.e.tensor_copy(eivt[:, u * 64:(u + 1) * 64],
                                    P[0][:, u * 64:(u + 1) * 64]))
        E.w(t)
        t2 = None
        for u in range(4):
            t2 = E.t(E.e.tensor_mul(kivt[:, u * 64:(u + 1) * 64],
                                    kt4[:, u * 2048 + j * 64: u * 2048 + (j + 1) * 64],
                                    eivt[:, u * 64:(u + 1) * 64]))
        ett[0] = t2
    pr.sec("vector", f_etc)

    for u in range(4):
        pt = [None]

        def f_pt(E, j=j, u=u, pt=pt, ett=ett):
            E.w(ett[0], self_tok(pr, "vector"))
            i = E.e.matmul(P[1][:, 0:128],
                           kiv4[:, u * 128:(u + 1) * 128],
                           qb4[:, u * 128:(u + 1) * 128],
                           start=True, stop=True)
            pt[0] = E.t(i)
        pr.sec("tensor", f_pt)

        def f_ptm(E, j=j, u=u, pt=pt):
            E.w(pt[0])
            pt[0] = E.t(E.e.tensor_mul(PTm[:, :], P[1][:, 0:128], tri[:, :]))
        pr.sec("vector", f_ptm)

        pv = [None]

        def f_pv(E, j=j, u=u, pt=pt, pv=pv):
            E.w(pt[0])
            i = E.e.matmul(P[2][0:64, u * 128:(u + 1) * 128],
                           vt4[:, u * 2048 + j * 64: u * 2048 + (j + 1) * 64],
                           PTm[:, :], start=True, stop=False, skip_group_check=True)
            pv[0] = E.t(i)
        pr.sec("tensor", f_pv)

        for s in range(8):
            sup = [None]

            def f_int(E, j=j, u=u, s=s, pv=pv):
                E.w(self_tok(pr, "vector"))
                E.e.matmul(P[2][0:64, u * 128 + s * 16: u * 128 + (s + 1) * 16],
                           S4[:, u * 64:(u + 1) * 64],
                           qb4[:, u * 128 + s * 16: u * 128 + (s + 1) * 16],
                           start=False, stop=(s == 7), skip_group_check=True)
            pr.sec("tensor", f_int)

            def f_kms(E, j=j, u=u, s=s, sup=sup):
                E.w(self_tok(pr, "tensor"))
                sup[0] = E.t(E.e.tensor_scalar_mul(kms[:, :],
                                                   kivt[:, u * 64:(u + 1) * 64],
                                                   subm[:, s:s + 1]))
            pr.sec("vector", f_kms)

            um = [None]

            def f_umm(E, j=j, u=u, s=s, sup=sup, um=um):
                E.w(sup[0])
                i = E.e.matmul(P[3][0:64, 0:64], kms[:, :],
                               vt4[:, u * 2048 + j * 64: u * 2048 + (j + 1) * 64],
                               start=True, stop=True)
                um[0] = E.t(i)
            pr.sec("tensor", f_umm)

            def f_sup(E, j=j, u=u, s=s, um=um):
                E.w(um[0])
                t1 = E.t(E.e.tensor_add(St[:, u * 64:(u + 1) * 64],
                                        S4[:, u * 64:(u + 1) * 64], P[3][0:64, 0:64]))
                E.w(t1)
                E.t(E.e.tensor_scalar_mul(
                    S4[:, u * 64:(u + 1) * 64], St[:, u * 64:(u + 1) * 64],
                    BT[:, u * 128 + (s + 1) * 16 - 1: u * 128 + (s + 1) * 16]))
            pr.sec("vector", f_sup)

        def f_oev(E, j=j, u=u):
            E.w(self_tok(pr, "tensor"))
            E.t(E.e.tensor_copy(oacc[:, u * NKV + j * 128: u * NKV + (j + 1) * 128],
                                P[2][0:64, u * 128:(u + 1) * 128]))
        pr.sec("vector", f_oev)



def _gla_out_phase(pr, io, res_d, oloc, gtok2, ctoks):
    xT_d = io["xT"]
    won_d = io["won"]
    yT_d = io["yT"]

    xT = pr.sb["sxT"]
    resb = pr.sbuf("resb", [128, KCH * 512], BF16)
    ot = pr.sbuf("ot", [128, 4 * TOK], BF16)
    wbuf2 = pr.sbuf("wbuf2", [128, 4 * D], BF16)
    outsb = pr.sbuf("goutsb", [128, 512], F32)
    P = pr.ps

    wt = [None]
    olt = [None]

    def f_ldw(E):
        wt[0] = E.dma(sb3(wbuf2[:, :], 4), won_d.rearrange("(k p) n -> p k n", p=128), sem="dG")
    pr.sec("gpsimd", f_ldw)

    def f_ldo(E):
        E.w(gtok2)
        olt[0] = E.dma(ot[:, :].rearrange("p (k t) -> p k t", k=4),
                       oloc.rearrange("(k hh) d t -> (hh d) k t", k=4))
    pr.sec("sync", f_ldo)

    for g in range(4):
        xg = [None]

        def f_ldx(E, g=g):
            E.w(("dS", pr.cnt["dS"]))
            E.dma(sb3(xT[:, :], KCH), rearr_kp(xT_d)[:, :, g * 512:(g + 1) * 512])
            xg[0] = E.dma(sb3(resb[:, :], KCH), rearr_kp(res_d)[:, :, g * 512:(g + 1) * 512])
        pr.sec("sync", f_ldx)

        for o in range(KCH):
            om = [None]

            def f_omm(E, g=g, o=o, om=om):
                E.w(wt[0], olt[0], self_tok(pr, "vector"))
                for k in range(4):
                    i = E.e.matmul(P[4 + o % 2][:, :],
                                   wbuf2[:, k * D + o * 128: k * D + o * 128 + 128],
                                   ot[:, k * TOK + g * 512: k * TOK + (g + 1) * 512],
                                   start=(k == 0), stop=(k == 3))
                om[0] = E.t(i)
            pr.sec("tensor", f_omm)

            def f_oadd(E, g=g, o=o, om=om):
                E.w(om[0], xg[0])
                E.w(("dS", pr.cnt["dS"]))
                t1 = E.t(E.e.tensor_tensor(outsb[:, :], xT[:, o * 512:(o + 1) * 512],
                                           P[4 + o % 2][:, :], ALU.add))
                E.w(t1)
                E.t(E.e.tensor_tensor(outsb[:, :], outsb[:, :],
                                      resb[:, o * 512:(o + 1) * 512], ALU.add))
            pr.sec("vector", f_oadd)

            def f_ost(E, g=g, o=o):
                E.w(self_tok(pr, "vector"))
                E.dma(rearr_kp(yT_d)[:, o, g * 512:(g + 1) * 512], outsb[:, :])
            pr.sec("sync", f_ost)


def _colchunks(v):
    return np.ascontiguousarray(v.reshape(8, 128).T)



def alloc_quant(pr, W):
    pr.sbuf("qa", [128, W], F32)
    pr.sbuf("qA", [128, W], F32)
    pr.sbuf("qB", [128, W], F32)
    pr.sbuf("qm", [128, W], F32)
    pr.sbuf("qs", [128, W // 16], F32)
    pr.sbuf("qr", [128, W // 16], F32)


def emit_quant(pr, src_fp, dst_bf, W, deps):
    qa = pr.sb["qa"]; qA = pr.sb["qA"]; qB = pr.sb["qB"]
    qm = pr.sb["qm"]; qs = pr.sb["qs"]; qr = pr.sb["qr"]
    NB = W // 16
    out_tok = [None]
    st = [None]

    def f_abs(E):
        E.w(*deps)
        E.w(self_tok(pr, "vector"))
        st[0] = E.t(E.e.activation(qa[:, 0:W], src_fp, AF.Abs))
    pr.sec("scalar", f_abs)

    def f_max(E):
        E.w(st[0])
        t = E.t(E.e.tensor_max(qm[:, 0:W // 2], qa[:, 0:W:2], qa[:, 1:W:2]))
        E.w(t)
        t = E.t(E.e.tensor_max(qm[:, W // 2:W // 2 + W // 4],
                               qm[:, 0:W // 2:2], qm[:, 1:W // 2:2]))
        E.w(t)
        t = E.t(E.e.tensor_max(qm[:, 0:W // 8],
                               qm[:, W // 2:W // 2 + W // 4:2],
                               qm[:, W // 2 + 1:W // 2 + W // 4:2]))
        E.w(t)
        t = E.t(E.e.tensor_max(qm[:, W // 2:W // 2 + NB],
                               qm[:, 0:W // 8:2], qm[:, 1:W // 8:2]))
        E.w(t)
        t1 = E.t(E.e.tensor_scalar(qs[:, 0:NB], qm[:, W // 2:W // 2 + NB],
                                   1.0 / 6.0, 1e-12, ALU.mult, ALU.max))
        E.w(t1)
        t2 = E.t(E.e.reciprocal(qr[:, 0:NB], qs[:, 0:NB]))
        E.w(t2)
        t3 = E.t(E.e.tensor_mul(
            qa[:, 0:W].rearrange("p (b k) -> p b k", k=16),
            src_fp.rearrange("p (b k) -> p b k", k=16),
            qr[:, 0:NB].unsqueeze(2).to_broadcast([128, NB, 16])))
        st[0] = t3
    pr.sec("vector", f_max)

    def f_t(E):
        E.w(st[0])
        st[0] = E.t(E.e.activation(qm[:, 0:W], qa[:, 0:W], AF.Abs))
    pr.sec("scalar", f_t)

    def f_lv(E):
        t_ = qm
        E.w(st[0])
        t = E.t(E.e.tensor_single_scalar(qB[:, 0:W], t_[:, 0:W], 5.0, ALU.is_gt))
        E.w(t)
        t = E.t(E.e.tensor_scalar_mul(qB[:, 0:W], qB[:, 0:W], 2.0))
        E.w(t)
        for thr in (2.5, 3.5):
            t = E.t(E.e.scalar_tensor_tensor(qB[:, 0:W], t_[:, 0:W], thr,
                                             qB[:, 0:W], ALU.is_gt, ALU.add))
            E.w(t)
        t = E.t(E.e.tensor_single_scalar(qA[:, 0:W], t_[:, 0:W], 0.25, ALU.is_gt))
        E.w(t)
        for thr in (0.75, 1.25, 1.75):
            t = E.t(E.e.scalar_tensor_tensor(qA[:, 0:W], t_[:, 0:W], thr,
                                             qA[:, 0:W], ALU.is_gt, ALU.add))
            E.w(t)
        t = E.t(E.e.scalar_tensor_tensor(qA[:, 0:W], qA[:, 0:W], 0.5,
                                         qB[:, 0:W], ALU.mult, ALU.add))
        st[0] = t
    pr.sec("vector", f_lv)

    def f_sg(E):
        E.w(st[0])
        st[0] = E.t(E.e.sign(qm[:, 0:W], qa[:, 0:W]))
    pr.sec("scalar", f_sg)

    def f_fin(E):
        E.w(st[0])
        t1 = E.t(E.e.tensor_mul(qA[:, 0:W], qA[:, 0:W], qm[:, 0:W]))
        E.w(t1)
        t2 = E.t(E.e.tensor_mul(
            dst_bf.rearrange("p (b k) -> p b k", k=16),
            qA[:, 0:W].rearrange("p (b k) -> p b k", k=16),
            qs[:, 0:NB].unsqueeze(2).to_broadcast([128, NB, 16])))
        out_tok[0] = t2
    pr.sec("vector", f_fin)
    return out_tok[0]


def phase_l1wq(pr, io, ctoks0):
    ctoks = list(ctoks0)
    pr.push_scope()
    sid = pr._scope_id
    wg_h = io["l1wg_h"]; wu_h = io["l1wu_h"]; wd_h = io["l1wd_h"]
    wgq_h = pr.dram(f"wgq{sid}", [FI, D], BF16, shared=True)
    wuq_h = pr.dram(f"wuq{sid}", [FI, D], BF16, shared=True)
    wdq_h = pr.dram(f"wdq{sid}", [D, FI], BF16, shared=True)
    wgT_l = pr.dram(f"wgTl{sid}", [D, FI], BF16)
    wuT_l = pr.dram(f"wuTl{sid}", [D, FI], BF16)
    wdT_l = pr.dram(f"wdTl{sid}", [FI, D], BF16)

    wt = pr.sbuf("wt", [128, FI], F32)
    ob = pr.sbuf("ob", [128, FI], BF16)
    alloc_quant(pr, FI)
    hloc = pr.dram(f"wqhl{sid}", [1408, D], F32).ap()
    hlocd = pr.dram(f"wqhld{sid}", [512, FI], F32).ap()
    oloc = pr.dram(f"wqol{sid}", [1408, D], BF16).ap()
    olocd = pr.dram(f"wqold{sid}", [512, FI], BF16).ap()

    for (in_h, out_sh, nrows, W, lin, lout) in (
            (io["l1wg"], wgq_h, FI, D, hloc, oloc),
            (io["l1wu"], wuq_h, FI, D, hloc, oloc),
            (io["l1wd"], wdq_h, D, FI, hlocd, olocd)):
        half = nrows // 2

        def f_gather(par, dd, in_h=in_h, lin=lin, half=half, W=W):
            dd.wait(("dS", pr.cnt["dS"]), ("dG", pr.cnt["dG"]))
            dd(lin[0:half, :], in_h[par * half:(par + 1) * half, :])
        pr.pswitch(f_gather)

        for r0 in range(0, half, 128):
            ld = [None]

            def f_ld(E, lin=lin, r0=r0, W=W):
                E.w(("vector", pr.cnt["vector"]), ("scalar", pr.cnt["scalar"]),
                    ("dS", pr.cnt["dS"]))
                ld[0] = E.dma(wt[:, 0:W], lin[r0:r0 + 128, :])
            pr.sec("sync", f_ld)

            tok = emit_quant(pr, wt[:, 0:W], ob[:, 0:W], W,
                             [ld[0], ("dS", pr.cnt["dS"])])

            def f_st(E, lout=lout, r0=r0, W=W, tok=tok):
                E.w(tok)
                E.dma(lout[r0:r0 + 128, :], ob[:, 0:W])
            pr.sec("sync", f_st)

        pr.fence()

        def f_scatter(par, dd, out_sh=out_sh, lout=lout, half=half):
            dd.wait(("dS", pr.cnt["dS"]))
            dd(out_sh.ap()[par * half:(par + 1) * half, :], lout[0:half, :])
        pr.pswitch(f_scatter)

    pr.core_barrier()

    tb = pr.sbuf("tb", [128, FI], BF16)
    for (q_h, t_l, R, W) in ((wgq_h, wgT_l, FI, D), (wuq_h, wuT_l, FI, D),
                             (wdq_h, wdT_l, D, FI)):
        for k in range(W // 128):
            def f_tc(E, q_h=q_h, t_l=t_l, R=R, W=W, k=k):
                E.w(("dS", pr.cnt["dS"]))
                E.dma(tb[:, 0:R], q_h.ap()[0:R, k * 128:(k + 1) * 128], transpose=True)
                E.w(("dS", pr.cnt["dS"]))
                E.dma(t_l.ap()[k * 128:(k + 1) * 128, :], tb[:, 0:R])
            pr.sec("sync", f_tc)
    pr.fence()
    pr.pop_scope()
    return {"wgT": wgT_l.ap(), "wuT": wuT_l.ap(), "wdT": wdT_l.ap()}


FSUB = 512
FI = 2816
FSL = (512, 512, 512, 512, 512, 256)


def phase_ffn1(pr, io, wq3, ctoks0):
    pr.push_scope()
    ctoks = list(ctoks0)
    xT_d = io["xT"]
    wgT_d = wq3["wgT"]; wuT_d = wq3["wuT"]; wdT_d = wq3["wdT"]
    yT_d = io["yT"]

    pr.sbuf("sxT", [128, KCH * FSUB], F32)
    pr.sbuf("lnf", [128, KCH * FSUB], F32)
    pr.sbuf("wbuf", [128, 22 * D], BF16)
    pr.sbuf("xqT", [128, KCH * FSUB], BF16)
    pr.sbuf("hqT", [128, 22 * FSUB], BF16)
    pr.sbuf("gsl", [128, 512], BF16)
    pr.sbuf("hsl", [128, 512], F32)
    pr.sbuf("hqsl", [128, 512], BF16)
    pr.sbuf("xtok", [128, D], F32)
    pr.sbuf("xq", [128, D], BF16)
    pr.sbuf("outsb", [128, 512], F32)
    alloc_quant(pr, D)
    alloc_ln(pr, "att")

    for sub in range(TOK // FSUB):
        _ffn1_sub(pr, sub, xT_d, wgT_d, wuT_d, wdT_d, yT_d, ctoks)
    pr.fence()
    pr.pop_scope()


def _ffn1_sub(pr, sub, xT_d, wgT_d, wuT_d, wdT_d, yT_d, ctoks):
    P = pr.ps
    sb = pr.sb
    xT = sb["sxT"]; lnf = sb["lnf"]; wbuf = sb["wbuf"]; xqT = sb["xqT"]
    hqT = sb["hqT"]; gsl = sb["gsl"]; hsl = sb["hsl"]; hqsl = sb["hqsl"]
    xtok = sb["xtok"]; xq = sb["xq"]; outsb = sb["outsb"]
    tok0 = sub * FSUB
    pr.fence()
    ld = [None]

    def f_ld(E, sub=sub):
        ld[0] = E.dma(sb3(xT[:, :], KCH), rearr_kp(xT_d)[:, :, tok0:tok0 + FSUB])
    pr.sec("sync", f_ld)

    ln_toks = emit_ln(pr, xT, lnf, FSUB, [ld[0]] + ctoks, "att")

    for ts in range(FSUB // 128):
        for k in range(KCH):
            tt = [None]

            def f_tr(E, ts=ts, k=k, tt=tt):
                E.w(ln_toks[-1])
                E.w(self_tok(pr, "vector"))
                i = E.e.transpose(P[4 + k % 2][:, 0:128],
                                  lnf[:, k * FSUB + ts * 128: k * FSUB + (ts + 1) * 128],
                                  pr.sb["eye"][:, :])
                tt[0] = E.t(i)
            pr.sec("tensor", f_tr)

            def f_cp(E, ts=ts, k=k, tt=tt):
                E.w(tt[0])
                E.t(E.e.tensor_copy(xtok[:, k * 128:(k + 1) * 128], P[4 + k % 2][:, 0:128]))
            pr.sec("vector", f_cp)

        qtok = emit_quant(pr, xtok[:, :], xq[:, :], D, [self_tok(pr, "vector")])

        for k in range(KCH):
            tt = [None]

            def f_tr2(E, ts=ts, k=k, tt=tt, qtok=qtok):
                E.w(qtok)
                E.w(self_tok(pr, "vector"))
                i = E.e.transpose(P[4 + k % 2].ap().bitcast(BF16)[0:128, 0:128],
                                  xq[:, k * 128:(k + 1) * 128], pr.sb["eyeb"][:, :])
                tt[0] = E.t(i)
            pr.sec("tensor", f_tr2)

            def f_cp2(E, ts=ts, k=k, tt=tt):
                E.w(tt[0])
                E.t(E.e.tensor_copy(xqT[:, k * FSUB + ts * 128: k * FSUB + (ts + 1) * 128],
                                    P[4 + k % 2].ap().bitcast(BF16)[0:128, 0:128]))
            pr.sec("vector", f_cp2)

    pr.fence()

    off = 0
    for osl, w in enumerate(FSL):
        o0 = off
        off += w
        wt = [None]

        def f_ldw(E, o0=o0, w=w):
            E.w(self_tok(pr, "tensor"))
            E.dma(sb3(wbuf[:, 0:KCH * 512], KCH)[:, :, 0:w], rearr_kp(wgT_d)[:, :, o0:o0 + w])
            wt[0] = E.dma(sb3(wbuf[:, KCH * 512:2 * KCH * 512], KCH)[:, :, 0:w],
                          rearr_kp(wuT_d)[:, :, o0:o0 + w])
        pr.sec("gpsimd", f_ldw)

        for ts in range(FSUB // 128):
            gm = [None]

            def f_gmm(E, ts=ts, w=w, gm=gm, wt=wt):
                E.w(wt[0], self_tok(pr, "scalar"))
                for k in range(KCH):
                    i = E.e.matmul(P[4][:, 0:w],
                                   xqT[:, k * FSUB + ts * 128: k * FSUB + (ts + 1) * 128],
                                   wbuf[:, k * 512:k * 512 + w],
                                   start=(k == 0), stop=(k == KCH - 1))
                gm[0] = E.t(i)
            pr.sec("tensor", f_gmm)

            def f_silu(E, ts=ts, w=w, gm=gm):
                E.w(gm[0])
                gm[0] = E.t(E.e.activation(gsl[:, 0:w], P[4][:, 0:w], AF.Silu))
            pr.sec("scalar", f_silu)

            um = [None]

            def f_umm(E, ts=ts, w=w, um=um, gm=gm):
                E.w(gm[0], self_tok(pr, "vector"))
                for k in range(KCH):
                    i = E.e.matmul(P[5][:, 0:w],
                                   xqT[:, k * FSUB + ts * 128: k * FSUB + (ts + 1) * 128],
                                   wbuf[:, KCH * 512 + k * 512: KCH * 512 + k * 512 + w],
                                   start=(k == 0), stop=(k == KCH - 1))
                um[0] = E.t(i)
            pr.sec("tensor", f_umm)

            def f_h(E, ts=ts, w=w, um=um):
                E.w(um[0])
                E.t(E.e.tensor_mul(hsl[:, 0:w], gsl[:, 0:w], P[5][:, 0:w]))
            pr.sec("vector", f_h)

            qtok = emit_quant(pr, hsl[:, 0:w], hqsl[:, 0:w], w, [self_tok(pr, "vector")])

            nk = w // 128
            for k in range(nk):
                tt = [None]

                def f_tr(E, ts=ts, k=k, o0=o0, tt=tt, qtok=qtok):
                    E.w(qtok)
                    E.w(self_tok(pr, "vector"))
                    i = E.e.transpose(P[6].ap().bitcast(BF16)[0:128, 0:128],
                                      hqsl[:, k * 128:(k + 1) * 128], pr.sb["eyeb"][:, :])
                    tt[0] = E.t(i)
                pr.sec("tensor", f_tr)

                def f_cp(E, ts=ts, k=k, o0=o0, tt=tt):
                    E.w(tt[0])
                    kk = (o0 + k * 128) // 128
                    E.t(E.e.tensor_copy(hqT[:, kk * FSUB + ts * 128: kk * FSUB + (ts + 1) * 128],
                                        P[6].ap().bitcast(BF16)[0:128, 0:128]))
                pr.sec("vector", f_cp)

    pr.fence()

    wt = [None]

    def f_ldwd(E):
        wt[0] = E.dma(sb3(wbuf[:, :], 22), wdT_d.rearrange("(k p) n -> p k n", p=128), sem="dG")
    pr.sec("gpsimd", f_ldwd)

    for o in range(KCH):
        dm = [None]

        def f_dmm(E, o=o, dm=dm, wt=wt):
            E.w(wt[0], self_tok(pr, "vector"))
            for k in range(22):
                i = E.e.matmul(P[4 + o % 2][:, :],
                               wbuf[:, k * D + o * 128: k * D + o * 128 + 128],
                               hqT[:, k * FSUB:(k + 1) * FSUB],
                               start=(k == 0), stop=(k == 21))
            dm[0] = E.t(i)
        pr.sec("tensor", f_dmm)

        def f_res(E, o=o, dm=dm):
            E.w(dm[0])
            E.w(("dS", pr.cnt["dS"]))
            E.t(E.e.tensor_tensor(outsb[:, :], xT[:, o * FSUB:(o + 1) * FSUB],
                                  P[4 + o % 2][:, :], ALU.add))
        pr.sec("vector", f_res)

        def f_st(E, o=o, sub=sub):
            E.w(self_tok(pr, "vector"))
            E.dma(rearr_kp(yT_d)[:, o, tok0:tok0 + FSUB], outsb[:, :])
        pr.sec("sync", f_st)




def _gammas_np():
    lo = float(np.log(1.0 - 0.999))
    hi = float(np.log(1.0 - 0.9))
    return 1.0 - np.exp(np.linspace(lo, hi, RH, dtype=np.float32))


def phase_ret(pr, io, ctoks0):
    ctoks = list(ctoks0)
    nc = pr.nc
    pr.push_scope()
    sid = pr._scope_id
    xT_d = io["xT"]
    wqT_d = io["wqT"]; wkT_d = io["wkT"]; wvT_d = io["wvT"]
    qsh_h = pr.dram(f"retq{sid}", [RH, 64, NKV], BF16, shared=True)
    ksh_h = pr.dram(f"retk{sid}", [RH, 64, NKV], BF16, shared=True)
    vtsh_h = pr.dram(f"retv{sid}", [RH, NKV, 64], BF16, shared=True)
    ysh_h = pr.dram(f"rety{sid}", [RH, 64, NKV], BF16, shared=True)
    qloc = pr.dram(f"retql{sid}", [RH, 64, TOK], BF16).ap()
    kloc = pr.dram(f"retkl{sid}", [RH, 64, TOK], BF16).ap()
    vtloc = pr.dram(f"retvl{sid}", [RH, TOK, 64], BF16).ap()
    qun = pr.dram(f"retqu{sid}", [8, 64, NKV], BF16).ap()
    kun = pr.dram(f"retku{sid}", [8, 64, NKV], BF16).ap()
    vtun = pr.dram(f"retvu{sid}", [8, NKV, 64], BF16).ap()
    yun = pr.dram(f"retyu{sid}", [8, 64, NKV], BF16).ap()
    yloc = pr.dram(f"retyl{sid}", [RH, 64, TOK], BF16).ap()

    xT = pr.sbuf("sxT", [128, KCH * 512], F32)
    xln = pr.sbuf("xln", [128, KCH * 512], BF16)
    wbuf = pr.sbuf("wbuf", [128, KCH * D], BF16)
    qf = pr.sbuf("qf", [128, 512], BF16)
    ktn = pr.sbuf("ktn", [128, 128], BF16)
    alloc_ln(pr, "att")
    P = pr.ps

    for r in range(TOK // 512):
        pr.fence()
        ld = [None]

        def f_ld(E, r=r):
            ld[0] = E.dma(sb3(xT[:, :], KCH), rearr_kp(xT_d)[:, :, r * 512:(r + 1) * 512])
        pr.sec("sync", f_ld)

        ln_toks = emit_ln(pr, xT, xln, 512, [ld[0]] + ctoks, "att")

        for wi, (wd, dst_d, tokmajor) in enumerate(
                ((wqT_d, qloc, False), (wkT_d, kloc, False), (wvT_d, vtloc, True))):
            wt = [None]

            def f_ldw(E, wd=wd):
                E.w(self_tok(pr, "tensor"))
                wt[0] = E.dma(sb3(wbuf[:, :], KCH), rearr_kp(wd), sem="dG")
            pr.sec("gpsimd", f_ldw)

            for o in range(KCH):
                pm = [None]

                def f_mm(E, r=r, o=o, wt=wt, pm=pm):
                    E.w(wt[0], ln_toks[-1])
                    E.w(self_tok(pr, "vector"), ("dS", pr.cnt["dS"]))
                    for k in range(KCH):
                        i = E.e.matmul(P[4][:, :],
                                       wbuf[:, k * D + o * 128: k * D + o * 128 + 128],
                                       xln[:, k * 512:(k + 1) * 512],
                                       start=(k == 0), stop=(k == KCH - 1))
                    pm[0] = E.t(i)
                pr.sec("tensor", f_mm)

                def f_ev(E, r=r, o=o, pm=pm):
                    E.w(pm[0])
                    pm[0] = E.t(E.e.tensor_copy(qf[:, :], P[4][:, :]))
                pr.sec("vector", f_ev)

                if not tokmajor:
                    def f_st(E, r=r, o=o, dst_d=dst_d, pm=pm):
                        E.w(pm[0])
                        for hh in range(2):
                            h = o * 2 + hh
                            E.dma(dst_d[h, :, r * 512:(r + 1) * 512],
                                  qf[hh * 64:(hh + 1) * 64, :])
                    pr.sec("sync", f_st)
                else:
                    for ts in range(4):
                        tt = [None]

                        def f_tr(E, o=o, ts=ts, tt=tt, pm=pm):
                            E.w(pm[0], self_tok(pr, "vector"), ("dS", pr.cnt["dS"]))
                            i = E.e.transpose(P[5].ap().bitcast(BF16)[0:128, 0:128],
                                              qf[:, ts * 128:(ts + 1) * 128],
                                              pr.sb["eyeb"][:, :])
                            tt[0] = E.t(i)
                        pr.sec("tensor", f_tr)

                        def f_trc(E, o=o, ts=ts, tt=tt):
                            E.w(tt[0])
                            tt[0] = E.t(E.e.tensor_copy(ktn[:, :],
                                                        P[5].ap().bitcast(BF16)[0:128, 0:128]))
                        pr.sec("vector", f_trc)

                        def f_st2(E, r=r, o=o, ts=ts, dst_d=dst_d, tt=tt):
                            E.w(tt[0])
                            for hh in range(2):
                                h = o * 2 + hh
                                E.dma(dst_d[h, r * 512 + ts * 128: r * 512 + (ts + 1) * 128, :],
                                      ktn[:, hh * 64:(hh + 1) * 64])
                        pr.sec("sync", f_st2)

    pr.fence()

    def f_out(par, dd):
        dd.wait(("dS", pr.cnt["dS"]))
        c0 = par * TOK
        dd(qsh_h.ap()[:, :, c0:c0 + TOK], qloc)
        dd(ksh_h.ap()[:, :, c0:c0 + TOK], kloc)
        dd(vtsh_h.ap()[:, c0:c0 + TOK, :], vtloc)
    pr.pswitch(f_out)
    pr.core_barrier()

    def f_in(par, dd):
        h0 = par * 8
        dd(qun, qsh_h.ap()[h0:h0 + 8, :, :])
        dd(kun, ksh_h.ap()[h0:h0 + 8, :, :])
        dd(vtun, vtsh_h.ap()[h0:h0 + 8, :, :])
    gtok = pr.pswitch(f_in)

    pr.pop_scope()
    pr.push_scope()
    _ret_scan_phase(pr, io, qun, kun, vtun, yun, gtok, ctoks)
    pr.fence()

    def f_yout(par, dd):
        dd.wait(("dS", pr.cnt["dS"]))
        dd(ysh_h.ap()[par * 8:par * 8 + 8, :, :], yun)
    pr.pswitch(f_yout)
    pr.core_barrier()

    def f_yin(par, dd):
        c0 = par * TOK
        dd(yloc, ysh_h.ap()[:, :, c0:c0 + TOK])
    gtok2 = pr.pswitch(f_yin)

    pr.pop_scope()
    pr.push_scope()
    _ret_out_phase(pr, io, yloc, gtok2, ctoks)
    pr.fence()
    pr.pop_scope()


def _ret_scan_phase(pr, io, qun, kun, vtun, yun, gtok, ctoks0):
    ctoks = list(ctoks0) + [gtok]
    nc = pr.nc
    dm_d = io["dmask"]; qg_d = io["qgrow"]; kg_d = io["kgcol"]
    g128_d = io["g128col"]; gnw_d = io["gnwcol"]; gnb_d = io["gnbcol"]

    qT = pr.sbuf("qT", [64, 8 * NKV // 8], BF16)
    kT = pr.sbuf("kT", [64, NKV], BF16)
    vt = pr.sbuf("vt", [128, 32 * 64], BF16)
    yb = pr.sbuf("yb", [64, NKV], BF16)
    dm = pr.sbuf("dm", [128, 8 * 128], BF16)
    qg = pr.sbuf("qg", [64, 8 * 128], BF16)
    kg = pr.sbuf("kg", [128, 8], F32)
    g128 = pr.sbuf("g128", [64, 8], F32)
    gnw = pr.sbuf("gnw", [64, 8], F32)
    gnb = pr.sbuf("gnb", [64, 8], F32)
    qgb = pr.sbuf("qgb", [64, 128], BF16)
    kgb = pr.sbuf("kgb", [128, 64], BF16)
    PTm = pr.sbuf("PTm", [128, 128], BF16)
    S = pr.sbuf("Srt", [64, 64], BF16)
    sacc = pr.sbuf("sacc", [64, 2], F32)
    stat = pr.sbuf("stat", [64, 8], F32)
    srow = pr.sbuf("srow", [1, 128], F32)
    mrr = pr.sbuf("mrr", [1, 4], F32)
    sq64 = pr.sbuf("sq64", [64, 128], BF16)
    P = pr.ps

    def f_i(E):
        ctoks.append(E.dma(dm[:, :], dm_d, sem="dG"))
        ctoks.append(E.dma(qg[:, :], qg_d, sem="dG"))
    pr.sec("gpsimd", f_i)

    def f_i2(E):
        ctoks.append(E.dma(kg[:, :], kg_d))
        ctoks.append(E.dma(g128[:, :], g128_d))
        ctoks.append(E.dma(gnw[:, :], gnw_d))
        ctoks.append(E.dma(gnb[:, :], gnb_d))
    pr.sec("sync", f_i2)

    for u in range(8):
        _ret_unit(pr, u, qun, kun, vtun, yun, ctoks)


def _ret_unit(pr, u, qun, kun, vtun, yun, ctoks):
    P = pr.ps
    sb = pr.sb
    qT = sb["qT"]; kT = sb["kT"]; vt = sb["vt"]; yb = sb["yb"]
    dm = sb["dm"]; qg = sb["qg"]; kg = sb["kg"]; g128 = sb["g128"]
    gnw = sb["gnw"]; gnb = sb["gnb"]; qgb = sb["qgb"]; kgb = sb["kgb"]
    PTm = sb["PTm"]; S = sb["Srt"]; sacc = sb["sacc"]; stat = sb["stat"]
    srow = sb["srow"]; mrr = sb["mrr"]; sq64 = sb["sq64"]
    pr.fence()
    ld = [None]

    def f_ld(E, u=u):
        E.dma(qT[:, 0:NKV], qun[u, :, :])
        E.dma(kT[:, :], kun[u, :, :])
        ld[0] = E.dma(sb3(vt[:, :], 32),
                      vtun[u, :, :].rearrange("(r p) d -> p r d", p=128))
    pr.sec("sync", f_ld)

    def f_z(E, u=u):
        E.t(E.e.memset(S[:, :], 0.0))
        E.t(E.e.memset(sacc[:, :], 0.0))
    pr.sec("gpsimd", f_z)

    for j in range(32):
        pt = [None]

        def f_pt(E, u=u, j=j, pt=pt):
            E.w(ld[0], *ctoks)
            E.w(self_tok(pr, "vector"), self_tok(pr, "gpsimd"))
            i = E.e.matmul(P[0][:, 0:128],
                           kT[:, j * 128:(j + 1) * 128],
                           qT[:, j * 128:(j + 1) * 128], start=True, stop=True)
            pt[0] = E.t(i)
        pr.sec("tensor", f_pt)

        def f_ptm(E, u=u, j=j, pt=pt):
            E.w(pt[0])
            t1 = E.t(E.e.tensor_mul(PTm[:, :], P[0][:, 0:128], dm[:, u * 128:(u + 1) * 128]))
            E.w(t1)
            t2 = E.t(E.e.tensor_mul(qgb[:, :], qT[:, j * 128:(j + 1) * 128],
                                    qg[:, u * 128:(u + 1) * 128]))
            E.w(t2)
            pt[0] = t2
        pr.sec("vector", f_ptm)

        om = [None]

        def f_om(E, u=u, j=j, pt=pt, om=om):
            E.w(pt[0])
            E.e.matmul(P[1][0:64, 0:128],
                       vt[:, j * 64:(j + 1) * 64], PTm[:, :],
                       start=True, stop=False, skip_group_check=True)
            i = E.e.matmul(P[1][0:64, 0:128], S[:, :], qgb[:, :],
                           start=False, stop=True, skip_group_check=True)
            om[0] = E.t(i)
        pr.sec("tensor", f_om)

        def f_oev(E, u=u, j=j, om=om):
            E.w(om[0])
            om[0] = E.t(E.e.tensor_copy(yb[:, j * 128:(j + 1) * 128], P[1][0:64, 0:128]))
        pr.sec("vector", f_oev)

        ktt = [None]

        def f_ktr(E, u=u, j=j, ktt=ktt):
            E.w(self_tok(pr, "vector"))
            i = E.e.transpose(P[2].ap().bitcast(BF16)[0:128, 0:64],
                              kT[:, j * 128:(j + 1) * 128], pr.sb["eyeb"][0:64, 0:64])
            ktt[0] = E.t(i)
        pr.sec("tensor", f_ktr)

        def f_ktc(E, u=u, j=j, ktt=ktt):
            E.w(ktt[0])
            ktt[0] = E.t(E.e.tensor_scalar_mul(kgb[:, :],
                                               P[2].ap().bitcast(BF16)[0:128, 0:64],
                                               kg[:, u:u + 1]))
        pr.sec("vector", f_ktc)

        um = [None]

        def f_um(E, u=u, j=j, ktt=ktt, um=um):
            E.w(ktt[0])
            i = E.e.matmul(P[3][0:64, 0:64], kgb[:, :], vt[:, j * 64:(j + 1) * 64],
                           start=True, stop=True)
            um[0] = E.t(i)
        pr.sec("tensor", f_um)

        def f_sup(E, u=u, j=j, um=um):
            E.w(um[0])
            E.t(E.e.scalar_tensor_tensor(S[:, :], S[:, :], g128[:, u:u + 1],
                                         P[3][0:64, 0:64], ALU.mult, ALU.add))
        pr.sec("vector", f_sup)

        def f_gn1(E, u=u, j=j):
            E.w(self_tok(pr, "vector"))
            E.t(E.e.activation(sq64[:, :], yb[:, j * 128:(j + 1) * 128], AF.Square))
        pr.sec("scalar", f_gn1)

        def f_gn2(E, u=u, j=j):
            E.w(self_tok(pr, "scalar"))
            t1 = E.t(E.e.reduce_sum(stat[:, 0:1], yb[:, j * 128:(j + 1) * 128], axis=AX.X))
            E.w(t1)
            t2 = E.t(E.e.reduce_sum(stat[:, 1:2], sq64[:, :], axis=AX.X))
            E.w(t2)
            t3 = E.t(E.e.tensor_add(sacc[:, 0:1], sacc[:, 0:1], stat[:, 0:1]))
            E.w(t3)
            t4 = E.t(E.e.tensor_add(sacc[:, 1:2], sacc[:, 1:2], stat[:, 1:2]))
            E.w(t4)
        pr.sec("vector", f_gn2)

    fz = [None]

    def f_ftr(E, u=u, fz=fz):
        E.w(self_tok(pr, "vector"))
        E.e.transpose(P[4][0:1, 0:64], sacc[:, 0:1], pr.sb["eye"][0:64, 0:64])
        i = E.e.transpose(P[4][0:1, 64:128], sacc[:, 1:2], pr.sb["eye"][0:64, 0:64])
        fz[0] = E.t(i)
    pr.sec("tensor", f_ftr)

    def f_fred(E, u=u, fz=fz):
        E.w(fz[0])
        t1 = E.t(E.e.tensor_copy(srow[:, 0:64], P[4][0:1, 0:64]))
        E.w(t1)
        t2 = E.t(E.e.reduce_sum(mrr[:, 0:1], srow[:, 0:64], axis=AX.X))
        E.w(t2)
        fz[0] = t2
    pr.sec("vector", f_fred)

    def f_fred2(E, u=u, fz=fz):
        E.w(fz[0])
        t1 = E.t(E.e.tensor_copy(srow[:, 0:64], P[4][0:1, 64:128]))
        E.w(t1)
        t2 = E.t(E.e.reduce_sum(mrr[:, 1:2], srow[:, 0:64], axis=AX.X))
        E.w(t2)
        t3 = E.t(E.e.tensor_scalar_mul(mrr[:, 0:1], mrr[:, 0:1], 1.0 / (NKV * 64.0)))
        E.w(t3)
        t4 = E.t(E.e.tensor_scalar_mul(mrr[:, 1:2], mrr[:, 1:2], 1.0 / (NKV * 64.0)))
        E.w(t4)
        t5 = E.t(E.e.tensor_mul(mrr[:, 2:3], mrr[:, 0:1], mrr[:, 0:1]))
        E.w(t5)
        t6 = E.t(E.e.tensor_sub(mrr[:, 1:2], mrr[:, 1:2], mrr[:, 2:3]))
        E.w(t6)
        t7 = E.t(E.e.tensor_scalar(mrr[:, 1:2], mrr[:, 1:2], 1e-5, None, ALU.add))
        fz[0] = t7
    pr.sec("vector", f_fred2)

    def f_fs(E, u=u, fz=fz):
        E.w(fz[0])
        fz[0] = E.t(E.e.activation(mrr[:, 1:2], mrr[:, 1:2], AF.Sqrt))
    pr.sec("scalar", f_fs)

    def f_fr(E, u=u, fz=fz):
        E.w(fz[0])
        t1 = E.t(E.e.reciprocal(mrr[:, 1:2], mrr[:, 1:2]))
        fz[0] = t1
    pr.sec("vector", f_fr)

    def f_fbc(E, u=u, fz=fz):
        E.w(fz[0])
        E.e.matmul(P[5][0:64, 0:1], pr.sb["ones1r"][:, 0:64], mrr[:, 0:1],
                   start=True, stop=True)
        i = E.e.matmul(P[5][0:64, 1:2], pr.sb["ones1r"][:, 0:64], mrr[:, 1:2],
                       start=True, stop=True)
        fz[0] = E.t(i)
    pr.sec("tensor", f_fbc)

    def f_fcp(E, u=u, fz=fz):
        E.w(fz[0])
        t1 = E.t(E.e.tensor_copy(stat[:, 2:3], P[5][0:64, 0:1]))
        E.w(t1)
        t2 = E.t(E.e.tensor_copy(stat[:, 3:4], P[5][0:64, 1:2]))
        E.w(t2)
        fz[0] = t2
    pr.sec("vector", f_fcp)

    for j in range(32):
        def f_nap(E, u=u, j=j, fz=fz):
            E.w(fz[0])
            t1 = E.t(E.e.tensor_scalar(yb[:, j * 128:(j + 1) * 128],
                                       yb[:, j * 128:(j + 1) * 128],
                                       stat[:, 2:3], stat[:, 3:4],
                                       ALU.subtract, ALU.mult))
            E.w(t1)
            t2 = E.t(E.e.tensor_scalar(yb[:, j * 128:(j + 1) * 128],
                                       yb[:, j * 128:(j + 1) * 128],
                                       gnw[:, u:u + 1], gnb[:, u:u + 1],
                                       ALU.mult, ALU.add))
            E.w(t2)
        pr.sec("vector", f_nap)

    def f_st(E, u=u):
        E.w(self_tok(pr, "vector"))
        E.dma(yun[u, :, :], yb[:, :])
    pr.sec("sync", f_st)


def _ret_out_phase(pr, io, yloc, gtok2, ctoks):
    xT_d = io["xT"]
    woT_d = io["woT"]
    out_d = io["yT"]

    xT = pr.sbuf("sxT", [128, KCH * 512], F32)
    yt = pr.sbuf("yt", [128, KCH * TOK], BF16)
    wbuf = pr.sbuf("wbuf", [128, KCH * D], BF16)
    outsb = pr.sbuf("outsb", [128, 512], F32)
    P = pr.ps
    wt = [None]
    olt = [None]

    def f_ldw(E):
        wt[0] = E.dma(sb3(wbuf[:, :], KCH), rearr_kp(woT_d), sem="dG")
    pr.sec("gpsimd", f_ldw)

    def f_ldo(E):
        E.w(gtok2)
        olt[0] = E.dma(yt[:, :].rearrange("p (k t) -> p k t", k=KCH),
                       yloc.rearrange("(k hh) d t -> (hh d) k t", k=KCH))
    pr.sec("sync", f_ldo)

    for g in range(4):
        xg = [None]

        def f_ldx(E, g=g):
            E.w(("dS", pr.cnt["dS"]))
            xg[0] = E.dma(sb3(xT[:, :], KCH), rearr_kp(xT_d)[:, :, g * 512:(g + 1) * 512])
        pr.sec("sync", f_ldx)

        for o in range(KCH):
            om = [None]

            def f_omm(E, g=g, o=o, om=om):
                E.w(wt[0], olt[0], self_tok(pr, "vector"))
                for k in range(KCH):
                    i = E.e.matmul(P[4 + o % 2][:, :],
                                   wbuf[:, k * D + o * 128: k * D + o * 128 + 128],
                                   yt[:, k * TOK + g * 512: k * TOK + (g + 1) * 512],
                                   start=(k == 0), stop=(k == KCH - 1))
                om[0] = E.t(i)
            pr.sec("tensor", f_omm)

            def f_oadd(E, g=g, o=o, om=om):
                E.w(om[0], xg[0])
                E.w(("dS", pr.cnt["dS"]))
                E.t(E.e.tensor_tensor(outsb[:, :], xT[:, o * 512:(o + 1) * 512],
                                      P[4 + o % 2][:, :], ALU.add))
            pr.sec("vector", f_oadd)

            def f_ost(E, g=g, o=o):
                E.w(self_tok(pr, "vector"))
                E.dma(rearr_kp(out_d)[:, o, g * 512:(g + 1) * 512], outsb[:, :])
            pr.sec("sync", f_ost)




def build_fused(nph=9):
    pr = Pr(num_devices=8)
    nc = pr.nc
    io = {}

    def ext(n, shape, dt):
        io[n] = nc.dram_tensor(n, list(shape), dt, kind="ExternalInput").ap()
        return io[n]

    x0T = ext("x0T", [D, TOK], F32)
    yT = nc.dram_tensor("yT", [D, TOK], F32, kind="ExternalOutput").ap()
    xa = pr.dram("xa", [D, TOK], F32).ap()
    xb = pr.dram("xb", [D, TOK], F32).ap()

    for n in ("a_wqT", "a_wkT", "a_wvT", "a_woT"):
        ext(n, [D, D], F32)
    ext("mrow", [1, NKV], BF16)
    ext("f0_gwT", [D, 8], F32)
    for n in ("f0_w1T", "f0_w2T", "f0_w3T"):
        ext(n, [8, D, D], F32)
    for n in ("f0_b1r", "f0_b2r", "f0_b3r"):
        ext(n, [128, 64], F32)
    ext("f0_sw1T", [D, 2048], F32)
    ext("f0_sw2T", [D, 2048], F32)
    ext("f0_sw3T", [2048, D], F32)
    for n in ("p_wqT", "p_wkT", "p_wvT", "p_woT"):
        ext(n, [D, D], F32)
    ext("p_projT", [HEADS, 64, PM], F32)
    ext("p_bor", [128, KCH], F32)
    for n in ("g_wqn", "g_wkn", "g_wvn", "g_wan"):
        ext(n, [D, DI], F32)
    ext("g_bar", [128, 4], F32)
    ext("g_gr", [128, KCH], F32)
    ext("g_br", [128, KCH], F32)
    ext("g_won", [DI, D], F32)
    l1wg_h = nc.dram_tensor("l1wg", [FI, D], F32, kind="ExternalInput")
    l1wu_h = nc.dram_tensor("l1wu", [FI, D], F32, kind="ExternalInput")
    l1wd_h = nc.dram_tensor("l1wd", [D, FI], F32, kind="ExternalInput")
    io["l1wg"] = l1wg_h.ap(); io["l1wu"] = l1wu_h.ap(); io["l1wd"] = l1wd_h.ap()
    io["l1wg_h"] = l1wg_h; io["l1wu_h"] = l1wu_h; io["l1wd_h"] = l1wd_h
    for n in ("r_wqT", "r_wkT", "r_wvT", "r_woT"):
        ext(n, [D, D], F32)
    ext("r_dmask", [128, 8 * 128], F32)
    ext("r_qgrow", [64, 8 * 128], F32)
    ext("r_kgcol", [128, 8], F32)
    ext("r_g128col", [64, 8], F32)
    ext("r_gnwcol", [64, 8], F32)
    ext("r_gnbcol", [64, 8], F32)

    pr.start()
    pr.init_parity(("sync", "gpsimd"))
    ctoks = emit_consts(pr)

    wq3 = phase_l1wq(pr, io, ctoks)
    chain = []
    chain.append(("atta", lambda cur, nxt: phase_atta(pr, {
        "xT": cur, "wqT": io["a_wqT"], "wkT": io["a_wkT"], "wvT": io["a_wvT"],
        "woT": io["a_woT"], "mrow": io["mrow"], "yT": nxt}, ctoks)))
    chain.append(("ffn0a", lambda cur, nxt: phase_ffn0(pr, {
        "xT": cur, "gwT": io["f0_gwT"], "w1T": io["f0_w1T"], "w2T": io["f0_w2T"],
        "w3T": io["f0_w3T"], "b1r": io["f0_b1r"], "b2r": io["f0_b2r"],
        "b3r": io["f0_b3r"], "sw1T": io["f0_sw1T"], "sw2T": io["f0_sw2T"],
        "sw3T": io["f0_sw3T"], "yT": nxt}, ctoks)))
    chain.append(("perf", lambda cur, nxt: phase_perf(pr, {
        "xT": cur, "wqT": io["p_wqT"], "wkT": io["p_wkT"], "wvT": io["p_wvT"],
        "woT": io["p_woT"], "projT": io["p_projT"], "bor": io["p_bor"],
        "yT": nxt}, ctoks)))
    chain.append(("ffn0b", chain[1][1]))
    chain.append(("gla", lambda cur, nxt: phase_gla(pr, {
        "xT": cur, "wqn": io["g_wqn"], "wkn": io["g_wkn"], "wvn": io["g_wvn"],
        "wan": io["g_wan"], "bar": io["g_bar"], "gr": io["g_gr"], "br": io["g_br"],
        "won": io["g_won"], "yT": nxt}, ctoks)))
    chain.append(("ffn1a", lambda cur, nxt: phase_ffn1(pr, {
        "xT": cur, "yT": nxt}, wq3, ctoks)))
    chain.append(("ret", lambda cur, nxt: phase_ret(pr, {
        "xT": cur, "wqT": io["r_wqT"], "wkT": io["r_wkT"], "wvT": io["r_wvT"],
        "woT": io["r_woT"], "yT": nxt, "dmask": io["r_dmask"],
        "qgrow": io["r_qgrow"], "kgcol": io["r_kgcol"], "g128col": io["r_g128col"],
        "gnwcol": io["r_gnwcol"], "gnbcol": io["r_gnbcol"]}, ctoks)))
    chain.append(("ffn1b", chain[5][1]))

    nph = min(nph, len(chain))
    cur = x0T
    bufs = [xa, xb]
    for i in range(nph):
        nxt = yT if i == nph - 1 else bufs[i % 2]
        chain[i][1](cur, nxt)
        cur = nxt
    if nph == 0:
        def f_cp(E):
            E.dma(yT, x0T)
        pr.sec("sync", f_cp)
    return pr.finish()


_FUSE_NPH = 8


def _fused_inmaps(ins):
    import ml_dtypes
    x = np.ascontiguousarray(ins["L0_attn_a_x"].astype(np.float32).reshape(-1, D))
    shards = _shard_T(x)
    gam = _gammas_np()
    gnw_full = ins["L1_attn_b_gn_w"].reshape(RH, 64)
    gnb_full = ins["L1_attn_b_gn_b"].reshape(RH, 64)
    base = {
        "a_wqT": np.ascontiguousarray(ins["L0_attn_a_Wq"].T),
        "a_wkT": np.ascontiguousarray(ins["L0_attn_a_Wk"].T),
        "a_wvT": np.ascontiguousarray(ins["L0_attn_a_Wv"].T),
        "a_woT": np.ascontiguousarray(ins["L0_attn_a_Wo"].T),
        "f0_gwT": np.ascontiguousarray(ins["L0_ffn_gate_w"].T),
        "f0_w1T": np.ascontiguousarray(ins["L0_ffn_w1"].transpose(0, 2, 1)),
        "f0_w2T": np.ascontiguousarray(ins["L0_ffn_w2"].transpose(0, 2, 1)),
        "f0_w3T": np.ascontiguousarray(ins["L0_ffn_w3"].transpose(0, 2, 1)),
        "f0_b1r": _col_bias(ins["L0_ffn_b1"]),
        "f0_b2r": _col_bias(ins["L0_ffn_b2"]),
        "f0_b3r": _col_bias(ins["L0_ffn_b3"]),
        "f0_sw1T": np.ascontiguousarray(ins["L0_ffn_sw1"].T),
        "f0_sw2T": np.ascontiguousarray(ins["L0_ffn_sw2"].T),
        "f0_sw3T": np.ascontiguousarray(ins["L0_ffn_sw3"].T),
        "p_wqT": np.ascontiguousarray(ins["L0_attn_b_Wq"].T),
        "p_wkT": np.ascontiguousarray(ins["L0_attn_b_Wk"].T),
        "p_wvT": np.ascontiguousarray(ins["L0_attn_b_Wv"].T),
        "p_woT": np.ascontiguousarray(ins["L0_attn_b_Wo"].T),
        "p_projT": np.ascontiguousarray(ins["L0_attn_b_proj"].transpose(0, 2, 1)),
        "p_bor": np.ascontiguousarray(ins["L0_attn_b_bo"].reshape(8, 128).T),
        "g_wqn": np.ascontiguousarray(ins["L1_attn_a_Wq"]),
        "g_wkn": np.ascontiguousarray(ins["L1_attn_a_Wk"]),
        "g_wvn": np.ascontiguousarray(ins["L1_attn_a_Wv"]),
        "g_wan": np.ascontiguousarray(ins["L1_attn_a_Wa"]),
        "g_bar": np.ascontiguousarray(ins["L1_attn_a_ba"].reshape(4, 128).T),
        "g_gr": _colchunks(ins["L1_attn_a_gamma"]),
        "g_br": _colchunks(ins["L1_attn_a_beta"]),
        "g_won": np.ascontiguousarray(ins["L1_attn_a_Wo"]),
        "l1wg": np.ascontiguousarray(ins["L1_ffn_w_gate"]),
        "l1wu": np.ascontiguousarray(ins["L1_ffn_w_up"]),
        "l1wd": np.ascontiguousarray(ins["L1_ffn_w_down"]),
        "r_wqT": np.ascontiguousarray(ins["L1_attn_b_wq"].T),
        "r_wkT": np.ascontiguousarray(ins["L1_attn_b_wk"].T),
        "r_wvT": np.ascontiguousarray(ins["L1_attn_b_wv"].T),
        "r_woT": np.ascontiguousarray(ins["L1_attn_b_wo"].T),
    }
    in_maps = []
    for c in range(NCORES):
        half = c % 2
        if half == 0:
            mrow = np.concatenate([np.full((1, TOK), -1e9, np.float32),
                                   np.zeros((1, TOK), np.float32)], axis=1)
        else:
            mrow = np.zeros((1, NKV), np.float32)
        dmask = np.zeros((128, 8 * 128), np.float32)
        qgrow = np.zeros((64, 8 * 128), np.float32)
        kgcol = np.zeros((128, 8), np.float32)
        g128c = np.zeros((64, 8), np.float32)
        gnwc = np.zeros((64, 8), np.float32)
        gnbc = np.zeros((64, 8), np.float32)
        idx = np.arange(128)
        for i in range(8):
            h = half * 8 + i
            g = gam[h]
            dmask[:, i * 128:(i + 1) * 128] = np.where(
                idx[None, :] >= idx[:, None], g ** (idx[None, :] - idx[:, None]), 0.0)
            qgrow[:, i * 128:(i + 1) * 128] = (g ** (idx + 1.0))[None, :]
            kgcol[:, i] = g ** (127.0 - idx)
            g128c[:, i] = g ** 128.0
            gnwc[:, i] = gnw_full[h]
            gnbc[:, i] = gnb_full[h]
        in_maps.append(dict(
            base, x0T=shards[c],
            mrow=mrow.astype(ml_dtypes.bfloat16),
            r_dmask=dmask, r_qgrow=qgrow, r_kgcol=kgcol,
            r_g128col=g128c, r_gnwcol=gnwc, r_gnbcol=gnbc))
    return in_maps


def kernel(**inputs):
    ins = {k: np.asarray(v) for k, v in inputs.items()}
    nc = _get(f"fused{_FUSE_NPH}", lambda: build_fused(_FUSE_NPH))
    in_maps = _fused_inmaps(ins)
    res = run_bass_kernel_spmd(nc, in_maps, core_ids=list(range(NCORES)))
    y = _unshard_T([res.results[c]["yT"] for c in range(NCORES)])
    return y.reshape(4, NKV, D).astype(np.float32)


# version 07
# speedup vs baseline: 1.8764x; 1.0442x over previous
import contextlib
import numpy as np

import concourse.bass as bass
import concourse.mybir as mybir
from concourse.bass_utils import run_bass_kernel_spmd

F32 = mybir.dt.float32
BF16 = mybir.dt.bfloat16
AF = mybir.ActivationFunctionType
ALU = mybir.AluOpType
AX = mybir.AxisListType

NCORES = 8
D = 1024
KCH = 8
TOK = 2048
SUB = 1024
LN_EPS = 1e-5



class Pr:
    ENGINES = ("tensor", "vector", "scalar", "gpsimd", "sync")

    def __init__(self, num_devices=None):
        self.nc = bass.Bass("TRN2", target_bir_lowering=False, num_devices=num_devices)
        self._stack = contextlib.ExitStack()
        self._scopes = []
        self.sems = {}
        self.cnt = {}
        self.hw = {}
        self.block = None
        self.blockcm = None
        self.sb = _SbMap(self)
        self.ps = []
        self._scope_id = 0
        self.regs = {}

    def sbuf(self, name, shape, dtype):
        if self._scopes:
            st, d = self._scopes[-1]
            h = st.enter_context(self.nc.sbuf_tensor(
                f"s{self._scope_id}_{name}", list(shape), dtype))
            d[name] = h
        else:
            h = self._stack.enter_context(self.nc.sbuf_tensor(name, list(shape), dtype))
            self.sb._root[name] = h
        return h

    def push_scope(self):
        self._scope_id += 1
        self._scopes.append((contextlib.ExitStack(), {}))

    def pop_scope(self):
        st, _ = self._scopes.pop()
        st.close()

    def const(self, data, name):
        self._scope_id += 1
        return self.nc.inline_tensor(np.ascontiguousarray(data),
                                     name=f"{name}_{self._scope_id}")

    def dram(self, name, shape, dtype, kind="Internal", shared=False):
        return self.nc.dram_tensor(name, list(shape), dtype, kind=kind,
                                   addr_space="Shared" if shared else "Local")

    def start(self, npsum=8):
        for i in range(npsum):
            self.ps.append(self._stack.enter_context(
                self.nc.psum_tensor(f"P{i}", [128, 512], F32)))
        for e in self.ENGINES:
            self.sems[e] = self._stack.enter_context(self.nc.semaphore("sp_" + e))
            self.cnt[e] = 0
        for dname in ("dS", "dG"):
            self.sems[dname] = self._stack.enter_context(self.nc.semaphore("sm_" + dname))
            self.cnt[dname] = 0
        self.blockcm = self.nc.Block()
        self.block = self.blockcm.__enter__()
        self._mbn = 0
        self._engobj = {"tensor": self.nc.tensor, "vector": self.nc.vector,
                        "scalar": self.nc.scalar, "gpsimd": self.nc.gpsimd,
                        "sync": self.nc.sync}
        self._alleng = [mybir.EngineType.PE, mybir.EngineType.DVE,
                        mybir.EngineType.Activation, mybir.EngineType.Pool,
                        mybir.EngineType.SP]
        self._open_master()

    def _open_master(self):
        self._mb = self.nc.body(f"master{self._mbn}", valid_engines=self._alleng)
        self._mb.__enter__()
        self._mbn += 1

    def _close_master(self):
        self._mb.__exit__(None, None, None)

    def init_parity(self, engines=("sync", "gpsimd")):
        for en in engines:
            def f(E, en=en):
                eng = E.e
                r_par = eng.alloc_register("par_" + en)
                r_xpar = eng.alloc_register("xpar_" + en)
                r_tmp = eng.alloc_register("tmp_" + en)
                pid = eng.partition_id()
                eng.reg_alu(r_par, pid, 1, ALU.bitwise_and)
                eng.reg_alu(r_xpar, r_par, 1, ALU.bitwise_xor)
                self.regs[en] = (r_par, r_xpar, r_tmp)
            self.sec(en, f)

    def sec(self, ename, body):
        body(_Emit(self, ename, self._engobj[ename]))

    def now(self, ename):
        return (ename, self.cnt[ename])

    def fence(self):
        toks = [self.now(e) for e in self.ENGINES] + [self.now("dS"), self.now("dG")]

        for e in self.ENGINES:
            def f(E, toks=toks):
                E.w(*toks)
            self.sec(e, f)

    def pswitch(self, fn, nbody=128):
        self._close_master()
        self.blockcm.__exit__(None, None, None)
        counts = []

        class _DD:
            def __init__(s, pr):
                s.n = 0
                s.pr = pr

            def __call__(s, out, in_):
                i = s.pr.nc.sync.dma_start(out=out, in_=in_)
                i.then_inc(s.pr.sems["dS"], 16)
                s.n += 1

            def wait(s, *toks):
                for t in toks:
                    if t is None:
                        continue
                    sname, val = t
                    if val > 0:
                        s.pr.nc.sync.wait_ge(s.pr.sems[sname], val)

        for arm in self.nc.switch_core_id(engines=[self.nc.sync], n=nbody):
            dd = _DD(self)
            fn(int(arm) & 1 if not hasattr(arm, "value") else arm.value & 1, dd)
            counts.append(dd.n)
        assert len(set(counts)) == 1, counts
        self.cnt["dS"] += 16 * counts[0]
        self.blockcm = self.nc.Block()
        self.block = self.blockcm.__enter__()
        self._open_master()
        return ("dS", self.cnt["dS"])

    def core_barrier(self):
        self.fence()
        self._close_master()
        self.blockcm.__exit__(None, None, None)
        self.nc.all_core_barrier()
        self.blockcm = self.nc.Block()
        self.block = self.blockcm.__enter__()
        self._open_master()

    def finish(self):
        def f(E):
            E.w(self.now("dS"), self.now("dG"))
        self.sec("sync", f)
        self._close_master()
        self.blockcm.__exit__(None, None, None)
        return self.nc


class _SbMap:
    def __init__(self, pr):
        self._pr = pr
        self._root = {}

    def __setitem__(self, k, v):
        self._root[k] = v

    def __contains__(self, k):
        try:
            self[k]
            return True
        except KeyError:
            return False

    def __getitem__(self, k):
        for _, d in reversed(self._pr._scopes):
            if k in d:
                return d[k]
        return self._root[k]


class _Emit:
    def __init__(self, pr, ename, eng):
        self.pr = pr
        self.en = ename
        self.e = eng

    def dyn(self, handle, which, slot_elems, base_elems, dims):
        r_par, r_xpar, r_tmp = self.pr.regs[self.en]
        sel = r_par if which == "own" else r_xpar
        self.e.reg_mul(r_tmp, sel, int(slot_elems))
        if base_elems:
            self.e.reg_add(r_tmp, r_tmp, int(base_elems))
        return bass.AP(handle, r_tmp, [list(d) for d in dims])

    def w(self, *tokens):
        for t in tokens:
            if t is None:
                continue
            sname, val = t
            if val <= 0:
                continue
            if sname == self.en and val <= self.pr.cnt[sname]:
                pass
            key = (self.en, sname)
            if self.pr.hw.get(key, 0) < val:
                self.e.wait_ge(self.pr.sems[sname], val)
                self.pr.hw[key] = val

    def t(self, instr):
        instr.then_inc(self.pr.sems[self.en], 1)
        self.pr.cnt[self.en] += 1
        return (self.en, self.pr.cnt[self.en])

    def dma(self, out, in_, sem=None, **kw):
        sem = sem or ("dG" if self.en == "gpsimd" else "dS")
        i = self.e.dma_start(out=out, in_=in_, **kw)
        i.then_inc(self.pr.sems[sem], 16)
        self.pr.cnt[sem] += 16
        return (sem, self.pr.cnt[sem])


def rearr_kp(ap):
    return ap.rearrange("(k p) n -> p k n", p=128)


def sb3(tile_ap, k):
    return tile_ap.rearrange("p (k n) -> p k n", k=k)



def emit_consts(pr):
    eye_c = pr.const(np.eye(128, dtype=np.float32), "eye_c")
    pr.sbuf("ones128", [128, 1], F32)
    pr.sbuf("ones128b", [128, 1], BF16)
    pr.sbuf("ones1r", [1, 128], F32)
    pr.sbuf("eye", [128, 128], F32)
    pr.sbuf("eyeb", [128, 128], BF16)
    tok = [None]

    def f(E):
        t1 = E.dma(pr.sb["eye"][:, :], eye_c.ap(), sem="dG")
        t1b = E.dma(pr.sb["eyeb"][:, :], eye_c.ap(), sem="dG")
        t2 = E.t(E.e.memset(pr.sb["ones128"][:, :], 1.0))
        t2b = E.t(E.e.memset(pr.sb["ones128b"][:, :], 1.0))
        t3 = E.t(E.e.memset(pr.sb["ones1r"][:, :], 1.0))
        tok[0] = t3
    pr.sec("gpsimd", f)
    return [tok[0], ("dG", pr.cnt["dG"])]


def emit_ln(pr, xT, xln, n_tok, dep_toks, name, gamma_beta=None):
    nt = n_tok // 512
    ones = pr.sb["ones128"]
    onesr = pr.sb["ones1r"]
    sq = pr.sb[name + "_sq"]
    rows = pr.sb[name + "_rows"]
    tmp = pr.sb[name + "_tmp"]
    P = pr.ps
    out_toks = []

    for t in range(nt):
        col = lambda k, t=t: slice(k * n_tok + t * 512, k * n_tok + (t + 1) * 512)
        sq_tok = [None] * KCH
        qmm = [None] * KCH
        smm = [None]
        rd = [None]
        bc = [None]

        def f_smm(E, t=t, smm=smm):
            E.w(*dep_toks)
            for k in range(KCH):
                i = E.e.matmul(P[0][0:1, :], ones[:, :], xT[:, col(k)],
                               start=(k == 0), stop=(k == KCH - 1))
            smm[0] = E.t(i)
        pr.sec("tensor", f_smm)

        for k in range(KCH):
            def f_sqk(E, t=t, k=k, sq_tok=sq_tok, qmm=qmm):
                E.w(*dep_toks)
                if k >= 2:
                    E.w(qmm[k - 2])
                half = sq[:, (k % 2) * 512:(k % 2) * 512 + 512]
                sq_tok[k] = E.t(E.e.activation(half, xT[:, col(k)], AF.Square))
            pr.sec("scalar", f_sqk)

            def f_qmmk(E, t=t, k=k, sq_tok=sq_tok, qmm=qmm):
                E.w(sq_tok[k])
                half = sq[:, (k % 2) * 512:(k % 2) * 512 + 512]
                i = E.e.matmul(P[1][0:1, :], pr.sb["ones128b"][:, :], half,
                               start=(k == 0), stop=(k == KCH - 1))
                qmm[k] = E.t(i)
            pr.sec("tensor", f_qmmk)

        def f_r1(E, t=t, qmm=qmm, smm=smm, rd=rd):
            m = rows[:, 0:512]; v = rows[:, 512:1024]; b = rows[:, 1024:1536]
            E.w(smm[0], qmm[KCH - 1])
            t1 = E.t(E.e.tensor_scalar_mul(m, P[0][0:1, :], 1.0 / D))
            t2 = E.t(E.e.tensor_scalar_mul(v, P[1][0:1, :], 1.0 / D))
            E.w(t1, t2)
            t3 = E.t(E.e.tensor_mul(b, m, m))
            E.w(t3)
            t4 = E.t(E.e.tensor_tensor(v, v, b, ALU.subtract))
            E.w(t4)
            t5 = E.t(E.e.tensor_scalar(v, v, LN_EPS, None, ALU.add))
            rd[0] = t5
        pr.sec("vector", f_r1)

        def f_r2(E, t=t, rd=rd):
            v = rows[:, 512:1024]
            E.w(rd[0])
            rd[0] = E.t(E.e.activation(v, v, AF.Sqrt))
        pr.sec("scalar", f_r2)

        def f_r3(E, t=t, rd=rd):
            m = rows[:, 0:512]; v = rows[:, 512:1024]; b = rows[:, 1024:1536]
            E.w(rd[0])
            t1 = E.t(E.e.reciprocal(v, v))
            E.w(t1)
            t2 = E.t(E.e.tensor_mul(b, m, v))
            E.w(t2)
            t3 = E.t(E.e.tensor_scalar_mul(b, b, -1.0))
            rd[0] = t3
        pr.sec("vector", f_r3)

        def f_bc(E, t=t, rd=rd, bc=bc):
            a = rows[:, 512:1024]; b = rows[:, 1024:1536]
            E.w(rd[0])
            if out_toks:
                E.w(out_toks[-1])
            E.e.matmul(P[2][:, :], onesr[:, :], a, start=True, stop=True)
            i = E.e.matmul(P[3][:, :], onesr[:, :], b, start=True, stop=True)
            bc[0] = E.t(i)
        pr.sec("tensor", f_bc)

        def f_ap(E, t=t, bc=bc):
            E.w(bc[0])
            last = None
            for k in range(KCH):
                t1 = E.t(E.e.tensor_tensor(tmp[:, :], xT[:, col(k)], P[2][:, :], ALU.mult))
                E.w(t1)
                if gamma_beta is None:
                    last = E.t(E.e.tensor_tensor(xln[:, col(k)], tmp[:, :], P[3][:, :], ALU.add))
                else:
                    last = E.t(E.e.tensor_tensor(tmp[:, :], tmp[:, :], P[3][:, :], ALU.add))
                    E.w(last)
                    g_sb, b_sb = gamma_beta
                    last = E.t(E.e.tensor_scalar(xln[:, col(k)], tmp[:, :],
                                                 g_sb[:, k:k + 1], b_sb[:, k:k + 1],
                                                 ALU.mult, ALU.add))
                E.w(last)
            out_toks.append(last)
        pr.sec("vector", f_ap)

    return out_toks


def alloc_ln(pr, name):
    pr.sbuf(name + "_sq", [128, 1024], BF16)
    pr.sbuf(name + "_rows", [1, 1536], F32)
    pr.sbuf(name + "_tmp", [128, 512], F32)


NKV = 4096
HEADS = 16
PM = 256
DI = 512
GH = 8
SB16 = 16
RH = 16



FFN0_KEYS = ("xT", "gwT", "w1T", "w2T", "w3T", "b1r", "b2r", "b3r",
             "sw1T", "sw2T", "sw3T", "yT")


def build_ffn0():
    pr = Pr()
    nc = pr.nc
    io = {}
    io["xT"] = nc.dram_tensor("xT", [D, TOK], F32, kind="ExternalInput").ap()
    io["gwT"] = nc.dram_tensor("gwT", [D, 8], F32, kind="ExternalInput").ap()
    io["w1T"] = nc.dram_tensor("w1T", [8, D, D], F32, kind="ExternalInput").ap()
    io["w2T"] = nc.dram_tensor("w2T", [8, D, D], F32, kind="ExternalInput").ap()
    io["w3T"] = nc.dram_tensor("w3T", [8, D, D], F32, kind="ExternalInput").ap()
    io["b1r"] = nc.dram_tensor("b1r", [128, 64], F32, kind="ExternalInput").ap()
    io["b2r"] = nc.dram_tensor("b2r", [128, 64], F32, kind="ExternalInput").ap()
    io["b3r"] = nc.dram_tensor("b3r", [128, 64], F32, kind="ExternalInput").ap()
    io["sw1T"] = nc.dram_tensor("sw1T", [D, 2048], F32, kind="ExternalInput").ap()
    io["sw2T"] = nc.dram_tensor("sw2T", [D, 2048], F32, kind="ExternalInput").ap()
    io["sw3T"] = nc.dram_tensor("sw3T", [2048, D], F32, kind="ExternalInput").ap()
    io["yT"] = nc.dram_tensor("yT", [D, TOK], F32, kind="ExternalOutput").ap()
    pr.start()
    ctoks = emit_consts(pr)
    phase_ffn0(pr, io, ctoks)
    return pr.finish()


def phase_ffn0(pr, io, ctoks0):
    pr.push_scope()
    ctoks = list(ctoks0)
    xT_d = io["xT"]; gwT_d = io["gwT"]
    w1T_d = io["w1T"]; w2T_d = io["w2T"]; w3T_d = io["w3T"]
    b1r_d = io["b1r"]; b2r_d = io["b2r"]; b3r_d = io["b3r"]
    sw1T_d = io["sw1T"]; sw2T_d = io["sw2T"]; sw3T_d = io["sw3T"]
    yT_d = io["yT"]

    sel = np.zeros((8, 8 * 128), dtype=np.float32)
    for e in range(8):
        sel[e, e * 128:(e + 1) * 128] = 1.0
    sel_c = pr.const(sel, "sel8_c")

    xT = pr.sbuf("sxT", [128, KCH * SUB], F32)
    xln = pr.sbuf("xln", [128, KCH * SUB], BF16)
    Y = pr.sbuf("Y", [128, KCH * SUB], F32)
    wpool = pr.sbuf("wpool", [128, 16 * D], BF16)
    xs = pr.sbuf("xs", [128, KCH * SUB], BF16)
    hb = pr.sbuf("hb", [128, KCH * SUB], BF16)
    x3sb = pr.sbuf("x3sb", [128, KCH * SUB], BF16)
    pr.sbuf("sel8", [8, 1024], BF16)
    pr.sbuf("gw", [128, KCH * 8], BF16)
    pr.sbuf("b1", [128, 64], F32)
    pr.sbuf("b2", [128, 64], F32)
    pr.sbuf("b3", [128, 64], F32)
    pr.sbuf("WEr", [8, SUB], BF16)
    pr.sbuf("rt", [128, 48], F32)
    pr.sbuf("rt2", [128, 8], F32)
    pr.sbuf("webc", [128, 2 * 512], BF16)
    pr.sbuf("mebc", [128, 2 * 512], BF16)
    pr.sbuf("x2", [128, 512], BF16)
    pr.sbuf("prod", [128, 512], BF16)
    alloc_ln(pr, "ffn")

    def f_init(E):
        t1 = E.dma(pr.sb["sel8"][:, :], sel_c.ap(), sem="dG")
        t2 = E.dma(sb3(pr.sb["gw"][:, :], KCH), rearr_kp(gwT_d), sem="dG")
    pr.sec("gpsimd", f_init)

    def f_init2(E):
        E.dma(pr.sb["b1"][:, :], b1r_d)
        E.dma(pr.sb["b2"][:, :], b2r_d)
        E.dma(pr.sb["b3"][:, :], b3r_d)
    pr.sec("sync", f_init2)
    ctoks += [("dG", pr.cnt["dG"]), ("dS", pr.cnt["dS"])]

    for sub in range(TOK // SUB):
        _ffn0_sub(pr, sub, xT_d, yT_d, w1T_d, w2T_d, w3T_d,
                  sw1T_d, sw2T_d, sw3T_d, ctoks)
    pr.fence()
    pr.pop_scope()


def _ffn0_sub(pr, sub, xT_d, yT_d, w1T_d, w2T_d, w3T_d, sw1T_d, sw2T_d, sw3T_d, ctoks):
    P = pr.ps
    xT = pr.sb["sxT"]; xln = pr.sb["xln"]; Y = pr.sb["Y"]
    wpool = pr.sb["wpool"]; xs = pr.sb["xs"]; hb = pr.sb["hb"]; x3sb = pr.sb["x3sb"]
    WEr = pr.sb["WEr"]; rt = pr.sb["rt"]; rt2 = pr.sb["rt2"]
    webc = pr.sb["webc"]; mebc = pr.sb["mebc"]; x2 = pr.sb["x2"]; prod = pr.sb["prod"]
    b1 = pr.sb["b1"]; b2 = pr.sb["b2"]; b3 = pr.sb["b3"]
    tok0 = sub * SUB
    nt = SUB // 512

    pr.fence()

    ld = [None]

    def f_ldx(E):
        src = rearr_kp(xT_d)
        ld[0] = E.dma(sb3(xT[:, :], KCH)[:, :, 0:SUB] if tok0 == 0
                      else sb3(xT[:, :], KCH)[:, :, 0:SUB],
                      src[:, :, tok0:tok0 + SUB])
    pr.sec("sync", f_ldx)

    def f_yinit(E):
        E.w(ld[0])
        for k in range(KCH):
            E.t(E.e.tensor_copy(Y[:, k * SUB:(k + 1) * SUB], xT[:, k * SUB:(k + 1) * SUB]))
    pr.sec("gpsimd", f_yinit)

    ln_toks = emit_ln(pr, xT, xln, SUB, [ld[0]] + ctoks, "ffn")

    tr_last = [None]
    for tt in range(SUB // 128):
        tile_ln = ln_toks[(tt * 128) // 512]
        st = [None]

        def f_rmm(E, tt=tt, tile_ln=tile_ln, st=st):
            E.w(tile_ln, *ctoks)
            E.w(self_tok(pr, "vector"))
            for k in range(KCH):
                i = E.e.matmul(P[0][:, 0:8],
                               xln[:, k * SUB + tt * 128: k * SUB + (tt + 1) * 128],
                               pr.sb["gw"][:, k * 8:(k + 1) * 8],
                               start=(k == 0), stop=(k == KCH - 1))
            st[0] = E.t(i)
        pr.sec("tensor", f_rmm)

        def f_rv(E, tt=tt, st=st):
            m = rt2[:, 0:1]; nm = rt2[:, 1:2]
            E.w(st[0])
            t1 = E.t(E.e.reduce_max(m, P[0][:, 0:8], axis=AX.X))
            E.w(t1)
            t2 = E.t(E.e.tensor_scalar_mul(nm, m, -1.0))
            E.w(t2)
            st[0] = t2
        pr.sec("vector", f_rv)

        def f_rs(E, tt=tt, st=st):
            ex = rt[:, 8:16]; nm = rt2[:, 1:2]
            E.w(st[0])
            st[0] = E.t(E.e.activation(ex, P[0][:, 0:8], AF.Exp, bias=nm))
        pr.sec("scalar", f_rs)

        def f_rv2(E, tt=tt, st=st):
            s = rt[:, 0:8]; ex = rt[:, 8:16]; ge = rt[:, 16:24]
            we = rt[:, 24:32]; msk = rt[:, 32:40]
            ssum = rt2[:, 2:3]; rsum = rt2[:, 3:4]; m1 = rt2[:, 4:5]; m2 = rt2[:, 5:6]
            E.w(st[0])
            t1 = E.t(E.e.reduce_sum(ssum, ex, axis=AX.X))
            E.w(t1)
            t2 = E.t(E.e.reciprocal(rsum, ssum))
            E.w(t2)
            t3 = E.t(E.e.tensor_scalar_mul(s, ex, rsum))
            E.w(t3)
            t4 = E.t(E.e.reduce_max(m1, s, axis=AX.X))
            E.w(t4)
            t5 = E.t(E.e.tensor_single_scalar(ge, s, m1, ALU.is_ge))
            E.w(t5)
            t6 = E.t(E.e.scalar_tensor_tensor(msk, ge, -1e9, s, ALU.mult, ALU.add))
            E.w(t6)
            t7 = E.t(E.e.reduce_max(m2, msk, axis=AX.X))
            E.w(t7)
            t8 = E.t(E.e.tensor_single_scalar(ge, s, m2, ALU.is_ge))
            E.w(t8)
            t9 = E.t(E.e.tensor_mul(we, s, ge))
            st[0] = t9
        pr.sec("vector", f_rv2)

        def f_rtr(E, tt=tt, st=st):
            E.w(st[0])
            if tr_last[0] is not None:
                E.w(tr_last[0])
            i = E.e.transpose(P[1][0:8, 0:128], rt[:, 24:32], pr.sb["eye"][:, :])
            st[0] = E.t(i)
        pr.sec("tensor", f_rtr)

        def f_rcp(E, tt=tt, st=st):
            E.w(st[0])
            tr_last[0] = E.t(E.e.tensor_copy(WEr[:, tt * 128:(tt + 1) * 128], P[1][0:8, 0:128]))
        pr.sec("vector", f_rcp)

    pr.fence()

    wst = {"slot": 0, "lastc": [None, None]}
    for e in range(8):
        _ffn0_expert(pr, e, w1T_d, w2T_d, w3T_d, wst)

    pr.fence()

    _ffn0_swiglu(pr, sw1T_d, sw2T_d, sw3T_d)

    pr.fence()

    def f_st(E):
        dst = rearr_kp(yT_d)
        E.dma(dst[:, :, tok0:tok0 + SUB], sb3(Y[:, :], KCH))
    pr.sec("sync", f_st)


def self_tok(pr, ename):
    return (ename, pr.cnt[ename])


def _ffn0_expert(pr, e, w1T_d, w2T_d, w3T_d, wst):
    P = pr.ps
    nt = SUB // 512
    xln = pr.sb["xln"]; Y = pr.sb["Y"]; wpool = pr.sb["wpool"]
    xs = pr.sb["xs"]; hb = pr.sb["hb"]; x3sb = pr.sb["x3sb"]
    WEr = pr.sb["WEr"]; webc = pr.sb["webc"]; mebc = pr.sb["mebc"]
    x2 = pr.sb["x2"]; prod = pr.sb["prod"]
    b1 = pr.sb["b1"]; b2 = pr.sb["b2"]; b3 = pr.sb["b3"]

    WHALF = KCH * D

    def wslice(slot, k, o):
        return wpool[:, slot * WHALF + k * D + o * 128: slot * WHALF + k * D + o * 128 + 128]

    wtok = [None]
    s1 = wst["slot"]

    def f_ldw1(E, e=e, s1=s1):
        E.w(wst["lastc"][s1])
        wtok[0] = E.dma(sb3(wpool[:, s1 * WHALF:(s1 + 1) * WHALF], KCH),
                        w1T_d[e, :, :].rearrange("(k p) h -> p k h", p=128), sem="dG")
    pr.sec("gpsimd", f_ldw1)
    wst["slot"] ^= 1

    for t in range(nt):
        bt = [None]

        def f_bc(E, e=e, t=t, bt=bt):
            E.w(self_tok(pr, "vector"))
            i = E.e.matmul(P[2][:, :], pr.sb["sel8"][:, e * 128:(e + 1) * 128],
                           WEr[:, t * 512:(t + 1) * 512], start=True, stop=True)
            bt[0] = E.t(i)
        pr.sec("tensor", f_bc)

        def f_bcc(E, t=t, bt=bt):
            wc = webc[:, t * 512:(t + 1) * 512]
            mc = mebc[:, t * 512:(t + 1) * 512]
            E.w(bt[0])
            t1 = E.t(E.e.tensor_copy(wc, P[2][:, :]))
            E.w(t1)
            t2 = E.t(E.e.tensor_single_scalar(mc, wc, 0.0, ALU.is_gt))
            E.w(t2)
            for k in range(KCH):
                E.t(E.e.tensor_mul(xs[:, k * SUB + t * 512: k * SUB + (t + 1) * 512],
                                   xln[:, k * SUB + t * 512: k * SUB + (t + 1) * 512], wc))
        pr.sec("vector", f_bcc)

    xs_done = self_tok(pr, "vector")

    silu_toks = {}
    for t in range(nt):
        for o in range(KCH):
            h1 = [None]

            def f_h1(E, t=t, o=o, h1=h1):
                E.w(wtok[0], xs_done)
                if o >= 2:
                    E.w(silu_toks[(t, o - 2)])
                elif t >= 1:
                    E.w(silu_toks[(t - 1, KCH - 2 + o)])
                for k in range(KCH):
                    i = E.e.matmul(P[4 + o % 2][:, :],
                                   wslice(s1, k, o),
                                   xs[:, k * SUB + t * 512: k * SUB + (t + 1) * 512],
                                   start=(k == 0), stop=(k == KCH - 1))
                h1[0] = E.t(i)
                wst["lastc"][s1] = h1[0]
            pr.sec("tensor", f_h1)

            def f_silu(E, e=e, t=t, o=o, h1=h1):
                E.w(h1[0])
                silu_toks[(t, o)] = E.t(E.e.activation(
                    hb[:, o * SUB + t * 512: o * SUB + (t + 1) * 512],
                    P[4 + o % 2][:, :], AF.Silu, bias=b1[:, e * 8 + o: e * 8 + o + 1]))
            pr.sec("scalar", f_silu)

    wtok3 = [None]
    s3 = wst["slot"]

    def f_ldw3(E, e=e, s3=s3):
        E.w(wst["lastc"][s3])
        wtok3[0] = E.dma(sb3(wpool[:, s3 * WHALF:(s3 + 1) * WHALF], KCH),
                         w3T_d[e, :, :].rearrange("(k p) h -> p k h", p=128), sem="dG")
    pr.sec("gpsimd", f_ldw3)
    wst["slot"] ^= 1

    x3_toks = {}
    for t in range(nt):
        for o in range(KCH):
            h3 = [None]

            def f_h3(E, t=t, o=o, h3=h3):
                E.w(wtok3[0])
                if o >= 2:
                    E.w(x3_toks[(t, o - 2)])
                elif t >= 1:
                    E.w(x3_toks[(t - 1, KCH - 2 + o)])
                for k in range(KCH):
                    i = E.e.matmul(P[4 + o % 2][:, :],
                                   wslice(s3, k, o),
                                   xs[:, k * SUB + t * 512: k * SUB + (t + 1) * 512],
                                   start=(k == 0), stop=(k == KCH - 1))
                h3[0] = E.t(i)
                wst["lastc"][s3] = h3[0]
            pr.sec("tensor", f_h3)

            def f_x3(E, e=e, t=t, o=o, h3=h3):
                mc = mebc[:, t * 512:(t + 1) * 512]
                E.w(h3[0])
                x3_toks[(t, o)] = E.t(E.e.scalar_tensor_tensor(
                    x3sb[:, o * SUB + t * 512: o * SUB + (t + 1) * 512],
                    mc, b3[:, e * 8 + o: e * 8 + o + 1], P[4 + o % 2][:, :],
                    ALU.mult, ALU.add))
            pr.sec("vector", f_x3)

    wtok2 = [None]
    s2 = wst["slot"]

    def f_ldw2(E, e=e, s2=s2):
        E.w(wst["lastc"][s2])
        wtok2[0] = E.dma(sb3(wpool[:, s2 * WHALF:(s2 + 1) * WHALF], KCH),
                         w2T_d[e, :, :].rearrange("(k p) h -> p k h", p=128), sem="dG")
    pr.sec("gpsimd", f_ldw2)
    wst["slot"] ^= 1

    for t in range(nt):
        for o in range(KCH):
            h2 = [None]
            x2t = [None]
            pt = [None]

            def f_h2(E, t=t, o=o, h2=h2):
                E.w(wtok2[0])
                E.w(self_tok(pr, "scalar"))
                E.w(self_tok(pr, "gpsimd"))
                for k in range(KCH):
                    i = E.e.matmul(P[4 + o % 2][:, :],
                                   wslice(s2, k, o),
                                   hb[:, k * SUB + t * 512: k * SUB + (t + 1) * 512],
                                   start=(k == 0), stop=(k == KCH - 1))
                h2[0] = E.t(i)
                wst["lastc"][s2] = h2[0]
            pr.sec("tensor", f_h2)

            def f_x2(E, e=e, t=t, o=o, h2=h2, x2t=x2t):
                E.w(h2[0])
                x2t[0] = E.t(E.e.activation(x2[:, :], P[4 + o % 2][:, :], AF.Identity,
                                            bias=b2[:, e * 8 + o: e * 8 + o + 1]))
            pr.sec("scalar", f_x2)

            def f_pr(E, t=t, o=o, x2t=x2t, pt=pt):
                E.w(x2t[0], x3_toks[(t, o)])
                t1 = E.t(E.e.tensor_mul(prod[:, :], x2[:, :],
                                        x3sb[:, o * SUB + t * 512: o * SUB + (t + 1) * 512]))
                E.w(t1)
                pt[0] = E.t(E.e.tensor_tensor(
                    Y[:, o * SUB + t * 512: o * SUB + (t + 1) * 512],
                    Y[:, o * SUB + t * 512: o * SUB + (t + 1) * 512],
                    prod[:, :], ALU.add))
            pr.sec("gpsimd", f_pr)


def _ffn0_swiglu(pr, sw1T_d, sw2T_d, sw3T_d):
    P = pr.ps
    nt = SUB // 512
    xln = pr.sb["xln"]; Y = pr.sb["Y"]; wpool = pr.sb["wpool"]
    G = pr.sb["sxT"].ap().bitcast(BF16)
    xs = pr.sb["xs"]; hb = pr.sb["hb"]

    wtok = [None]

    def f_ld1(E):
        E.w(self_tok(pr, "tensor"))
        wtok[0] = E.dma(sb3(wpool[:, :], KCH), rearr_kp(sw1T_d), sem="dG")
    pr.sec("gpsimd", f_ld1)

    g_toks = {}
    for t in range(nt):
        for o in range(16):
            gm = [None]

            def f_gmm(E, t=t, o=o, gm=gm):
                E.w(wtok[0])
                if o >= 2:
                    E.w(g_toks[(t, o - 2)])
                elif t >= 1:
                    E.w(g_toks[(t - 1, 14 + o)])
                for k in range(KCH):
                    i = E.e.matmul(P[4 + o % 2][:, :],
                                   wpool[:, k * 2048 + o * 128: k * 2048 + o * 128 + 128],
                                   xln[:, k * SUB + t * 512: k * SUB + (t + 1) * 512],
                                   start=(k == 0), stop=(k == KCH - 1))
                gm[0] = E.t(i)
            pr.sec("tensor", f_gmm)

            def f_gsilu(E, t=t, o=o, gm=gm):
                E.w(gm[0])
                g_toks[(t, o)] = E.t(E.e.activation(
                    G[:, o * SUB + t * 512: o * SUB + (t + 1) * 512],
                    P[4 + o % 2][:, :], AF.Silu))
            pr.sec("scalar", f_gsilu)

    def f_ld2(E):
        E.w(self_tok(pr, "tensor"))
        wtok[0] = E.dma(sb3(wpool[:, :], KCH), rearr_kp(sw2T_d), sem="dG")
    pr.sec("gpsimd", f_ld2)

    p_toks = {}
    for t in range(nt):
        for o in range(16):
            um = [None]

            def f_umm(E, t=t, o=o, um=um):
                E.w(wtok[0], self_tok(pr, "scalar"))
                if o >= 2:
                    E.w(p_toks[(t, o - 2)])
                elif t >= 1:
                    E.w(p_toks[(t - 1, 14 + o)])
                for k in range(KCH):
                    i = E.e.matmul(P[4 + o % 2][:, :],
                                   wpool[:, k * 2048 + o * 128: k * 2048 + o * 128 + 128],
                                   xln[:, k * SUB + t * 512: k * SUB + (t + 1) * 512],
                                   start=(k == 0), stop=(k == KCH - 1))
                um[0] = E.t(i)
            pr.sec("tensor", f_umm)

            def f_up(E, t=t, o=o, um=um):
                E.w(um[0])
                dst = xs if o < 8 else hb
                oo = o % 8
                p_toks[(t, o)] = E.t(E.e.tensor_tensor(
                    dst[:, oo * SUB + t * 512: oo * SUB + (t + 1) * 512],
                    G[:, o * SUB + t * 512: o * SUB + (t + 1) * 512],
                    P[4 + o % 2][:, :], ALU.mult))
            pr.sec("vector", f_up)

    def f_ld3(E):
        E.w(self_tok(pr, "tensor"))
        wtok[0] = E.dma(sb3(wpool[:, :], 16), rearr_kp(sw3T_d), sem="dG")
    pr.sec("gpsimd", f_ld3)

    zdone = {}
    for t in range(nt):
        for o in range(KCH):
            zm = [None]

            def f_zmm(E, t=t, o=o, zm=zm):
                E.w(wtok[0], self_tok(pr, "vector"))
                if o >= 2:
                    E.w(zdone[(t, o - 2)])
                elif t >= 1:
                    E.w(zdone[(t - 1, KCH - 2 + o)])
                for k in range(16):
                    srcb = xs if k < 8 else hb
                    kk = k % 8
                    i = E.e.matmul(P[4 + o % 2][:, :],
                                   wpool[:, k * D + o * 128: k * D + o * 128 + 128],
                                   srcb[:, kk * SUB + t * 512: kk * SUB + (t + 1) * 512],
                                   start=(k == 0), stop=(k == 15))
                zm[0] = E.t(i)
            pr.sec("tensor", f_zmm)

            def f_zacc(E, t=t, o=o, zm=zm):
                E.w(zm[0])
                zdone[(t, o)] = E.t(E.e.tensor_tensor(
                    Y[:, o * SUB + t * 512: o * SUB + (t + 1) * 512],
                    Y[:, o * SUB + t * 512: o * SUB + (t + 1) * 512],
                    P[4 + o % 2][:, :], ALU.add))
            pr.sec("vector", f_zacc)



_BUILD_CACHE = {}


def _get(name, builder):
    if name not in _BUILD_CACHE:
        _BUILD_CACHE[name] = builder()
    return _BUILD_CACHE[name]


def _shard_T(x_flat):
    out = []
    for c in range(NCORES):
        sh = x_flat[c * TOK:(c + 1) * TOK]
        out.append(np.ascontiguousarray(sh.T))
    return out


def _unshard_T(shards):
    return np.concatenate([s.T for s in shards], axis=0)


def _col_bias(b):
    return np.ascontiguousarray(b.reshape(8, 8, 128).transpose(2, 0, 1).reshape(128, 64))



def phase_atta(pr, io, ctoks0):
    pr.push_scope()
    ctoks = list(ctoks0)
    nc = pr.nc
    sid = pr._scope_id
    xT_d = io["xT"]
    wqT_d = io["wqT"]; wkT_d = io["wkT"]; wvT_d = io["wvT"]; woT_d = io["woT"]
    mrow_d = io["mrow"]
    yT_d = io["yT"]
    scrk_d = pr.dram(f"scrk{sid}", [HEADS, 64, NKV], BF16).ap()
    xsh_h = pr.dram(f"xsh{sid}", [2, D, TOK], BF16, shared=True)
    xloc_d = pr.dram(f"xloc{sid}", [D, TOK], BF16).ap()
    xkv_d = pr.dram(f"xkv{sid}", [D, NKV], BF16).ap()

    trim = np.zeros((4, 128, 512), dtype=np.float32)
    for j in range(4):
        for p in range(128):
            trim[j, p, j * 128 + p:] = 1.0
    tri_c = pr.const(trim.reshape(4 * 128, 512), "tri_c")

    xT = pr.sbuf("sxT", [128, KCH * 512], F32)
    xln = pr.sbuf("xln", [128, 2 * KCH * 512], BF16)
    wbuf = pr.sbuf("wbuf", [128, KCH * D], BF16)
    vres = pr.sbuf("vres", [128, 32 * HEADS * 65], BF16)
    qtb = pr.sbuf("qtb", [128, KCH * TOK], BF16)
    qst = pr.sbuf("qst", [66, 2 * 512], BF16)
    psb = pr.sbuf("psb", [128, 4 * 512], BF16)
    tri = pr.sbuf("tri", [128, 4 * 512], BF16)
    oall = pr.sbuf("oall", [128, KCH * TOK], BF16)
    osb = pr.sbuf("osb", [64, 512], BF16)
    kevac = pr.sbuf("kevac", [128, 2 * 512], BF16)
    mcol = pr.sbuf("mcol", [128, 8], F32)
    lrow = pr.sbuf("lrow", [1, 512], F32)
    outsb = pr.sbuf("outsb", [128, 512], F32)
    alloc_ln(pr, "att")

    def f_init(E):
        ctoks.append(E.dma(sb3(tri[:, :], 4), tri_c.ap().rearrange("(j p) c -> p j c", p=128), sem="dG"))
        ctoks.append(E.t(E.e.memset(vres[:, :], 1.0)))
    pr.sec("gpsimd", f_init)

    P = pr.ps
    for r in range(4):
        ldt = [None]

        def f_ldo(E, r=r):
            E.w(("tensor", pr.cnt["tensor"]), ("scalar", pr.cnt["scalar"]),
                ("vector", pr.cnt["vector"]), ("dS", pr.cnt["dS"]))
            ldt[0] = E.dma(sb3(xT[:, :], KCH), rearr_kp(xT_d)[:, :, r * 512:(r + 1) * 512])
        pr.sec("sync", f_ldo)

        lt = emit_ln(pr, xT, xln[:, 0:KCH * 512], 512, [ldt[0]] + ctoks, "att")

        def f_wsh(E, r=r, lt=lt):
            E.w(lt[-1])
            E.dma(rearr_kp(xloc_d)[:, :, r * 512:(r + 1) * 512], sb3(xln[:, 0:KCH * 512], KCH))
        pr.sec("sync", f_wsh)

    pr.fence()

    def f_xout(par, dd):
        dd.wait(("dS", pr.cnt["dS"]))
        dd(xsh_h.ap()[par, :, :], xloc_d)
    pr.pswitch(f_xout)
    pr.core_barrier()

    def f_xin(par, dd):
        dd(xkv_d[:, 0:TOK], xsh_h.ap()[1 - par, :, :])
        dd(xkv_d[:, TOK:NKV], xsh_h.ap()[par, :, :])
    gtok = pr.pswitch(f_xin)

    _atta_proj(pr, xkv_d, gtok, wqT_d, wkT_d, wvT_d, scrk_d, ctoks)

    pr.fence()

    kstv = pr.sb["wbuf"][0:66, 0:NKV]

    def f_init2(E):
        ctoks.append(E.dma(kstv[65:66, :], mrow_d))
    pr.sec("sync", f_init2)

    def f_init3(E):
        E.w(*[c for c in ctoks if c])
        ctoks.append(E.t(E.e.memset(kstv[64:65, :], -1.0)))
    pr.sec("gpsimd", f_init3)

    for h in range(HEADS):
        _atta_head(pr, h, scrk_d, ctoks)

    pr.fence()

    wo_tok = [None]

    def f_ldwo(E):
        wo_tok[0] = E.dma(sb3(wbuf[:, :], KCH), rearr_kp(woT_d), sem="dG")
    pr.sec("gpsimd", f_ldwo)

    for g in range(4):
        xg = [None]

        def f_ldxg(E, g=g):
            E.w(self_tok(pr, "vector"))
            src = rearr_kp(xT_d)
            xg[0] = E.dma(sb3(xT[:, :], KCH), src[:, :, g * 512:(g + 1) * 512])
        pr.sec("sync", f_ldxg)

        for o in range(KCH):
            om = [None]

            def f_omm(E, g=g, o=o, om=om):
                E.w(wo_tok[0], self_tok(pr, "vector"))
                for k in range(KCH):
                    i = E.e.matmul(P[4 + o % 2][:, :],
                                   wbuf[:, k * D + o * 128: k * D + o * 128 + 128],
                                   pr.sb["oall"][:, k * TOK + g * 512: k * TOK + (g + 1) * 512],
                                   start=(k == 0), stop=(k == KCH - 1))
                om[0] = E.t(i)
            pr.sec("tensor", f_omm)

            def f_oadd(E, g=g, o=o, om=om):
                E.w(om[0], xg[0])
                E.w(("dS", pr.cnt["dS"]))
                t1 = E.t(E.e.tensor_tensor(outsb[:, :], xT[:, o * 512:(o + 1) * 512],
                                           P[4 + o % 2][:, :], ALU.add))
                E.w(t1)
            pr.sec("vector", f_oadd)

            def f_ost(E, g=g, o=o):
                E.w(self_tok(pr, "vector"))
                dst = rearr_kp(yT_d)
                E.dma(dst[:, o, g * 512:(g + 1) * 512], outsb[:, :])
            pr.sec("sync", f_ost)

    pr.fence()
    pr.pop_scope()


def _atta_proj(pr, xkv_d, gtok, wqT_d, wkT_d, wvT_d, scrk_d, ctoks):
    P = pr.ps
    xln = pr.sb["xln"]; wbuf = pr.sb["wbuf"]
    vres = pr.sb["vres"]; qtb = pr.sb["qtb"]; kevac = pr.sb["kevac"]
    XW = KCH * 512
    pr.fence()
    wt = [None]
    xt = {}

    def f_ldwk(E):
        wt[0] = E.dma(sb3(wbuf[:, :], KCH), rearr_kp(wkT_d), sem="dG")
    pr.sec("gpsimd", f_ldwk)

    def ldx(r, dep):
        sl = r % 2

        def f_ldx(E, r=r, sl=sl, dep=dep):
            E.w(gtok, dep)
            xt[r] = E.dma(sb3(xln[:, sl * XW:(sl + 1) * XW], KCH),
                          rearr_kp(xkv_d)[:, :, r * 512:(r + 1) * 512])
        pr.sec("sync", f_ldx)

    kmmt = {}; kevt = {}; kstt = {}
    for r in range(8):
        ldx(r, kmmt.get(r * 8 - 9))
        sl = r % 2
        for o in range(KCH):
            n = r * 8 + o

            def f_kmm(E, r=r, o=o, n=n, sl=sl):
                E.w(wt[0], xt[r], kevt.get(n - 2))
                for k in range(KCH):
                    i = E.e.matmul(P[4 + o % 2][:, :],
                                   wbuf[:, k * D + o * 128: k * D + o * 128 + 128],
                                   xln[:, sl * XW + k * 512:sl * XW + (k + 1) * 512],
                                   start=(k == 0), stop=(k == KCH - 1))
                kmmt[n] = E.t(i)
            pr.sec("tensor", f_kmm)

            def f_kev(E, o=o, n=n):
                E.w(kmmt[n], kstt.get(n - 2))
                kevt[n] = E.t(E.e.tensor_copy(kevac[:, (o % 2) * 512:(o % 2) * 512 + 512],
                                              P[4 + o % 2][:, :]))
            pr.sec("vector", f_kev)

            def f_kst(E, r=r, o=o, n=n):
                E.w(kevt[n])
                for hh in range(2):
                    h = o * 2 + hh
                    t = E.dma(scrk_d[h, :, r * 512:(r + 1) * 512],
                              kevac[hh * 64:(hh + 1) * 64, (o % 2) * 512:(o % 2) * 512 + 512])
                kstt[n] = t
            pr.sec("sync", f_kst)

    wv = [None]

    def f_ldwv(E):
        E.w(kmmt[63])
        wv[0] = E.dma(sb3(wbuf[:, :], KCH), rearr_kp(wvT_d), sem="dG")
    pr.sec("gpsimd", f_ldwv)

    vmmt = {}; vevt = {}
    for r in range(8):
        ldx(r, kmmt.get(55 + 8 * r) if r < 2 else vmmt.get(r * 8 - 9))
        sl = r % 2
        for ts in range(4):
            for o in range(2):
                m = r * 8 + ts * 2 + o

                def f_vmm(E, r=r, ts=ts, o=o, m=m, sl=sl):
                    E.w(wv[0], xt[r], vevt.get(m - 2), *ctoks)
                    for k in range(KCH):
                        i = E.e.matmul(P[4 + m % 2][:, :],
                                       xln[:, sl * XW + k * 512 + ts * 128:sl * XW + k * 512 + (ts + 1) * 128],
                                       wbuf[:, k * D + o * 512: k * D + (o + 1) * 512],
                                       start=(k == 0), stop=(k == KCH - 1))
                    vmmt[m] = E.t(i)
                pr.sec("tensor", f_vmm)

                def f_vev(E, r=r, ts=ts, o=o, m=m):
                    E.w(vmmt[m])
                    rt = r * 4 + ts
                    for hh in range(8):
                        h = o * 8 + hh
                        t = E.t(E.e.tensor_copy(
                            vres[:, rt * HEADS * 65 + h * 65: rt * HEADS * 65 + h * 65 + 64],
                            P[4 + m % 2][:, hh * 64:(hh + 1) * 64]))
                    vevt[m] = t
                pr.sec("vector", f_vev)

    wqq = [None]

    def f_ldwq(E):
        E.w(vmmt[63])
        wqq[0] = E.dma(sb3(wbuf[:, :], KCH), rearr_kp(wqT_d), sem="dG")
    pr.sec("gpsimd", f_ldwq)

    qmmt = {}; qevt = {}
    for idx, r in enumerate([6, 7, 4, 5]):
        if idx >= 2:
            ldx(r, qmmt.get((idx - 2) * 8 + 7))
        sl = r % 2
        for o in range(KCH):
            p = idx * 8 + o

            def f_qmm(E, r=r, o=o, p=p, sl=sl):
                E.w(wqq[0], xt[r], qevt.get(p - 2))
                for k in range(KCH):
                    i = E.e.matmul(P[4 + p % 2][:, :],
                                   wbuf[:, k * D + o * 128: k * D + o * 128 + 128],
                                   xln[:, sl * XW + k * 512:sl * XW + (k + 1) * 512],
                                   start=(k == 0), stop=(k == KCH - 1))
                qmmt[p] = E.t(i)
            pr.sec("tensor", f_qmm)

            def f_qev(E, r=r, o=o, p=p):
                E.w(qmmt[p])
                qevt[p] = E.t(E.e.activation(qtb[:, o * TOK + (r - 4) * 512: o * TOK + (r - 3) * 512],
                                             P[4 + p % 2][:, :], AF.Copy, scale=0.125))
            pr.sec("scalar", f_qev)


def _atta_head(pr, h, scrk_d, ctoks):
    P = pr.ps
    kst = pr.sb["wbuf"][0:66, 0:NKV]
    qst = pr.sb["qst"]; qtb = pr.sb["qtb"]
    vres = pr.sb["vres"]; psb = pr.sb["psb"]; tri = pr.sb["tri"]
    oall = pr.sb["oall"]; osb = pr.sb["osb"]; mcol = pr.sb["mcol"]; lrow = pr.sb["lrow"]
    atmp = pr.sb["att_tmp"]
    ch, hh = h // 2, h % 2

    kld = [None]

    def f_ldk(E, h=h):
        E.w(self_tok(pr, "tensor"))
        kld[0] = E.dma(kst[0:64, :], scrk_d[h, :, :])
    pr.sec("sync", f_ldk)

    stg_fin = {}
    pvlast = {}
    normv_tok = {}
    normbc_tok = {}
    normap_tok = {}
    mtr_tok = {}
    banklast = {0: None, 1: None}

    def mk_units_A(g):
        gp = g % 2
        q0 = g * 512
        c0 = gp * 512
        units = []
        stgq = [None]

        def f_stgq(E, g=g, gp=gp, q0=q0, c0=c0, stgq=stgq):
            E.w(pvlast.get(g - 2))
            stgq[0] = E.t(E.e.tensor_copy(qst[0:64, c0:c0 + 512],
                                          qtb[hh * 64:hh * 64 + 64,
                                              ch * TOK + q0:ch * TOK + q0 + 512]))
        units.append(lambda f=f_stgq: pr.sec("vector", f))

        mt = [None] * 4
        for qs in range(4):
            for ks in range(5 + g):
                sm = [None]
                bk = ks % 2

                def f_smm(E, g=g, gp=gp, c0=c0, qs=qs, ks=ks, sm=sm, bk=bk, stgq=stgq):
                    E.w(kld[0], *ctoks)
                    E.w(stgq[0], stg_fin.get(g - 1))
                    E.w(banklast[bk])
                    i = E.e.matmul(P[bk][:, :],
                                   qst[0:64, c0 + qs * 128:c0 + (qs + 1) * 128],
                                   kst[0:64, ks * 512:(ks + 1) * 512],
                                   start=True, stop=True)
                    sm[0] = E.t(i)

                def f_red(E, g=g, gp=gp, qs=qs, ks=ks, sm=sm, bk=bk, mt=mt):
                    E.w(sm[0])
                    if ks == 0 and qs == 0:
                        E.w(mtr_tok.get(g - 2))
                    if ks == 0:
                        t = E.t(E.e.reduce_max(mcol[:, gp * 4 + qs:gp * 4 + qs + 1],
                                               P[bk][:, :], axis=AX.X))
                    else:
                        t1 = E.t(E.e.reduce_max(atmp[:, gp:gp + 1], P[bk][:, :], axis=AX.X))
                        E.w(t1)
                        t = E.t(E.e.tensor_max(mcol[:, gp * 4 + qs:gp * 4 + qs + 1],
                                               mcol[:, gp * 4 + qs:gp * 4 + qs + 1],
                                               atmp[:, gp:gp + 1]))
                    banklast[bk] = t
                    if ks == 4 + g:
                        mt[qs] = t

                units.append(lambda f=f_smm, f2=f_red: (pr.sec("tensor", f),
                                                        pr.sec("vector", f2)))

        stg = [None]

        def f_mtr(E, g=g, gp=gp, mt=mt, stg=stg):
            E.w(*mt)
            E.w(stg_fin.get(g - 1))
            for qs in range(4):
                i = E.e.transpose(P[1][0:1, qs * 128:(qs + 1) * 128],
                                  mcol[:, gp * 4 + qs:gp * 4 + qs + 1], pr.sb["eye"][:, :])
            stg[0] = E.t(i)
            mtr_tok[g] = stg[0]

        def f_stg(E, g=g, c0=c0, stg=stg):
            E.w(stg[0])
            t0 = E.t(E.e.memset(qst[64:66, c0:c0 + 512], 1.0))
            E.w(t0)
            t2 = E.t(E.e.tensor_copy(qst[64:65, c0:c0 + 512], P[1][0:1, :]))
            E.w(t2)
            stg_fin[g] = t2

        units.append(lambda f=f_mtr, f2=f_stg: (pr.sec("tensor", f),
                                                pr.sec("vector", f2)))
        return units

    def mk_units_B(g):
        gp = g % 2
        nkt = 20 + 4 * g
        q0 = g * 512
        c0 = gp * 512
        units = []
        pv_tok = {}
        exp_tok = {}
        pend_pv = []

        for kt in range(nkt):
            et = [None]
            stb = 2 + kt % 4
            pslot = kt % 4

            def f_stmm(E, g=g, c0=c0, kt=kt, et=et, stb=stb):
                E.w(stg_fin[g])
                E.w(exp_tok.get(kt - 4))
                if kt == 0:
                    E.w(normap_tok.get(g - 1))
                i = E.e.matmul(P[stb][:, :],
                               kst[:, kt * 128:(kt + 1) * 128],
                               qst[:, c0:c0 + 512], start=True, stop=True)
                et[0] = E.t(i)

            def f_exp(E, g=g, kt=kt, et=et, stb=stb, pslot=pslot):
                E.w(et[0])
                E.w(pv_tok.get(kt - 4))
                t = E.t(E.e.activation(psb[:, pslot * 512:(pslot + 1) * 512],
                                       P[stb][:, :], AF.Exp))
                exp_tok[kt] = t
                et[0] = t

            if kt >= nkt - 4:
                j = kt - (nkt - 4)

                def f_tri(E, g=g, kt=kt, j=j, et=et, pslot=pslot):
                    E.w(et[0])
                    et[0] = E.t(E.e.tensor_mul(psb[:, pslot * 512:(pslot + 1) * 512],
                                               psb[:, pslot * 512:(pslot + 1) * 512],
                                               tri[:, j * 512:(j + 1) * 512]))
            else:
                f_tri = None

            def f_pv(E, g=g, kt=kt, nkt=nkt, et=et, pslot=pslot):
                E.w(et[0])
                if kt == 0:
                    E.w(normv_tok.get(g - 1))
                i = E.e.matmul(P[6][0:65, :],
                               vres[:, kt * HEADS * 65 + h * 65: kt * HEADS * 65 + h * 65 + 65],
                               psb[:, pslot * 512:(pslot + 1) * 512],
                               start=(kt == 0), stop=(kt == nkt - 1))
                pv_tok[kt] = E.t(i)
                if kt == nkt - 1:
                    pvlast[g] = pv_tok[kt]

            def emit(f1=f_stmm, f2=f_exp, f3=f_tri, f4s=pend_pv, f4=f_pv, kt=kt):
                pr.sec("tensor", f1)
                pr.sec("scalar", f2)
                if f3 is not None:
                    pr.sec("vector", f3)
                f4s.append(f4)
                if kt >= 2:
                    pr.sec("tensor", f4s.pop(0))
            units.append(emit)

        def flush(f4s=pend_pv):
            while f4s:
                pr.sec("tensor", f4s.pop(0))
        units.append(flush)
        return units, pv_tok, nkt

    def emit_norm(g, pv_tok, nkt):
        gp = g % 2
        q0 = g * 512
        nt_ = [None]

        def f_norm_v(E, g=g, nt_=nt_):
            E.w(pv_tok[nkt - 1])
            E.w(normbc_tok.get(g - 1))
            t1 = E.t(E.e.reciprocal(lrow[:, :], P[6][64:65, :]))
            E.w(t1)
            t2 = E.t(E.e.tensor_copy(osb[:, :], P[6][0:64, :]))
            nt_[0] = t2
            normv_tok[g] = t2
        pr.sec("vector", f_norm_v)

        def f_norm_bc(E, g=g, nt_=nt_):
            E.w(nt_[0])
            i = E.e.matmul(P[2][0:64, :], pr.sb["ones1r"][:, 0:64], lrow[:, :],
                           start=True, stop=True)
            nt_[0] = E.t(i)
            normbc_tok[g] = nt_[0]
        pr.sec("tensor", f_norm_bc)

        def f_norm_ap(E, g=g, q0=q0, nt_=nt_):
            E.w(nt_[0])
            normap_tok[g] = E.t(E.e.tensor_mul(oall[hh * 64:hh * 64 + 64,
                                                    ch * TOK + q0:ch * TOK + q0 + 512],
                                               osb[:, :], P[2][0:64, :]))
        pr.sec("vector", f_norm_ap)

    for u in mk_units_A(0):
        u()
    for g in range(4):
        unitsB, pv_tok, nkt = mk_units_B(g)
        unitsA = mk_units_A(g + 1) if g < 3 else []
        na, nb = len(unitsA), len(unitsB)
        ia = 0
        for ib, ub in enumerate(unitsB):
            ub()
            want = (ib + 1) * na // nb
            while ia < want:
                unitsA[ia]()
                ia += 1
        while ia < na:
            unitsA[ia]()
            ia += 1
        emit_norm(g, pv_tok, nkt)


def phase_perf(pr, io, ctoks0):
    pr.push_scope()
    ctoks = list(ctoks0)
    nc = pr.nc
    sid = pr._scope_id
    xT_d = io["xT"]
    wqT_d = io["wqT"]; wkT_d = io["wkT"]; wvT_d = io["wvT"]; woT_d = io["woT"]
    projT_d = io["projT"]; bor_d = io["bor"]
    yT_d = io["yT"]
    scrk_d = pr.dram(f"scrk{sid}", [HEADS, 64, NKV], BF16).ap()
    xsh_h = pr.dram(f"xsh{sid}", [2, D, TOK], BF16, shared=True)
    xloc_d = pr.dram(f"xloc{sid}", [D, TOK], BF16).ap()
    xkv_d = pr.dram(f"xkv{sid}", [D, NKV], BF16).ap()

    xT = pr.sbuf("sxT", [128, KCH * 512], F32)
    xln = pr.sbuf("xln", [128, 2 * KCH * 512], BF16)
    wbuf = pr.sbuf("wbuf", [128, KCH * D], BF16)
    vres = pr.sbuf("vres", [128, 32 * HEADS * 65], BF16)
    qtb = pr.sbuf("qtb", [128, KCH * TOK], BF16)
    prj = pr.sbuf("prj", [64, HEADS * PM], BF16)
    ones64b = pr.sbuf("ones64b", [64, 1], BF16)
    bias4 = pr.sbuf("bias4", [128, 24], F32)
    phik = pr.sbuf("phik", [128, 2 * 512], BF16)
    phiqt = pr.sbuf("phiqt", [128, 2 * 512], BF16)
    kvsb = pr.sbuf("kvsb", [128, 2 * 65], BF16)
    oall = pr.sbuf("oall", [128, KCH * TOK], BF16)
    outsb = pr.sbuf("outsb", [128, 512], F32)
    bo = pr.sbuf("bo", [128, KCH], F32)
    pr.sbuf("kevb", [128, 2 * 512], BF16)
    alloc_ln(pr, "att")

    def f_init(E):
        ctoks.append(E.dma(prj[:, :].rearrange("d (h m) -> d h m", h=HEADS),
                           projT_d.rearrange("h d m -> d h m"), sem="dG"))
        ctoks.append(E.t(E.e.memset(ones64b[:, :], 1.0)))
        ctoks.append(E.t(E.e.memset(vres[:, :], 1.0)))
    pr.sec("gpsimd", f_init)

    def f_init2(E):
        ctoks.append(E.dma(bo[:, :], bor_d))
    pr.sec("sync", f_init2)

    P = pr.ps
    for r in range(4):
        ldt = [None]

        def f_ldo(E, r=r):
            E.w(("tensor", pr.cnt["tensor"]), ("scalar", pr.cnt["scalar"]),
                ("vector", pr.cnt["vector"]), ("dS", pr.cnt["dS"]))
            ldt[0] = E.dma(sb3(xT[:, :], KCH), rearr_kp(xT_d)[:, :, r * 512:(r + 1) * 512])
        pr.sec("sync", f_ldo)

        lt = emit_ln(pr, xT, xln[:, 0:KCH * 512], 512, [ldt[0]] + ctoks, "att")

        def f_wsh(E, r=r, lt=lt):
            E.w(lt[-1])
            E.dma(rearr_kp(xloc_d)[:, :, r * 512:(r + 1) * 512], sb3(xln[:, 0:KCH * 512], KCH))
        pr.sec("sync", f_wsh)

    pr.fence()

    def f_xout(par, dd):
        dd.wait(("dS", pr.cnt["dS"]))
        dd(xsh_h.ap()[par, :, :], xloc_d)
    pr.pswitch(f_xout)
    pr.core_barrier()

    def f_xin(par, dd):
        dd(xkv_d[:, 0:TOK], xsh_h.ap()[1 - par, :, :])
        dd(xkv_d[:, TOK:NKV], xsh_h.ap()[par, :, :])
    gtok = pr.pswitch(f_xin)

    _perf_proj(pr, xkv_d, gtok, wqT_d, wkT_d, wvT_d, scrk_d, ctoks)

    pr.fence()

    for h in range(HEADS):
        _perf_head(pr, h, scrk_d, ctoks)

    pr.fence()

    wo_tok = [None]

    def f_ldwo(E):
        wo_tok[0] = E.dma(sb3(wbuf[:, :], KCH), rearr_kp(woT_d), sem="dG")
    pr.sec("gpsimd", f_ldwo)

    for g in range(4):
        xg = [None]

        def f_ldxg(E, g=g):
            E.w(("dS", pr.cnt["dS"]))
            src = rearr_kp(xT_d)
            xg[0] = E.dma(sb3(xT[:, :], KCH), src[:, :, g * 512:(g + 1) * 512])
        pr.sec("sync", f_ldxg)

        for o in range(KCH):
            om = [None]
            ev = [None]

            def f_omm(E, g=g, o=o, om=om):
                E.w(wo_tok[0], self_tok(pr, "vector"))
                for k in range(KCH):
                    i = E.e.matmul(P[4 + o % 2][:, :],
                                   wbuf[:, k * D + o * 128: k * D + o * 128 + 128],
                                   pr.sb["oall"][:, k * TOK + g * 512: k * TOK + (g + 1) * 512],
                                   start=(k == 0), stop=(k == KCH - 1))
                om[0] = E.t(i)
            pr.sec("tensor", f_omm)

            def f_obias(E, g=g, o=o, om=om, ev=ev):
                E.w(om[0])
                E.w(("dS", pr.cnt["dS"]))
                ev[0] = E.t(E.e.activation(outsb[:, :], P[4 + o % 2][:, :], AF.Identity,
                                           bias=bo[:, o:o + 1]))
            pr.sec("scalar", f_obias)

            def f_oadd(E, g=g, o=o, ev=ev):
                E.w(ev[0], xg[0])
                E.t(E.e.tensor_tensor(outsb[:, :], outsb[:, :],
                                      xT[:, o * 512:(o + 1) * 512], ALU.add))
            pr.sec("vector", f_oadd)

            def f_ost(E, g=g, o=o):
                E.w(self_tok(pr, "vector"))
                dst = rearr_kp(yT_d)
                E.dma(dst[:, o, g * 512:(g + 1) * 512], outsb[:, :])
            pr.sec("sync", f_ost)

    pr.fence()
    pr.pop_scope()


def _perf_proj(pr, xkv_d, gtok, wqT_d, wkT_d, wvT_d, scrk_d, ctoks):
    P = pr.ps
    xln = pr.sb["xln"]; wbuf = pr.sb["wbuf"]
    vres = pr.sb["vres"]; qtb = pr.sb["qtb"]; kevb = pr.sb["kevb"]
    XW = KCH * 512
    pr.fence()
    wt = [None]
    xt = {}

    def f_ldwk(E):
        wt[0] = E.dma(sb3(wbuf[:, :], KCH), rearr_kp(wkT_d), sem="dG")
    pr.sec("gpsimd", f_ldwk)

    def ldx(r, dep):
        sl = r % 2

        def f_ldx(E, r=r, sl=sl, dep=dep):
            E.w(gtok, dep)
            xt[r] = E.dma(sb3(xln[:, sl * XW:(sl + 1) * XW], KCH),
                          rearr_kp(xkv_d)[:, :, r * 512:(r + 1) * 512])
        pr.sec("sync", f_ldx)

    kmmt = {}; kevt = {}; kstt = {}
    for r in range(8):
        ldx(r, kmmt.get(r * 8 - 9))
        sl = r % 2
        for o in range(KCH):
            n = r * 8 + o

            def f_kmm(E, r=r, o=o, n=n, sl=sl):
                E.w(wt[0], xt[r], kevt.get(n - 2))
                for k in range(KCH):
                    i = E.e.matmul(P[4 + o % 2][:, :],
                                   wbuf[:, k * D + o * 128: k * D + o * 128 + 128],
                                   xln[:, sl * XW + k * 512:sl * XW + (k + 1) * 512],
                                   start=(k == 0), stop=(k == KCH - 1))
                kmmt[n] = E.t(i)
            pr.sec("tensor", f_kmm)

            def f_kev(E, o=o, n=n):
                E.w(kmmt[n], kstt.get(n - 2))
                kevt[n] = E.t(E.e.activation(kevb[:, (o % 2) * 512:(o % 2) * 512 + 512],
                                             P[4 + o % 2][:, :], AF.Copy, scale=0.125))
            pr.sec("scalar", f_kev)

            def f_kst(E, r=r, o=o, n=n):
                E.w(kevt[n])
                for hh in range(2):
                    h = o * 2 + hh
                    t = E.dma(scrk_d[h, :, r * 512:(r + 1) * 512],
                              kevb[hh * 64:(hh + 1) * 64, (o % 2) * 512:(o % 2) * 512 + 512])
                kstt[n] = t
            pr.sec("sync", f_kst)

    wv = [None]

    def f_ldwv(E):
        E.w(kmmt[63])
        wv[0] = E.dma(sb3(wbuf[:, :], KCH), rearr_kp(wvT_d), sem="dG")
    pr.sec("gpsimd", f_ldwv)

    vmmt = {}; vevt = {}
    for r in range(8):
        ldx(r, kmmt.get(55 + 8 * r) if r < 2 else vmmt.get(r * 8 - 9))
        sl = r % 2
        for ts in range(4):
            for o in range(2):
                m = r * 8 + ts * 2 + o

                def f_vmm(E, r=r, ts=ts, o=o, m=m, sl=sl):
                    E.w(wv[0], xt[r], vevt.get(m - 2), *ctoks)
                    for k in range(KCH):
                        i = E.e.matmul(P[4 + m % 2][:, :],
                                       xln[:, sl * XW + k * 512 + ts * 128:sl * XW + k * 512 + (ts + 1) * 128],
                                       wbuf[:, k * D + o * 512: k * D + (o + 1) * 512],
                                       start=(k == 0), stop=(k == KCH - 1))
                    vmmt[m] = E.t(i)
                pr.sec("tensor", f_vmm)

                def f_vev(E, r=r, ts=ts, o=o, m=m):
                    E.w(vmmt[m])
                    rt = r * 4 + ts
                    for hh in range(8):
                        h = o * 8 + hh
                        t = E.t(E.e.tensor_copy(
                            vres[:, rt * HEADS * 65 + h * 65: rt * HEADS * 65 + h * 65 + 64],
                            P[4 + m % 2][:, hh * 64:(hh + 1) * 64]))
                    vevt[m] = t
                pr.sec("vector", f_vev)

    wqq = [None]

    def f_ldwq(E):
        E.w(vmmt[63])
        wqq[0] = E.dma(sb3(wbuf[:, :], KCH), rearr_kp(wqT_d), sem="dG")
    pr.sec("gpsimd", f_ldwq)

    qmmt = {}; qevt = {}
    for idx, r in enumerate([6, 7, 4, 5]):
        if idx >= 2:
            ldx(r, qmmt.get((idx - 2) * 8 + 7))
        sl = r % 2
        for o in range(KCH):
            p = idx * 8 + o

            def f_qmm(E, r=r, o=o, p=p, sl=sl):
                E.w(wqq[0], xt[r], qevt.get(p - 2))
                for k in range(KCH):
                    i = E.e.matmul(P[4 + p % 2][:, :],
                                   wbuf[:, k * D + o * 128: k * D + o * 128 + 128],
                                   xln[:, sl * XW + k * 512:sl * XW + (k + 1) * 512],
                                   start=(k == 0), stop=(k == KCH - 1))
                qmmt[p] = E.t(i)
            pr.sec("tensor", f_qmm)

            def f_qev(E, r=r, o=o, p=p):
                E.w(qmmt[p])
                qevt[p] = E.t(E.e.activation(qtb[:, o * TOK + (r - 4) * 512: o * TOK + (r - 3) * 512],
                                             P[4 + p % 2][:, :], AF.Copy, scale=0.125))
            pr.sec("scalar", f_qev)


def _perf_head(pr, h, scrk_d, ctoks):
    P = pr.ps
    kst2 = pr.sb["wbuf"][0:64, 1024:1024 + NKV]
    qhst = pr.sb["wbuf"][0:64, 1024 + NKV:1024 + NKV + TOK]
    qtb = pr.sb["qtb"]
    vres = pr.sb["vres"]; phik = pr.sb["phik"]
    phiqt = pr.sb["phiqt"]; kvsb = pr.sb["kvsb"]; oall = pr.sb["oall"]
    drow = pr.sb["att_rows"]
    numsb = pr.sb["wbuf"][0:64, 512:1024]
    prj = pr.sb["prj"]; bias4 = pr.sb["bias4"]
    sqs = pr.sb["wbuf"][0:64, 0:512]
    ones64b = pr.sb["ones64b"]
    ch, hh = h // 2, h % 2
    kld = [None]

    def f_ldk(E, h=h):
        E.w(self_tok(pr, "tensor"))
        kld[0] = E.dma(kst2[:, :], scrk_d[h, :, :])
    pr.sec("sync", f_ldk)

    def f_stgq(E, h=h):
        E.w(self_tok(pr, "tensor"), kld[0])
        E.t(E.e.tensor_copy(qhst[:, :], qtb[hh * 64:hh * 64 + 64,
                                            ch * TOK:(ch + 1) * TOK]))
    pr.sec("vector", f_stgq)
    stgq_tok = self_tok(pr, "vector")

    xw_t = {}; sq_t = {}; nq_t = {}; max_t = {}; exp_t = {}; eps_t = {}
    kv_t = {}

    def phi_pair(j, src, xwb, nqb, cb, qphase):
        p = j % 2

        def f_xw(E, j=j, p=p):
            E.w(kld[0] if not qphase else stgq_tok, *ctoks)
            E.w(exp_t.get(j - 2))
            if j < 2 and qphase:
                E.w(self_tok(pr, "vector"))
            if j < 2 and not qphase:
                E.w(self_tok(pr, "scalar"))
            for s in range(2):
                i = E.e.matmul(P[xwb + p][:, s * 256:(s + 1) * 256],
                               src[0:64, (j * 2 + s) * 128:(j * 2 + s + 1) * 128],
                               prj[:, h * PM:(h + 1) * PM], start=True, stop=True)
            xw_t[j] = E.t(i)
        pr.sec("tensor", f_xw)

        def f_sq(E, j=j, p=p):
            E.w(kld[0] if not qphase else stgq_tok)
            E.w(nq_t.get(j - 2))
            sq_t[j] = E.t(E.e.activation(sqs[:, p * 256:(p + 1) * 256],
                                         src[0:64, j * 256:(j + 1) * 256], AF.Square))
        pr.sec("scalar", f_sq)

        def f_nq(E, j=j, p=p):
            E.w(sq_t[j], max_t.get(j - 2))
            for s in range(2):
                i = E.e.matmul(P[nqb + p][:, s:s + 1],
                               sqs[:, p * 256 + s * 128:p * 256 + (s + 1) * 128],
                               ones64b[:, :], start=True, stop=True)
            nq_t[j] = E.t(i)
        pr.sec("tensor", f_nq)

        def f_max(E, j=j, p=p):
            E.w(nq_t[j], exp_t.get(j - 2))
            for s in range(2):
                c = cb + p * 6 + s * 3
                t1 = E.t(E.e.reduce_max(bias4[:, c:c + 1],
                                        P[xwb + p][:, s * 256:(s + 1) * 256], axis=AX.X))
                E.w(t1)
                t2 = E.t(E.e.tensor_copy(bias4[:, c + 1:c + 2], P[nqb + p][:, s:s + 1]))
                E.w(t2)
                t3 = E.t(E.e.scalar_tensor_tensor(bias4[:, c + 2:c + 3],
                                                  bias4[:, c + 1:c + 2], -0.5,
                                                  bias4[:, c:c + 1],
                                                  ALU.mult, ALU.subtract))
            max_t[j] = t3
        pr.sec("vector", f_max)

        def f_exp(E, j=j, p=p):
            E.w(max_t[j])
            E.w((kv_t if not qphase else tr_t).get(j - 2))
            for s in range(2):
                c = cb + p * 6 + s * 3
                t = E.t(E.e.activation(phik[:, p * 512 + s * 256:p * 512 + (s + 1) * 256],
                                       P[xwb + p][:, s * 256:(s + 1) * 256], AF.Exp,
                                       bias=bias4[:, c + 2:c + 3]))
            exp_t[j] = t
        pr.sec("scalar", f_exp)

        def f_eps(E, j=j, p=p):
            E.w(exp_t[j])
            eps_t[j] = E.t(E.e.tensor_scalar(phik[:, p * 512:(p + 1) * 512],
                                             phik[:, p * 512:(p + 1) * 512],
                                             1e-6, None, ALU.add))
        pr.sec("vector", f_eps)

    pend = []
    for j in range(16):
        p = j % 2
        phi_pair(j, kst2, 4, 2, 0, False)

        def f_kv(E, j=j, p=p):
            E.w(eps_t[j], *ctoks)
            if j == 0:
                E.w(self_tok(pr, "vector"))
            for s in range(2):
                rt = j * 2 + s
                for mc in range(2):
                    i = E.e.matmul(P[6 + mc][:, 0:65],
                                   phik[:, p * 512 + s * 256 + mc * 128:
                                        p * 512 + s * 256 + (mc + 1) * 128],
                                   vres[:, rt * HEADS * 65 + h * 65: rt * HEADS * 65 + (h + 1) * 65],
                                   start=(j == 0 and s == 0), stop=(j == 15 and s == 1))
            kv_t[j] = E.t(i)
        pend.append(f_kv)
        if j >= 1:
            pr.sec("tensor", pend.pop(0))
    while pend:
        pr.sec("tensor", pend.pop(0))

    def f_kvev(E, h=h):
        E.w(kv_t[15])
        E.w(self_tok(pr, "tensor"))
        t1 = E.t(E.e.tensor_copy(kvsb[:, 0:65], P[6][:, 0:65]))
        E.w(t1)
        E.t(E.e.tensor_copy(kvsb[:, 65:130], P[7][:, 0:65]))
    pr.sec("vector", f_kvev)
    kvev_tok = self_tok(pr, "vector")

    xw_t = {}; sq_t = {}; nq_t = {}; max_t = {}; exp_t = {}; eps_t = {}
    tr_t = {}; trc_t = {}; num_t = {}; den_t = {}; dbc_t = {}; oap_t = {}

    pendq = []
    for j in range(8):
        p = j % 2
        phi_pair(j, qhst, 2, 0, 12, True)

        def f_tr(E, j=j, p=p):
            E.w(eps_t[j], trc_t.get(j - 2))
            for s in range(2):
                for mc in range(2):
                    i = E.e.transpose(
                        P[6].ap().bitcast(BF16)[0:128,
                                                (p * 2 + s) * 256 + mc * 128:
                                                (p * 2 + s) * 256 + (mc + 1) * 128],
                        phik[:, p * 512 + s * 256 + mc * 128:
                             p * 512 + s * 256 + (mc + 1) * 128],
                        pr.sb["eyeb"][:, :])
            tr_t[j] = E.t(i)
        pendq.append(("tensor", f_tr))

        def f_trc(E, j=j, p=p):
            E.w(tr_t[j], num_t.get(j - 2))
            trc_t[j] = E.t(E.e.tensor_copy(
                phiqt[:, p * 512:(p + 1) * 512],
                P[6].ap().bitcast(BF16)[0:128, p * 512:(p + 1) * 512]))
        pendq.append(("vector", f_trc))

        def f_num(E, j=j, p=p):
            E.w(trc_t[j], den_t.get(j - 2), kvev_tok)
            for s in range(2):
                for mc in range(2):
                    i = E.e.matmul(P[7][0:65, (p * 2 + s) * 128:(p * 2 + s + 1) * 128],
                                   kvsb[:, mc * 65:(mc + 1) * 65],
                                   phiqt[:, p * 512 + s * 256 + mc * 128:
                                         p * 512 + s * 256 + (mc + 1) * 128],
                                   start=(mc == 0), stop=(mc == 1))
            num_t[j] = E.t(i)
        pendq.append(("tensor", f_num))

        def f_den(E, j=j, p=p):
            E.w(num_t[j], dbc_t.get(j - 2))
            for s in range(2):
                c = (p * 2 + s) * 128
                t1 = E.t(E.e.tensor_scalar(drow[:, c:c + 128],
                                           P[7][64:65, c:c + 128], 1e-6, None, ALU.add))
                E.w(t1)
                t2 = E.t(E.e.reciprocal(drow[:, c:c + 128], drow[:, c:c + 128]))
                E.w(t2)
                t3 = E.t(E.e.tensor_copy(numsb[:, c:c + 128], P[7][0:64, c:c + 128]))
            den_t[j] = t3
        pendq.append(("vector", f_den))

        def f_dbc(E, j=j, p=p):
            E.w(den_t[j], oap_t.get(j - 2))
            for s in range(2):
                c = (p * 2 + s) * 128
                i = E.e.matmul(P[0 + p][0:64, 128 + s * 128:256 + s * 128],
                               pr.sb["ones1r"][:, 0:64], drow[:, c:c + 128],
                               start=True, stop=True)
            dbc_t[j] = E.t(i)
        pendq.append(("tensor", f_dbc))

        def f_oap(E, j=j, p=p):
            E.w(dbc_t[j])
            for s in range(2):
                c = (p * 2 + s) * 128
                rt = j * 2 + s
                t = E.t(E.e.tensor_mul(oall[hh * 64:hh * 64 + 64,
                                            ch * TOK + rt * 128: ch * TOK + (rt + 1) * 128],
                                       numsb[:, c:c + 128],
                                       P[0 + p][0:64, 128 + s * 128:256 + s * 128]))
            oap_t[j] = t
        pendq.append(("vector", f_oap))
        if True:
            for _ in range(6):
                en, fn = pendq.pop(0)
                pr.sec(en, fn)
    while pendq:
        en, fn = pendq.pop(0)
        pr.sec(en, fn)


def phase_gla(pr, io, ctoks0):
    pr.push_scope()
    ctoks = list(ctoks0)
    nc = pr.nc
    sid = pr._scope_id
    xT_d = io["xT"]
    wqn_d = io["wqn"]; wkn_d = io["wkn"]; wvn_d = io["wvn"]; wan_d = io["wan"]
    bar_d = io["bar"]; gr_d = io["gr"]; br_d = io["br"]
    res_d = pr.dram(f"glares{sid}", [D, TOK], BF16).ap()
    qsh_h = pr.dram(f"glaq{sid}", [GH, 64, NKV], BF16, shared=True)
    ksh_h = pr.dram(f"glak{sid}", [GH, 64, NKV], BF16, shared=True)
    lash_h = pr.dram(f"glala{sid}", [GH, 64, NKV], F32, shared=True)
    ktsh_h = pr.dram(f"glakt{sid}", [GH, NKV, 64], BF16, shared=True)
    vtsh_h = pr.dram(f"glavt{sid}", [GH, NKV, 64], BF16, shared=True)
    osh_h = pr.dram(f"glao{sid}", [GH, 64, NKV], BF16, shared=True)
    qloc = pr.dram(f"glaql{sid}", [GH, 64, TOK], BF16).ap()
    kloc = pr.dram(f"glakl{sid}", [GH, 64, TOK], BF16).ap()
    laloc = pr.dram(f"glalal{sid}", [GH, 64, TOK], F32).ap()
    ktloc = pr.dram(f"glaktl{sid}", [GH, TOK, 64], BF16).ap()
    vtloc = pr.dram(f"glavtl{sid}", [GH, TOK, 64], BF16).ap()
    qun = pr.dram(f"glaqu{sid}", [4, 64, NKV], BF16).ap()
    kun = pr.dram(f"glaku{sid}", [4, 64, NKV], BF16).ap()
    laun = pr.dram(f"glalau{sid}", [4, 64, NKV], F32).ap()
    ktun = pr.dram(f"glaktu{sid}", [4, NKV, 64], BF16).ap()
    vtun = pr.dram(f"glavtu{sid}", [4, NKV, 64], BF16).ap()
    oun = pr.dram(f"glaou{sid}", [4, 64, NKV], BF16).ap()
    oloc = pr.dram(f"glaol{sid}", [GH, 64, TOK], BF16).ap()

    xT = pr.sbuf("sxT", [128, KCH * 512], F32)
    ln1 = pr.sbuf("ln1", [128, KCH * 512], F32)
    ln1b = pr.sbuf("ln1b", [128, KCH * 512], BF16)
    xn = pr.sbuf("xn", [128, KCH * 512], BF16)
    wbuf = pr.sbuf("wbuf", [128, KCH * DI], BF16)
    gam = pr.sbuf("gam", [128, KCH], F32)
    bet = pr.sbuf("bet", [128, KCH], F32)
    bacol = pr.sbuf("bacol", [128, 4], F32)
    qf = pr.sbuf("qf", [128, 4 * 512], BF16)
    kf = pr.sbuf("kf", [128, 4 * 512], BF16)
    vf = pr.sbuf("vf", [128, 4 * 512], BF16)
    af = pr.sbuf("af", [128, 4 * 512], F32)
    ksq = pr.sbuf("ksq", [128, 512], BF16)
    nrow = pr.sbuf("nrow", [1, 2 * 512], F32)
    ktn = pr.sbuf("ktn", [128, 512], BF16)
    alloc_ln(pr, "att")

    def f_i2(E):
        ctoks.append(E.dma(gam[:, :], gr_d))
        ctoks.append(E.dma(bet[:, :], br_d))
        ctoks.append(E.dma(bacol[:, :], bar_d))
    pr.sec("sync", f_i2)

    P = pr.ps
    for r in range(TOK // 512):
        _glaq_tile(pr, r, xT_d, wqn_d, wkn_d, wvn_d, wan_d,
                   res_d, qloc, kloc, laloc, ktloc, vtloc, ctoks)

    pr.fence()

    def f_out(par, dd):
        dd.wait(("dS", pr.cnt["dS"]))
        c0 = par * TOK
        dd(qsh_h.ap()[:, :, c0:c0 + TOK], qloc)
        dd(ksh_h.ap()[:, :, c0:c0 + TOK], kloc)
        dd(lash_h.ap()[:, :, c0:c0 + TOK], laloc)
        dd(ktsh_h.ap()[:, c0:c0 + TOK, :], ktloc)
        dd(vtsh_h.ap()[:, c0:c0 + TOK, :], vtloc)
    pr.pswitch(f_out)
    pr.core_barrier()

    def f_in(par, dd):
        h0 = par * 4
        dd(qun, qsh_h.ap()[h0:h0 + 4, :, :])
        dd(kun, ksh_h.ap()[h0:h0 + 4, :, :])
        dd(laun, lash_h.ap()[h0:h0 + 4, :, :])
        dd(ktun, ktsh_h.ap()[h0:h0 + 4, :, :])
        dd(vtun, vtsh_h.ap()[h0:h0 + 4, :, :])
    gtok = pr.pswitch(f_in)

    pr.pop_scope()
    pr.push_scope()
    _gla_scan_phase(pr, qun, kun, laun, ktun, vtun, oun, gtok, ctoks)
    pr.fence()

    def f_oout(par, dd):
        dd.wait(("dS", pr.cnt["dS"]))
        dd(osh_h.ap()[par * 4:par * 4 + 4, :, :], oun)
    pr.pswitch(f_oout)
    pr.core_barrier()

    def f_oin(par, dd):
        c0 = par * TOK
        dd(oloc, osh_h.ap()[:, :, c0:c0 + TOK])
    gtok2 = pr.pswitch(f_oin)

    pr.pop_scope()
    pr.push_scope()
    xT2 = pr.sbuf("sxT", [128, KCH * 512], F32)
    _gla_out_phase(pr, io, res_d, oloc, gtok2, ctoks)
    pr.fence()
    pr.pop_scope()


def _glaq_tile(pr, r, xT_d, wqn_d, wkn_d, wvn_d, wan_d,
               res_d, qloc, kloc, laloc, ktloc, vtloc, ctoks):
    P = pr.ps
    xT = pr.sb["sxT"]; ln1 = pr.sb["ln1"]; ln1b = pr.sb["ln1b"]; xn = pr.sb["xn"]
    wbuf = pr.sb["wbuf"]; qf = pr.sb["qf"]; kf = pr.sb["kf"]; vf = pr.sb["vf"]
    af = pr.sb["af"]; ksq = pr.sb["ksq"]; nrow = pr.sb["nrow"]; ktn = pr.sb["ktn"]
    pr.fence()
    ld = [None]

    def f_ld(E, r=r):
        src = rearr_kp(xT_d)
        ld[0] = E.dma(sb3(xT[:, :], KCH), src[:, :, r * 512:(r + 1) * 512])
    pr.sec("sync", f_ld)

    l1 = emit_ln(pr, xT, ln1, 512, [ld[0]] + ctoks, "att")
    def f_cp(E, r=r):
        E.w(l1[-1])
        for k in range(KCH):
            E.t(E.e.tensor_copy(ln1b[:, k * 512:(k + 1) * 512], ln1[:, k * 512:(k + 1) * 512]))
    pr.sec("gpsimd", f_cp)

    def f_str(E, r=r):
        E.w(self_tok(pr, "gpsimd"))
        dst = rearr_kp(res_d)
        E.dma(dst[:, :, r * 512:(r + 1) * 512], sb3(ln1b[:, :], KCH))
    pr.sec("sync", f_str)

    l2 = emit_ln(pr, ln1, xn, 512, [l1[-1]], "att",
                 gamma_beta=(pr.sb["gam"], pr.sb["bet"]))

    for (wd, dstb, isa) in ((wqn_d, qf, 0), (wkn_d, kf, 0), (wvn_d, vf, 0), (wan_d, af, 1)):
        wt = [None]

        def f_ldw(E, wd=wd):
            E.w(self_tok(pr, "tensor"))
            wt[0] = E.dma(sb3(wbuf[:, 0:KCH * DI], KCH),
                          wd.rearrange("(k p) n -> p k n", p=128), sem="dG")
        pr.sec("gpsimd", f_ldw)

        for o in range(4):
            pm = [None]

            def f_mm(E, r=r, o=o, wt=wt, pm=pm):
                E.w(wt[0], l2[-1])
                E.w(self_tok(pr, "scalar"), self_tok(pr, "vector"))
                for k in range(KCH):
                    i = E.e.matmul(P[4 + o % 2][:, :],
                                   wbuf[:, k * DI + o * 128: k * DI + o * 128 + 128],
                                   xn[:, k * 512:(k + 1) * 512],
                                   start=(k == 0), stop=(k == KCH - 1))
                pm[0] = E.t(i)
            pr.sec("tensor", f_mm)

            if isa == 0:
                def f_ev(E, o=o, pm=pm, dstb=dstb):
                    E.w(pm[0])
                    E.t(E.e.activation(dstb[:, o * 512:(o + 1) * 512],
                                       P[4 + o % 2][:, :], AF.Copy))
                pr.sec("scalar", f_ev)
            else:
                sg = [None]

                def f_sig(E, o=o, pm=pm, sg=sg):
                    E.w(pm[0])
                    t1 = E.t(E.e.activation(pr.sb["att_tmp"][:, :], P[4 + o % 2][:, :],
                                            AF.Sigmoid, bias=pr.sb["bacol"][:, o:o + 1]))
                    E.w(t1)
                    sg[0] = E.t(E.e.activation(af[:, o * 512:(o + 1) * 512],
                                               pr.sb["att_tmp"][:, :], AF.Ln))
                pr.sec("scalar", f_sig)

                def f_cl(E, o=o, sg=sg):
                    E.w(sg[0])
                    E.t(E.e.tensor_scalar_max(af[:, o * 512:(o + 1) * 512],
                                              af[:, o * 512:(o + 1) * 512], -5.0))
                pr.sec("vector", f_cl)

    for o in range(4):
        sqt = [None]
        nr = [None]

        def f_ksq(E, r=r, o=o, sqt=sqt):
            E.w(self_tok(pr, "tensor"), self_tok(pr, "vector"))
            sqt[0] = E.t(E.e.activation(ksq[:, :], kf[:, o * 512:(o + 1) * 512], AF.Square))
        pr.sec("scalar", f_ksq)

        def f_nmm(E, o=o, sqt=sqt, nr=nr):
            E.w(sqt[0])
            E.w(self_tok(pr, "vector"))
            i = E.e.matmul(P[0][0:1, :], pr.sb["ones128b"][0:64, :], ksq[0:64, :],
                           start=True, stop=True)
            i = E.e.matmul(P[1][0:1, :], pr.sb["ones128b"][64:128, :], ksq[64:128, :],
                           start=True, stop=True)
            nr[0] = E.t(i)
        pr.sec("tensor", f_nmm)

        def f_nr(E, o=o, nr=nr):
            E.w(nr[0])
            t1 = E.t(E.e.tensor_copy(nrow[:, 0:512], P[0][0:1, :]))
            t2 = E.t(E.e.tensor_copy(nrow[:, 512:1024], P[1][0:1, :]))
            E.w(t1, t2)
            nr[0] = t2
        pr.sec("vector", f_nr)

        def f_ns(E, o=o, nr=nr):
            E.w(nr[0])
            nr[0] = E.t(E.e.activation(nrow[:, :], nrow[:, :], AF.Sqrt))
        pr.sec("scalar", f_ns)

        def f_nrec(E, o=o, nr=nr):
            E.w(nr[0])
            t1 = E.t(E.e.tensor_scalar_max(nrow[:, :], nrow[:, :], 1e-12))
            E.w(t1)
            t2 = E.t(E.e.reciprocal(nrow[:, 0:512], nrow[:, 0:512]))
            E.w(t2)
            t3 = E.t(E.e.reciprocal(nrow[:, 512:1024], nrow[:, 512:1024]))
            nr[0] = t3
        pr.sec("vector", f_nrec)

        def f_nbc(E, o=o, nr=nr):
            E.w(nr[0])
            E.e.matmul(P[2][0:64, :], pr.sb["ones1r"][:, 0:64], nrow[:, 0:512],
                       start=True, stop=True)
            i = E.e.matmul(P[2][64:128, :], pr.sb["ones1r"][:, 0:64], nrow[:, 512:1024],
                           start=True, stop=True)
            nr[0] = E.t(i)
        pr.sec("tensor", f_nbc)

        def f_nap(E, o=o, nr=nr):
            E.w(nr[0])
            E.t(E.e.tensor_mul(kf[:, o * 512:(o + 1) * 512],
                               kf[:, o * 512:(o + 1) * 512], P[2][:, :]))
        pr.sec("vector", f_nap)

    pr.fence()

    def f_stq(E, r=r):
        for o in range(4):
            for hh in range(2):
                h = o * 2 + hh
                E.dma(qloc[h, :, r * 512:(r + 1) * 512],
                      qf[hh * 64:(hh + 1) * 64, o * 512:(o + 1) * 512])
                E.dma(kloc[h, :, r * 512:(r + 1) * 512],
                      kf[hh * 64:(hh + 1) * 64, o * 512:(o + 1) * 512])
                E.dma(laloc[h, :, r * 512:(r + 1) * 512],
                      af[hh * 64:(hh + 1) * 64, o * 512:(o + 1) * 512])
    pr.sec("sync", f_stq)

    for src_b, dst_h in ((pr.sb["kf"], ktloc), (pr.sb["vf"], vtloc)):
        for o in range(4):
            for ts in range(4):
                tt = [None]

                def f_tr(E, o=o, ts=ts, src_b=src_b, tt=tt):
                    E.w(self_tok(pr, "vector"), self_tok(pr, "sync"))
                    E.w(("dS", pr.cnt["dS"]))
                    i = E.e.transpose(P[3].ap().bitcast(BF16)[0:128, 0:128],
                                      src_b[:, o * 512 + ts * 128: o * 512 + (ts + 1) * 128],
                                      pr.sb["eyeb"][:, :])
                    tt[0] = E.t(i)
                pr.sec("tensor", f_tr)

                def f_trc(E, o=o, ts=ts, tt=tt):
                    E.w(tt[0])
                    tt[0] = E.t(E.e.tensor_copy(ktn[:, 0:128],
                                                P[3].ap().bitcast(BF16)[0:128, 0:128]))
                pr.sec("vector", f_trc)

                def f_st2(E, r=r, o=o, ts=ts, dst_h=dst_h, tt=tt):
                    E.w(tt[0])
                    for hh in range(2):
                        h = o * 2 + hh
                        E.dma(dst_h[h, r * 512 + ts * 128: r * 512 + (ts + 1) * 128, :],
                              ktn[:, hh * 64:(hh + 1) * 64])
                pr.sec("sync", f_st2)


def _gla_scan_phase(pr, qun, kun, laun, ktun, vtun, oun, gtok, ctoks0):
    nc = pr.nc
    ctoks = list(ctoks0) + [gtok]

    bd = np.zeros((128, 128), dtype=np.float32)
    for s in range(128):
        sub = s // 16
        bd[s, s:(sub + 1) * 16] = 1.0
    tri_c = pr.const(bd, "bd128_c")

    qT4 = pr.sbuf("qT4", [64, 4 * NKV], BF16)
    kT4 = pr.sbuf("kT4", [64, 4 * NKV], BF16)
    kt4 = pr.sbuf("kt4", [128, 4 * 32 * 64], BF16)
    vt4 = pr.sbuf("vt4", [128, 4 * 32 * 64], BF16)
    oacc = pr.sbuf("oacc", [64, 4 * NKV], BF16)
    lab = pr.sbuf("lab", [64, 2 * 4 * 128], F32)
    LS = pr.sbuf("LS", [64, 4 * 128], F32)
    nLS = pr.sbuf("nLS", [64, 4 * 128], F32)
    reb = pr.sbuf("reb", [64, 4 * 128], F32)
    dif = pr.sbuf("dif", [64, 4 * 128], F32)
    BT = pr.sbuf("BT", [64, 4 * 128], F32)
    EIV = pr.sbuf("EIV", [64, 4 * 128], F32)
    qb4 = pr.sbuf("qb4", [64, 4 * 128], BF16)
    kiv4 = pr.sbuf("kiv4", [64, 4 * 128], BF16)
    eivt = pr.sbuf("eivt", [128, 4 * 64], BF16)
    kivt = pr.sbuf("kivt", [128, 4 * 64], BF16)
    S4 = pr.sbuf("S4", [64, 4 * 64], BF16)
    St = pr.sbuf("St", [64, 4 * 64], BF16)
    PTm = pr.sbuf("PTm", [128, 128], BF16)
    tri = pr.sbuf("tri16", [128, 128], BF16)
    zc = pr.sbuf("zc", [64, 1], F32)
    subm = pr.sbuf("subm", [128, 8], F32)
    kms = pr.sbuf("kms", [128, 64], BF16)

    sm = np.zeros((128, 8), dtype=np.float32)
    for s in range(8):
        sm[s * 16:(s + 1) * 16, s] = 1.0
    sm_c = pr.const(sm, "subm_c")

    def f_i(E):
        ctoks.append(E.dma(tri[:, :], tri_c.ap(), sem="dG"))
        ctoks.append(E.dma(subm[:, :], sm_c.ap(), sem="dG"))
        ctoks.append(E.t(E.e.memset(zc[:, :], 0.0)))
        ctoks.append(E.t(E.e.memset(pr.sb["S4"][:, :], 0.0)))
    pr.sec("gpsimd", f_i)

    ldt = [None]
    SLOT = 4 * 64 * NKV
    SLOTV = 4 * NKV * 64

    def f_ld0(E):
        E.w(gtok)
        for u in range(4):
            E.dma(qT4[:, u * NKV:(u + 1) * NKV], qun[u, :, :])
            E.dma(kT4[:, u * NKV:(u + 1) * NKV], kun[u, :, :])
            E.dma(sb3(kt4[:, u * 2048:(u + 1) * 2048], 32),
                  ktun[u, :, :].rearrange("(r p) d -> p r d", p=128))
            ldt[0] = E.dma(sb3(vt4[:, u * 2048:(u + 1) * 2048], 32),
                           vtun[u, :, :].rearrange("(r p) d -> p r d", p=128))
    pr.sec("sync", f_ld0)
    ctoks.append(ldt[0])

    P = pr.ps
    la_tok = [None, None]

    def f_lal(E, j=0):
        E.w(gtok)
        for u in range(4):
            la_tok[0] = E.dma(lab[:, u * 128:(u + 1) * 128], laun[u, :, 0:128])
    pr.sec("sync", f_lal)

    for j in range(32):
        _gla_block(pr, j, laun, la_tok, ctoks)

    pr.fence()

    def f_st(E):
        for u in range(4):
            E.dma(oun[u, :, :], oacc[:, u * NKV:(u + 1) * NKV])
    pr.sec("sync", f_st)


def _gla_block(pr, j, laun, la_tok, ctoks):
    P = pr.ps
    sb = pr.sb
    buf = j % 2
    nbuf = (j + 1) % 2
    lab = sb["lab"]; LS = sb["LS"]; nLS = sb["nLS"]; reb = sb["reb"]; dif = sb["dif"]
    BT = sb["BT"]; EIV = sb["EIV"]; qb4 = sb["qb4"]; kiv4 = sb["kiv4"]
    eivt = sb["eivt"]; kivt = sb["kivt"]; S4 = sb["S4"]; St = sb["St"]
    PTm = sb["PTm"]; tri = sb["tri16"]; zc = sb["zc"]; subm = sb["subm"]; kms = sb["kms"]
    qT4 = sb["qT4"]; kT4 = sb["kT4"]; kt4 = sb["kt4"]; vt4 = sb["vt4"]; oacc = sb["oacc"]
    lslice = lambda u: slice(buf * 512 + u * 128, buf * 512 + (u + 1) * 128)

    if j < 31:
        def f_lal(E, j=j):
            E.w(self_tok(pr, "vector"))
            for u in range(4):
                la_tok[0] = E.dma(lab[:, nbuf * 512 + u * 128: nbuf * 512 + (u + 1) * 128],
                                  laun[u, :, (j + 1) * 128:(j + 2) * 128])
        pr.sec("sync", f_lal)

    cu = [None]

    def f_scan(E, j=j, cu=cu):
        E.w(("dS", pr.cnt["dS"]), *ctoks)
        t = None
        for u in range(4):
            t = E.t(E.e.tensor_tensor_scan(LS[:, u * 128:(u + 1) * 128],
                                           lab[:, lslice(u)], lab[:, lslice(u)],
                                           0.0, ALU.add, ALU.bypass))
        E.w(t)
        tz = None
        for u in range(4):
            for s in range(8):
                srccol = zc[:, 0:1] if s == 0 else LS[:, u * 128 + s * 16 - 1: u * 128 + s * 16]
                tz = E.t(E.e.tensor_copy(
                    reb[:, u * 128 + s * 16: u * 128 + (s + 1) * 16],
                    srccol.to_broadcast([64, 16])))
        E.w(tz)
        t3 = E.t(E.e.tensor_sub(dif[:, :], LS[:, :], reb[:, :]))
        E.w(t3)
        cu[0] = t3
    pr.sec("vector", f_scan)

    ex = [None, None]

    def f_exp(E, j=j, cu=cu, ex=ex):
        E.w(cu[0])
        ex[0] = E.t(E.e.activation(BT[:, :], dif[:, :], AF.Exp))
        ex[1] = E.t(E.e.activation(EIV[:, :], dif[:, :], AF.Exp, scale=-1.0))
    pr.sec("scalar", f_exp)

    mu = [None]

    def f_mul2(E, j=j, ex=ex, mu=mu):
        E.w(ex[0], ex[1])
        E.w(self_tok(pr, "tensor"))
        t = None
        for u in range(4):
            t = E.t(E.e.tensor_mul(qb4[:, u * 128:(u + 1) * 128],
                                   qT4[:, u * NKV + j * 128: u * NKV + (j + 1) * 128],
                                   BT[:, u * 128:(u + 1) * 128]))
            t = E.t(E.e.tensor_mul(kiv4[:, u * 128:(u + 1) * 128],
                                   kT4[:, u * NKV + j * 128: u * NKV + (j + 1) * 128],
                                   EIV[:, u * 128:(u + 1) * 128]))
        mu[0] = t
    pr.sec("vector", f_mul2)

    ett = [None]

    def f_etr(E, j=j, mu=mu, ett=ett):
        E.w(mu[0])
        for u in range(4):
            i = E.e.transpose(P[0][:, u * 64:(u + 1) * 64],
                              EIV[:, u * 128:(u + 1) * 128],
                              pr.sb["eye"][0:64, 0:64])
        ett[0] = E.t(i)
    pr.sec("tensor", f_etr)

    def f_etc(E, j=j, ett=ett):
        E.w(ett[0])
        t = None
        for u in range(4):
            t = E.t(E.e.tensor_copy(eivt[:, u * 64:(u + 1) * 64],
                                    P[0][:, u * 64:(u + 1) * 64]))
        E.w(t)
        t2 = None
        for u in range(4):
            t2 = E.t(E.e.tensor_mul(kivt[:, u * 64:(u + 1) * 64],
                                    kt4[:, u * 2048 + j * 64: u * 2048 + (j + 1) * 64],
                                    eivt[:, u * 64:(u + 1) * 64]))
        ett[0] = t2
    pr.sec("vector", f_etc)

    for u in range(4):
        pt = [None]

        def f_pt(E, j=j, u=u, pt=pt, ett=ett):
            E.w(ett[0], self_tok(pr, "vector"))
            i = E.e.matmul(P[1][:, 0:128],
                           kiv4[:, u * 128:(u + 1) * 128],
                           qb4[:, u * 128:(u + 1) * 128],
                           start=True, stop=True)
            pt[0] = E.t(i)
        pr.sec("tensor", f_pt)

        def f_ptm(E, j=j, u=u, pt=pt):
            E.w(pt[0])
            pt[0] = E.t(E.e.tensor_mul(PTm[:, :], P[1][:, 0:128], tri[:, :]))
        pr.sec("vector", f_ptm)

        pv = [None]

        def f_pv(E, j=j, u=u, pt=pt, pv=pv):
            E.w(pt[0])
            i = E.e.matmul(P[2][0:64, u * 128:(u + 1) * 128],
                           vt4[:, u * 2048 + j * 64: u * 2048 + (j + 1) * 64],
                           PTm[:, :], start=True, stop=False, skip_group_check=True)
            pv[0] = E.t(i)
        pr.sec("tensor", f_pv)

        for s in range(8):
            sup = [None]

            def f_int(E, j=j, u=u, s=s, pv=pv):
                E.w(self_tok(pr, "vector"))
                E.e.matmul(P[2][0:64, u * 128 + s * 16: u * 128 + (s + 1) * 16],
                           S4[:, u * 64:(u + 1) * 64],
                           qb4[:, u * 128 + s * 16: u * 128 + (s + 1) * 16],
                           start=False, stop=(s == 7), skip_group_check=True)
            pr.sec("tensor", f_int)

            def f_kms(E, j=j, u=u, s=s, sup=sup):
                E.w(self_tok(pr, "tensor"))
                sup[0] = E.t(E.e.tensor_scalar_mul(kms[:, :],
                                                   kivt[:, u * 64:(u + 1) * 64],
                                                   subm[:, s:s + 1]))
            pr.sec("vector", f_kms)

            um = [None]

            def f_umm(E, j=j, u=u, s=s, sup=sup, um=um):
                E.w(sup[0])
                i = E.e.matmul(P[3][0:64, 0:64], kms[:, :],
                               vt4[:, u * 2048 + j * 64: u * 2048 + (j + 1) * 64],
                               start=True, stop=True)
                um[0] = E.t(i)
            pr.sec("tensor", f_umm)

            def f_sup(E, j=j, u=u, s=s, um=um):
                E.w(um[0])
                t1 = E.t(E.e.tensor_add(St[:, u * 64:(u + 1) * 64],
                                        S4[:, u * 64:(u + 1) * 64], P[3][0:64, 0:64]))
                E.w(t1)
                E.t(E.e.tensor_scalar_mul(
                    S4[:, u * 64:(u + 1) * 64], St[:, u * 64:(u + 1) * 64],
                    BT[:, u * 128 + (s + 1) * 16 - 1: u * 128 + (s + 1) * 16]))
            pr.sec("vector", f_sup)

        def f_oev(E, j=j, u=u):
            E.w(self_tok(pr, "tensor"))
            E.t(E.e.tensor_copy(oacc[:, u * NKV + j * 128: u * NKV + (j + 1) * 128],
                                P[2][0:64, u * 128:(u + 1) * 128]))
        pr.sec("vector", f_oev)



def _gla_out_phase(pr, io, res_d, oloc, gtok2, ctoks):
    xT_d = io["xT"]
    won_d = io["won"]
    yT_d = io["yT"]

    xT = pr.sb["sxT"]
    resb = pr.sbuf("resb", [128, KCH * 512], BF16)
    ot = pr.sbuf("ot", [128, 4 * TOK], BF16)
    wbuf2 = pr.sbuf("wbuf2", [128, 4 * D], BF16)
    outsb = pr.sbuf("goutsb", [128, 512], F32)
    P = pr.ps

    wt = [None]
    olt = [None]

    def f_ldw(E):
        wt[0] = E.dma(sb3(wbuf2[:, :], 4), won_d.rearrange("(k p) n -> p k n", p=128), sem="dG")
    pr.sec("gpsimd", f_ldw)

    def f_ldo(E):
        E.w(gtok2)
        olt[0] = E.dma(ot[:, :].rearrange("p (k t) -> p k t", k=4),
                       oloc.rearrange("(k hh) d t -> (hh d) k t", k=4))
    pr.sec("sync", f_ldo)

    for g in range(4):
        xg = [None]

        def f_ldx(E, g=g):
            E.w(("dS", pr.cnt["dS"]))
            E.dma(sb3(xT[:, :], KCH), rearr_kp(xT_d)[:, :, g * 512:(g + 1) * 512])
            xg[0] = E.dma(sb3(resb[:, :], KCH), rearr_kp(res_d)[:, :, g * 512:(g + 1) * 512])
        pr.sec("sync", f_ldx)

        for o in range(KCH):
            om = [None]

            def f_omm(E, g=g, o=o, om=om):
                E.w(wt[0], olt[0], self_tok(pr, "vector"))
                for k in range(4):
                    i = E.e.matmul(P[4 + o % 2][:, :],
                                   wbuf2[:, k * D + o * 128: k * D + o * 128 + 128],
                                   ot[:, k * TOK + g * 512: k * TOK + (g + 1) * 512],
                                   start=(k == 0), stop=(k == 3))
                om[0] = E.t(i)
            pr.sec("tensor", f_omm)

            def f_oadd(E, g=g, o=o, om=om):
                E.w(om[0], xg[0])
                E.w(("dS", pr.cnt["dS"]))
                t1 = E.t(E.e.tensor_tensor(outsb[:, :], xT[:, o * 512:(o + 1) * 512],
                                           P[4 + o % 2][:, :], ALU.add))
                E.w(t1)
                E.t(E.e.tensor_tensor(outsb[:, :], outsb[:, :],
                                      resb[:, o * 512:(o + 1) * 512], ALU.add))
            pr.sec("vector", f_oadd)

            def f_ost(E, g=g, o=o):
                E.w(self_tok(pr, "vector"))
                E.dma(rearr_kp(yT_d)[:, o, g * 512:(g + 1) * 512], outsb[:, :])
            pr.sec("sync", f_ost)


def _colchunks(v):
    return np.ascontiguousarray(v.reshape(8, 128).T)



def alloc_quant(pr, W, slots=1):
    pr.sbuf("qa", [128, slots * W], F32)
    pr.sbuf("qA", [128, slots * W], F32)
    pr.sbuf("qB", [128, slots * W], F32)
    pr.sbuf("qm", [128, slots * W], F32)
    pr.sbuf("qs", [128, slots * (W // 16)], F32)
    pr.sbuf("qr", [128, slots * (W // 16)], F32)


def emit_quant(pr, src_fp, dst_bf, W, deps, slot=0, W0=None, precise=False):
    W0 = W0 or W
    b = slot * W0
    nb = slot * (W0 // 16)
    qa = pr.sb["qa"][:, b:b + W0]; qA = pr.sb["qA"][:, b:b + W0]
    qB = pr.sb["qB"][:, b:b + W0]; qm = pr.sb["qm"][:, b:b + W0]
    qs = pr.sb["qs"][:, nb:nb + W0 // 16]; qr = pr.sb["qr"][:, nb:nb + W0 // 16]
    NB = W // 16
    out_tok = [None]
    st = [None]

    def f_abs(E):
        E.w(*deps)
        if not precise:
            E.w(self_tok(pr, "vector"))
        st[0] = E.t(E.e.activation(qa[:, 0:W], src_fp, AF.Abs))
    pr.sec("scalar", f_abs)

    def f_max(E):
        E.w(st[0])
        t = E.t(E.e.tensor_max(qm[:, 0:W // 2], qa[:, 0:W:2], qa[:, 1:W:2]))
        E.w(t)
        t = E.t(E.e.tensor_max(qm[:, W // 2:W // 2 + W // 4],
                               qm[:, 0:W // 2:2], qm[:, 1:W // 2:2]))
        E.w(t)
        t = E.t(E.e.tensor_max(qm[:, 0:W // 8],
                               qm[:, W // 2:W // 2 + W // 4:2],
                               qm[:, W // 2 + 1:W // 2 + W // 4:2]))
        E.w(t)
        t = E.t(E.e.tensor_max(qm[:, W // 2:W // 2 + NB],
                               qm[:, 0:W // 8:2], qm[:, 1:W // 8:2]))
        E.w(t)
        t1 = E.t(E.e.tensor_scalar(qs[:, 0:NB], qm[:, W // 2:W // 2 + NB],
                                   1.0 / 6.0, 1e-12, ALU.mult, ALU.max))
        E.w(t1)
        t2 = E.t(E.e.reciprocal(qr[:, 0:NB], qs[:, 0:NB]))
        E.w(t2)
        t3 = E.t(E.e.tensor_mul(
            qa[:, 0:W].rearrange("p (b k) -> p b k", k=16),
            src_fp.rearrange("p (b k) -> p b k", k=16),
            qr[:, 0:NB].unsqueeze(2).to_broadcast([128, NB, 16])))
        st[0] = t3
    pr.sec("vector", f_max)

    def f_t(E):
        E.w(st[0])
        st[0] = E.t(E.e.activation(qm[:, 0:W], qa[:, 0:W], AF.Abs))
    pr.sec("scalar", f_t)

    def f_lv(E):
        t_ = qm
        E.w(st[0])
        t = E.t(E.e.tensor_single_scalar(qB[:, 0:W], t_[:, 0:W], 5.0, ALU.is_gt))
        E.w(t)
        t = E.t(E.e.tensor_scalar_mul(qB[:, 0:W], qB[:, 0:W], 2.0))
        E.w(t)
        for thr in (2.5, 3.5):
            t = E.t(E.e.scalar_tensor_tensor(qB[:, 0:W], t_[:, 0:W], thr,
                                             qB[:, 0:W], ALU.is_gt, ALU.add))
            E.w(t)
        t = E.t(E.e.tensor_single_scalar(qA[:, 0:W], t_[:, 0:W], 0.25, ALU.is_gt))
        E.w(t)
        for thr in (0.75, 1.25, 1.75):
            t = E.t(E.e.scalar_tensor_tensor(qA[:, 0:W], t_[:, 0:W], thr,
                                             qA[:, 0:W], ALU.is_gt, ALU.add))
            E.w(t)
        t = E.t(E.e.scalar_tensor_tensor(qA[:, 0:W], qA[:, 0:W], 0.5,
                                         qB[:, 0:W], ALU.mult, ALU.add))
        st[0] = t
    pr.sec("vector", f_lv)

    def f_sg(E):
        E.w(st[0])
        st[0] = E.t(E.e.sign(qm[:, 0:W], qa[:, 0:W]))
    pr.sec("scalar", f_sg)

    def f_fin(E):
        E.w(st[0])
        t1 = E.t(E.e.tensor_mul(qA[:, 0:W], qA[:, 0:W], qm[:, 0:W]))
        E.w(t1)
        t2 = E.t(E.e.tensor_mul(
            dst_bf.rearrange("p (b k) -> p b k", k=16),
            qA[:, 0:W].rearrange("p (b k) -> p b k", k=16),
            qs[:, 0:NB].unsqueeze(2).to_broadcast([128, NB, 16])))
        out_tok[0] = t2
    pr.sec("vector", f_fin)
    return out_tok[0]


def phase_l1wq(pr, io, ctoks0):
    ctoks = list(ctoks0)
    pr.push_scope()
    sid = pr._scope_id
    wg_h = io["l1wg_h"]; wu_h = io["l1wu_h"]; wd_h = io["l1wd_h"]
    wgq_h = pr.dram(f"wgq{sid}", [FI, D], BF16, shared=True)
    wuq_h = pr.dram(f"wuq{sid}", [FI, D], BF16, shared=True)
    wdq_h = pr.dram(f"wdq{sid}", [D, FI], BF16, shared=True)
    wgT_l = pr.dram(f"wgTl{sid}", [D, FI], BF16)
    wuT_l = pr.dram(f"wuTl{sid}", [D, FI], BF16)
    wdT_l = pr.dram(f"wdTl{sid}", [FI, D], BF16)

    wt = pr.sbuf("wt", [128, FI], F32)
    ob = pr.sbuf("ob", [128, FI], BF16)
    alloc_quant(pr, FI)
    hloc = pr.dram(f"wqhl{sid}", [1408, D], F32).ap()
    hlocd = pr.dram(f"wqhld{sid}", [512, FI], F32).ap()
    oloc = pr.dram(f"wqol{sid}", [1408, D], BF16).ap()
    olocd = pr.dram(f"wqold{sid}", [512, FI], BF16).ap()

    for (in_h, out_sh, nrows, W, lin, lout) in (
            (io["l1wg"], wgq_h, FI, D, hloc, oloc),
            (io["l1wu"], wuq_h, FI, D, hloc, oloc),
            (io["l1wd"], wdq_h, D, FI, hlocd, olocd)):
        half = nrows // 2

        def f_gather(par, dd, in_h=in_h, lin=lin, half=half, W=W):
            dd.wait(("dS", pr.cnt["dS"]), ("dG", pr.cnt["dG"]))
            dd(lin[0:half, :], in_h[par * half:(par + 1) * half, :])
        pr.pswitch(f_gather)

        for r0 in range(0, half, 128):
            ld = [None]

            def f_ld(E, lin=lin, r0=r0, W=W):
                E.w(("vector", pr.cnt["vector"]), ("scalar", pr.cnt["scalar"]),
                    ("dS", pr.cnt["dS"]))
                ld[0] = E.dma(wt[:, 0:W], lin[r0:r0 + 128, :])
            pr.sec("sync", f_ld)

            tok = emit_quant(pr, wt[:, 0:W], ob[:, 0:W], W,
                             [ld[0], ("dS", pr.cnt["dS"])])

            def f_st(E, lout=lout, r0=r0, W=W, tok=tok):
                E.w(tok)
                E.dma(lout[r0:r0 + 128, :], ob[:, 0:W])
            pr.sec("sync", f_st)

        pr.fence()

        def f_scatter(par, dd, out_sh=out_sh, lout=lout, half=half):
            dd.wait(("dS", pr.cnt["dS"]))
            dd(out_sh.ap()[par * half:(par + 1) * half, :], lout[0:half, :])
        pr.pswitch(f_scatter)

    pr.core_barrier()

    tb = pr.sbuf("tb", [128, FI], BF16)
    for (q_h, t_l, R, W) in ((wgq_h, wgT_l, FI, D), (wuq_h, wuT_l, FI, D),
                             (wdq_h, wdT_l, D, FI)):
        for k in range(W // 128):
            def f_tc(E, q_h=q_h, t_l=t_l, R=R, W=W, k=k):
                E.w(("dS", pr.cnt["dS"]))
                E.dma(tb[:, 0:R], q_h.ap()[0:R, k * 128:(k + 1) * 128], transpose=True)
                E.w(("dS", pr.cnt["dS"]))
                E.dma(t_l.ap()[k * 128:(k + 1) * 128, :], tb[:, 0:R])
            pr.sec("sync", f_tc)
    pr.fence()
    pr.pop_scope()
    return {"wgT": wgT_l.ap(), "wuT": wuT_l.ap(), "wdT": wdT_l.ap()}


FSUB = 512
FI = 2816
FSL = (512, 512, 512, 512, 512, 256)


def phase_ffn1(pr, io, wq3, ctoks0):
    pr.push_scope()
    ctoks = list(ctoks0)
    xT_d = io["xT"]
    wgT_d = wq3["wgT"]; wuT_d = wq3["wuT"]; wdT_d = wq3["wdT"]
    yT_d = io["yT"]

    pr.sbuf("sxT", [128, KCH * FSUB], F32)
    pr.sbuf("lnf", [128, KCH * FSUB], F32)
    pr.sbuf("wbuf", [128, 22 * D], BF16)
    pr.sbuf("xqT", [128, KCH * FSUB], BF16)
    pr.sbuf("hqT", [128, 22 * FSUB], BF16)
    pr.sbuf("gsl", [128, 2 * 512], BF16)
    pr.sbuf("hsl", [128, 2 * 512], F32)
    pr.sbuf("hqsl", [128, 2 * 512], BF16)
    pr.sbuf("xtok", [128, 2 * D], F32)
    pr.sbuf("xq", [128, 2 * D], BF16)
    pr.sbuf("outsb", [128, 512], F32)
    alloc_quant(pr, D, slots=2)
    alloc_ln(pr, "att")

    for sub in range(TOK // FSUB):
        _ffn1_sub(pr, sub, xT_d, wgT_d, wuT_d, wdT_d, yT_d, ctoks)
    pr.fence()
    pr.pop_scope()


def _ffn1_sub(pr, sub, xT_d, wgT_d, wuT_d, wdT_d, yT_d, ctoks):
    P = pr.ps
    sb = pr.sb
    xT = sb["sxT"]; lnf = sb["lnf"]; wbuf = sb["wbuf"]; xqT = sb["xqT"]
    hqT = sb["hqT"]; gsl = sb["gsl"]; hsl = sb["hsl"]; hqsl = sb["hqsl"]
    xtok = sb["xtok"]; xq = sb["xq"]; outsb = sb["outsb"]
    tok0 = sub * FSUB
    pr.fence()
    ld = [None]

    def f_ld(E, sub=sub):
        ld[0] = E.dma(sb3(xT[:, :], KCH), rearr_kp(xT_d)[:, :, tok0:tok0 + FSUB])
    pr.sec("sync", f_ld)

    ln_toks = emit_ln(pr, xT, lnf, FSUB, [ld[0]] + ctoks, "att")

    xtr_t = {}; xcp_t = {}; qx_t = {}; xtr2_t = {}; xcp2_t = {}
    for ts in range(FSUB // 128):
        s = ts % 2

        def f_xtr(E, ts=ts, s=s):
            E.w(ln_toks[-1], xcp_t.get(ts - 1))
            for k in range(KCH):
                i = E.e.transpose(P[4 + k // 4][:, (k % 4) * 128:(k % 4 + 1) * 128],
                                  lnf[:, k * FSUB + ts * 128: k * FSUB + (ts + 1) * 128],
                                  pr.sb["eye"][:, :])
            xtr_t[ts] = E.t(i)
        pr.sec("tensor", f_xtr)

        def f_xcp(E, ts=ts, s=s):
            E.w(xtr_t[ts], qx_t.get(ts - 2))
            t = E.t(E.e.tensor_copy(xtok[:, s * D:s * D + 512], P[4][:, :]))
            E.w(t)
            t = E.t(E.e.tensor_copy(xtok[:, s * D + 512:s * D + 1024], P[5][:, :]))
            xcp_t[ts] = t
        pr.sec("vector", f_xcp)

        qx_t[ts] = emit_quant(pr, xtok[:, s * D:(s + 1) * D], xq[:, s * D:(s + 1) * D], D,
                              [xcp_t[ts], qx_t.get(ts - 2), xtr2_t.get(ts - 2)],
                              slot=s, W0=D, precise=True)

        def f_xtr2(E, ts=ts, s=s):
            E.w(qx_t[ts], xcp2_t.get(ts - 1))
            for k in range(KCH):
                i = E.e.transpose(P[6].ap().bitcast(BF16)[0:128, k * 128:(k + 1) * 128],
                                  xq[:, s * D + k * 128:s * D + (k + 1) * 128],
                                  pr.sb["eyeb"][:, :])
            xtr2_t[ts] = E.t(i)
        pr.sec("tensor", f_xtr2)

        def f_xcp2(E, ts=ts, s=s):
            E.w(xtr2_t[ts])
            xcp2_t[ts] = E.t(E.e.tensor_copy(
                xqT[:, :].rearrange("p (k f) -> p k f", f=FSUB)[:, :, ts * 128:(ts + 1) * 128],
                P[6].ap().bitcast(BF16)[0:128, 0:KCH * 128].rearrange("p (k f) -> p k f", f=128)))
        pr.sec("vector", f_xcp2)

    pr.fence()

    gm_t = {}; si_t = {}; um_t = {}; h_t = {}; q_t = {}; tr_t = {}; cp_t = {}
    wlt = {}
    off = 0
    for osl, w in enumerate(FSL):
        o0 = off
        off += w
        ws = osl % 2

        def f_ldw(E, osl=osl, o0=o0, w=w, ws=ws):
            E.w(um_t.get((osl - 2) * 4 + 3))
            E.dma(sb3(wbuf[:, ws * 8192:ws * 8192 + KCH * 512], KCH)[:, :, 0:w],
                  rearr_kp(wgT_d)[:, :, o0:o0 + w])
            wlt[osl] = E.dma(sb3(wbuf[:, ws * 8192 + KCH * 512:ws * 8192 + 2 * KCH * 512], KCH)[:, :, 0:w],
                             rearr_kp(wuT_d)[:, :, o0:o0 + w])
        pr.sec("gpsimd", f_ldw)

        for ts in range(FSUB // 128):
            i = osl * 4 + ts
            s = i % 2

            def f_gmm(E, osl=osl, ts=ts, w=w, i=i, ws=ws):
                E.w(wlt[osl], si_t.get(i - 1))
                for k in range(KCH):
                    t = E.e.matmul(P[4][:, 0:w],
                                   xqT[:, k * FSUB + ts * 128: k * FSUB + (ts + 1) * 128],
                                   wbuf[:, ws * 8192 + k * 512:ws * 8192 + k * 512 + w],
                                   start=(k == 0), stop=(k == KCH - 1))
                gm_t[i] = E.t(t)
            pr.sec("tensor", f_gmm)

            def f_silu(E, w=w, i=i, s=s):
                E.w(gm_t[i], h_t.get(i - 2))
                si_t[i] = E.t(E.e.activation(gsl[:, s * 512:s * 512 + w], P[4][:, 0:w], AF.Silu))
            pr.sec("scalar", f_silu)

            def f_umm(E, ts=ts, w=w, i=i, ws=ws):
                E.w(wlt[osl], h_t.get(i - 1))
                for k in range(KCH):
                    t = E.e.matmul(P[5][:, 0:w],
                                   xqT[:, k * FSUB + ts * 128: k * FSUB + (ts + 1) * 128],
                                   wbuf[:, ws * 8192 + KCH * 512 + k * 512: ws * 8192 + KCH * 512 + k * 512 + w],
                                   start=(k == 0), stop=(k == KCH - 1))
                um_t[i] = E.t(t)
            pr.sec("tensor", f_umm)

            def f_h(E, w=w, i=i, s=s):
                E.w(um_t[i], si_t[i], q_t.get(i - 2))
                h_t[i] = E.t(E.e.tensor_mul(hsl[:, s * 512:s * 512 + w],
                                            gsl[:, s * 512:s * 512 + w], P[5][:, 0:w]))
            pr.sec("vector", f_h)

            q_t[i] = emit_quant(pr, hsl[:, s * 512:s * 512 + w], hqsl[:, s * 512:s * 512 + w], w,
                                [h_t[i], q_t.get(i - 2), tr_t.get(i - 2)],
                                slot=s, W0=512, precise=True)

            nk = w // 128

            def f_tr(E, ts=ts, w=w, i=i, s=s, nk=nk):
                E.w(q_t[i], cp_t.get(i - 1))
                for k in range(nk):
                    t = E.e.transpose(P[6].ap().bitcast(BF16)[0:128, k * 128:(k + 1) * 128],
                                      hqsl[:, s * 512 + k * 128:s * 512 + (k + 1) * 128],
                                      pr.sb["eyeb"][:, :])
                tr_t[i] = E.t(t)
            pr.sec("tensor", f_tr)

            def f_cp(E, ts=ts, o0=o0, i=i, nk=nk):
                E.w(tr_t[i])
                kk = o0 // 128
                cp_t[i] = E.t(E.e.tensor_copy(
                    hqT[:, :].rearrange("p (k f) -> p k f", f=FSUB)[:, kk:kk + nk, ts * 128:(ts + 1) * 128],
                    P[6].ap().bitcast(BF16)[0:128, 0:nk * 128].rearrange("p (k f) -> p k f", f=128)))
            pr.sec("vector", f_cp)

    pr.fence()

    wt = [None]

    def f_ldwd(E):
        wt[0] = E.dma(sb3(wbuf[:, :], 22), wdT_d.rearrange("(k p) n -> p k n", p=128), sem="dG")
    pr.sec("gpsimd", f_ldwd)

    for o in range(KCH):
        dm = [None]

        def f_dmm(E, o=o, dm=dm, wt=wt):
            E.w(wt[0], self_tok(pr, "vector"))
            for k in range(22):
                i = E.e.matmul(P[4 + o % 2][:, :],
                               wbuf[:, k * D + o * 128: k * D + o * 128 + 128],
                               hqT[:, k * FSUB:(k + 1) * FSUB],
                               start=(k == 0), stop=(k == 21))
            dm[0] = E.t(i)
        pr.sec("tensor", f_dmm)

        def f_res(E, o=o, dm=dm):
            E.w(dm[0])
            E.w(("dS", pr.cnt["dS"]))
            E.t(E.e.tensor_tensor(outsb[:, :], xT[:, o * FSUB:(o + 1) * FSUB],
                                  P[4 + o % 2][:, :], ALU.add))
        pr.sec("vector", f_res)

        def f_st(E, o=o, sub=sub):
            E.w(self_tok(pr, "vector"))
            E.dma(rearr_kp(yT_d)[:, o, tok0:tok0 + FSUB], outsb[:, :])
        pr.sec("sync", f_st)




def _gammas_np():
    lo = float(np.log(1.0 - 0.999))
    hi = float(np.log(1.0 - 0.9))
    return 1.0 - np.exp(np.linspace(lo, hi, RH, dtype=np.float32))


def phase_ret(pr, io, ctoks0):
    ctoks = list(ctoks0)
    nc = pr.nc
    pr.push_scope()
    sid = pr._scope_id
    xT_d = io["xT"]
    wqT_d = io["wqT"]; wkT_d = io["wkT"]; wvT_d = io["wvT"]
    qsh_h = pr.dram(f"retq{sid}", [RH, 64, NKV], BF16, shared=True)
    ksh_h = pr.dram(f"retk{sid}", [RH, 64, NKV], BF16, shared=True)
    vtsh_h = pr.dram(f"retv{sid}", [RH, NKV, 64], BF16, shared=True)
    ysh_h = pr.dram(f"rety{sid}", [RH, 64, NKV], BF16, shared=True)
    qloc = pr.dram(f"retql{sid}", [RH, 64, TOK], BF16).ap()
    kloc = pr.dram(f"retkl{sid}", [RH, 64, TOK], BF16).ap()
    vtloc = pr.dram(f"retvl{sid}", [RH, TOK, 64], BF16).ap()
    qun = pr.dram(f"retqu{sid}", [8, 64, NKV], BF16).ap()
    kun = pr.dram(f"retku{sid}", [8, 64, NKV], BF16).ap()
    vtun = pr.dram(f"retvu{sid}", [8, NKV, 64], BF16).ap()
    yun = pr.dram(f"retyu{sid}", [8, 64, NKV], BF16).ap()
    yloc = pr.dram(f"retyl{sid}", [RH, 64, TOK], BF16).ap()

    xT = pr.sbuf("sxT", [128, KCH * 512], F32)
    xln = pr.sbuf("xln", [128, KCH * 512], BF16)
    wbuf = pr.sbuf("wbuf", [128, KCH * D], BF16)
    qf = pr.sbuf("qf", [128, 512], BF16)
    ktn = pr.sbuf("ktn", [128, 128], BF16)
    alloc_ln(pr, "att")
    P = pr.ps

    for r in range(TOK // 512):
        pr.fence()
        ld = [None]

        def f_ld(E, r=r):
            ld[0] = E.dma(sb3(xT[:, :], KCH), rearr_kp(xT_d)[:, :, r * 512:(r + 1) * 512])
        pr.sec("sync", f_ld)

        ln_toks = emit_ln(pr, xT, xln, 512, [ld[0]] + ctoks, "att")

        for wi, (wd, dst_d, tokmajor) in enumerate(
                ((wqT_d, qloc, False), (wkT_d, kloc, False), (wvT_d, vtloc, True))):
            wt = [None]

            def f_ldw(E, wd=wd):
                E.w(self_tok(pr, "tensor"))
                wt[0] = E.dma(sb3(wbuf[:, :], KCH), rearr_kp(wd), sem="dG")
            pr.sec("gpsimd", f_ldw)

            for o in range(KCH):
                pm = [None]

                def f_mm(E, r=r, o=o, wt=wt, pm=pm):
                    E.w(wt[0], ln_toks[-1])
                    E.w(self_tok(pr, "vector"), ("dS", pr.cnt["dS"]))
                    for k in range(KCH):
                        i = E.e.matmul(P[4][:, :],
                                       wbuf[:, k * D + o * 128: k * D + o * 128 + 128],
                                       xln[:, k * 512:(k + 1) * 512],
                                       start=(k == 0), stop=(k == KCH - 1))
                    pm[0] = E.t(i)
                pr.sec("tensor", f_mm)

                def f_ev(E, r=r, o=o, pm=pm):
                    E.w(pm[0])
                    pm[0] = E.t(E.e.tensor_copy(qf[:, :], P[4][:, :]))
                pr.sec("vector", f_ev)

                if not tokmajor:
                    def f_st(E, r=r, o=o, dst_d=dst_d, pm=pm):
                        E.w(pm[0])
                        for hh in range(2):
                            h = o * 2 + hh
                            E.dma(dst_d[h, :, r * 512:(r + 1) * 512],
                                  qf[hh * 64:(hh + 1) * 64, :])
                    pr.sec("sync", f_st)
                else:
                    for ts in range(4):
                        tt = [None]

                        def f_tr(E, o=o, ts=ts, tt=tt, pm=pm):
                            E.w(pm[0], self_tok(pr, "vector"), ("dS", pr.cnt["dS"]))
                            i = E.e.transpose(P[5].ap().bitcast(BF16)[0:128, 0:128],
                                              qf[:, ts * 128:(ts + 1) * 128],
                                              pr.sb["eyeb"][:, :])
                            tt[0] = E.t(i)
                        pr.sec("tensor", f_tr)

                        def f_trc(E, o=o, ts=ts, tt=tt):
                            E.w(tt[0])
                            tt[0] = E.t(E.e.tensor_copy(ktn[:, :],
                                                        P[5].ap().bitcast(BF16)[0:128, 0:128]))
                        pr.sec("vector", f_trc)

                        def f_st2(E, r=r, o=o, ts=ts, dst_d=dst_d, tt=tt):
                            E.w(tt[0])
                            for hh in range(2):
                                h = o * 2 + hh
                                E.dma(dst_d[h, r * 512 + ts * 128: r * 512 + (ts + 1) * 128, :],
                                      ktn[:, hh * 64:(hh + 1) * 64])
                        pr.sec("sync", f_st2)

    pr.fence()

    def f_out(par, dd):
        dd.wait(("dS", pr.cnt["dS"]))
        c0 = par * TOK
        dd(qsh_h.ap()[:, :, c0:c0 + TOK], qloc)
        dd(ksh_h.ap()[:, :, c0:c0 + TOK], kloc)
        dd(vtsh_h.ap()[:, c0:c0 + TOK, :], vtloc)
    pr.pswitch(f_out)
    pr.core_barrier()

    def f_in(par, dd):
        h0 = par * 8
        dd(qun, qsh_h.ap()[h0:h0 + 8, :, :])
        dd(kun, ksh_h.ap()[h0:h0 + 8, :, :])
        dd(vtun, vtsh_h.ap()[h0:h0 + 8, :, :])
    gtok = pr.pswitch(f_in)

    pr.pop_scope()
    pr.push_scope()
    _ret_scan_phase(pr, io, qun, kun, vtun, yun, gtok, ctoks)
    pr.fence()

    def f_yout(par, dd):
        dd.wait(("dS", pr.cnt["dS"]))
        dd(ysh_h.ap()[par * 8:par * 8 + 8, :, :], yun)
    pr.pswitch(f_yout)
    pr.core_barrier()

    def f_yin(par, dd):
        c0 = par * TOK
        dd(yloc, ysh_h.ap()[:, :, c0:c0 + TOK])
    gtok2 = pr.pswitch(f_yin)

    pr.pop_scope()
    pr.push_scope()
    _ret_out_phase(pr, io, yloc, gtok2, ctoks)
    pr.fence()
    pr.pop_scope()


def _ret_scan_phase(pr, io, qun, kun, vtun, yun, gtok, ctoks0):
    ctoks = list(ctoks0) + [gtok]
    nc = pr.nc
    dm_d = io["dmask"]; qg_d = io["qgrow"]; kg_d = io["kgcol"]
    g128_d = io["g128col"]; gnw_d = io["gnwcol"]; gnb_d = io["gnbcol"]

    qT = pr.sbuf("qT", [64, 8 * NKV // 8], BF16)
    kT = pr.sbuf("kT", [64, NKV], BF16)
    vt = pr.sbuf("vt", [128, 32 * 64], BF16)
    yb = pr.sbuf("yb", [64, NKV], BF16)
    dm = pr.sbuf("dm", [128, 8 * 128], BF16)
    qg = pr.sbuf("qg", [64, 8 * 128], BF16)
    kg = pr.sbuf("kg", [128, 8], F32)
    g128 = pr.sbuf("g128", [64, 8], F32)
    gnw = pr.sbuf("gnw", [64, 8], F32)
    gnb = pr.sbuf("gnb", [64, 8], F32)
    qgb = pr.sbuf("qgb", [64, 128], BF16)
    kgb = pr.sbuf("kgb", [128, 64], BF16)
    PTm = pr.sbuf("PTm", [128, 128], BF16)
    S = pr.sbuf("Srt", [64, 64], BF16)
    sacc = pr.sbuf("sacc", [64, 2], F32)
    stat = pr.sbuf("stat", [64, 8], F32)
    srow = pr.sbuf("srow", [1, 128], F32)
    mrr = pr.sbuf("mrr", [1, 4], F32)
    sq64 = pr.sbuf("sq64", [64, 128], BF16)
    P = pr.ps

    def f_i(E):
        ctoks.append(E.dma(dm[:, :], dm_d, sem="dG"))
        ctoks.append(E.dma(qg[:, :], qg_d, sem="dG"))
    pr.sec("gpsimd", f_i)

    def f_i2(E):
        ctoks.append(E.dma(kg[:, :], kg_d))
        ctoks.append(E.dma(g128[:, :], g128_d))
        ctoks.append(E.dma(gnw[:, :], gnw_d))
        ctoks.append(E.dma(gnb[:, :], gnb_d))
    pr.sec("sync", f_i2)

    for u in range(8):
        _ret_unit(pr, u, qun, kun, vtun, yun, ctoks)


def _ret_unit(pr, u, qun, kun, vtun, yun, ctoks):
    P = pr.ps
    sb = pr.sb
    qT = sb["qT"]; kT = sb["kT"]; vt = sb["vt"]; yb = sb["yb"]
    dm = sb["dm"]; qg = sb["qg"]; kg = sb["kg"]; g128 = sb["g128"]
    gnw = sb["gnw"]; gnb = sb["gnb"]; qgb = sb["qgb"]; kgb = sb["kgb"]
    PTm = sb["PTm"]; S = sb["Srt"]; sacc = sb["sacc"]; stat = sb["stat"]
    srow = sb["srow"]; mrr = sb["mrr"]; sq64 = sb["sq64"]
    pr.fence()
    ld = [None]

    def f_ld(E, u=u):
        E.dma(qT[:, 0:NKV], qun[u, :, :])
        E.dma(kT[:, :], kun[u, :, :])
        ld[0] = E.dma(sb3(vt[:, :], 32),
                      vtun[u, :, :].rearrange("(r p) d -> p r d", p=128))
    pr.sec("sync", f_ld)

    def f_z(E, u=u):
        E.t(E.e.memset(S[:, :], 0.0))
        E.t(E.e.memset(sacc[:, :], 0.0))
    pr.sec("gpsimd", f_z)

    for j in range(32):
        pt = [None]

        def f_pt(E, u=u, j=j, pt=pt):
            E.w(ld[0], *ctoks)
            E.w(self_tok(pr, "vector"), self_tok(pr, "gpsimd"))
            i = E.e.matmul(P[0][:, 0:128],
                           kT[:, j * 128:(j + 1) * 128],
                           qT[:, j * 128:(j + 1) * 128], start=True, stop=True)
            pt[0] = E.t(i)
        pr.sec("tensor", f_pt)

        def f_ptm(E, u=u, j=j, pt=pt):
            E.w(pt[0])
            t1 = E.t(E.e.tensor_mul(PTm[:, :], P[0][:, 0:128], dm[:, u * 128:(u + 1) * 128]))
            E.w(t1)
            t2 = E.t(E.e.tensor_mul(qgb[:, :], qT[:, j * 128:(j + 1) * 128],
                                    qg[:, u * 128:(u + 1) * 128]))
            E.w(t2)
            pt[0] = t2
        pr.sec("vector", f_ptm)

        om = [None]

        def f_om(E, u=u, j=j, pt=pt, om=om):
            E.w(pt[0])
            E.e.matmul(P[1][0:64, 0:128],
                       vt[:, j * 64:(j + 1) * 64], PTm[:, :],
                       start=True, stop=False, skip_group_check=True)
            i = E.e.matmul(P[1][0:64, 0:128], S[:, :], qgb[:, :],
                           start=False, stop=True, skip_group_check=True)
            om[0] = E.t(i)
        pr.sec("tensor", f_om)

        def f_oev(E, u=u, j=j, om=om):
            E.w(om[0])
            om[0] = E.t(E.e.tensor_copy(yb[:, j * 128:(j + 1) * 128], P[1][0:64, 0:128]))
        pr.sec("vector", f_oev)

        ktt = [None]

        def f_ktr(E, u=u, j=j, ktt=ktt):
            E.w(self_tok(pr, "vector"))
            i = E.e.transpose(P[2].ap().bitcast(BF16)[0:128, 0:64],
                              kT[:, j * 128:(j + 1) * 128], pr.sb["eyeb"][0:64, 0:64])
            ktt[0] = E.t(i)
        pr.sec("tensor", f_ktr)

        def f_ktc(E, u=u, j=j, ktt=ktt):
            E.w(ktt[0])
            ktt[0] = E.t(E.e.tensor_scalar_mul(kgb[:, :],
                                               P[2].ap().bitcast(BF16)[0:128, 0:64],
                                               kg[:, u:u + 1]))
        pr.sec("vector", f_ktc)

        um = [None]

        def f_um(E, u=u, j=j, ktt=ktt, um=um):
            E.w(ktt[0])
            i = E.e.matmul(P[3][0:64, 0:64], kgb[:, :], vt[:, j * 64:(j + 1) * 64],
                           start=True, stop=True)
            um[0] = E.t(i)
        pr.sec("tensor", f_um)

        def f_sup(E, u=u, j=j, um=um):
            E.w(um[0])
            E.t(E.e.scalar_tensor_tensor(S[:, :], S[:, :], g128[:, u:u + 1],
                                         P[3][0:64, 0:64], ALU.mult, ALU.add))
        pr.sec("vector", f_sup)

        def f_gn1(E, u=u, j=j):
            E.w(self_tok(pr, "vector"))
            E.t(E.e.activation(sq64[:, :], yb[:, j * 128:(j + 1) * 128], AF.Square))
        pr.sec("scalar", f_gn1)

        def f_gn2(E, u=u, j=j):
            E.w(self_tok(pr, "scalar"))
            t1 = E.t(E.e.reduce_sum(stat[:, 0:1], yb[:, j * 128:(j + 1) * 128], axis=AX.X))
            E.w(t1)
            t2 = E.t(E.e.reduce_sum(stat[:, 1:2], sq64[:, :], axis=AX.X))
            E.w(t2)
            t3 = E.t(E.e.tensor_add(sacc[:, 0:1], sacc[:, 0:1], stat[:, 0:1]))
            E.w(t3)
            t4 = E.t(E.e.tensor_add(sacc[:, 1:2], sacc[:, 1:2], stat[:, 1:2]))
            E.w(t4)
        pr.sec("vector", f_gn2)

    fz = [None]

    def f_ftr(E, u=u, fz=fz):
        E.w(self_tok(pr, "vector"))
        E.e.transpose(P[4][0:1, 0:64], sacc[:, 0:1], pr.sb["eye"][0:64, 0:64])
        i = E.e.transpose(P[4][0:1, 64:128], sacc[:, 1:2], pr.sb["eye"][0:64, 0:64])
        fz[0] = E.t(i)
    pr.sec("tensor", f_ftr)

    def f_fred(E, u=u, fz=fz):
        E.w(fz[0])
        t1 = E.t(E.e.tensor_copy(srow[:, 0:64], P[4][0:1, 0:64]))
        E.w(t1)
        t2 = E.t(E.e.reduce_sum(mrr[:, 0:1], srow[:, 0:64], axis=AX.X))
        E.w(t2)
        fz[0] = t2
    pr.sec("vector", f_fred)

    def f_fred2(E, u=u, fz=fz):
        E.w(fz[0])
        t1 = E.t(E.e.tensor_copy(srow[:, 0:64], P[4][0:1, 64:128]))
        E.w(t1)
        t2 = E.t(E.e.reduce_sum(mrr[:, 1:2], srow[:, 0:64], axis=AX.X))
        E.w(t2)
        t3 = E.t(E.e.tensor_scalar_mul(mrr[:, 0:1], mrr[:, 0:1], 1.0 / (NKV * 64.0)))
        E.w(t3)
        t4 = E.t(E.e.tensor_scalar_mul(mrr[:, 1:2], mrr[:, 1:2], 1.0 / (NKV * 64.0)))
        E.w(t4)
        t5 = E.t(E.e.tensor_mul(mrr[:, 2:3], mrr[:, 0:1], mrr[:, 0:1]))
        E.w(t5)
        t6 = E.t(E.e.tensor_sub(mrr[:, 1:2], mrr[:, 1:2], mrr[:, 2:3]))
        E.w(t6)
        t7 = E.t(E.e.tensor_scalar(mrr[:, 1:2], mrr[:, 1:2], 1e-5, None, ALU.add))
        fz[0] = t7
    pr.sec("vector", f_fred2)

    def f_fs(E, u=u, fz=fz):
        E.w(fz[0])
        fz[0] = E.t(E.e.activation(mrr[:, 1:2], mrr[:, 1:2], AF.Sqrt))
    pr.sec("scalar", f_fs)

    def f_fr(E, u=u, fz=fz):
        E.w(fz[0])
        t1 = E.t(E.e.reciprocal(mrr[:, 1:2], mrr[:, 1:2]))
        fz[0] = t1
    pr.sec("vector", f_fr)

    def f_fbc(E, u=u, fz=fz):
        E.w(fz[0])
        E.e.matmul(P[5][0:64, 0:1], pr.sb["ones1r"][:, 0:64], mrr[:, 0:1],
                   start=True, stop=True)
        i = E.e.matmul(P[5][0:64, 1:2], pr.sb["ones1r"][:, 0:64], mrr[:, 1:2],
                       start=True, stop=True)
        fz[0] = E.t(i)
    pr.sec("tensor", f_fbc)

    def f_fcp(E, u=u, fz=fz):
        E.w(fz[0])
        t1 = E.t(E.e.tensor_copy(stat[:, 2:3], P[5][0:64, 0:1]))
        E.w(t1)
        t2 = E.t(E.e.tensor_copy(stat[:, 3:4], P[5][0:64, 1:2]))
        E.w(t2)
        fz[0] = t2
    pr.sec("vector", f_fcp)

    for j in range(32):
        def f_nap(E, u=u, j=j, fz=fz):
            E.w(fz[0])
            t1 = E.t(E.e.tensor_scalar(yb[:, j * 128:(j + 1) * 128],
                                       yb[:, j * 128:(j + 1) * 128],
                                       stat[:, 2:3], stat[:, 3:4],
                                       ALU.subtract, ALU.mult))
            E.w(t1)
            t2 = E.t(E.e.tensor_scalar(yb[:, j * 128:(j + 1) * 128],
                                       yb[:, j * 128:(j + 1) * 128],
                                       gnw[:, u:u + 1], gnb[:, u:u + 1],
                                       ALU.mult, ALU.add))
            E.w(t2)
        pr.sec("vector", f_nap)

    def f_st(E, u=u):
        E.w(self_tok(pr, "vector"))
        E.dma(yun[u, :, :], yb[:, :])
    pr.sec("sync", f_st)


def _ret_out_phase(pr, io, yloc, gtok2, ctoks):
    xT_d = io["xT"]
    woT_d = io["woT"]
    out_d = io["yT"]

    xT = pr.sbuf("sxT", [128, KCH * 512], F32)
    yt = pr.sbuf("yt", [128, KCH * TOK], BF16)
    wbuf = pr.sbuf("wbuf", [128, KCH * D], BF16)
    outsb = pr.sbuf("outsb", [128, 512], F32)
    P = pr.ps
    wt = [None]
    olt = [None]

    def f_ldw(E):
        wt[0] = E.dma(sb3(wbuf[:, :], KCH), rearr_kp(woT_d), sem="dG")
    pr.sec("gpsimd", f_ldw)

    def f_ldo(E):
        E.w(gtok2)
        olt[0] = E.dma(yt[:, :].rearrange("p (k t) -> p k t", k=KCH),
                       yloc.rearrange("(k hh) d t -> (hh d) k t", k=KCH))
    pr.sec("sync", f_ldo)

    for g in range(4):
        xg = [None]

        def f_ldx(E, g=g):
            E.w(("dS", pr.cnt["dS"]))
            xg[0] = E.dma(sb3(xT[:, :], KCH), rearr_kp(xT_d)[:, :, g * 512:(g + 1) * 512])
        pr.sec("sync", f_ldx)

        for o in range(KCH):
            om = [None]

            def f_omm(E, g=g, o=o, om=om):
                E.w(wt[0], olt[0], self_tok(pr, "vector"))
                for k in range(KCH):
                    i = E.e.matmul(P[4 + o % 2][:, :],
                                   wbuf[:, k * D + o * 128: k * D + o * 128 + 128],
                                   yt[:, k * TOK + g * 512: k * TOK + (g + 1) * 512],
                                   start=(k == 0), stop=(k == KCH - 1))
                om[0] = E.t(i)
            pr.sec("tensor", f_omm)

            def f_oadd(E, g=g, o=o, om=om):
                E.w(om[0], xg[0])
                E.w(("dS", pr.cnt["dS"]))
                E.t(E.e.tensor_tensor(outsb[:, :], xT[:, o * 512:(o + 1) * 512],
                                      P[4 + o % 2][:, :], ALU.add))
            pr.sec("vector", f_oadd)

            def f_ost(E, g=g, o=o):
                E.w(self_tok(pr, "vector"))
                E.dma(rearr_kp(out_d)[:, o, g * 512:(g + 1) * 512], outsb[:, :])
            pr.sec("sync", f_ost)




def build_fused(nph=9):
    pr = Pr(num_devices=8)
    nc = pr.nc
    io = {}

    def ext(n, shape, dt):
        io[n] = nc.dram_tensor(n, list(shape), dt, kind="ExternalInput").ap()
        return io[n]

    x0T = ext("x0T", [D, TOK], F32)
    yT = nc.dram_tensor("yT", [D, TOK], F32, kind="ExternalOutput").ap()
    xa = pr.dram("xa", [D, TOK], F32).ap()
    xb = pr.dram("xb", [D, TOK], F32).ap()

    for n in ("a_wqT", "a_wkT", "a_wvT", "a_woT"):
        ext(n, [D, D], F32)
    ext("mrow", [1, NKV], BF16)
    ext("f0_gwT", [D, 8], F32)
    for n in ("f0_w1T", "f0_w2T", "f0_w3T"):
        ext(n, [8, D, D], F32)
    for n in ("f0_b1r", "f0_b2r", "f0_b3r"):
        ext(n, [128, 64], F32)
    ext("f0_sw1T", [D, 2048], F32)
    ext("f0_sw2T", [D, 2048], F32)
    ext("f0_sw3T", [2048, D], F32)
    for n in ("p_wqT", "p_wkT", "p_wvT", "p_woT"):
        ext(n, [D, D], F32)
    ext("p_projT", [HEADS, 64, PM], F32)
    ext("p_bor", [128, KCH], F32)
    for n in ("g_wqn", "g_wkn", "g_wvn", "g_wan"):
        ext(n, [D, DI], F32)
    ext("g_bar", [128, 4], F32)
    ext("g_gr", [128, KCH], F32)
    ext("g_br", [128, KCH], F32)
    ext("g_won", [DI, D], F32)
    l1wg_h = nc.dram_tensor("l1wg", [FI, D], F32, kind="ExternalInput")
    l1wu_h = nc.dram_tensor("l1wu", [FI, D], F32, kind="ExternalInput")
    l1wd_h = nc.dram_tensor("l1wd", [D, FI], F32, kind="ExternalInput")
    io["l1wg"] = l1wg_h.ap(); io["l1wu"] = l1wu_h.ap(); io["l1wd"] = l1wd_h.ap()
    io["l1wg_h"] = l1wg_h; io["l1wu_h"] = l1wu_h; io["l1wd_h"] = l1wd_h
    for n in ("r_wqT", "r_wkT", "r_wvT", "r_woT"):
        ext(n, [D, D], F32)
    ext("r_dmask", [128, 8 * 128], F32)
    ext("r_qgrow", [64, 8 * 128], F32)
    ext("r_kgcol", [128, 8], F32)
    ext("r_g128col", [64, 8], F32)
    ext("r_gnwcol", [64, 8], F32)
    ext("r_gnbcol", [64, 8], F32)

    pr.start()
    pr.init_parity(("sync", "gpsimd"))
    ctoks = emit_consts(pr)

    wq3 = phase_l1wq(pr, io, ctoks)
    chain = []
    chain.append(("atta", lambda cur, nxt: phase_atta(pr, {
        "xT": cur, "wqT": io["a_wqT"], "wkT": io["a_wkT"], "wvT": io["a_wvT"],
        "woT": io["a_woT"], "mrow": io["mrow"], "yT": nxt}, ctoks)))
    chain.append(("ffn0a", lambda cur, nxt: phase_ffn0(pr, {
        "xT": cur, "gwT": io["f0_gwT"], "w1T": io["f0_w1T"], "w2T": io["f0_w2T"],
        "w3T": io["f0_w3T"], "b1r": io["f0_b1r"], "b2r": io["f0_b2r"],
        "b3r": io["f0_b3r"], "sw1T": io["f0_sw1T"], "sw2T": io["f0_sw2T"],
        "sw3T": io["f0_sw3T"], "yT": nxt}, ctoks)))
    chain.append(("perf", lambda cur, nxt: phase_perf(pr, {
        "xT": cur, "wqT": io["p_wqT"], "wkT": io["p_wkT"], "wvT": io["p_wvT"],
        "woT": io["p_woT"], "projT": io["p_projT"], "bor": io["p_bor"],
        "yT": nxt}, ctoks)))
    chain.append(("ffn0b", chain[1][1]))
    chain.append(("gla", lambda cur, nxt: phase_gla(pr, {
        "xT": cur, "wqn": io["g_wqn"], "wkn": io["g_wkn"], "wvn": io["g_wvn"],
        "wan": io["g_wan"], "bar": io["g_bar"], "gr": io["g_gr"], "br": io["g_br"],
        "won": io["g_won"], "yT": nxt}, ctoks)))
    chain.append(("ffn1a", lambda cur, nxt: phase_ffn1(pr, {
        "xT": cur, "yT": nxt}, wq3, ctoks)))
    chain.append(("ret", lambda cur, nxt: phase_ret(pr, {
        "xT": cur, "wqT": io["r_wqT"], "wkT": io["r_wkT"], "wvT": io["r_wvT"],
        "woT": io["r_woT"], "yT": nxt, "dmask": io["r_dmask"],
        "qgrow": io["r_qgrow"], "kgcol": io["r_kgcol"], "g128col": io["r_g128col"],
        "gnwcol": io["r_gnwcol"], "gnbcol": io["r_gnbcol"]}, ctoks)))
    chain.append(("ffn1b", chain[5][1]))

    nph = min(nph, len(chain))
    cur = x0T
    bufs = [xa, xb]
    for i in range(nph):
        nxt = yT if i == nph - 1 else bufs[i % 2]
        chain[i][1](cur, nxt)
        cur = nxt
    if nph == 0:
        def f_cp(E):
            E.dma(yT, x0T)
        pr.sec("sync", f_cp)
    return pr.finish()


_FUSE_NPH = 8


def _fused_inmaps(ins):
    import ml_dtypes
    x = np.ascontiguousarray(ins["L0_attn_a_x"].astype(np.float32).reshape(-1, D))
    shards = _shard_T(x)
    gam = _gammas_np()
    gnw_full = ins["L1_attn_b_gn_w"].reshape(RH, 64)
    gnb_full = ins["L1_attn_b_gn_b"].reshape(RH, 64)
    base = {
        "a_wqT": np.ascontiguousarray(ins["L0_attn_a_Wq"].T),
        "a_wkT": np.ascontiguousarray(ins["L0_attn_a_Wk"].T),
        "a_wvT": np.ascontiguousarray(ins["L0_attn_a_Wv"].T),
        "a_woT": np.ascontiguousarray(ins["L0_attn_a_Wo"].T),
        "f0_gwT": np.ascontiguousarray(ins["L0_ffn_gate_w"].T),
        "f0_w1T": np.ascontiguousarray(ins["L0_ffn_w1"].transpose(0, 2, 1)),
        "f0_w2T": np.ascontiguousarray(ins["L0_ffn_w2"].transpose(0, 2, 1)),
        "f0_w3T": np.ascontiguousarray(ins["L0_ffn_w3"].transpose(0, 2, 1)),
        "f0_b1r": _col_bias(ins["L0_ffn_b1"]),
        "f0_b2r": _col_bias(ins["L0_ffn_b2"]),
        "f0_b3r": _col_bias(ins["L0_ffn_b3"]),
        "f0_sw1T": np.ascontiguousarray(ins["L0_ffn_sw1"].T),
        "f0_sw2T": np.ascontiguousarray(ins["L0_ffn_sw2"].T),
        "f0_sw3T": np.ascontiguousarray(ins["L0_ffn_sw3"].T),
        "p_wqT": np.ascontiguousarray(ins["L0_attn_b_Wq"].T),
        "p_wkT": np.ascontiguousarray(ins["L0_attn_b_Wk"].T),
        "p_wvT": np.ascontiguousarray(ins["L0_attn_b_Wv"].T),
        "p_woT": np.ascontiguousarray(ins["L0_attn_b_Wo"].T),
        "p_projT": np.ascontiguousarray(ins["L0_attn_b_proj"].transpose(0, 2, 1)),
        "p_bor": np.ascontiguousarray(ins["L0_attn_b_bo"].reshape(8, 128).T),
        "g_wqn": np.ascontiguousarray(ins["L1_attn_a_Wq"]),
        "g_wkn": np.ascontiguousarray(ins["L1_attn_a_Wk"]),
        "g_wvn": np.ascontiguousarray(ins["L1_attn_a_Wv"]),
        "g_wan": np.ascontiguousarray(ins["L1_attn_a_Wa"]),
        "g_bar": np.ascontiguousarray(ins["L1_attn_a_ba"].reshape(4, 128).T),
        "g_gr": _colchunks(ins["L1_attn_a_gamma"]),
        "g_br": _colchunks(ins["L1_attn_a_beta"]),
        "g_won": np.ascontiguousarray(ins["L1_attn_a_Wo"]),
        "l1wg": np.ascontiguousarray(ins["L1_ffn_w_gate"]),
        "l1wu": np.ascontiguousarray(ins["L1_ffn_w_up"]),
        "l1wd": np.ascontiguousarray(ins["L1_ffn_w_down"]),
        "r_wqT": np.ascontiguousarray(ins["L1_attn_b_wq"].T),
        "r_wkT": np.ascontiguousarray(ins["L1_attn_b_wk"].T),
        "r_wvT": np.ascontiguousarray(ins["L1_attn_b_wv"].T),
        "r_woT": np.ascontiguousarray(ins["L1_attn_b_wo"].T),
    }
    in_maps = []
    for c in range(NCORES):
        half = c % 2
        if half == 0:
            mrow = np.concatenate([np.full((1, TOK), -1e9, np.float32),
                                   np.zeros((1, TOK), np.float32)], axis=1)
        else:
            mrow = np.zeros((1, NKV), np.float32)
        dmask = np.zeros((128, 8 * 128), np.float32)
        qgrow = np.zeros((64, 8 * 128), np.float32)
        kgcol = np.zeros((128, 8), np.float32)
        g128c = np.zeros((64, 8), np.float32)
        gnwc = np.zeros((64, 8), np.float32)
        gnbc = np.zeros((64, 8), np.float32)
        idx = np.arange(128)
        for i in range(8):
            h = half * 8 + i
            g = gam[h]
            dmask[:, i * 128:(i + 1) * 128] = np.where(
                idx[None, :] >= idx[:, None], g ** (idx[None, :] - idx[:, None]), 0.0)
            qgrow[:, i * 128:(i + 1) * 128] = (g ** (idx + 1.0))[None, :]
            kgcol[:, i] = g ** (127.0 - idx)
            g128c[:, i] = g ** 128.0
            gnwc[:, i] = gnw_full[h]
            gnbc[:, i] = gnb_full[h]
        in_maps.append(dict(
            base, x0T=shards[c],
            mrow=mrow.astype(ml_dtypes.bfloat16),
            r_dmask=dmask, r_qgrow=qgrow, r_kgcol=kgcol,
            r_g128col=g128c, r_gnwcol=gnwc, r_gnbcol=gnbc))
    return in_maps


def kernel(**inputs):
    ins = {k: np.asarray(v) for k, v in inputs.items()}
    nc = _get(f"fused{_FUSE_NPH}", lambda: build_fused(_FUSE_NPH))
    in_maps = _fused_inmaps(ins)
    res = run_bass_kernel_spmd(nc, in_maps, core_ids=list(range(NCORES)))
    y = _unshard_T([res.results[c]["yT"] for c in range(NCORES)])
    return y.reshape(4, NKV, D).astype(np.float32)


# version 08
# speedup vs baseline: 1.8802x; 1.0020x over previous
import contextlib
import numpy as np

import concourse.bass as bass
import concourse.mybir as mybir
from concourse.bass_utils import run_bass_kernel_spmd

F32 = mybir.dt.float32
BF16 = mybir.dt.bfloat16
AF = mybir.ActivationFunctionType
ALU = mybir.AluOpType
AX = mybir.AxisListType

TRACE_LOG = None

NCORES = 8
D = 1024
KCH = 8
TOK = 2048
SUB = 1024
LN_EPS = 1e-5



class Pr:
    ENGINES = ("tensor", "vector", "scalar", "gpsimd", "sync")

    def __init__(self, num_devices=None):
        self.nc = bass.Bass("TRN2", target_bir_lowering=False, num_devices=num_devices)
        self._stack = contextlib.ExitStack()
        self._scopes = []
        self.sems = {}
        self.cnt = {}
        self.hw = {}
        self.block = None
        self.blockcm = None
        self.sb = _SbMap(self)
        self.ps = []
        self._scope_id = 0
        self.regs = {}

    def sbuf(self, name, shape, dtype):
        if self._scopes:
            st, d = self._scopes[-1]
            h = st.enter_context(self.nc.sbuf_tensor(
                f"s{self._scope_id}_{name}", list(shape), dtype))
            d[name] = h
        else:
            h = self._stack.enter_context(self.nc.sbuf_tensor(name, list(shape), dtype))
            self.sb._root[name] = h
        return h

    def push_scope(self):
        self._scope_id += 1
        self._scopes.append((contextlib.ExitStack(), {}))

    def pop_scope(self):
        st, _ = self._scopes.pop()
        st.close()

    def const(self, data, name):
        self._scope_id += 1
        return self.nc.inline_tensor(np.ascontiguousarray(data),
                                     name=f"{name}_{self._scope_id}")

    def dram(self, name, shape, dtype, kind="Internal", shared=False):
        return self.nc.dram_tensor(name, list(shape), dtype, kind=kind,
                                   addr_space="Shared" if shared else "Local")

    def start(self, npsum=8):
        for i in range(npsum):
            self.ps.append(self._stack.enter_context(
                self.nc.psum_tensor(f"P{i}", [128, 512], F32)))
        for e in self.ENGINES:
            self.sems[e] = self._stack.enter_context(self.nc.semaphore("sp_" + e))
            self.cnt[e] = 0
        for dname in ("dS", "dG"):
            self.sems[dname] = self._stack.enter_context(self.nc.semaphore("sm_" + dname))
            self.cnt[dname] = 0
        self.blockcm = self.nc.Block()
        self.block = self.blockcm.__enter__()
        self._mbn = 0
        self._engobj = {"tensor": self.nc.tensor, "vector": self.nc.vector,
                        "scalar": self.nc.scalar, "gpsimd": self.nc.gpsimd,
                        "sync": self.nc.sync}
        self._alleng = [mybir.EngineType.PE, mybir.EngineType.DVE,
                        mybir.EngineType.Activation, mybir.EngineType.Pool,
                        mybir.EngineType.SP]
        self._open_master()

    def _open_master(self):
        self._mb = self.nc.body(f"master{self._mbn}", valid_engines=self._alleng)
        self._mb.__enter__()
        self._mbn += 1

    def _close_master(self):
        self._mb.__exit__(None, None, None)

    def init_parity(self, engines=("sync", "gpsimd")):
        for en in engines:
            def f(E, en=en):
                eng = E.e
                r_par = eng.alloc_register("par_" + en)
                r_xpar = eng.alloc_register("xpar_" + en)
                r_tmp = eng.alloc_register("tmp_" + en)
                pid = eng.partition_id()
                eng.reg_alu(r_par, pid, 1, ALU.bitwise_and)
                eng.reg_alu(r_xpar, r_par, 1, ALU.bitwise_xor)
                self.regs[en] = (r_par, r_xpar, r_tmp)
            self.sec(en, f)

    def sec(self, ename, body):
        body(_Emit(self, ename, self._engobj[ename]))

    def now(self, ename):
        return (ename, self.cnt[ename])

    def fence(self):
        toks = [self.now(e) for e in self.ENGINES] + [self.now("dS"), self.now("dG")]

        for e in self.ENGINES:
            def f(E, toks=toks):
                E.w(*toks)
            self.sec(e, f)

    def pswitch(self, fn, nbody=128):
        self._close_master()
        self.blockcm.__exit__(None, None, None)
        counts = []

        class _DD:
            def __init__(s, pr):
                s.n = 0
                s.pr = pr

            def __call__(s, out, in_):
                i = s.pr.nc.sync.dma_start(out=out, in_=in_)
                i.then_inc(s.pr.sems["dS"], 16)
                s.n += 1
                if getattr(s, "log", False):
                    TRACE_LOG.append(("sync", "inc", "dS", 16))

            def wait(s, *toks):
                for t in toks:
                    if t is None:
                        continue
                    sname, val = t
                    if val > 0:
                        s.pr.nc.sync.wait_ge(s.pr.sems[sname], val)
                        if getattr(s, "log", False):
                            TRACE_LOG.append(("sync", "wait", sname, val))

        for arm in self.nc.switch_core_id(engines=[self.nc.sync], n=nbody):
            dd = _DD(self)
            dd.log = (TRACE_LOG is not None and not counts)
            fn(int(arm) & 1 if not hasattr(arm, "value") else arm.value & 1, dd)
            counts.append(dd.n)
        assert len(set(counts)) == 1, counts
        self.cnt["dS"] += 16 * counts[0]
        self.blockcm = self.nc.Block()
        self.block = self.blockcm.__enter__()
        self._open_master()
        return ("dS", self.cnt["dS"])

    def core_barrier(self):
        self.fence()
        self._close_master()
        self.blockcm.__exit__(None, None, None)
        self.nc.all_core_barrier()
        self.blockcm = self.nc.Block()
        self.block = self.blockcm.__enter__()
        self._open_master()

    def finish(self):
        def f(E):
            E.w(self.now("dS"), self.now("dG"))
        self.sec("sync", f)
        self._close_master()
        self.blockcm.__exit__(None, None, None)
        return self.nc


class _SbMap:
    def __init__(self, pr):
        self._pr = pr
        self._root = {}

    def __setitem__(self, k, v):
        self._root[k] = v

    def __contains__(self, k):
        try:
            self[k]
            return True
        except KeyError:
            return False

    def __getitem__(self, k):
        for _, d in reversed(self._pr._scopes):
            if k in d:
                return d[k]
        return self._root[k]


class _Emit:
    def __init__(self, pr, ename, eng):
        self.pr = pr
        self.en = ename
        self.e = eng

    def dyn(self, handle, which, slot_elems, base_elems, dims):
        r_par, r_xpar, r_tmp = self.pr.regs[self.en]
        sel = r_par if which == "own" else r_xpar
        self.e.reg_mul(r_tmp, sel, int(slot_elems))
        if base_elems:
            self.e.reg_add(r_tmp, r_tmp, int(base_elems))
        return bass.AP(handle, r_tmp, [list(d) for d in dims])

    def w(self, *tokens):
        for t in tokens:
            if t is None:
                continue
            sname, val = t
            if val <= 0:
                continue
            if sname == self.en and val <= self.pr.cnt[sname]:
                pass
            key = (self.en, sname)
            if self.pr.hw.get(key, 0) < val:
                self.e.wait_ge(self.pr.sems[sname], val)
                self.pr.hw[key] = val
                if TRACE_LOG is not None:
                    TRACE_LOG.append((self.en, "wait", sname, val))

    def t(self, instr):
        instr.then_inc(self.pr.sems[self.en], 1)
        self.pr.cnt[self.en] += 1
        if TRACE_LOG is not None:
            TRACE_LOG.append((self.en, "inc", self.en, 1))
        return (self.en, self.pr.cnt[self.en])

    def dma(self, out, in_, sem=None, **kw):
        sem = sem or ("dG" if self.en == "gpsimd" else "dS")
        i = self.e.dma_start(out=out, in_=in_, **kw)
        i.then_inc(self.pr.sems[sem], 16)
        self.pr.cnt[sem] += 16
        if TRACE_LOG is not None:
            TRACE_LOG.append((self.en, "inc", sem, 16))
        return (sem, self.pr.cnt[sem])


def rearr_kp(ap):
    return ap.rearrange("(k p) n -> p k n", p=128)


def sb3(tile_ap, k):
    return tile_ap.rearrange("p (k n) -> p k n", k=k)



def emit_consts(pr):
    eye_c = pr.const(np.eye(128, dtype=np.float32), "eye_c")
    pr.sbuf("ones128", [128, 1], F32)
    pr.sbuf("ones128b", [128, 1], BF16)
    pr.sbuf("ones1r", [1, 128], F32)
    pr.sbuf("eye", [128, 128], F32)
    pr.sbuf("eyeb", [128, 128], BF16)
    tok = [None]

    def f(E):
        t1 = E.dma(pr.sb["eye"][:, :], eye_c.ap(), sem="dG")
        t1b = E.dma(pr.sb["eyeb"][:, :], eye_c.ap(), sem="dG")
        t2 = E.t(E.e.memset(pr.sb["ones128"][:, :], 1.0))
        t2b = E.t(E.e.memset(pr.sb["ones128b"][:, :], 1.0))
        t3 = E.t(E.e.memset(pr.sb["ones1r"][:, :], 1.0))
        tok[0] = t3
    pr.sec("gpsimd", f)
    return [tok[0], ("dG", pr.cnt["dG"])]


def emit_ln(pr, xT, xln, n_tok, dep_toks, name, gamma_beta=None):
    nt = n_tok // 512
    ones = pr.sb["ones128"]
    onesr = pr.sb["ones1r"]
    sq = pr.sb[name + "_sq"]
    rows = pr.sb[name + "_rows"]
    tmp = pr.sb[name + "_tmp"]
    P = pr.ps
    out_toks = []

    for t in range(nt):
        col = lambda k, t=t: slice(k * n_tok + t * 512, k * n_tok + (t + 1) * 512)
        sq_tok = [None] * KCH
        qmm = [None] * KCH
        smm = [None]
        rd = [None]
        bc = [None]

        def f_smm(E, t=t, smm=smm):
            E.w(*dep_toks)
            for k in range(KCH):
                i = E.e.matmul(P[0][0:1, :], ones[:, :], xT[:, col(k)],
                               start=(k == 0), stop=(k == KCH - 1))
            smm[0] = E.t(i)
        pr.sec("tensor", f_smm)

        for k in range(KCH):
            def f_sqk(E, t=t, k=k, sq_tok=sq_tok, qmm=qmm):
                E.w(*dep_toks)
                if k >= 2:
                    E.w(qmm[k - 2])
                half = sq[:, (k % 2) * 512:(k % 2) * 512 + 512]
                sq_tok[k] = E.t(E.e.activation(half, xT[:, col(k)], AF.Square))
            pr.sec("scalar", f_sqk)

            def f_qmmk(E, t=t, k=k, sq_tok=sq_tok, qmm=qmm):
                E.w(sq_tok[k])
                half = sq[:, (k % 2) * 512:(k % 2) * 512 + 512]
                i = E.e.matmul(P[1][0:1, :], pr.sb["ones128b"][:, :], half,
                               start=(k == 0), stop=(k == KCH - 1))
                qmm[k] = E.t(i)
            pr.sec("tensor", f_qmmk)

        def f_r1(E, t=t, qmm=qmm, smm=smm, rd=rd):
            m = rows[:, 0:512]; v = rows[:, 512:1024]; b = rows[:, 1024:1536]
            E.w(smm[0], qmm[KCH - 1])
            t1 = E.t(E.e.tensor_scalar_mul(m, P[0][0:1, :], 1.0 / D))
            t2 = E.t(E.e.tensor_scalar_mul(v, P[1][0:1, :], 1.0 / D))
            E.w(t1, t2)
            t3 = E.t(E.e.tensor_mul(b, m, m))
            E.w(t3)
            t4 = E.t(E.e.tensor_tensor(v, v, b, ALU.subtract))
            E.w(t4)
            t5 = E.t(E.e.tensor_scalar(v, v, LN_EPS, None, ALU.add))
            rd[0] = t5
        pr.sec("vector", f_r1)

        def f_r2(E, t=t, rd=rd):
            v = rows[:, 512:1024]
            E.w(rd[0])
            rd[0] = E.t(E.e.activation(v, v, AF.Sqrt))
        pr.sec("scalar", f_r2)

        def f_r3(E, t=t, rd=rd):
            m = rows[:, 0:512]; v = rows[:, 512:1024]; b = rows[:, 1024:1536]
            E.w(rd[0])
            t1 = E.t(E.e.reciprocal(v, v))
            E.w(t1)
            t2 = E.t(E.e.tensor_mul(b, m, v))
            E.w(t2)
            t3 = E.t(E.e.tensor_scalar_mul(b, b, -1.0))
            rd[0] = t3
        pr.sec("vector", f_r3)

        def f_bc(E, t=t, rd=rd, bc=bc):
            a = rows[:, 512:1024]; b = rows[:, 1024:1536]
            E.w(rd[0])
            if out_toks:
                E.w(out_toks[-1])
            E.e.matmul(P[2][:, :], onesr[:, :], a, start=True, stop=True)
            i = E.e.matmul(P[3][:, :], onesr[:, :], b, start=True, stop=True)
            bc[0] = E.t(i)
        pr.sec("tensor", f_bc)

        def f_ap(E, t=t, bc=bc):
            E.w(bc[0])
            last = None
            for k in range(KCH):
                t1 = E.t(E.e.tensor_tensor(tmp[:, :], xT[:, col(k)], P[2][:, :], ALU.mult))
                E.w(t1)
                if gamma_beta is None:
                    last = E.t(E.e.tensor_tensor(xln[:, col(k)], tmp[:, :], P[3][:, :], ALU.add))
                else:
                    last = E.t(E.e.tensor_tensor(tmp[:, :], tmp[:, :], P[3][:, :], ALU.add))
                    E.w(last)
                    g_sb, b_sb = gamma_beta
                    last = E.t(E.e.tensor_scalar(xln[:, col(k)], tmp[:, :],
                                                 g_sb[:, k:k + 1], b_sb[:, k:k + 1],
                                                 ALU.mult, ALU.add))
                E.w(last)
            out_toks.append(last)
        pr.sec("vector", f_ap)

    return out_toks


def alloc_ln(pr, name):
    pr.sbuf(name + "_sq", [128, 1024], BF16)
    pr.sbuf(name + "_rows", [1, 1536], F32)
    pr.sbuf(name + "_tmp", [128, 512], F32)


NKV = 4096
HEADS = 16
PM = 256
DI = 512
GH = 8
SB16 = 16
RH = 16



FFN0_KEYS = ("xT", "gwT", "w1T", "w2T", "w3T", "b1r", "b2r", "b3r",
             "sw1T", "sw2T", "sw3T", "yT")


def build_ffn0():
    pr = Pr()
    nc = pr.nc
    io = {}
    io["xT"] = nc.dram_tensor("xT", [D, TOK], F32, kind="ExternalInput").ap()
    io["gwT"] = nc.dram_tensor("gwT", [D, 8], F32, kind="ExternalInput").ap()
    io["w1T"] = nc.dram_tensor("w1T", [8, D, D], F32, kind="ExternalInput").ap()
    io["w2T"] = nc.dram_tensor("w2T", [8, D, D], F32, kind="ExternalInput").ap()
    io["w3T"] = nc.dram_tensor("w3T", [8, D, D], F32, kind="ExternalInput").ap()
    io["b1r"] = nc.dram_tensor("b1r", [128, 64], F32, kind="ExternalInput").ap()
    io["b2r"] = nc.dram_tensor("b2r", [128, 64], F32, kind="ExternalInput").ap()
    io["b3r"] = nc.dram_tensor("b3r", [128, 64], F32, kind="ExternalInput").ap()
    io["sw1T"] = nc.dram_tensor("sw1T", [D, 2048], F32, kind="ExternalInput").ap()
    io["sw2T"] = nc.dram_tensor("sw2T", [D, 2048], F32, kind="ExternalInput").ap()
    io["sw3T"] = nc.dram_tensor("sw3T", [2048, D], F32, kind="ExternalInput").ap()
    io["yT"] = nc.dram_tensor("yT", [D, TOK], F32, kind="ExternalOutput").ap()
    pr.start()
    ctoks = emit_consts(pr)
    phase_ffn0(pr, io, ctoks)
    return pr.finish()


def phase_ffn0(pr, io, ctoks0):
    pr.push_scope()
    ctoks = list(ctoks0)
    xT_d = io["xT"]; gwT_d = io["gwT"]
    w1T_d = io["w1T"]; w2T_d = io["w2T"]; w3T_d = io["w3T"]
    b1r_d = io["b1r"]; b2r_d = io["b2r"]; b3r_d = io["b3r"]
    sw1T_d = io["sw1T"]; sw2T_d = io["sw2T"]; sw3T_d = io["sw3T"]
    yT_d = io["yT"]

    sel = np.zeros((8, 8 * 128), dtype=np.float32)
    for e in range(8):
        sel[e, e * 128:(e + 1) * 128] = 1.0
    sel_c = pr.const(sel, "sel8_c")

    xT = pr.sbuf("sxT", [128, KCH * SUB], F32)
    xln = pr.sbuf("xln", [128, KCH * SUB], BF16)
    Y = pr.sbuf("Y", [128, KCH * SUB], F32)
    wpool = pr.sbuf("wpool", [128, 16 * D], BF16)
    xs = pr.sbuf("xs", [128, KCH * SUB], BF16)
    hb = pr.sbuf("hb", [128, KCH * SUB], BF16)
    x3sb = pr.sbuf("x3sb", [128, KCH * SUB], BF16)
    pr.sbuf("sel8", [8, 1024], BF16)
    pr.sbuf("gw", [128, KCH * 8], BF16)
    pr.sbuf("b1", [128, 64], F32)
    pr.sbuf("b2", [128, 64], F32)
    pr.sbuf("b3", [128, 64], F32)
    pr.sbuf("WEr", [8, SUB], BF16)
    pr.sbuf("rt", [128, 48], F32)
    pr.sbuf("rt2", [128, 8], F32)
    pr.sbuf("webc", [128, 2 * 512], BF16)
    pr.sbuf("mebc", [128, 2 * 512], BF16)
    pr.sbuf("x2", [128, 512], BF16)
    pr.sbuf("prod", [128, 512], BF16)
    alloc_ln(pr, "ffn")

    def f_init(E):
        t1 = E.dma(pr.sb["sel8"][:, :], sel_c.ap(), sem="dG")
        t2 = E.dma(sb3(pr.sb["gw"][:, :], KCH), rearr_kp(gwT_d), sem="dG")
    pr.sec("gpsimd", f_init)

    def f_init2(E):
        E.dma(pr.sb["b1"][:, :], b1r_d)
        E.dma(pr.sb["b2"][:, :], b2r_d)
        E.dma(pr.sb["b3"][:, :], b3r_d)
    pr.sec("sync", f_init2)
    ctoks += [("dG", pr.cnt["dG"]), ("dS", pr.cnt["dS"])]

    for sub in range(TOK // SUB):
        _ffn0_sub(pr, sub, xT_d, yT_d, w1T_d, w2T_d, w3T_d,
                  sw1T_d, sw2T_d, sw3T_d, ctoks)
    pr.fence()
    pr.pop_scope()


def _ffn0_sub(pr, sub, xT_d, yT_d, w1T_d, w2T_d, w3T_d, sw1T_d, sw2T_d, sw3T_d, ctoks):
    P = pr.ps
    xT = pr.sb["sxT"]; xln = pr.sb["xln"]; Y = pr.sb["Y"]
    wpool = pr.sb["wpool"]; xs = pr.sb["xs"]; hb = pr.sb["hb"]; x3sb = pr.sb["x3sb"]
    WEr = pr.sb["WEr"]; rt = pr.sb["rt"]; rt2 = pr.sb["rt2"]
    webc = pr.sb["webc"]; mebc = pr.sb["mebc"]; x2 = pr.sb["x2"]; prod = pr.sb["prod"]
    b1 = pr.sb["b1"]; b2 = pr.sb["b2"]; b3 = pr.sb["b3"]
    tok0 = sub * SUB
    nt = SUB // 512

    pr.fence()

    ld = [None]

    def f_ldx(E):
        src = rearr_kp(xT_d)
        ld[0] = E.dma(sb3(xT[:, :], KCH)[:, :, 0:SUB] if tok0 == 0
                      else sb3(xT[:, :], KCH)[:, :, 0:SUB],
                      src[:, :, tok0:tok0 + SUB])
    pr.sec("sync", f_ldx)

    def f_yinit(E):
        E.w(ld[0])
        for k in range(KCH):
            E.t(E.e.tensor_copy(Y[:, k * SUB:(k + 1) * SUB], xT[:, k * SUB:(k + 1) * SUB]))
    pr.sec("gpsimd", f_yinit)

    ln_toks = emit_ln(pr, xT, xln, SUB, [ld[0]] + ctoks, "ffn")

    tr_last = [None]
    for tt in range(SUB // 128):
        tile_ln = ln_toks[(tt * 128) // 512]
        st = [None]

        def f_rmm(E, tt=tt, tile_ln=tile_ln, st=st):
            E.w(tile_ln, *ctoks)
            E.w(self_tok(pr, "vector"))
            for k in range(KCH):
                i = E.e.matmul(P[0][:, 0:8],
                               xln[:, k * SUB + tt * 128: k * SUB + (tt + 1) * 128],
                               pr.sb["gw"][:, k * 8:(k + 1) * 8],
                               start=(k == 0), stop=(k == KCH - 1))
            st[0] = E.t(i)
        pr.sec("tensor", f_rmm)

        def f_rv(E, tt=tt, st=st):
            m = rt2[:, 0:1]; nm = rt2[:, 1:2]
            E.w(st[0])
            t1 = E.t(E.e.reduce_max(m, P[0][:, 0:8], axis=AX.X))
            E.w(t1)
            t2 = E.t(E.e.tensor_scalar_mul(nm, m, -1.0))
            E.w(t2)
            st[0] = t2
        pr.sec("vector", f_rv)

        def f_rs(E, tt=tt, st=st):
            ex = rt[:, 8:16]; nm = rt2[:, 1:2]
            E.w(st[0])
            st[0] = E.t(E.e.activation(ex, P[0][:, 0:8], AF.Exp, bias=nm))
        pr.sec("scalar", f_rs)

        def f_rv2(E, tt=tt, st=st):
            s = rt[:, 0:8]; ex = rt[:, 8:16]; ge = rt[:, 16:24]
            we = rt[:, 24:32]; msk = rt[:, 32:40]
            ssum = rt2[:, 2:3]; rsum = rt2[:, 3:4]; m1 = rt2[:, 4:5]; m2 = rt2[:, 5:6]
            E.w(st[0])
            t1 = E.t(E.e.reduce_sum(ssum, ex, axis=AX.X))
            E.w(t1)
            t2 = E.t(E.e.reciprocal(rsum, ssum))
            E.w(t2)
            t3 = E.t(E.e.tensor_scalar_mul(s, ex, rsum))
            E.w(t3)
            t4 = E.t(E.e.reduce_max(m1, s, axis=AX.X))
            E.w(t4)
            t5 = E.t(E.e.tensor_single_scalar(ge, s, m1, ALU.is_ge))
            E.w(t5)
            t6 = E.t(E.e.scalar_tensor_tensor(msk, ge, -1e9, s, ALU.mult, ALU.add))
            E.w(t6)
            t7 = E.t(E.e.reduce_max(m2, msk, axis=AX.X))
            E.w(t7)
            t8 = E.t(E.e.tensor_single_scalar(ge, s, m2, ALU.is_ge))
            E.w(t8)
            t9 = E.t(E.e.tensor_mul(we, s, ge))
            st[0] = t9
        pr.sec("vector", f_rv2)

        def f_rtr(E, tt=tt, st=st):
            E.w(st[0])
            if tr_last[0] is not None:
                E.w(tr_last[0])
            i = E.e.transpose(P[1][0:8, 0:128], rt[:, 24:32], pr.sb["eye"][:, :])
            st[0] = E.t(i)
        pr.sec("tensor", f_rtr)

        def f_rcp(E, tt=tt, st=st):
            E.w(st[0])
            tr_last[0] = E.t(E.e.tensor_copy(WEr[:, tt * 128:(tt + 1) * 128], P[1][0:8, 0:128]))
        pr.sec("vector", f_rcp)

    pr.fence()

    wst = {"slot": 0, "lastc": [None, None]}
    for e in range(8):
        _ffn0_expert(pr, e, w1T_d, w2T_d, w3T_d, wst)

    pr.fence()

    _ffn0_swiglu(pr, sw1T_d, sw2T_d, sw3T_d)

    pr.fence()

    def f_st(E):
        dst = rearr_kp(yT_d)
        E.dma(dst[:, :, tok0:tok0 + SUB], sb3(Y[:, :], KCH))
    pr.sec("sync", f_st)


def self_tok(pr, ename):
    return (ename, pr.cnt[ename])


def _ffn0_expert(pr, e, w1T_d, w2T_d, w3T_d, wst):
    P = pr.ps
    nt = SUB // 512
    xln = pr.sb["xln"]; Y = pr.sb["Y"]; wpool = pr.sb["wpool"]
    xs = pr.sb["xs"]; hb = pr.sb["hb"]; x3sb = pr.sb["x3sb"]
    WEr = pr.sb["WEr"]; webc = pr.sb["webc"]; mebc = pr.sb["mebc"]
    x2 = pr.sb["x2"]; prod = pr.sb["prod"]
    b1 = pr.sb["b1"]; b2 = pr.sb["b2"]; b3 = pr.sb["b3"]

    WHALF = KCH * D

    def wslice(slot, k, o):
        return wpool[:, slot * WHALF + k * D + o * 128: slot * WHALF + k * D + o * 128 + 128]

    wtok = [None]
    s1 = wst["slot"]

    def f_ldw1(E, e=e, s1=s1):
        E.w(wst["lastc"][s1])
        wtok[0] = E.dma(sb3(wpool[:, s1 * WHALF:(s1 + 1) * WHALF], KCH),
                        w1T_d[e, :, :].rearrange("(k p) h -> p k h", p=128), sem="dG")
    pr.sec("gpsimd", f_ldw1)
    wst["slot"] ^= 1

    for t in range(nt):
        bt = [None]

        def f_bc(E, e=e, t=t, bt=bt):
            E.w(self_tok(pr, "vector"))
            i = E.e.matmul(P[2][:, :], pr.sb["sel8"][:, e * 128:(e + 1) * 128],
                           WEr[:, t * 512:(t + 1) * 512], start=True, stop=True)
            bt[0] = E.t(i)
        pr.sec("tensor", f_bc)

        def f_bcc(E, t=t, bt=bt):
            wc = webc[:, t * 512:(t + 1) * 512]
            mc = mebc[:, t * 512:(t + 1) * 512]
            E.w(bt[0])
            t1 = E.t(E.e.tensor_copy(wc, P[2][:, :]))
            E.w(t1)
            t2 = E.t(E.e.tensor_single_scalar(mc, wc, 0.0, ALU.is_gt))
            E.w(t2)
            for k in range(KCH):
                E.t(E.e.tensor_mul(xs[:, k * SUB + t * 512: k * SUB + (t + 1) * 512],
                                   xln[:, k * SUB + t * 512: k * SUB + (t + 1) * 512], wc))
        pr.sec("vector", f_bcc)

    xs_done = self_tok(pr, "vector")

    silu_toks = {}
    for t in range(nt):
        for o in range(KCH):
            h1 = [None]

            def f_h1(E, t=t, o=o, h1=h1):
                E.w(wtok[0], xs_done)
                if o >= 2:
                    E.w(silu_toks[(t, o - 2)])
                elif t >= 1:
                    E.w(silu_toks[(t - 1, KCH - 2 + o)])
                for k in range(KCH):
                    i = E.e.matmul(P[4 + o % 2][:, :],
                                   wslice(s1, k, o),
                                   xs[:, k * SUB + t * 512: k * SUB + (t + 1) * 512],
                                   start=(k == 0), stop=(k == KCH - 1))
                h1[0] = E.t(i)
                wst["lastc"][s1] = h1[0]
            pr.sec("tensor", f_h1)

            def f_silu(E, e=e, t=t, o=o, h1=h1):
                E.w(h1[0])
                silu_toks[(t, o)] = E.t(E.e.activation(
                    hb[:, o * SUB + t * 512: o * SUB + (t + 1) * 512],
                    P[4 + o % 2][:, :], AF.Silu, bias=b1[:, e * 8 + o: e * 8 + o + 1]))
            pr.sec("scalar", f_silu)

    wtok3 = [None]
    s3 = wst["slot"]

    def f_ldw3(E, e=e, s3=s3):
        E.w(wst["lastc"][s3])
        wtok3[0] = E.dma(sb3(wpool[:, s3 * WHALF:(s3 + 1) * WHALF], KCH),
                         w3T_d[e, :, :].rearrange("(k p) h -> p k h", p=128), sem="dG")
    pr.sec("gpsimd", f_ldw3)
    wst["slot"] ^= 1

    x3_toks = {}
    for t in range(nt):
        for o in range(KCH):
            h3 = [None]

            def f_h3(E, t=t, o=o, h3=h3):
                E.w(wtok3[0])
                if o >= 2:
                    E.w(x3_toks[(t, o - 2)])
                elif t >= 1:
                    E.w(x3_toks[(t - 1, KCH - 2 + o)])
                for k in range(KCH):
                    i = E.e.matmul(P[4 + o % 2][:, :],
                                   wslice(s3, k, o),
                                   xs[:, k * SUB + t * 512: k * SUB + (t + 1) * 512],
                                   start=(k == 0), stop=(k == KCH - 1))
                h3[0] = E.t(i)
                wst["lastc"][s3] = h3[0]
            pr.sec("tensor", f_h3)

            def f_x3(E, e=e, t=t, o=o, h3=h3):
                mc = mebc[:, t * 512:(t + 1) * 512]
                E.w(h3[0])
                x3_toks[(t, o)] = E.t(E.e.scalar_tensor_tensor(
                    x3sb[:, o * SUB + t * 512: o * SUB + (t + 1) * 512],
                    mc, b3[:, e * 8 + o: e * 8 + o + 1], P[4 + o % 2][:, :],
                    ALU.mult, ALU.add))
            pr.sec("vector", f_x3)

    wtok2 = [None]
    s2 = wst["slot"]

    def f_ldw2(E, e=e, s2=s2):
        E.w(wst["lastc"][s2])
        wtok2[0] = E.dma(sb3(wpool[:, s2 * WHALF:(s2 + 1) * WHALF], KCH),
                         w2T_d[e, :, :].rearrange("(k p) h -> p k h", p=128), sem="dG")
    pr.sec("gpsimd", f_ldw2)
    wst["slot"] ^= 1

    for t in range(nt):
        for o in range(KCH):
            h2 = [None]
            x2t = [None]
            pt = [None]

            def f_h2(E, t=t, o=o, h2=h2):
                E.w(wtok2[0])
                E.w(self_tok(pr, "scalar"))
                E.w(self_tok(pr, "gpsimd"))
                for k in range(KCH):
                    i = E.e.matmul(P[4 + o % 2][:, :],
                                   wslice(s2, k, o),
                                   hb[:, k * SUB + t * 512: k * SUB + (t + 1) * 512],
                                   start=(k == 0), stop=(k == KCH - 1))
                h2[0] = E.t(i)
                wst["lastc"][s2] = h2[0]
            pr.sec("tensor", f_h2)

            def f_x2(E, e=e, t=t, o=o, h2=h2, x2t=x2t):
                E.w(h2[0])
                x2t[0] = E.t(E.e.activation(x2[:, :], P[4 + o % 2][:, :], AF.Identity,
                                            bias=b2[:, e * 8 + o: e * 8 + o + 1]))
            pr.sec("scalar", f_x2)

            def f_pr(E, t=t, o=o, x2t=x2t, pt=pt):
                E.w(x2t[0], x3_toks[(t, o)])
                t1 = E.t(E.e.tensor_mul(prod[:, :], x2[:, :],
                                        x3sb[:, o * SUB + t * 512: o * SUB + (t + 1) * 512]))
                E.w(t1)
                pt[0] = E.t(E.e.tensor_tensor(
                    Y[:, o * SUB + t * 512: o * SUB + (t + 1) * 512],
                    Y[:, o * SUB + t * 512: o * SUB + (t + 1) * 512],
                    prod[:, :], ALU.add))
            pr.sec("gpsimd", f_pr)


def _ffn0_swiglu(pr, sw1T_d, sw2T_d, sw3T_d):
    P = pr.ps
    nt = SUB // 512
    xln = pr.sb["xln"]; Y = pr.sb["Y"]; wpool = pr.sb["wpool"]
    G = pr.sb["sxT"].ap().bitcast(BF16)
    xs = pr.sb["xs"]; hb = pr.sb["hb"]

    wtok = [None]

    def f_ld1(E):
        E.w(self_tok(pr, "tensor"))
        wtok[0] = E.dma(sb3(wpool[:, :], KCH), rearr_kp(sw1T_d), sem="dG")
    pr.sec("gpsimd", f_ld1)

    g_toks = {}
    for t in range(nt):
        for o in range(16):
            gm = [None]

            def f_gmm(E, t=t, o=o, gm=gm):
                E.w(wtok[0])
                if o >= 2:
                    E.w(g_toks[(t, o - 2)])
                elif t >= 1:
                    E.w(g_toks[(t - 1, 14 + o)])
                for k in range(KCH):
                    i = E.e.matmul(P[4 + o % 2][:, :],
                                   wpool[:, k * 2048 + o * 128: k * 2048 + o * 128 + 128],
                                   xln[:, k * SUB + t * 512: k * SUB + (t + 1) * 512],
                                   start=(k == 0), stop=(k == KCH - 1))
                gm[0] = E.t(i)
            pr.sec("tensor", f_gmm)

            def f_gsilu(E, t=t, o=o, gm=gm):
                E.w(gm[0])
                g_toks[(t, o)] = E.t(E.e.activation(
                    G[:, o * SUB + t * 512: o * SUB + (t + 1) * 512],
                    P[4 + o % 2][:, :], AF.Silu))
            pr.sec("scalar", f_gsilu)

    def f_ld2(E):
        E.w(self_tok(pr, "tensor"))
        wtok[0] = E.dma(sb3(wpool[:, :], KCH), rearr_kp(sw2T_d), sem="dG")
    pr.sec("gpsimd", f_ld2)

    p_toks = {}
    for t in range(nt):
        for o in range(16):
            um = [None]

            def f_umm(E, t=t, o=o, um=um):
                E.w(wtok[0], self_tok(pr, "scalar"))
                if o >= 2:
                    E.w(p_toks[(t, o - 2)])
                elif t >= 1:
                    E.w(p_toks[(t - 1, 14 + o)])
                for k in range(KCH):
                    i = E.e.matmul(P[4 + o % 2][:, :],
                                   wpool[:, k * 2048 + o * 128: k * 2048 + o * 128 + 128],
                                   xln[:, k * SUB + t * 512: k * SUB + (t + 1) * 512],
                                   start=(k == 0), stop=(k == KCH - 1))
                um[0] = E.t(i)
            pr.sec("tensor", f_umm)

            def f_up(E, t=t, o=o, um=um):
                E.w(um[0])
                dst = xs if o < 8 else hb
                oo = o % 8
                p_toks[(t, o)] = E.t(E.e.tensor_tensor(
                    dst[:, oo * SUB + t * 512: oo * SUB + (t + 1) * 512],
                    G[:, o * SUB + t * 512: o * SUB + (t + 1) * 512],
                    P[4 + o % 2][:, :], ALU.mult))
            pr.sec("vector", f_up)

    def f_ld3(E):
        E.w(self_tok(pr, "tensor"))
        wtok[0] = E.dma(sb3(wpool[:, :], 16), rearr_kp(sw3T_d), sem="dG")
    pr.sec("gpsimd", f_ld3)

    zdone = {}
    for t in range(nt):
        for o in range(KCH):
            zm = [None]

            def f_zmm(E, t=t, o=o, zm=zm):
                E.w(wtok[0], self_tok(pr, "vector"))
                if o >= 2:
                    E.w(zdone[(t, o - 2)])
                elif t >= 1:
                    E.w(zdone[(t - 1, KCH - 2 + o)])
                for k in range(16):
                    srcb = xs if k < 8 else hb
                    kk = k % 8
                    i = E.e.matmul(P[4 + o % 2][:, :],
                                   wpool[:, k * D + o * 128: k * D + o * 128 + 128],
                                   srcb[:, kk * SUB + t * 512: kk * SUB + (t + 1) * 512],
                                   start=(k == 0), stop=(k == 15))
                zm[0] = E.t(i)
            pr.sec("tensor", f_zmm)

            def f_zacc(E, t=t, o=o, zm=zm):
                E.w(zm[0])
                zdone[(t, o)] = E.t(E.e.tensor_tensor(
                    Y[:, o * SUB + t * 512: o * SUB + (t + 1) * 512],
                    Y[:, o * SUB + t * 512: o * SUB + (t + 1) * 512],
                    P[4 + o % 2][:, :], ALU.add))
            pr.sec("vector", f_zacc)



_BUILD_CACHE = {}


def _get(name, builder):
    if name not in _BUILD_CACHE:
        _BUILD_CACHE[name] = builder()
    return _BUILD_CACHE[name]


def _shard_T(x_flat):
    out = []
    for c in range(NCORES):
        sh = x_flat[c * TOK:(c + 1) * TOK]
        out.append(np.ascontiguousarray(sh.T))
    return out


def _unshard_T(shards):
    return np.concatenate([s.T for s in shards], axis=0)


def _col_bias(b):
    return np.ascontiguousarray(b.reshape(8, 8, 128).transpose(2, 0, 1).reshape(128, 64))



def phase_atta(pr, io, ctoks0):
    pr.push_scope()
    ctoks = list(ctoks0)
    nc = pr.nc
    sid = pr._scope_id
    xT_d = io["xT"]
    wqT_d = io["wqT"]; wkT_d = io["wkT"]; wvT_d = io["wvT"]; woT_d = io["woT"]
    mrow_d = io["mrow"]
    yT_d = io["yT"]
    scrk_d = pr.dram(f"scrk{sid}", [HEADS, 64, NKV], BF16).ap()
    xsh_h = pr.dram(f"xsh{sid}", [2, D, TOK], BF16, shared=True)
    xloc_d = pr.dram(f"xloc{sid}", [D, TOK], BF16).ap()
    xkv_d = pr.dram(f"xkv{sid}", [D, NKV], BF16).ap()

    trim = np.zeros((4, 128, 512), dtype=np.float32)
    for j in range(4):
        for p in range(128):
            trim[j, p, j * 128 + p:] = 1.0
    tri_c = pr.const(trim.reshape(4 * 128, 512), "tri_c")

    xT = pr.sbuf("sxT", [128, KCH * 512], F32)
    xln = pr.sbuf("xln", [128, 2 * KCH * 512], BF16)
    wbuf = pr.sbuf("wbuf", [128, KCH * D], BF16)
    vres = pr.sbuf("vres", [128, 32 * HEADS * 65], BF16)
    qtb = pr.sbuf("qtb", [128, KCH * TOK], BF16)
    qst = pr.sbuf("qst", [66, 2 * 512], BF16)
    psb = pr.sbuf("psb", [128, 4 * 512], BF16)
    tri = pr.sbuf("tri", [128, 4 * 512], BF16)
    oall = pr.sbuf("oall", [128, KCH * TOK], BF16)
    osb = pr.sbuf("osb", [64, 512], BF16)
    kevac = pr.sbuf("kevac", [128, 2 * 512], BF16)
    mcol = pr.sbuf("mcol", [128, 8], F32)
    lrow = pr.sbuf("lrow", [1, 512], F32)
    outsb = pr.sbuf("outsb", [128, 512], F32)
    alloc_ln(pr, "att")

    def f_init(E):
        ctoks.append(E.dma(sb3(tri[:, :], 4), tri_c.ap().rearrange("(j p) c -> p j c", p=128), sem="dG"))
        ctoks.append(E.t(E.e.memset(vres[:, :], 1.0)))
    pr.sec("gpsimd", f_init)

    P = pr.ps
    for r in range(4):
        ldt = [None]

        def f_ldo(E, r=r):
            E.w(("tensor", pr.cnt["tensor"]), ("scalar", pr.cnt["scalar"]),
                ("vector", pr.cnt["vector"]), ("dS", pr.cnt["dS"]))
            ldt[0] = E.dma(sb3(xT[:, :], KCH), rearr_kp(xT_d)[:, :, r * 512:(r + 1) * 512])
        pr.sec("sync", f_ldo)

        lt = emit_ln(pr, xT, xln[:, 0:KCH * 512], 512, [ldt[0]] + ctoks, "att")

        def f_wsh(E, r=r, lt=lt):
            E.w(lt[-1])
            E.dma(rearr_kp(xloc_d)[:, :, r * 512:(r + 1) * 512], sb3(xln[:, 0:KCH * 512], KCH))
        pr.sec("sync", f_wsh)

    pr.fence()

    def f_xout(par, dd):
        dd.wait(("dS", pr.cnt["dS"]))
        dd(xsh_h.ap()[par, :, :], xloc_d)
    pr.pswitch(f_xout)
    pr.core_barrier()

    def f_xin(par, dd):
        dd(xkv_d[:, 0:TOK], xsh_h.ap()[1 - par, :, :])
        dd(xkv_d[:, TOK:NKV], xsh_h.ap()[par, :, :])
    gtok = pr.pswitch(f_xin)

    _atta_proj(pr, xkv_d, gtok, wqT_d, wkT_d, wvT_d, scrk_d, ctoks)

    pr.fence()

    kstv = pr.sb["wbuf"][0:66, 0:NKV]

    def f_init2(E):
        ctoks.append(E.dma(kstv[65:66, :], mrow_d))
    pr.sec("sync", f_init2)

    def f_init3(E):
        E.w(*[c for c in ctoks if c])
        ctoks.append(E.t(E.e.memset(kstv[64:65, :], -1.0)))
    pr.sec("gpsimd", f_init3)

    for h in range(HEADS):
        _atta_head(pr, h, scrk_d, ctoks)

    pr.fence()

    wo_tok = [None]

    def f_ldwo(E):
        wo_tok[0] = E.dma(sb3(wbuf[:, :], KCH), rearr_kp(woT_d), sem="dG")
    pr.sec("gpsimd", f_ldwo)

    for g in range(4):
        xg = [None]

        def f_ldxg(E, g=g):
            E.w(self_tok(pr, "vector"))
            src = rearr_kp(xT_d)
            xg[0] = E.dma(sb3(xT[:, :], KCH), src[:, :, g * 512:(g + 1) * 512])
        pr.sec("sync", f_ldxg)

        for o in range(KCH):
            om = [None]

            def f_omm(E, g=g, o=o, om=om):
                E.w(wo_tok[0], self_tok(pr, "vector"))
                for k in range(KCH):
                    i = E.e.matmul(P[4 + o % 2][:, :],
                                   wbuf[:, k * D + o * 128: k * D + o * 128 + 128],
                                   pr.sb["oall"][:, k * TOK + g * 512: k * TOK + (g + 1) * 512],
                                   start=(k == 0), stop=(k == KCH - 1))
                om[0] = E.t(i)
            pr.sec("tensor", f_omm)

            def f_oadd(E, g=g, o=o, om=om):
                E.w(om[0], xg[0])
                E.w(("dS", pr.cnt["dS"]))
                t1 = E.t(E.e.tensor_tensor(outsb[:, :], xT[:, o * 512:(o + 1) * 512],
                                           P[4 + o % 2][:, :], ALU.add))
                E.w(t1)
            pr.sec("vector", f_oadd)

            def f_ost(E, g=g, o=o):
                E.w(self_tok(pr, "vector"))
                dst = rearr_kp(yT_d)
                E.dma(dst[:, o, g * 512:(g + 1) * 512], outsb[:, :])
            pr.sec("sync", f_ost)

    pr.fence()
    pr.pop_scope()


def _atta_proj(pr, xkv_d, gtok, wqT_d, wkT_d, wvT_d, scrk_d, ctoks):
    P = pr.ps
    xln = pr.sb["xln"]; wbuf = pr.sb["wbuf"]
    vres = pr.sb["vres"]; qtb = pr.sb["qtb"]; kevac = pr.sb["kevac"]
    XW = KCH * 512
    pr.fence()
    wt = [None]
    xt = {}

    def f_ldwk(E):
        wt[0] = E.dma(sb3(wbuf[:, :], KCH), rearr_kp(wkT_d), sem="dG")
    pr.sec("gpsimd", f_ldwk)

    def ldx(r, dep):
        sl = r % 2

        def f_ldx(E, r=r, sl=sl, dep=dep):
            E.w(gtok, dep)
            xt[r] = E.dma(sb3(xln[:, sl * XW:(sl + 1) * XW], KCH),
                          rearr_kp(xkv_d)[:, :, r * 512:(r + 1) * 512])
        pr.sec("sync", f_ldx)

    kmmt = {}; kevt = {}; kstt = {}
    for r in range(8):
        ldx(r, kmmt.get(r * 8 - 9))
        sl = r % 2
        for o in range(KCH):
            n = r * 8 + o

            def f_kmm(E, r=r, o=o, n=n, sl=sl):
                E.w(wt[0], xt[r], kevt.get(n - 2))
                for k in range(KCH):
                    i = E.e.matmul(P[4 + o % 2][:, :],
                                   wbuf[:, k * D + o * 128: k * D + o * 128 + 128],
                                   xln[:, sl * XW + k * 512:sl * XW + (k + 1) * 512],
                                   start=(k == 0), stop=(k == KCH - 1))
                kmmt[n] = E.t(i)
            pr.sec("tensor", f_kmm)

            def f_kev(E, o=o, n=n):
                E.w(kmmt[n], kstt.get(n - 2))
                kevt[n] = E.t(E.e.tensor_copy(kevac[:, (o % 2) * 512:(o % 2) * 512 + 512],
                                              P[4 + o % 2][:, :]))
            pr.sec("vector", f_kev)

            def f_kst(E, r=r, o=o, n=n):
                E.w(kevt[n])
                for hh in range(2):
                    h = o * 2 + hh
                    t = E.dma(scrk_d[h, :, r * 512:(r + 1) * 512],
                              kevac[hh * 64:(hh + 1) * 64, (o % 2) * 512:(o % 2) * 512 + 512])
                kstt[n] = t
            pr.sec("sync", f_kst)

    wv = [None]

    def f_ldwv(E):
        E.w(kmmt[63])
        wv[0] = E.dma(sb3(wbuf[:, :], KCH), rearr_kp(wvT_d), sem="dG")
    pr.sec("gpsimd", f_ldwv)

    vmmt = {}; vevt = {}
    for r in range(8):
        ldx(r, kmmt.get(55 + 8 * r) if r < 2 else vmmt.get(r * 8 - 9))
        sl = r % 2
        for ts in range(4):
            for o in range(2):
                m = r * 8 + ts * 2 + o

                def f_vmm(E, r=r, ts=ts, o=o, m=m, sl=sl):
                    E.w(wv[0], xt[r], vevt.get(m - 2), *ctoks)
                    for k in range(KCH):
                        i = E.e.matmul(P[4 + m % 2][:, :],
                                       xln[:, sl * XW + k * 512 + ts * 128:sl * XW + k * 512 + (ts + 1) * 128],
                                       wbuf[:, k * D + o * 512: k * D + (o + 1) * 512],
                                       start=(k == 0), stop=(k == KCH - 1))
                    vmmt[m] = E.t(i)
                pr.sec("tensor", f_vmm)

                def f_vev(E, r=r, ts=ts, o=o, m=m):
                    E.w(vmmt[m])
                    rt = r * 4 + ts
                    for hh in range(8):
                        h = o * 8 + hh
                        t = E.t(E.e.tensor_copy(
                            vres[:, rt * HEADS * 65 + h * 65: rt * HEADS * 65 + h * 65 + 64],
                            P[4 + m % 2][:, hh * 64:(hh + 1) * 64]))
                    vevt[m] = t
                pr.sec("vector", f_vev)

    wqq = [None]

    def f_ldwq(E):
        E.w(vmmt[63])
        wqq[0] = E.dma(sb3(wbuf[:, :], KCH), rearr_kp(wqT_d), sem="dG")
    pr.sec("gpsimd", f_ldwq)

    qmmt = {}; qevt = {}
    for idx, r in enumerate([6, 7, 4, 5]):
        if idx >= 2:
            ldx(r, qmmt.get((idx - 2) * 8 + 7))
        sl = r % 2
        for o in range(KCH):
            p = idx * 8 + o

            def f_qmm(E, r=r, o=o, p=p, sl=sl):
                E.w(wqq[0], xt[r], qevt.get(p - 2))
                for k in range(KCH):
                    i = E.e.matmul(P[4 + p % 2][:, :],
                                   wbuf[:, k * D + o * 128: k * D + o * 128 + 128],
                                   xln[:, sl * XW + k * 512:sl * XW + (k + 1) * 512],
                                   start=(k == 0), stop=(k == KCH - 1))
                qmmt[p] = E.t(i)
            pr.sec("tensor", f_qmm)

            def f_qev(E, r=r, o=o, p=p):
                E.w(qmmt[p])
                qevt[p] = E.t(E.e.activation(qtb[:, o * TOK + (r - 4) * 512: o * TOK + (r - 3) * 512],
                                             P[4 + p % 2][:, :], AF.Copy, scale=0.125))
            pr.sec("scalar", f_qev)


def _atta_head(pr, h, scrk_d, ctoks):
    P = pr.ps
    kst = pr.sb["wbuf"][0:66, 0:NKV]
    qst = pr.sb["qst"]; qtb = pr.sb["qtb"]
    vres = pr.sb["vres"]; psb = pr.sb["psb"]; tri = pr.sb["tri"]
    oall = pr.sb["oall"]; osb = pr.sb["osb"]; mcol = pr.sb["mcol"]; lrow = pr.sb["lrow"]
    atmp = pr.sb["att_tmp"]
    ch, hh = h // 2, h % 2

    kld = [None]

    def f_ldk(E, h=h):
        E.w(self_tok(pr, "tensor"))
        kld[0] = E.dma(kst[0:64, :], scrk_d[h, :, :])
    pr.sec("sync", f_ldk)

    stg_fin = {}
    pvlast = {}
    normv_tok = {}
    normbc_tok = {}
    normap_tok = {}
    mtr_tok = {}
    banklast = {0: None, 1: None}

    def mk_units_A(g):
        gp = g % 2
        q0 = g * 512
        c0 = gp * 512
        units = []
        stgq = [None]

        def f_stgq(E, g=g, gp=gp, q0=q0, c0=c0, stgq=stgq):
            E.w(pvlast.get(g - 2))
            stgq[0] = E.t(E.e.tensor_copy(qst[0:64, c0:c0 + 512],
                                          qtb[hh * 64:hh * 64 + 64,
                                              ch * TOK + q0:ch * TOK + q0 + 512]))
        units.append(lambda f=f_stgq: pr.sec("vector", f))

        mt = [None] * 4
        for qs in range(4):
            for ks in range(5 + g):
                sm = [None]
                bk = ks % 2

                def f_smm(E, g=g, gp=gp, c0=c0, qs=qs, ks=ks, sm=sm, bk=bk, stgq=stgq):
                    E.w(kld[0], *ctoks)
                    E.w(stgq[0], stg_fin.get(g - 1))
                    E.w(banklast[bk])
                    i = E.e.matmul(P[bk][:, :],
                                   qst[0:64, c0 + qs * 128:c0 + (qs + 1) * 128],
                                   kst[0:64, ks * 512:(ks + 1) * 512],
                                   start=True, stop=True)
                    sm[0] = E.t(i)

                def f_red(E, g=g, gp=gp, qs=qs, ks=ks, sm=sm, bk=bk, mt=mt):
                    E.w(sm[0])
                    if ks == 0 and qs == 0:
                        E.w(mtr_tok.get(g - 2))
                    if ks == 0:
                        t = E.t(E.e.reduce_max(mcol[:, gp * 4 + qs:gp * 4 + qs + 1],
                                               P[bk][:, :], axis=AX.X))
                    else:
                        t1 = E.t(E.e.reduce_max(atmp[:, gp:gp + 1], P[bk][:, :], axis=AX.X))
                        E.w(t1)
                        t = E.t(E.e.tensor_max(mcol[:, gp * 4 + qs:gp * 4 + qs + 1],
                                               mcol[:, gp * 4 + qs:gp * 4 + qs + 1],
                                               atmp[:, gp:gp + 1]))
                    banklast[bk] = t
                    if ks == 4 + g:
                        mt[qs] = t

                units.append(lambda f=f_smm, f2=f_red: (pr.sec("tensor", f),
                                                        pr.sec("vector", f2)))

        stg = [None]

        def f_mtr(E, g=g, gp=gp, mt=mt, stg=stg):
            E.w(*mt)
            E.w(stg_fin.get(g - 1))
            for qs in range(4):
                i = E.e.transpose(P[1][0:1, qs * 128:(qs + 1) * 128],
                                  mcol[:, gp * 4 + qs:gp * 4 + qs + 1], pr.sb["eye"][:, :])
            stg[0] = E.t(i)
            mtr_tok[g] = stg[0]

        def f_stg(E, g=g, c0=c0, stg=stg):
            E.w(stg[0])
            t0 = E.t(E.e.memset(qst[64:66, c0:c0 + 512], 1.0))
            E.w(t0)
            t2 = E.t(E.e.tensor_copy(qst[64:65, c0:c0 + 512], P[1][0:1, :]))
            E.w(t2)
            stg_fin[g] = t2

        units.append(lambda f=f_mtr, f2=f_stg: (pr.sec("tensor", f),
                                                pr.sec("vector", f2)))
        return units

    def mk_units_B(g):
        gp = g % 2
        nkt = 20 + 4 * g
        q0 = g * 512
        c0 = gp * 512
        units = []
        pv_tok = {}
        exp_tok = {}
        pend_pv = []

        for kt in range(nkt):
            et = [None]
            stb = 2 + kt % 4
            pslot = kt % 4

            def f_stmm(E, g=g, c0=c0, kt=kt, et=et, stb=stb):
                E.w(stg_fin[g])
                E.w(exp_tok.get(kt - 4))
                if kt == 0:
                    E.w(normap_tok.get(g - 1))
                i = E.e.matmul(P[stb][:, :],
                               kst[:, kt * 128:(kt + 1) * 128],
                               qst[:, c0:c0 + 512], start=True, stop=True)
                et[0] = E.t(i)

            def f_exp(E, g=g, kt=kt, et=et, stb=stb, pslot=pslot):
                E.w(et[0])
                E.w(pv_tok.get(kt - 4))
                t = E.t(E.e.activation(psb[:, pslot * 512:(pslot + 1) * 512],
                                       P[stb][:, :], AF.Exp))
                exp_tok[kt] = t
                et[0] = t

            if kt >= nkt - 4:
                j = kt - (nkt - 4)

                def f_tri(E, g=g, kt=kt, j=j, et=et, pslot=pslot):
                    E.w(et[0])
                    et[0] = E.t(E.e.tensor_mul(psb[:, pslot * 512:(pslot + 1) * 512],
                                               psb[:, pslot * 512:(pslot + 1) * 512],
                                               tri[:, j * 512:(j + 1) * 512]))
            else:
                f_tri = None

            def f_pv(E, g=g, kt=kt, nkt=nkt, et=et, pslot=pslot):
                E.w(et[0])
                if kt == 0:
                    E.w(normv_tok.get(g - 1))
                i = E.e.matmul(P[6][0:65, :],
                               vres[:, kt * HEADS * 65 + h * 65: kt * HEADS * 65 + h * 65 + 65],
                               psb[:, pslot * 512:(pslot + 1) * 512],
                               start=(kt == 0), stop=(kt == nkt - 1))
                pv_tok[kt] = E.t(i)
                if kt == nkt - 1:
                    pvlast[g] = pv_tok[kt]

            def emit(f1=f_stmm, f2=f_exp, f3=f_tri, f4s=pend_pv, f4=f_pv, kt=kt):
                pr.sec("tensor", f1)
                pr.sec("scalar", f2)
                if f3 is not None:
                    pr.sec("vector", f3)
                f4s.append(f4)
                if kt >= 2:
                    pr.sec("tensor", f4s.pop(0))
            units.append(emit)

        def flush(f4s=pend_pv):
            while f4s:
                pr.sec("tensor", f4s.pop(0))
        units.append(flush)
        return units, pv_tok, nkt

    def emit_norm(g, pv_tok, nkt):
        gp = g % 2
        q0 = g * 512
        nt_ = [None]

        def f_norm_v(E, g=g, nt_=nt_):
            E.w(pv_tok[nkt - 1])
            E.w(normbc_tok.get(g - 1))
            t1 = E.t(E.e.reciprocal(lrow[:, :], P[6][64:65, :]))
            E.w(t1)
            t2 = E.t(E.e.tensor_copy(osb[:, :], P[6][0:64, :]))
            nt_[0] = t2
            normv_tok[g] = t2
        pr.sec("vector", f_norm_v)

        def f_norm_bc(E, g=g, nt_=nt_):
            E.w(nt_[0])
            i = E.e.matmul(P[2][0:64, :], pr.sb["ones1r"][:, 0:64], lrow[:, :],
                           start=True, stop=True)
            nt_[0] = E.t(i)
            normbc_tok[g] = nt_[0]
        pr.sec("tensor", f_norm_bc)

        def f_norm_ap(E, g=g, q0=q0, nt_=nt_):
            E.w(nt_[0])
            normap_tok[g] = E.t(E.e.tensor_mul(oall[hh * 64:hh * 64 + 64,
                                                    ch * TOK + q0:ch * TOK + q0 + 512],
                                               osb[:, :], P[2][0:64, :]))
        pr.sec("vector", f_norm_ap)

    for u in mk_units_A(0):
        u()
    for g in range(4):
        unitsB, pv_tok, nkt = mk_units_B(g)
        unitsA = mk_units_A(g + 1) if g < 3 else []
        na, nb = len(unitsA), len(unitsB)
        ia = 0
        for ib, ub in enumerate(unitsB):
            ub()
            want = (ib + 1) * na // nb
            while ia < want:
                unitsA[ia]()
                ia += 1
        while ia < na:
            unitsA[ia]()
            ia += 1
        emit_norm(g, pv_tok, nkt)


def phase_perf(pr, io, ctoks0):
    pr.push_scope()
    ctoks = list(ctoks0)
    nc = pr.nc
    sid = pr._scope_id
    xT_d = io["xT"]
    wqT_d = io["wqT"]; wkT_d = io["wkT"]; wvT_d = io["wvT"]; woT_d = io["woT"]
    projT_d = io["projT"]; bor_d = io["bor"]
    yT_d = io["yT"]
    scrk_d = pr.dram(f"scrk{sid}", [HEADS, 64, NKV], BF16).ap()
    xsh_h = pr.dram(f"xsh{sid}", [2, D, TOK], BF16, shared=True)
    xloc_d = pr.dram(f"xloc{sid}", [D, TOK], BF16).ap()
    xkv_d = pr.dram(f"xkv{sid}", [D, NKV], BF16).ap()

    xT = pr.sbuf("sxT", [128, KCH * 512], F32)
    xln = pr.sbuf("xln", [128, 2 * KCH * 512], BF16)
    wbuf = pr.sbuf("wbuf", [128, KCH * D], BF16)
    vres = pr.sbuf("vres", [128, 32 * HEADS * 65], BF16)
    qtb = pr.sbuf("qtb", [128, KCH * TOK], BF16)
    prj = pr.sbuf("prj", [64, HEADS * PM], BF16)
    ones64b = pr.sbuf("ones64b", [64, 1], BF16)
    bias4 = pr.sbuf("bias4", [128, 24], F32)
    phik = pr.sbuf("phik", [128, 2 * 512], BF16)
    phiqt = pr.sbuf("phiqt", [128, 2 * 512], BF16)
    kvsb = pr.sbuf("kvsb", [128, 2 * 65], BF16)
    oall = pr.sbuf("oall", [128, KCH * TOK], BF16)
    outsb = pr.sbuf("outsb", [128, 512], F32)
    bo = pr.sbuf("bo", [128, KCH], F32)
    pr.sbuf("kevb", [128, 2 * 512], BF16)
    alloc_ln(pr, "att")

    def f_init(E):
        ctoks.append(E.dma(prj[:, :].rearrange("d (h m) -> d h m", h=HEADS),
                           projT_d.rearrange("h d m -> d h m"), sem="dG"))
        ctoks.append(E.t(E.e.memset(ones64b[:, :], 1.0)))
        ctoks.append(E.t(E.e.memset(vres[:, :], 1.0)))
    pr.sec("gpsimd", f_init)

    def f_init2(E):
        ctoks.append(E.dma(bo[:, :], bor_d))
    pr.sec("sync", f_init2)

    P = pr.ps
    for r in range(4):
        ldt = [None]

        def f_ldo(E, r=r):
            E.w(("tensor", pr.cnt["tensor"]), ("scalar", pr.cnt["scalar"]),
                ("vector", pr.cnt["vector"]), ("dS", pr.cnt["dS"]))
            ldt[0] = E.dma(sb3(xT[:, :], KCH), rearr_kp(xT_d)[:, :, r * 512:(r + 1) * 512])
        pr.sec("sync", f_ldo)

        lt = emit_ln(pr, xT, xln[:, 0:KCH * 512], 512, [ldt[0]] + ctoks, "att")

        def f_wsh(E, r=r, lt=lt):
            E.w(lt[-1])
            E.dma(rearr_kp(xloc_d)[:, :, r * 512:(r + 1) * 512], sb3(xln[:, 0:KCH * 512], KCH))
        pr.sec("sync", f_wsh)

    pr.fence()

    def f_xout(par, dd):
        dd.wait(("dS", pr.cnt["dS"]))
        dd(xsh_h.ap()[par, :, :], xloc_d)
    pr.pswitch(f_xout)
    pr.core_barrier()

    def f_xin(par, dd):
        dd(xkv_d[:, 0:TOK], xsh_h.ap()[1 - par, :, :])
        dd(xkv_d[:, TOK:NKV], xsh_h.ap()[par, :, :])
    gtok = pr.pswitch(f_xin)

    _perf_proj(pr, xkv_d, gtok, wqT_d, wkT_d, wvT_d, scrk_d, ctoks)

    pr.fence()

    for h in range(HEADS):
        _perf_head(pr, h, scrk_d, ctoks)

    pr.fence()

    wo_tok = [None]

    def f_ldwo(E):
        wo_tok[0] = E.dma(sb3(wbuf[:, :], KCH), rearr_kp(woT_d), sem="dG")
    pr.sec("gpsimd", f_ldwo)

    for g in range(4):
        xg = [None]

        def f_ldxg(E, g=g):
            E.w(("dS", pr.cnt["dS"]))
            src = rearr_kp(xT_d)
            xg[0] = E.dma(sb3(xT[:, :], KCH), src[:, :, g * 512:(g + 1) * 512])
        pr.sec("sync", f_ldxg)

        for o in range(KCH):
            om = [None]
            ev = [None]

            def f_omm(E, g=g, o=o, om=om):
                E.w(wo_tok[0], self_tok(pr, "vector"))
                for k in range(KCH):
                    i = E.e.matmul(P[4 + o % 2][:, :],
                                   wbuf[:, k * D + o * 128: k * D + o * 128 + 128],
                                   pr.sb["oall"][:, k * TOK + g * 512: k * TOK + (g + 1) * 512],
                                   start=(k == 0), stop=(k == KCH - 1))
                om[0] = E.t(i)
            pr.sec("tensor", f_omm)

            def f_obias(E, g=g, o=o, om=om, ev=ev):
                E.w(om[0])
                E.w(("dS", pr.cnt["dS"]))
                ev[0] = E.t(E.e.activation(outsb[:, :], P[4 + o % 2][:, :], AF.Identity,
                                           bias=bo[:, o:o + 1]))
            pr.sec("scalar", f_obias)

            def f_oadd(E, g=g, o=o, ev=ev):
                E.w(ev[0], xg[0])
                E.t(E.e.tensor_tensor(outsb[:, :], outsb[:, :],
                                      xT[:, o * 512:(o + 1) * 512], ALU.add))
            pr.sec("vector", f_oadd)

            def f_ost(E, g=g, o=o):
                E.w(self_tok(pr, "vector"))
                dst = rearr_kp(yT_d)
                E.dma(dst[:, o, g * 512:(g + 1) * 512], outsb[:, :])
            pr.sec("sync", f_ost)

    pr.fence()
    pr.pop_scope()


def _perf_proj(pr, xkv_d, gtok, wqT_d, wkT_d, wvT_d, scrk_d, ctoks):
    P = pr.ps
    xln = pr.sb["xln"]; wbuf = pr.sb["wbuf"]
    vres = pr.sb["vres"]; qtb = pr.sb["qtb"]; kevb = pr.sb["kevb"]
    XW = KCH * 512
    pr.fence()
    wt = [None]
    xt = {}

    def f_ldwk(E):
        wt[0] = E.dma(sb3(wbuf[:, :], KCH), rearr_kp(wkT_d), sem="dG")
    pr.sec("gpsimd", f_ldwk)

    def ldx(r, dep):
        sl = r % 2

        def f_ldx(E, r=r, sl=sl, dep=dep):
            E.w(gtok, dep)
            xt[r] = E.dma(sb3(xln[:, sl * XW:(sl + 1) * XW], KCH),
                          rearr_kp(xkv_d)[:, :, r * 512:(r + 1) * 512])
        pr.sec("sync", f_ldx)

    kmmt = {}; kevt = {}; kstt = {}
    for r in range(8):
        ldx(r, kmmt.get(r * 8 - 9))
        sl = r % 2
        for o in range(KCH):
            n = r * 8 + o

            def f_kmm(E, r=r, o=o, n=n, sl=sl):
                E.w(wt[0], xt[r], kevt.get(n - 2))
                for k in range(KCH):
                    i = E.e.matmul(P[4 + o % 2][:, :],
                                   wbuf[:, k * D + o * 128: k * D + o * 128 + 128],
                                   xln[:, sl * XW + k * 512:sl * XW + (k + 1) * 512],
                                   start=(k == 0), stop=(k == KCH - 1))
                kmmt[n] = E.t(i)
            pr.sec("tensor", f_kmm)

            def f_kev(E, o=o, n=n):
                E.w(kmmt[n], kstt.get(n - 2))
                kevt[n] = E.t(E.e.activation(kevb[:, (o % 2) * 512:(o % 2) * 512 + 512],
                                             P[4 + o % 2][:, :], AF.Copy, scale=0.125))
            pr.sec("scalar", f_kev)

            def f_kst(E, r=r, o=o, n=n):
                E.w(kevt[n])
                for hh in range(2):
                    h = o * 2 + hh
                    t = E.dma(scrk_d[h, :, r * 512:(r + 1) * 512],
                              kevb[hh * 64:(hh + 1) * 64, (o % 2) * 512:(o % 2) * 512 + 512])
                kstt[n] = t
            pr.sec("sync", f_kst)

    wv = [None]

    def f_ldwv(E):
        E.w(kmmt[63])
        wv[0] = E.dma(sb3(wbuf[:, :], KCH), rearr_kp(wvT_d), sem="dG")
    pr.sec("gpsimd", f_ldwv)

    vmmt = {}; vevt = {}
    for r in range(8):
        ldx(r, kmmt.get(55 + 8 * r) if r < 2 else vmmt.get(r * 8 - 9))
        sl = r % 2
        for ts in range(4):
            for o in range(2):
                m = r * 8 + ts * 2 + o

                def f_vmm(E, r=r, ts=ts, o=o, m=m, sl=sl):
                    E.w(wv[0], xt[r], vevt.get(m - 2), *ctoks)
                    for k in range(KCH):
                        i = E.e.matmul(P[4 + m % 2][:, :],
                                       xln[:, sl * XW + k * 512 + ts * 128:sl * XW + k * 512 + (ts + 1) * 128],
                                       wbuf[:, k * D + o * 512: k * D + (o + 1) * 512],
                                       start=(k == 0), stop=(k == KCH - 1))
                    vmmt[m] = E.t(i)
                pr.sec("tensor", f_vmm)

                def f_vev(E, r=r, ts=ts, o=o, m=m):
                    E.w(vmmt[m])
                    rt = r * 4 + ts
                    for hh in range(8):
                        h = o * 8 + hh
                        t = E.t(E.e.tensor_copy(
                            vres[:, rt * HEADS * 65 + h * 65: rt * HEADS * 65 + h * 65 + 64],
                            P[4 + m % 2][:, hh * 64:(hh + 1) * 64]))
                    vevt[m] = t
                pr.sec("vector", f_vev)

    wqq = [None]

    def f_ldwq(E):
        E.w(vmmt[63])
        wqq[0] = E.dma(sb3(wbuf[:, :], KCH), rearr_kp(wqT_d), sem="dG")
    pr.sec("gpsimd", f_ldwq)

    qmmt = {}; qevt = {}
    for idx, r in enumerate([6, 7, 4, 5]):
        if idx >= 2:
            ldx(r, qmmt.get((idx - 2) * 8 + 7))
        sl = r % 2
        for o in range(KCH):
            p = idx * 8 + o

            def f_qmm(E, r=r, o=o, p=p, sl=sl):
                E.w(wqq[0], xt[r], qevt.get(p - 2))
                for k in range(KCH):
                    i = E.e.matmul(P[4 + p % 2][:, :],
                                   wbuf[:, k * D + o * 128: k * D + o * 128 + 128],
                                   xln[:, sl * XW + k * 512:sl * XW + (k + 1) * 512],
                                   start=(k == 0), stop=(k == KCH - 1))
                qmmt[p] = E.t(i)
            pr.sec("tensor", f_qmm)

            def f_qev(E, r=r, o=o, p=p):
                E.w(qmmt[p])
                qevt[p] = E.t(E.e.activation(qtb[:, o * TOK + (r - 4) * 512: o * TOK + (r - 3) * 512],
                                             P[4 + p % 2][:, :], AF.Copy, scale=0.125))
            pr.sec("scalar", f_qev)


def _perf_head(pr, h, scrk_d, ctoks):
    P = pr.ps
    kst2 = pr.sb["wbuf"][0:64, 1024:1024 + NKV]
    qhst = pr.sb["wbuf"][0:64, 1024 + NKV:1024 + NKV + TOK]
    qtb = pr.sb["qtb"]
    vres = pr.sb["vres"]; phik = pr.sb["phik"]
    phiqt = pr.sb["phiqt"]; kvsb = pr.sb["kvsb"]; oall = pr.sb["oall"]
    drow = pr.sb["att_rows"]
    numsb = pr.sb["wbuf"][0:64, 512:1024]
    prj = pr.sb["prj"]; bias4 = pr.sb["bias4"]
    sqs = pr.sb["wbuf"][0:64, 0:512]
    ones64b = pr.sb["ones64b"]
    ch, hh = h // 2, h % 2
    kld = [None]

    def f_ldk(E, h=h):
        E.w(self_tok(pr, "tensor"))
        kld[0] = E.dma(kst2[:, :], scrk_d[h, :, :])
    pr.sec("sync", f_ldk)

    def f_stgq(E, h=h):
        E.w(self_tok(pr, "tensor"), kld[0])
        E.t(E.e.tensor_copy(qhst[:, :], qtb[hh * 64:hh * 64 + 64,
                                            ch * TOK:(ch + 1) * TOK]))
    pr.sec("vector", f_stgq)
    stgq_tok = self_tok(pr, "vector")

    xw_t = {}; sq_t = {}; nq_t = {}; max_t = {}; exp_t = {}; eps_t = {}
    kv_t = {}

    def phi_pair(j, src, xwb, nqb, cb, qphase):
        p = j % 2

        def f_xw(E, j=j, p=p):
            E.w(kld[0] if not qphase else stgq_tok, *ctoks)
            E.w(exp_t.get(j - 2))
            if j < 2 and qphase:
                E.w(self_tok(pr, "vector"))
            if j < 2 and not qphase:
                E.w(self_tok(pr, "scalar"))
            for s in range(2):
                i = E.e.matmul(P[xwb + p][:, s * 256:(s + 1) * 256],
                               src[0:64, (j * 2 + s) * 128:(j * 2 + s + 1) * 128],
                               prj[:, h * PM:(h + 1) * PM], start=True, stop=True)
            xw_t[j] = E.t(i)
        pr.sec("tensor", f_xw)

        def f_sq(E, j=j, p=p):
            E.w(kld[0] if not qphase else stgq_tok)
            E.w(nq_t.get(j - 2))
            sq_t[j] = E.t(E.e.activation(sqs[:, p * 256:(p + 1) * 256],
                                         src[0:64, j * 256:(j + 1) * 256], AF.Square))
        pr.sec("scalar", f_sq)

        def f_nq(E, j=j, p=p):
            E.w(sq_t[j], max_t.get(j - 2))
            for s in range(2):
                i = E.e.matmul(P[nqb + p][:, s:s + 1],
                               sqs[:, p * 256 + s * 128:p * 256 + (s + 1) * 128],
                               ones64b[:, :], start=True, stop=True)
            nq_t[j] = E.t(i)
        pr.sec("tensor", f_nq)

        def f_max(E, j=j, p=p):
            E.w(nq_t[j], exp_t.get(j - 2))
            for s in range(2):
                c = cb + p * 6 + s * 3
                t1 = E.t(E.e.reduce_max(bias4[:, c:c + 1],
                                        P[xwb + p][:, s * 256:(s + 1) * 256], axis=AX.X))
                E.w(t1)
                t2 = E.t(E.e.tensor_copy(bias4[:, c + 1:c + 2], P[nqb + p][:, s:s + 1]))
                E.w(t2)
                t3 = E.t(E.e.scalar_tensor_tensor(bias4[:, c + 2:c + 3],
                                                  bias4[:, c + 1:c + 2], -0.5,
                                                  bias4[:, c:c + 1],
                                                  ALU.mult, ALU.subtract))
            max_t[j] = t3
        pr.sec("vector", f_max)

        def f_exp(E, j=j, p=p):
            E.w(max_t[j])
            E.w((kv_t if not qphase else tr_t).get(j - 2))
            for s in range(2):
                c = cb + p * 6 + s * 3
                t = E.t(E.e.activation(phik[:, p * 512 + s * 256:p * 512 + (s + 1) * 256],
                                       P[xwb + p][:, s * 256:(s + 1) * 256], AF.Exp,
                                       bias=bias4[:, c + 2:c + 3]))
            exp_t[j] = t
        pr.sec("scalar", f_exp)

        def f_eps(E, j=j, p=p):
            E.w(exp_t[j])
            eps_t[j] = E.t(E.e.tensor_scalar(phik[:, p * 512:(p + 1) * 512],
                                             phik[:, p * 512:(p + 1) * 512],
                                             1e-6, None, ALU.add))
        pr.sec("vector", f_eps)

    pend = []
    for j in range(16):
        p = j % 2
        phi_pair(j, kst2, 4, 2, 0, False)

        def f_kv(E, j=j, p=p):
            E.w(eps_t[j], *ctoks)
            if j == 0:
                E.w(self_tok(pr, "vector"))
            for s in range(2):
                rt = j * 2 + s
                for mc in range(2):
                    i = E.e.matmul(P[6 + mc][:, 0:65],
                                   phik[:, p * 512 + s * 256 + mc * 128:
                                        p * 512 + s * 256 + (mc + 1) * 128],
                                   vres[:, rt * HEADS * 65 + h * 65: rt * HEADS * 65 + (h + 1) * 65],
                                   start=(j == 0 and s == 0), stop=(j == 15 and s == 1))
            kv_t[j] = E.t(i)
        pend.append(f_kv)
        if j >= 1:
            pr.sec("tensor", pend.pop(0))
    while pend:
        pr.sec("tensor", pend.pop(0))

    def f_kvev(E, h=h):
        E.w(kv_t[15])
        E.w(self_tok(pr, "tensor"))
        t1 = E.t(E.e.tensor_copy(kvsb[:, 0:65], P[6][:, 0:65]))
        E.w(t1)
        E.t(E.e.tensor_copy(kvsb[:, 65:130], P[7][:, 0:65]))
    pr.sec("vector", f_kvev)
    kvev_tok = self_tok(pr, "vector")

    xw_t = {}; sq_t = {}; nq_t = {}; max_t = {}; exp_t = {}; eps_t = {}
    tr_t = {}; trc_t = {}; num_t = {}; den_t = {}; dbc_t = {}; oap_t = {}

    pendq = []
    for j in range(8):
        p = j % 2
        phi_pair(j, qhst, 2, 0, 12, True)

        def f_tr(E, j=j, p=p):
            E.w(eps_t[j], trc_t.get(j - 2))
            for s in range(2):
                for mc in range(2):
                    i = E.e.transpose(
                        P[6].ap().bitcast(BF16)[0:128,
                                                (p * 2 + s) * 256 + mc * 128:
                                                (p * 2 + s) * 256 + (mc + 1) * 128],
                        phik[:, p * 512 + s * 256 + mc * 128:
                             p * 512 + s * 256 + (mc + 1) * 128],
                        pr.sb["eyeb"][:, :])
            tr_t[j] = E.t(i)
        pendq.append(("tensor", f_tr))

        def f_trc(E, j=j, p=p):
            E.w(tr_t[j], num_t.get(j - 2))
            trc_t[j] = E.t(E.e.tensor_copy(
                phiqt[:, p * 512:(p + 1) * 512],
                P[6].ap().bitcast(BF16)[0:128, p * 512:(p + 1) * 512]))
        pendq.append(("vector", f_trc))

        def f_num(E, j=j, p=p):
            E.w(trc_t[j], den_t.get(j - 2), kvev_tok)
            for s in range(2):
                for mc in range(2):
                    i = E.e.matmul(P[7][0:65, (p * 2 + s) * 128:(p * 2 + s + 1) * 128],
                                   kvsb[:, mc * 65:(mc + 1) * 65],
                                   phiqt[:, p * 512 + s * 256 + mc * 128:
                                         p * 512 + s * 256 + (mc + 1) * 128],
                                   start=(mc == 0), stop=(mc == 1))
            num_t[j] = E.t(i)
        pendq.append(("tensor", f_num))

        def f_den(E, j=j, p=p):
            E.w(num_t[j], dbc_t.get(j - 2))
            for s in range(2):
                c = (p * 2 + s) * 128
                t1 = E.t(E.e.tensor_scalar(drow[:, c:c + 128],
                                           P[7][64:65, c:c + 128], 1e-6, None, ALU.add))
                E.w(t1)
                t2 = E.t(E.e.reciprocal(drow[:, c:c + 128], drow[:, c:c + 128]))
                E.w(t2)
                t3 = E.t(E.e.tensor_copy(numsb[:, c:c + 128], P[7][0:64, c:c + 128]))
            den_t[j] = t3
        pendq.append(("vector", f_den))

        def f_dbc(E, j=j, p=p):
            E.w(den_t[j], oap_t.get(j - 2))
            for s in range(2):
                c = (p * 2 + s) * 128
                i = E.e.matmul(P[0 + p][0:64, 128 + s * 128:256 + s * 128],
                               pr.sb["ones1r"][:, 0:64], drow[:, c:c + 128],
                               start=True, stop=True)
            dbc_t[j] = E.t(i)
        pendq.append(("tensor", f_dbc))

        def f_oap(E, j=j, p=p):
            E.w(dbc_t[j])
            for s in range(2):
                c = (p * 2 + s) * 128
                rt = j * 2 + s
                t = E.t(E.e.tensor_mul(oall[hh * 64:hh * 64 + 64,
                                            ch * TOK + rt * 128: ch * TOK + (rt + 1) * 128],
                                       numsb[:, c:c + 128],
                                       P[0 + p][0:64, 128 + s * 128:256 + s * 128]))
            oap_t[j] = t
        pendq.append(("vector", f_oap))
        if True:
            for _ in range(6):
                en, fn = pendq.pop(0)
                pr.sec(en, fn)
    while pendq:
        en, fn = pendq.pop(0)
        pr.sec(en, fn)


def phase_gla(pr, io, ctoks0):
    pr.push_scope()
    ctoks = list(ctoks0)
    nc = pr.nc
    sid = pr._scope_id
    xT_d = io["xT"]
    wqn_d = io["wqn"]; wkn_d = io["wkn"]; wvn_d = io["wvn"]; wan_d = io["wan"]
    bar_d = io["bar"]; gr_d = io["gr"]; br_d = io["br"]
    res_d = pr.dram(f"glares{sid}", [D, TOK], BF16).ap()
    qsh_h = pr.dram(f"glaq{sid}", [GH, 64, NKV], BF16, shared=True)
    ksh_h = pr.dram(f"glak{sid}", [GH, 64, NKV], BF16, shared=True)
    lash_h = pr.dram(f"glala{sid}", [GH, 64, NKV], F32, shared=True)
    ktsh_h = pr.dram(f"glakt{sid}", [GH, NKV, 64], BF16, shared=True)
    vtsh_h = pr.dram(f"glavt{sid}", [GH, NKV, 64], BF16, shared=True)
    osh_h = pr.dram(f"glao{sid}", [GH, 64, NKV], BF16, shared=True)
    qloc = pr.dram(f"glaql{sid}", [GH, 64, TOK], BF16).ap()
    kloc = pr.dram(f"glakl{sid}", [GH, 64, TOK], BF16).ap()
    laloc = pr.dram(f"glalal{sid}", [GH, 64, TOK], F32).ap()
    ktloc = pr.dram(f"glaktl{sid}", [GH, TOK, 64], BF16).ap()
    vtloc = pr.dram(f"glavtl{sid}", [GH, TOK, 64], BF16).ap()
    qun = pr.dram(f"glaqu{sid}", [4, 64, NKV], BF16).ap()
    kun = pr.dram(f"glaku{sid}", [4, 64, NKV], BF16).ap()
    laun = pr.dram(f"glalau{sid}", [4, 64, NKV], F32).ap()
    ktun = pr.dram(f"glaktu{sid}", [4, NKV, 64], BF16).ap()
    vtun = pr.dram(f"glavtu{sid}", [4, NKV, 64], BF16).ap()
    oun = pr.dram(f"glaou{sid}", [4, 64, NKV], BF16).ap()
    oloc = pr.dram(f"glaol{sid}", [GH, 64, TOK], BF16).ap()

    xT = pr.sbuf("sxT", [128, KCH * 512], F32)
    ln1 = pr.sbuf("ln1", [128, KCH * 512], F32)
    ln1b = pr.sbuf("ln1b", [128, KCH * 512], BF16)
    xn = pr.sbuf("xn", [128, KCH * 512], BF16)
    wbuf = pr.sbuf("wbuf", [128, KCH * DI], BF16)
    gam = pr.sbuf("gam", [128, KCH], F32)
    bet = pr.sbuf("bet", [128, KCH], F32)
    bacol = pr.sbuf("bacol", [128, 4], F32)
    qf = pr.sbuf("qf", [128, 4 * 512], BF16)
    kf = pr.sbuf("kf", [128, 4 * 512], BF16)
    vf = pr.sbuf("vf", [128, 4 * 512], BF16)
    af = pr.sbuf("af", [128, 4 * 512], F32)
    ksq = pr.sbuf("ksq", [128, 512], BF16)
    nrow = pr.sbuf("nrow", [1, 2 * 512], F32)
    ktn = pr.sbuf("ktn", [128, 512], BF16)
    alloc_ln(pr, "att")

    def f_i2(E):
        ctoks.append(E.dma(gam[:, :], gr_d))
        ctoks.append(E.dma(bet[:, :], br_d))
        ctoks.append(E.dma(bacol[:, :], bar_d))
    pr.sec("sync", f_i2)

    P = pr.ps
    for r in range(TOK // 512):
        _glaq_tile(pr, r, xT_d, wqn_d, wkn_d, wvn_d, wan_d,
                   res_d, qloc, kloc, laloc, ktloc, vtloc, ctoks)

    pr.fence()

    def f_out(par, dd):
        dd.wait(("dS", pr.cnt["dS"]))
        c0 = par * TOK
        dd(qsh_h.ap()[:, :, c0:c0 + TOK], qloc)
        dd(ksh_h.ap()[:, :, c0:c0 + TOK], kloc)
        dd(lash_h.ap()[:, :, c0:c0 + TOK], laloc)
        dd(ktsh_h.ap()[:, c0:c0 + TOK, :], ktloc)
        dd(vtsh_h.ap()[:, c0:c0 + TOK, :], vtloc)
    pr.pswitch(f_out)
    pr.core_barrier()

    def f_in(par, dd):
        h0 = par * 4
        dd(qun, qsh_h.ap()[h0:h0 + 4, :, :])
        dd(kun, ksh_h.ap()[h0:h0 + 4, :, :])
        dd(laun, lash_h.ap()[h0:h0 + 4, :, :])
        dd(ktun, ktsh_h.ap()[h0:h0 + 4, :, :])
        dd(vtun, vtsh_h.ap()[h0:h0 + 4, :, :])
    gtok = pr.pswitch(f_in)

    pr.pop_scope()
    pr.push_scope()
    _gla_scan_phase(pr, qun, kun, laun, ktun, vtun, oun, gtok, ctoks)
    pr.fence()

    def f_oout(par, dd):
        dd.wait(("dS", pr.cnt["dS"]))
        dd(osh_h.ap()[par * 4:par * 4 + 4, :, :], oun)
    pr.pswitch(f_oout)
    pr.core_barrier()

    def f_oin(par, dd):
        c0 = par * TOK
        dd(oloc, osh_h.ap()[:, :, c0:c0 + TOK])
    gtok2 = pr.pswitch(f_oin)

    pr.pop_scope()
    pr.push_scope()
    xT2 = pr.sbuf("sxT", [128, KCH * 512], F32)
    _gla_out_phase(pr, io, res_d, oloc, gtok2, ctoks)
    pr.fence()
    pr.pop_scope()


def _glaq_tile(pr, r, xT_d, wqn_d, wkn_d, wvn_d, wan_d,
               res_d, qloc, kloc, laloc, ktloc, vtloc, ctoks):
    P = pr.ps
    xT = pr.sb["sxT"]; ln1 = pr.sb["ln1"]; ln1b = pr.sb["ln1b"]; xn = pr.sb["xn"]
    wbuf = pr.sb["wbuf"]; qf = pr.sb["qf"]; kf = pr.sb["kf"]; vf = pr.sb["vf"]
    af = pr.sb["af"]; ksq = pr.sb["ksq"]; nrow = pr.sb["nrow"]; ktn = pr.sb["ktn"]
    pr.fence()
    ld = [None]

    def f_ld(E, r=r):
        src = rearr_kp(xT_d)
        ld[0] = E.dma(sb3(xT[:, :], KCH), src[:, :, r * 512:(r + 1) * 512])
    pr.sec("sync", f_ld)

    l1 = emit_ln(pr, xT, ln1, 512, [ld[0]] + ctoks, "att")
    def f_cp(E, r=r):
        E.w(l1[-1])
        for k in range(KCH):
            E.t(E.e.tensor_copy(ln1b[:, k * 512:(k + 1) * 512], ln1[:, k * 512:(k + 1) * 512]))
    pr.sec("gpsimd", f_cp)

    def f_str(E, r=r):
        E.w(self_tok(pr, "gpsimd"))
        dst = rearr_kp(res_d)
        E.dma(dst[:, :, r * 512:(r + 1) * 512], sb3(ln1b[:, :], KCH))
    pr.sec("sync", f_str)

    l2 = emit_ln(pr, ln1, xn, 512, [l1[-1]], "att",
                 gamma_beta=(pr.sb["gam"], pr.sb["bet"]))

    for (wd, dstb, isa) in ((wqn_d, qf, 0), (wkn_d, kf, 0), (wvn_d, vf, 0), (wan_d, af, 1)):
        wt = [None]

        def f_ldw(E, wd=wd):
            E.w(self_tok(pr, "tensor"))
            wt[0] = E.dma(sb3(wbuf[:, 0:KCH * DI], KCH),
                          wd.rearrange("(k p) n -> p k n", p=128), sem="dG")
        pr.sec("gpsimd", f_ldw)

        for o in range(4):
            pm = [None]

            def f_mm(E, r=r, o=o, wt=wt, pm=pm):
                E.w(wt[0], l2[-1])
                E.w(self_tok(pr, "scalar"), self_tok(pr, "vector"))
                for k in range(KCH):
                    i = E.e.matmul(P[4 + o % 2][:, :],
                                   wbuf[:, k * DI + o * 128: k * DI + o * 128 + 128],
                                   xn[:, k * 512:(k + 1) * 512],
                                   start=(k == 0), stop=(k == KCH - 1))
                pm[0] = E.t(i)
            pr.sec("tensor", f_mm)

            if isa == 0:
                def f_ev(E, o=o, pm=pm, dstb=dstb):
                    E.w(pm[0])
                    E.t(E.e.activation(dstb[:, o * 512:(o + 1) * 512],
                                       P[4 + o % 2][:, :], AF.Copy))
                pr.sec("scalar", f_ev)
            else:
                sg = [None]

                def f_sig(E, o=o, pm=pm, sg=sg):
                    E.w(pm[0])
                    t1 = E.t(E.e.activation(pr.sb["att_tmp"][:, :], P[4 + o % 2][:, :],
                                            AF.Sigmoid, bias=pr.sb["bacol"][:, o:o + 1]))
                    E.w(t1)
                    sg[0] = E.t(E.e.activation(af[:, o * 512:(o + 1) * 512],
                                               pr.sb["att_tmp"][:, :], AF.Ln))
                pr.sec("scalar", f_sig)

                def f_cl(E, o=o, sg=sg):
                    E.w(sg[0])
                    E.t(E.e.tensor_scalar_max(af[:, o * 512:(o + 1) * 512],
                                              af[:, o * 512:(o + 1) * 512], -5.0))
                pr.sec("vector", f_cl)

    for o in range(4):
        sqt = [None]
        nr = [None]

        def f_ksq(E, r=r, o=o, sqt=sqt):
            E.w(self_tok(pr, "tensor"), self_tok(pr, "vector"))
            sqt[0] = E.t(E.e.activation(ksq[:, :], kf[:, o * 512:(o + 1) * 512], AF.Square))
        pr.sec("scalar", f_ksq)

        def f_nmm(E, o=o, sqt=sqt, nr=nr):
            E.w(sqt[0])
            E.w(self_tok(pr, "vector"))
            i = E.e.matmul(P[0][0:1, :], pr.sb["ones128b"][0:64, :], ksq[0:64, :],
                           start=True, stop=True)
            i = E.e.matmul(P[1][0:1, :], pr.sb["ones128b"][64:128, :], ksq[64:128, :],
                           start=True, stop=True)
            nr[0] = E.t(i)
        pr.sec("tensor", f_nmm)

        def f_nr(E, o=o, nr=nr):
            E.w(nr[0])
            t1 = E.t(E.e.tensor_copy(nrow[:, 0:512], P[0][0:1, :]))
            t2 = E.t(E.e.tensor_copy(nrow[:, 512:1024], P[1][0:1, :]))
            E.w(t1, t2)
            nr[0] = t2
        pr.sec("vector", f_nr)

        def f_ns(E, o=o, nr=nr):
            E.w(nr[0])
            nr[0] = E.t(E.e.activation(nrow[:, :], nrow[:, :], AF.Sqrt))
        pr.sec("scalar", f_ns)

        def f_nrec(E, o=o, nr=nr):
            E.w(nr[0])
            t1 = E.t(E.e.tensor_scalar_max(nrow[:, :], nrow[:, :], 1e-12))
            E.w(t1)
            t2 = E.t(E.e.reciprocal(nrow[:, 0:512], nrow[:, 0:512]))
            E.w(t2)
            t3 = E.t(E.e.reciprocal(nrow[:, 512:1024], nrow[:, 512:1024]))
            nr[0] = t3
        pr.sec("vector", f_nrec)

        def f_nbc(E, o=o, nr=nr):
            E.w(nr[0])
            E.e.matmul(P[2][0:64, :], pr.sb["ones1r"][:, 0:64], nrow[:, 0:512],
                       start=True, stop=True)
            i = E.e.matmul(P[2][64:128, :], pr.sb["ones1r"][:, 0:64], nrow[:, 512:1024],
                           start=True, stop=True)
            nr[0] = E.t(i)
        pr.sec("tensor", f_nbc)

        def f_nap(E, o=o, nr=nr):
            E.w(nr[0])
            E.t(E.e.tensor_mul(kf[:, o * 512:(o + 1) * 512],
                               kf[:, o * 512:(o + 1) * 512], P[2][:, :]))
        pr.sec("vector", f_nap)

    pr.fence()

    def f_stq(E, r=r):
        for o in range(4):
            for hh in range(2):
                h = o * 2 + hh
                E.dma(qloc[h, :, r * 512:(r + 1) * 512],
                      qf[hh * 64:(hh + 1) * 64, o * 512:(o + 1) * 512])
                E.dma(kloc[h, :, r * 512:(r + 1) * 512],
                      kf[hh * 64:(hh + 1) * 64, o * 512:(o + 1) * 512])
                E.dma(laloc[h, :, r * 512:(r + 1) * 512],
                      af[hh * 64:(hh + 1) * 64, o * 512:(o + 1) * 512])
    pr.sec("sync", f_stq)

    for src_b, dst_h in ((pr.sb["kf"], ktloc), (pr.sb["vf"], vtloc)):
        for o in range(4):
            for ts in range(4):
                tt = [None]

                def f_tr(E, o=o, ts=ts, src_b=src_b, tt=tt):
                    E.w(self_tok(pr, "vector"), self_tok(pr, "sync"))
                    E.w(("dS", pr.cnt["dS"]))
                    i = E.e.transpose(P[3].ap().bitcast(BF16)[0:128, 0:128],
                                      src_b[:, o * 512 + ts * 128: o * 512 + (ts + 1) * 128],
                                      pr.sb["eyeb"][:, :])
                    tt[0] = E.t(i)
                pr.sec("tensor", f_tr)

                def f_trc(E, o=o, ts=ts, tt=tt):
                    E.w(tt[0])
                    tt[0] = E.t(E.e.tensor_copy(ktn[:, 0:128],
                                                P[3].ap().bitcast(BF16)[0:128, 0:128]))
                pr.sec("vector", f_trc)

                def f_st2(E, r=r, o=o, ts=ts, dst_h=dst_h, tt=tt):
                    E.w(tt[0])
                    for hh in range(2):
                        h = o * 2 + hh
                        E.dma(dst_h[h, r * 512 + ts * 128: r * 512 + (ts + 1) * 128, :],
                              ktn[:, hh * 64:(hh + 1) * 64])
                pr.sec("sync", f_st2)


def _gla_scan_phase(pr, qun, kun, laun, ktun, vtun, oun, gtok, ctoks0):
    nc = pr.nc
    ctoks = list(ctoks0) + [gtok]

    bd = np.zeros((128, 128), dtype=np.float32)
    for s in range(128):
        sub = s // 16
        bd[s, s:(sub + 1) * 16] = 1.0
    tri_c = pr.const(bd, "bd128_c")

    qT4 = pr.sbuf("qT4", [64, 4 * NKV], BF16)
    kT4 = pr.sbuf("kT4", [64, 4 * NKV], BF16)
    kt4 = pr.sbuf("kt4", [128, 4 * 32 * 64], BF16)
    vt4 = pr.sbuf("vt4", [128, 4 * 32 * 64], BF16)
    oacc = pr.sbuf("oacc", [64, 4 * NKV], BF16)
    lab = pr.sbuf("lab", [64, 2 * 4 * 128], F32)
    LS = pr.sbuf("LS", [64, 4 * 128], F32)
    nLS = pr.sbuf("nLS", [64, 4 * 128], F32)
    reb = pr.sbuf("reb", [64, 4 * 128], F32)
    dif = pr.sbuf("dif", [64, 4 * 128], F32)
    BT = pr.sbuf("BT", [64, 2 * 4 * 128], F32)
    EIV = pr.sbuf("EIV", [64, 2 * 4 * 128], F32)
    qb4 = pr.sbuf("qb4", [64, 2 * 4 * 128], BF16)
    kiv4 = pr.sbuf("kiv4", [64, 2 * 4 * 128], BF16)
    eivt = pr.sbuf("eivt", [128, 4 * 64], BF16)
    kivt = pr.sbuf("kivt", [128, 2 * 4 * 64], BF16)
    S4 = pr.sbuf("S4", [64, 4 * 9 * 64], BF16)
    St = pr.sbuf("St", [64, 4 * 64], BF16)
    PTm = pr.sbuf("PTm", [128, 4 * 128], BF16)
    tri = pr.sbuf("tri16", [128, 128], BF16)
    zc = pr.sbuf("zc", [64, 1], F32)
    subm = pr.sbuf("subm", [128, 8], F32)
    kms = pr.sbuf("kms", [128, 4 * 64], BF16)

    sm = np.zeros((128, 8), dtype=np.float32)
    for s in range(8):
        sm[s * 16:(s + 1) * 16, s] = 1.0
    sm_c = pr.const(sm, "subm_c")

    def f_i(E):
        ctoks.append(E.dma(tri[:, :], tri_c.ap(), sem="dG"))
        ctoks.append(E.dma(subm[:, :], sm_c.ap(), sem="dG"))
        ctoks.append(E.t(E.e.memset(zc[:, :], 0.0)))
        ctoks.append(E.t(E.e.memset(pr.sb["S4"][:, :], 0.0)))
    pr.sec("gpsimd", f_i)

    ldt = [None]
    SLOT = 4 * 64 * NKV
    SLOTV = 4 * NKV * 64

    def f_ld0(E):
        E.w(gtok)
        for u in range(4):
            E.dma(qT4[:, u * NKV:(u + 1) * NKV], qun[u, :, :])
            E.dma(kT4[:, u * NKV:(u + 1) * NKV], kun[u, :, :])
            E.dma(sb3(kt4[:, u * 2048:(u + 1) * 2048], 32),
                  ktun[u, :, :].rearrange("(r p) d -> p r d", p=128))
            ldt[0] = E.dma(sb3(vt4[:, u * 2048:(u + 1) * 2048], 32),
                           vtun[u, :, :].rearrange("(r p) d -> p r d", p=128))
    pr.sec("sync", f_ld0)
    ctoks.append(ldt[0])

    P = pr.ps
    la_tok = [None, None]

    def f_lal(E, j=0):
        E.w(gtok)
        for u in range(4):
            la_tok[0] = E.dma(lab[:, u * 128:(u + 1) * 128], laun[u, :, 0:128])
    pr.sec("sync", f_lal)

    tk = {}
    for j in range(32):
        _gla_block(pr, j, laun, la_tok, ctoks, tk)

    pr.fence()

    def f_st(E):
        for u in range(4):
            E.dma(oun[u, :, :], oacc[:, u * NKV:(u + 1) * NKV])
    pr.sec("sync", f_st)


def _gla_block(pr, j, laun, la_tok, ctoks, tk):
    P = pr.ps
    sb = pr.sb
    buf = j % 2
    nbuf = (j + 1) % 2
    lab = sb["lab"]; LS = sb["LS"]; nLS = sb["nLS"]; reb = sb["reb"]; dif = sb["dif"]
    BT = sb["BT"]; EIV = sb["EIV"]; qb4 = sb["qb4"]; kiv4 = sb["kiv4"]
    eivt = sb["eivt"]; kivt = sb["kivt"]; S4 = sb["S4"]; St = sb["St"]
    PTm = sb["PTm"]; tri = sb["tri16"]; zc = sb["zc"]; subm = sb["subm"]; kms = sb["kms"]
    qT4 = sb["qT4"]; kT4 = sb["kT4"]; kt4 = sb["kt4"]; vt4 = sb["vt4"]; oacc = sb["oacc"]
    lslice = lambda u: slice(buf * 512 + u * 128, buf * 512 + (u + 1) * 128)

    if j < 31:
        def f_lal(E, j=j):
            E.w(tk.get(("scan", j - 1)))
            for u in range(4):
                la_tok[0] = E.dma(lab[:, nbuf * 512 + u * 128: nbuf * 512 + (u + 1) * 128],
                                  laun[u, :, (j + 1) * 128:(j + 2) * 128])
        pr.sec("sync", f_lal)

    cu = [None]

    def f_scan(E, j=j, cu=cu):
        E.w(("dS", pr.cnt["dS"]), *ctoks)
        E.w(tk.get(("exp", j - 1)))
        t = None
        for u in range(4):
            t = E.t(E.e.tensor_tensor_scan(LS[:, u * 128:(u + 1) * 128],
                                           lab[:, lslice(u)], lab[:, lslice(u)],
                                           0.0, ALU.add, ALU.bypass))
        E.w(t)
        tz = None
        for u in range(4):
            for s in range(8):
                srccol = zc[:, 0:1] if s == 0 else LS[:, u * 128 + s * 16 - 1: u * 128 + s * 16]
                tz = E.t(E.e.tensor_copy(
                    reb[:, u * 128 + s * 16: u * 128 + (s + 1) * 16],
                    srccol.to_broadcast([64, 16])))
            E.w(tz)
        t3 = E.t(E.e.tensor_sub(dif[:, :], LS[:, :], reb[:, :]))
        E.w(t3)
        cu[0] = t3
        tk[("scan", j)] = t3
    pr.sec("vector", f_scan)

    ex = [None, None]

    def f_exp(E, j=j, cu=cu, ex=ex):
        E.w(cu[0])
        E.w(tk.get(("suplast", j - 2)))
        ex[0] = E.t(E.e.activation(BT[:, buf * 512:(buf + 1) * 512], dif[:, :], AF.Exp))
        ex[1] = E.t(E.e.activation(EIV[:, buf * 512:(buf + 1) * 512], dif[:, :],
                                   AF.Exp, scale=-1.0))
        tk[("exp", j)] = ex[1]
    pr.sec("scalar", f_exp)

    mu = [None]

    def f_mul2(E, j=j, ex=ex, mu=mu):
        E.w(ex[0], ex[1])
        E.w(tk.get(("pelast", j - 2)))
        t = None
        for u in range(4):
            t = E.t(E.e.tensor_mul(qb4[:, buf * 512 + u * 128: buf * 512 + (u + 1) * 128],
                                   qT4[:, u * NKV + j * 128: u * NKV + (j + 1) * 128],
                                   BT[:, buf * 512 + u * 128: buf * 512 + (u + 1) * 128]))
            t = E.t(E.e.tensor_mul(kiv4[:, buf * 512 + u * 128: buf * 512 + (u + 1) * 128],
                                   kT4[:, u * NKV + j * 128: u * NKV + (j + 1) * 128],
                                   EIV[:, buf * 512 + u * 128: buf * 512 + (u + 1) * 128]))
        mu[0] = t
    pr.sec("vector", f_mul2)

    ett = [None]

    def f_etr(E, j=j, mu=mu, ett=ett):
        E.w(mu[0], tk.get(("etc", j - 1)))
        for u in range(4):
            i = E.e.transpose(P[0][:, u * 64:(u + 1) * 64],
                              EIV[:, buf * 512 + u * 128: buf * 512 + (u + 1) * 128],
                              pr.sb["eye"][0:64, 0:64])
        ett[0] = E.t(i)
    pr.sec("tensor", f_etr)

    def f_etc(E, j=j, ett=ett):
        E.w(ett[0])
        t = None
        for u in range(4):
            t = E.t(E.e.tensor_copy(eivt[:, u * 64:(u + 1) * 64],
                                    P[0][:, u * 64:(u + 1) * 64]))
        E.w(t)
        t2 = None
        for u in range(4):
            t2 = E.t(E.e.tensor_mul(kivt[:, buf * 256 + u * 64: buf * 256 + (u + 1) * 64],
                                    kt4[:, u * 2048 + j * 64: u * 2048 + (j + 1) * 64],
                                    eivt[:, u * 64:(u + 1) * 64]))
        ett[0] = t2
        tk[("etc", j)] = t2
    pr.sec("vector", f_etc)

    for u in range(4):
        pt = [None]

        def f_pt(E, j=j, u=u, pt=pt, ett=ett):
            E.w(ett[0], self_tok(pr, "vector"))
            i = E.e.matmul(P[1][:, u * 128:(u + 1) * 128],
                           kiv4[:, buf * 512 + u * 128: buf * 512 + (u + 1) * 128],
                           qb4[:, buf * 512 + u * 128: buf * 512 + (u + 1) * 128],
                           start=True, stop=True)
            pt[0] = E.t(i)
        pr.sec("tensor", f_pt)

        def f_ptm(E, j=j, u=u, pt=pt):
            E.w(pt[0])
            pt[0] = E.t(E.e.tensor_mul(PTm[:, u * 128:(u + 1) * 128], P[1][:, u * 128:(u + 1) * 128], tri[:, :]))
        pr.sec("vector", f_ptm)

        pv = [None]

        def f_pv(E, j=j, u=u, pt=pt, pv=pv):
            E.w(pt[0])
            i = E.e.matmul(P[4 + u][0:64, 0:128],
                           vt4[:, u * 2048 + j * 64: u * 2048 + (j + 1) * 64],
                           PTm[:, u * 128:(u + 1) * 128], start=True, stop=False, skip_group_check=True)
            pv[0] = E.t(i)
        pr.sec("tensor", f_pv)

        for s in range(8):
            sup = [None]

            def f_int(E, j=j, u=u, s=s, pv=pv):
                E.w(self_tok(pr, "vector"))
                E.e.matmul(P[4 + u][0:64, s * 16:(s + 1) * 16],
                           S4[:, u * 64:(u + 1) * 64],
                           qb4[:, buf * 512 + u * 128 + s * 16: buf * 512 + u * 128 + (s + 1) * 16],
                           start=False, stop=(s == 7), skip_group_check=True)
            pr.sec("tensor", f_int)

            def f_kms(E, j=j, u=u, s=s, sup=sup):
                E.w(self_tok(pr, "tensor"))
                sup[0] = E.t(E.e.tensor_scalar_mul(kms[:, u * 64:(u + 1) * 64],
                                                   kivt[:, buf * 256 + u * 64: buf * 256 + (u + 1) * 64],
                                                   subm[:, s:s + 1]))
            pr.sec("vector", f_kms)

            um = [None]

            def f_umm(E, j=j, u=u, s=s, sup=sup, um=um):
                E.w(sup[0])
                i = E.e.matmul(P[3][0:64, u * 64:(u + 1) * 64], kms[:, u * 64:(u + 1) * 64],
                               vt4[:, u * 2048 + j * 64: u * 2048 + (j + 1) * 64],
                               start=True, stop=True)
                um[0] = E.t(i)
            pr.sec("tensor", f_umm)

            def f_sup(E, j=j, u=u, s=s, um=um):
                E.w(um[0])
                t1 = E.t(E.e.tensor_add(St[:, u * 64:(u + 1) * 64],
                                        S4[:, u * 64:(u + 1) * 64], P[3][0:64, u * 64:(u + 1) * 64]))
                E.w(t1)
                E.t(E.e.tensor_scalar_mul(
                    S4[:, u * 64:(u + 1) * 64], St[:, u * 64:(u + 1) * 64],
                    BT[:, buf * 512 + u * 128 + (s + 1) * 16 - 1: buf * 512 + u * 128 + (s + 1) * 16]))
            pr.sec("vector", f_sup)

        def f_oev(E, j=j, u=u):
            E.w(self_tok(pr, "tensor"))
            E.t(E.e.tensor_copy(oacc[:, u * NKV + j * 128: u * NKV + (j + 1) * 128],
                                P[4 + u][0:64, 0:128]))
        pr.sec("vector", f_oev)



def _gla_out_phase(pr, io, res_d, oloc, gtok2, ctoks):
    xT_d = io["xT"]
    won_d = io["won"]
    yT_d = io["yT"]

    xT = pr.sb["sxT"]
    resb = pr.sbuf("resb", [128, KCH * 512], BF16)
    ot = pr.sbuf("ot", [128, 4 * TOK], BF16)
    wbuf2 = pr.sbuf("wbuf2", [128, 4 * D], BF16)
    outsb = pr.sbuf("goutsb", [128, 512], F32)
    P = pr.ps

    wt = [None]
    olt = [None]

    def f_ldw(E):
        wt[0] = E.dma(sb3(wbuf2[:, :], 4), won_d.rearrange("(k p) n -> p k n", p=128), sem="dG")
    pr.sec("gpsimd", f_ldw)

    def f_ldo(E):
        E.w(gtok2)
        olt[0] = E.dma(ot[:, :].rearrange("p (k t) -> p k t", k=4),
                       oloc.rearrange("(k hh) d t -> (hh d) k t", k=4))
    pr.sec("sync", f_ldo)

    for g in range(4):
        xg = [None]

        def f_ldx(E, g=g):
            E.w(("dS", pr.cnt["dS"]))
            E.dma(sb3(xT[:, :], KCH), rearr_kp(xT_d)[:, :, g * 512:(g + 1) * 512])
            xg[0] = E.dma(sb3(resb[:, :], KCH), rearr_kp(res_d)[:, :, g * 512:(g + 1) * 512])
        pr.sec("sync", f_ldx)

        for o in range(KCH):
            om = [None]

            def f_omm(E, g=g, o=o, om=om):
                E.w(wt[0], olt[0], self_tok(pr, "vector"))
                for k in range(4):
                    i = E.e.matmul(P[4 + o % 2][:, :],
                                   wbuf2[:, k * D + o * 128: k * D + o * 128 + 128],
                                   ot[:, k * TOK + g * 512: k * TOK + (g + 1) * 512],
                                   start=(k == 0), stop=(k == 3))
                om[0] = E.t(i)
            pr.sec("tensor", f_omm)

            def f_oadd(E, g=g, o=o, om=om):
                E.w(om[0], xg[0])
                E.w(("dS", pr.cnt["dS"]))
                t1 = E.t(E.e.tensor_tensor(outsb[:, :], xT[:, o * 512:(o + 1) * 512],
                                           P[4 + o % 2][:, :], ALU.add))
                E.w(t1)
                E.t(E.e.tensor_tensor(outsb[:, :], outsb[:, :],
                                      resb[:, o * 512:(o + 1) * 512], ALU.add))
            pr.sec("vector", f_oadd)

            def f_ost(E, g=g, o=o):
                E.w(self_tok(pr, "vector"))
                E.dma(rearr_kp(yT_d)[:, o, g * 512:(g + 1) * 512], outsb[:, :])
            pr.sec("sync", f_ost)


def _colchunks(v):
    return np.ascontiguousarray(v.reshape(8, 128).T)



def alloc_quant(pr, W, slots=1):
    pr.sbuf("qa", [128, slots * W], F32)
    pr.sbuf("qA", [128, slots * W], F32)
    pr.sbuf("qB", [128, slots * W], F32)
    pr.sbuf("qm", [128, slots * W], F32)
    pr.sbuf("qs", [128, slots * (W // 16)], F32)
    pr.sbuf("qr", [128, slots * (W // 16)], F32)


def emit_quant(pr, src_fp, dst_bf, W, deps, slot=0, W0=None, precise=False):
    W0 = W0 or W
    b = slot * W0
    nb = slot * (W0 // 16)
    qa = pr.sb["qa"][:, b:b + W0]; qA = pr.sb["qA"][:, b:b + W0]
    qB = pr.sb["qB"][:, b:b + W0]; qm = pr.sb["qm"][:, b:b + W0]
    qs = pr.sb["qs"][:, nb:nb + W0 // 16]; qr = pr.sb["qr"][:, nb:nb + W0 // 16]
    NB = W // 16
    out_tok = [None]
    st = [None]

    def f_abs(E):
        E.w(*deps)
        if not precise:
            E.w(self_tok(pr, "vector"))
        st[0] = E.t(E.e.activation(qa[:, 0:W], src_fp, AF.Abs))
    pr.sec("scalar", f_abs)

    def f_max(E):
        E.w(st[0])
        t = E.t(E.e.tensor_max(qm[:, 0:W // 2], qa[:, 0:W:2], qa[:, 1:W:2]))
        E.w(t)
        t = E.t(E.e.tensor_max(qm[:, W // 2:W // 2 + W // 4],
                               qm[:, 0:W // 2:2], qm[:, 1:W // 2:2]))
        E.w(t)
        t = E.t(E.e.tensor_max(qm[:, 0:W // 8],
                               qm[:, W // 2:W // 2 + W // 4:2],
                               qm[:, W // 2 + 1:W // 2 + W // 4:2]))
        E.w(t)
        t = E.t(E.e.tensor_max(qm[:, W // 2:W // 2 + NB],
                               qm[:, 0:W // 8:2], qm[:, 1:W // 8:2]))
        E.w(t)
        t1 = E.t(E.e.tensor_scalar(qs[:, 0:NB], qm[:, W // 2:W // 2 + NB],
                                   1.0 / 6.0, 1e-12, ALU.mult, ALU.max))
        E.w(t1)
        t2 = E.t(E.e.reciprocal(qr[:, 0:NB], qs[:, 0:NB]))
        E.w(t2)
        t3 = E.t(E.e.tensor_mul(
            qa[:, 0:W].rearrange("p (b k) -> p b k", k=16),
            src_fp.rearrange("p (b k) -> p b k", k=16),
            qr[:, 0:NB].unsqueeze(2).to_broadcast([128, NB, 16])))
        st[0] = t3
    pr.sec("vector", f_max)

    def f_t(E):
        E.w(st[0])
        st[0] = E.t(E.e.activation(qm[:, 0:W], qa[:, 0:W], AF.Abs))
    pr.sec("scalar", f_t)

    def f_lv(E):
        t_ = qm
        E.w(st[0])
        t = E.t(E.e.tensor_single_scalar(qB[:, 0:W], t_[:, 0:W], 5.0, ALU.is_gt))
        E.w(t)
        t = E.t(E.e.tensor_scalar_mul(qB[:, 0:W], qB[:, 0:W], 2.0))
        E.w(t)
        for thr in (2.5, 3.5):
            t = E.t(E.e.scalar_tensor_tensor(qB[:, 0:W], t_[:, 0:W], thr,
                                             qB[:, 0:W], ALU.is_gt, ALU.add))
            E.w(t)
        t = E.t(E.e.tensor_single_scalar(qA[:, 0:W], t_[:, 0:W], 0.25, ALU.is_gt))
        E.w(t)
        for thr in (0.75, 1.25, 1.75):
            t = E.t(E.e.scalar_tensor_tensor(qA[:, 0:W], t_[:, 0:W], thr,
                                             qA[:, 0:W], ALU.is_gt, ALU.add))
            E.w(t)
        t = E.t(E.e.scalar_tensor_tensor(qA[:, 0:W], qA[:, 0:W], 0.5,
                                         qB[:, 0:W], ALU.mult, ALU.add))
        st[0] = t
    pr.sec("vector", f_lv)

    def f_sg(E):
        E.w(st[0])
        st[0] = E.t(E.e.sign(qm[:, 0:W], qa[:, 0:W]))
    pr.sec("scalar", f_sg)

    def f_fin(E):
        E.w(st[0])
        t1 = E.t(E.e.tensor_mul(qA[:, 0:W], qA[:, 0:W], qm[:, 0:W]))
        E.w(t1)
        t2 = E.t(E.e.tensor_mul(
            dst_bf.rearrange("p (b k) -> p b k", k=16),
            qA[:, 0:W].rearrange("p (b k) -> p b k", k=16),
            qs[:, 0:NB].unsqueeze(2).to_broadcast([128, NB, 16])))
        out_tok[0] = t2
    pr.sec("vector", f_fin)
    return out_tok[0]


def phase_l1wq(pr, io, ctoks0):
    ctoks = list(ctoks0)
    pr.push_scope()
    sid = pr._scope_id
    wg_h = io["l1wg_h"]; wu_h = io["l1wu_h"]; wd_h = io["l1wd_h"]
    wgq_h = pr.dram(f"wgq{sid}", [FI, D], BF16, shared=True)
    wuq_h = pr.dram(f"wuq{sid}", [FI, D], BF16, shared=True)
    wdq_h = pr.dram(f"wdq{sid}", [D, FI], BF16, shared=True)
    wgT_l = pr.dram(f"wgTl{sid}", [D, FI], BF16)
    wuT_l = pr.dram(f"wuTl{sid}", [D, FI], BF16)
    wdT_l = pr.dram(f"wdTl{sid}", [FI, D], BF16)

    wt = pr.sbuf("wt", [128, FI], F32)
    ob = pr.sbuf("ob", [128, FI], BF16)
    alloc_quant(pr, FI)
    hloc = pr.dram(f"wqhl{sid}", [1408, D], F32).ap()
    hlocd = pr.dram(f"wqhld{sid}", [512, FI], F32).ap()
    oloc = pr.dram(f"wqol{sid}", [1408, D], BF16).ap()
    olocd = pr.dram(f"wqold{sid}", [512, FI], BF16).ap()

    for (in_h, out_sh, nrows, W, lin, lout) in (
            (io["l1wg"], wgq_h, FI, D, hloc, oloc),
            (io["l1wu"], wuq_h, FI, D, hloc, oloc),
            (io["l1wd"], wdq_h, D, FI, hlocd, olocd)):
        half = nrows // 2

        def f_gather(par, dd, in_h=in_h, lin=lin, half=half, W=W):
            dd.wait(("dS", pr.cnt["dS"]), ("dG", pr.cnt["dG"]))
            dd(lin[0:half, :], in_h[par * half:(par + 1) * half, :])
        pr.pswitch(f_gather)

        for r0 in range(0, half, 128):
            ld = [None]

            def f_ld(E, lin=lin, r0=r0, W=W):
                E.w(("vector", pr.cnt["vector"]), ("scalar", pr.cnt["scalar"]),
                    ("dS", pr.cnt["dS"]))
                ld[0] = E.dma(wt[:, 0:W], lin[r0:r0 + 128, :])
            pr.sec("sync", f_ld)

            tok = emit_quant(pr, wt[:, 0:W], ob[:, 0:W], W,
                             [ld[0], ("dS", pr.cnt["dS"])])

            def f_st(E, lout=lout, r0=r0, W=W, tok=tok):
                E.w(tok)
                E.dma(lout[r0:r0 + 128, :], ob[:, 0:W])
            pr.sec("sync", f_st)

        pr.fence()

        def f_scatter(par, dd, out_sh=out_sh, lout=lout, half=half):
            dd.wait(("dS", pr.cnt["dS"]))
            dd(out_sh.ap()[par * half:(par + 1) * half, :], lout[0:half, :])
        pr.pswitch(f_scatter)

    pr.core_barrier()

    tb = pr.sbuf("tb", [128, FI], BF16)
    for (q_h, t_l, R, W) in ((wgq_h, wgT_l, FI, D), (wuq_h, wuT_l, FI, D),
                             (wdq_h, wdT_l, D, FI)):
        for k in range(W // 128):
            def f_tc(E, q_h=q_h, t_l=t_l, R=R, W=W, k=k):
                E.w(("dS", pr.cnt["dS"]))
                E.dma(tb[:, 0:R], q_h.ap()[0:R, k * 128:(k + 1) * 128], transpose=True)
                E.w(("dS", pr.cnt["dS"]))
                E.dma(t_l.ap()[k * 128:(k + 1) * 128, :], tb[:, 0:R])
            pr.sec("sync", f_tc)
    pr.fence()
    pr.pop_scope()
    return {"wgT": wgT_l.ap(), "wuT": wuT_l.ap(), "wdT": wdT_l.ap()}


FSUB = 512
FI = 2816
FSL = (512, 512, 512, 512, 512, 256)


def phase_ffn1(pr, io, wq3, ctoks0):
    pr.push_scope()
    ctoks = list(ctoks0)
    xT_d = io["xT"]
    wgT_d = wq3["wgT"]; wuT_d = wq3["wuT"]; wdT_d = wq3["wdT"]
    yT_d = io["yT"]

    pr.sbuf("sxT", [128, KCH * FSUB], F32)
    pr.sbuf("lnf", [128, KCH * FSUB], F32)
    pr.sbuf("wbuf", [128, 22 * D], BF16)
    pr.sbuf("xqT", [128, KCH * FSUB], BF16)
    pr.sbuf("hqT", [128, 22 * FSUB], BF16)
    pr.sbuf("gsl", [128, 2 * 512], BF16)
    pr.sbuf("hsl", [128, 2 * 512], F32)
    pr.sbuf("hqsl", [128, 2 * 512], BF16)
    pr.sbuf("xtok", [128, 2 * D], F32)
    pr.sbuf("xq", [128, 2 * D], BF16)
    pr.sbuf("outsb", [128, 512], F32)
    alloc_quant(pr, D, slots=2)
    alloc_ln(pr, "att")

    for sub in range(TOK // FSUB):
        _ffn1_sub(pr, sub, xT_d, wgT_d, wuT_d, wdT_d, yT_d, ctoks)
    pr.fence()
    pr.pop_scope()


def _ffn1_sub(pr, sub, xT_d, wgT_d, wuT_d, wdT_d, yT_d, ctoks):
    P = pr.ps
    sb = pr.sb
    xT = sb["sxT"]; lnf = sb["lnf"]; wbuf = sb["wbuf"]; xqT = sb["xqT"]
    hqT = sb["hqT"]; gsl = sb["gsl"]; hsl = sb["hsl"]; hqsl = sb["hqsl"]
    xtok = sb["xtok"]; xq = sb["xq"]; outsb = sb["outsb"]
    tok0 = sub * FSUB
    pr.fence()
    ld = [None]

    def f_ld(E, sub=sub):
        ld[0] = E.dma(sb3(xT[:, :], KCH), rearr_kp(xT_d)[:, :, tok0:tok0 + FSUB])
    pr.sec("sync", f_ld)

    ln_toks = emit_ln(pr, xT, lnf, FSUB, [ld[0]] + ctoks, "att")

    xtr_t = {}; xcp_t = {}; qx_t = {}; xtr2_t = {}; xcp2_t = {}
    for ts in range(FSUB // 128):
        s = ts % 2

        def f_xtr(E, ts=ts, s=s):
            E.w(ln_toks[-1], xcp_t.get(ts - 1))
            for k in range(KCH):
                i = E.e.transpose(P[4 + k // 4][:, (k % 4) * 128:(k % 4 + 1) * 128],
                                  lnf[:, k * FSUB + ts * 128: k * FSUB + (ts + 1) * 128],
                                  pr.sb["eye"][:, :])
            xtr_t[ts] = E.t(i)
        pr.sec("tensor", f_xtr)

        def f_xcp(E, ts=ts, s=s):
            E.w(xtr_t[ts], qx_t.get(ts - 2))
            t = E.t(E.e.tensor_copy(xtok[:, s * D:s * D + 512], P[4][:, :]))
            E.w(t)
            t = E.t(E.e.tensor_copy(xtok[:, s * D + 512:s * D + 1024], P[5][:, :]))
            xcp_t[ts] = t
        pr.sec("vector", f_xcp)

        qx_t[ts] = emit_quant(pr, xtok[:, s * D:(s + 1) * D], xq[:, s * D:(s + 1) * D], D,
                              [xcp_t[ts], qx_t.get(ts - 2), xtr2_t.get(ts - 2)],
                              slot=s, W0=D, precise=True)

        def f_xtr2(E, ts=ts, s=s):
            E.w(qx_t[ts], xcp2_t.get(ts - 1))
            for k in range(KCH):
                i = E.e.transpose(P[6].ap().bitcast(BF16)[0:128, k * 128:(k + 1) * 128],
                                  xq[:, s * D + k * 128:s * D + (k + 1) * 128],
                                  pr.sb["eyeb"][:, :])
            xtr2_t[ts] = E.t(i)
        pr.sec("tensor", f_xtr2)

        def f_xcp2(E, ts=ts, s=s):
            E.w(xtr2_t[ts])
            xcp2_t[ts] = E.t(E.e.tensor_copy(
                xqT[:, :].rearrange("p (k f) -> p k f", f=FSUB)[:, :, ts * 128:(ts + 1) * 128],
                P[6].ap().bitcast(BF16)[0:128, 0:KCH * 128].rearrange("p (k f) -> p k f", f=128)))
        pr.sec("vector", f_xcp2)

    pr.fence()

    gm_t = {}; si_t = {}; um_t = {}; h_t = {}; q_t = {}; tr_t = {}; cp_t = {}
    wlt = {}
    off = 0
    for osl, w in enumerate(FSL):
        o0 = off
        off += w
        ws = osl % 2

        def f_ldw(E, osl=osl, o0=o0, w=w, ws=ws):
            E.w(um_t.get((osl - 2) * 4 + 3))
            E.dma(sb3(wbuf[:, ws * 8192:ws * 8192 + KCH * 512], KCH)[:, :, 0:w],
                  rearr_kp(wgT_d)[:, :, o0:o0 + w])
            wlt[osl] = E.dma(sb3(wbuf[:, ws * 8192 + KCH * 512:ws * 8192 + 2 * KCH * 512], KCH)[:, :, 0:w],
                             rearr_kp(wuT_d)[:, :, o0:o0 + w])
        pr.sec("gpsimd", f_ldw)

        for ts in range(FSUB // 128):
            i = osl * 4 + ts
            s = i % 2

            def f_gmm(E, osl=osl, ts=ts, w=w, i=i, ws=ws):
                E.w(wlt[osl], si_t.get(i - 1))
                for k in range(KCH):
                    t = E.e.matmul(P[4][:, 0:w],
                                   xqT[:, k * FSUB + ts * 128: k * FSUB + (ts + 1) * 128],
                                   wbuf[:, ws * 8192 + k * 512:ws * 8192 + k * 512 + w],
                                   start=(k == 0), stop=(k == KCH - 1))
                gm_t[i] = E.t(t)
            pr.sec("tensor", f_gmm)

            def f_silu(E, w=w, i=i, s=s):
                E.w(gm_t[i], h_t.get(i - 2))
                si_t[i] = E.t(E.e.activation(gsl[:, s * 512:s * 512 + w], P[4][:, 0:w], AF.Silu))
            pr.sec("scalar", f_silu)

            def f_umm(E, ts=ts, w=w, i=i, ws=ws):
                E.w(wlt[osl], h_t.get(i - 1))
                for k in range(KCH):
                    t = E.e.matmul(P[5][:, 0:w],
                                   xqT[:, k * FSUB + ts * 128: k * FSUB + (ts + 1) * 128],
                                   wbuf[:, ws * 8192 + KCH * 512 + k * 512: ws * 8192 + KCH * 512 + k * 512 + w],
                                   start=(k == 0), stop=(k == KCH - 1))
                um_t[i] = E.t(t)
            pr.sec("tensor", f_umm)

            def f_h(E, w=w, i=i, s=s):
                E.w(um_t[i], si_t[i], q_t.get(i - 2))
                h_t[i] = E.t(E.e.tensor_mul(hsl[:, s * 512:s * 512 + w],
                                            gsl[:, s * 512:s * 512 + w], P[5][:, 0:w]))
            pr.sec("vector", f_h)

            q_t[i] = emit_quant(pr, hsl[:, s * 512:s * 512 + w], hqsl[:, s * 512:s * 512 + w], w,
                                [h_t[i], q_t.get(i - 2), tr_t.get(i - 2)],
                                slot=s, W0=512, precise=True)

            nk = w // 128

            def f_tr(E, ts=ts, w=w, i=i, s=s, nk=nk):
                E.w(q_t[i], cp_t.get(i - 1))
                for k in range(nk):
                    t = E.e.transpose(P[6].ap().bitcast(BF16)[0:128, k * 128:(k + 1) * 128],
                                      hqsl[:, s * 512 + k * 128:s * 512 + (k + 1) * 128],
                                      pr.sb["eyeb"][:, :])
                tr_t[i] = E.t(t)
            pr.sec("tensor", f_tr)

            def f_cp(E, ts=ts, o0=o0, i=i, nk=nk):
                E.w(tr_t[i])
                kk = o0 // 128
                cp_t[i] = E.t(E.e.tensor_copy(
                    hqT[:, :].rearrange("p (k f) -> p k f", f=FSUB)[:, kk:kk + nk, ts * 128:(ts + 1) * 128],
                    P[6].ap().bitcast(BF16)[0:128, 0:nk * 128].rearrange("p (k f) -> p k f", f=128)))
            pr.sec("vector", f_cp)

    pr.fence()

    wt = [None]

    def f_ldwd(E):
        wt[0] = E.dma(sb3(wbuf[:, :], 22), wdT_d.rearrange("(k p) n -> p k n", p=128), sem="dG")
    pr.sec("gpsimd", f_ldwd)

    for o in range(KCH):
        dm = [None]

        def f_dmm(E, o=o, dm=dm, wt=wt):
            E.w(wt[0], self_tok(pr, "vector"))
            for k in range(22):
                i = E.e.matmul(P[4 + o % 2][:, :],
                               wbuf[:, k * D + o * 128: k * D + o * 128 + 128],
                               hqT[:, k * FSUB:(k + 1) * FSUB],
                               start=(k == 0), stop=(k == 21))
            dm[0] = E.t(i)
        pr.sec("tensor", f_dmm)

        def f_res(E, o=o, dm=dm):
            E.w(dm[0])
            E.w(("dS", pr.cnt["dS"]))
            E.t(E.e.tensor_tensor(outsb[:, :], xT[:, o * FSUB:(o + 1) * FSUB],
                                  P[4 + o % 2][:, :], ALU.add))
        pr.sec("vector", f_res)

        def f_st(E, o=o, sub=sub):
            E.w(self_tok(pr, "vector"))
            E.dma(rearr_kp(yT_d)[:, o, tok0:tok0 + FSUB], outsb[:, :])
        pr.sec("sync", f_st)




def _gammas_np():
    lo = float(np.log(1.0 - 0.999))
    hi = float(np.log(1.0 - 0.9))
    return 1.0 - np.exp(np.linspace(lo, hi, RH, dtype=np.float32))


def phase_ret(pr, io, ctoks0):
    ctoks = list(ctoks0)
    nc = pr.nc
    pr.push_scope()
    sid = pr._scope_id
    xT_d = io["xT"]
    wqT_d = io["wqT"]; wkT_d = io["wkT"]; wvT_d = io["wvT"]
    qsh_h = pr.dram(f"retq{sid}", [RH, 64, NKV], BF16, shared=True)
    ksh_h = pr.dram(f"retk{sid}", [RH, 64, NKV], BF16, shared=True)
    vtsh_h = pr.dram(f"retv{sid}", [RH, NKV, 64], BF16, shared=True)
    ysh_h = pr.dram(f"rety{sid}", [RH, 64, NKV], BF16, shared=True)
    qloc = pr.dram(f"retql{sid}", [RH, 64, TOK], BF16).ap()
    kloc = pr.dram(f"retkl{sid}", [RH, 64, TOK], BF16).ap()
    vtloc = pr.dram(f"retvl{sid}", [RH, TOK, 64], BF16).ap()
    qun = pr.dram(f"retqu{sid}", [8, 64, NKV], BF16).ap()
    kun = pr.dram(f"retku{sid}", [8, 64, NKV], BF16).ap()
    vtun = pr.dram(f"retvu{sid}", [8, NKV, 64], BF16).ap()
    yun = pr.dram(f"retyu{sid}", [8, 64, NKV], BF16).ap()
    yloc = pr.dram(f"retyl{sid}", [RH, 64, TOK], BF16).ap()

    xT = pr.sbuf("sxT", [128, KCH * 512], F32)
    xln = pr.sbuf("xln", [128, KCH * 512], BF16)
    wbuf = pr.sbuf("wbuf", [128, KCH * D], BF16)
    qf = pr.sbuf("qf", [128, 512], BF16)
    ktn = pr.sbuf("ktn", [128, 128], BF16)
    alloc_ln(pr, "att")
    P = pr.ps

    for r in range(TOK // 512):
        pr.fence()
        ld = [None]

        def f_ld(E, r=r):
            ld[0] = E.dma(sb3(xT[:, :], KCH), rearr_kp(xT_d)[:, :, r * 512:(r + 1) * 512])
        pr.sec("sync", f_ld)

        ln_toks = emit_ln(pr, xT, xln, 512, [ld[0]] + ctoks, "att")

        for wi, (wd, dst_d, tokmajor) in enumerate(
                ((wqT_d, qloc, False), (wkT_d, kloc, False), (wvT_d, vtloc, True))):
            wt = [None]

            def f_ldw(E, wd=wd):
                E.w(self_tok(pr, "tensor"))
                wt[0] = E.dma(sb3(wbuf[:, :], KCH), rearr_kp(wd), sem="dG")
            pr.sec("gpsimd", f_ldw)

            for o in range(KCH):
                pm = [None]

                def f_mm(E, r=r, o=o, wt=wt, pm=pm):
                    E.w(wt[0], ln_toks[-1])
                    E.w(self_tok(pr, "vector"), ("dS", pr.cnt["dS"]))
                    for k in range(KCH):
                        i = E.e.matmul(P[4][:, :],
                                       wbuf[:, k * D + o * 128: k * D + o * 128 + 128],
                                       xln[:, k * 512:(k + 1) * 512],
                                       start=(k == 0), stop=(k == KCH - 1))
                    pm[0] = E.t(i)
                pr.sec("tensor", f_mm)

                def f_ev(E, r=r, o=o, pm=pm):
                    E.w(pm[0])
                    pm[0] = E.t(E.e.tensor_copy(qf[:, :], P[4][:, :]))
                pr.sec("vector", f_ev)

                if not tokmajor:
                    def f_st(E, r=r, o=o, dst_d=dst_d, pm=pm):
                        E.w(pm[0])
                        for hh in range(2):
                            h = o * 2 + hh
                            E.dma(dst_d[h, :, r * 512:(r + 1) * 512],
                                  qf[hh * 64:(hh + 1) * 64, :])
                    pr.sec("sync", f_st)
                else:
                    for ts in range(4):
                        tt = [None]

                        def f_tr(E, o=o, ts=ts, tt=tt, pm=pm):
                            E.w(pm[0], self_tok(pr, "vector"), ("dS", pr.cnt["dS"]))
                            i = E.e.transpose(P[5].ap().bitcast(BF16)[0:128, 0:128],
                                              qf[:, ts * 128:(ts + 1) * 128],
                                              pr.sb["eyeb"][:, :])
                            tt[0] = E.t(i)
                        pr.sec("tensor", f_tr)

                        def f_trc(E, o=o, ts=ts, tt=tt):
                            E.w(tt[0])
                            tt[0] = E.t(E.e.tensor_copy(ktn[:, :],
                                                        P[5].ap().bitcast(BF16)[0:128, 0:128]))
                        pr.sec("vector", f_trc)

                        def f_st2(E, r=r, o=o, ts=ts, dst_d=dst_d, tt=tt):
                            E.w(tt[0])
                            for hh in range(2):
                                h = o * 2 + hh
                                E.dma(dst_d[h, r * 512 + ts * 128: r * 512 + (ts + 1) * 128, :],
                                      ktn[:, hh * 64:(hh + 1) * 64])
                        pr.sec("sync", f_st2)

    pr.fence()

    def f_out(par, dd):
        dd.wait(("dS", pr.cnt["dS"]))
        c0 = par * TOK
        dd(qsh_h.ap()[:, :, c0:c0 + TOK], qloc)
        dd(ksh_h.ap()[:, :, c0:c0 + TOK], kloc)
        dd(vtsh_h.ap()[:, c0:c0 + TOK, :], vtloc)
    pr.pswitch(f_out)
    pr.core_barrier()

    def f_in(par, dd):
        h0 = par * 8
        dd(qun, qsh_h.ap()[h0:h0 + 8, :, :])
        dd(kun, ksh_h.ap()[h0:h0 + 8, :, :])
        dd(vtun, vtsh_h.ap()[h0:h0 + 8, :, :])
    gtok = pr.pswitch(f_in)

    pr.pop_scope()
    pr.push_scope()
    _ret_scan_phase(pr, io, qun, kun, vtun, yun, gtok, ctoks)
    pr.fence()

    def f_yout(par, dd):
        dd.wait(("dS", pr.cnt["dS"]))
        dd(ysh_h.ap()[par * 8:par * 8 + 8, :, :], yun)
    pr.pswitch(f_yout)
    pr.core_barrier()

    def f_yin(par, dd):
        c0 = par * TOK
        dd(yloc, ysh_h.ap()[:, :, c0:c0 + TOK])
    gtok2 = pr.pswitch(f_yin)

    pr.pop_scope()
    pr.push_scope()
    _ret_out_phase(pr, io, yloc, gtok2, ctoks)
    pr.fence()
    pr.pop_scope()


def _ret_scan_phase(pr, io, qun, kun, vtun, yun, gtok, ctoks0):
    ctoks = list(ctoks0) + [gtok]
    nc = pr.nc
    dm_d = io["dmask"]; qg_d = io["qgrow"]; kg_d = io["kgcol"]
    g128_d = io["g128col"]; gnw_d = io["gnwcol"]; gnb_d = io["gnbcol"]

    qT = pr.sbuf("qT", [64, 8 * NKV // 8], BF16)
    kT = pr.sbuf("kT", [64, NKV], BF16)
    vt = pr.sbuf("vt", [128, 32 * 64], BF16)
    yb = pr.sbuf("yb", [64, NKV], BF16)
    dm = pr.sbuf("dm", [128, 8 * 128], BF16)
    qg = pr.sbuf("qg", [64, 8 * 128], BF16)
    kg = pr.sbuf("kg", [128, 8], F32)
    g128 = pr.sbuf("g128", [64, 8], F32)
    gnw = pr.sbuf("gnw", [64, 8], F32)
    gnb = pr.sbuf("gnb", [64, 8], F32)
    qgb = pr.sbuf("qgb", [64, 128], BF16)
    kgb = pr.sbuf("kgb", [128, 64], BF16)
    PTm = pr.sbuf("PTm", [128, 128], BF16)
    S = pr.sbuf("Srt", [64, 64], BF16)
    sacc = pr.sbuf("sacc", [64, 2], F32)
    stat = pr.sbuf("stat", [64, 8], F32)
    srow = pr.sbuf("srow", [1, 128], F32)
    mrr = pr.sbuf("mrr", [1, 4], F32)
    sq64 = pr.sbuf("sq64", [64, 128], BF16)
    P = pr.ps

    def f_i(E):
        ctoks.append(E.dma(dm[:, :], dm_d, sem="dG"))
        ctoks.append(E.dma(qg[:, :], qg_d, sem="dG"))
    pr.sec("gpsimd", f_i)

    def f_i2(E):
        ctoks.append(E.dma(kg[:, :], kg_d))
        ctoks.append(E.dma(g128[:, :], g128_d))
        ctoks.append(E.dma(gnw[:, :], gnw_d))
        ctoks.append(E.dma(gnb[:, :], gnb_d))
    pr.sec("sync", f_i2)

    for u in range(8):
        _ret_unit(pr, u, qun, kun, vtun, yun, ctoks)


def _ret_unit(pr, u, qun, kun, vtun, yun, ctoks):
    P = pr.ps
    sb = pr.sb
    qT = sb["qT"]; kT = sb["kT"]; vt = sb["vt"]; yb = sb["yb"]
    dm = sb["dm"]; qg = sb["qg"]; kg = sb["kg"]; g128 = sb["g128"]
    gnw = sb["gnw"]; gnb = sb["gnb"]; qgb = sb["qgb"]; kgb = sb["kgb"]
    PTm = sb["PTm"]; S = sb["Srt"]; sacc = sb["sacc"]; stat = sb["stat"]
    srow = sb["srow"]; mrr = sb["mrr"]; sq64 = sb["sq64"]
    pr.fence()
    ld = [None]

    def f_ld(E, u=u):
        E.dma(qT[:, 0:NKV], qun[u, :, :])
        E.dma(kT[:, :], kun[u, :, :])
        ld[0] = E.dma(sb3(vt[:, :], 32),
                      vtun[u, :, :].rearrange("(r p) d -> p r d", p=128))
    pr.sec("sync", f_ld)

    def f_z(E, u=u):
        E.t(E.e.memset(S[:, :], 0.0))
        E.t(E.e.memset(sacc[:, :], 0.0))
    pr.sec("gpsimd", f_z)

    for j in range(32):
        pt = [None]

        def f_pt(E, u=u, j=j, pt=pt):
            E.w(ld[0], *ctoks)
            E.w(self_tok(pr, "vector"), self_tok(pr, "gpsimd"))
            i = E.e.matmul(P[0][:, 0:128],
                           kT[:, j * 128:(j + 1) * 128],
                           qT[:, j * 128:(j + 1) * 128], start=True, stop=True)
            pt[0] = E.t(i)
        pr.sec("tensor", f_pt)

        def f_ptm(E, u=u, j=j, pt=pt):
            E.w(pt[0])
            t1 = E.t(E.e.tensor_mul(PTm[:, :], P[0][:, 0:128], dm[:, u * 128:(u + 1) * 128]))
            E.w(t1)
            t2 = E.t(E.e.tensor_mul(qgb[:, :], qT[:, j * 128:(j + 1) * 128],
                                    qg[:, u * 128:(u + 1) * 128]))
            E.w(t2)
            pt[0] = t2
        pr.sec("vector", f_ptm)

        om = [None]

        def f_om(E, u=u, j=j, pt=pt, om=om):
            E.w(pt[0])
            E.e.matmul(P[1][0:64, 0:128],
                       vt[:, j * 64:(j + 1) * 64], PTm[:, :],
                       start=True, stop=False, skip_group_check=True)
            i = E.e.matmul(P[1][0:64, 0:128], S[:, :], qgb[:, :],
                           start=False, stop=True, skip_group_check=True)
            om[0] = E.t(i)
        pr.sec("tensor", f_om)

        def f_oev(E, u=u, j=j, om=om):
            E.w(om[0])
            om[0] = E.t(E.e.tensor_copy(yb[:, j * 128:(j + 1) * 128], P[1][0:64, 0:128]))
        pr.sec("vector", f_oev)

        ktt = [None]

        def f_ktr(E, u=u, j=j, ktt=ktt):
            E.w(self_tok(pr, "vector"))
            i = E.e.transpose(P[2].ap().bitcast(BF16)[0:128, 0:64],
                              kT[:, j * 128:(j + 1) * 128], pr.sb["eyeb"][0:64, 0:64])
            ktt[0] = E.t(i)
        pr.sec("tensor", f_ktr)

        def f_ktc(E, u=u, j=j, ktt=ktt):
            E.w(ktt[0])
            ktt[0] = E.t(E.e.tensor_scalar_mul(kgb[:, :],
                                               P[2].ap().bitcast(BF16)[0:128, 0:64],
                                               kg[:, u:u + 1]))
        pr.sec("vector", f_ktc)

        um = [None]

        def f_um(E, u=u, j=j, ktt=ktt, um=um):
            E.w(ktt[0])
            i = E.e.matmul(P[3][0:64, 0:64], kgb[:, :], vt[:, j * 64:(j + 1) * 64],
                           start=True, stop=True)
            um[0] = E.t(i)
        pr.sec("tensor", f_um)

        def f_sup(E, u=u, j=j, um=um):
            E.w(um[0])
            E.t(E.e.scalar_tensor_tensor(S[:, :], S[:, :], g128[:, u:u + 1],
                                         P[3][0:64, 0:64], ALU.mult, ALU.add))
        pr.sec("vector", f_sup)

        def f_gn1(E, u=u, j=j):
            E.w(self_tok(pr, "vector"))
            E.t(E.e.activation(sq64[:, :], yb[:, j * 128:(j + 1) * 128], AF.Square))
        pr.sec("scalar", f_gn1)

        def f_gn2(E, u=u, j=j):
            E.w(self_tok(pr, "scalar"))
            t1 = E.t(E.e.reduce_sum(stat[:, 0:1], yb[:, j * 128:(j + 1) * 128], axis=AX.X))
            E.w(t1)
            t2 = E.t(E.e.reduce_sum(stat[:, 1:2], sq64[:, :], axis=AX.X))
            E.w(t2)
            t3 = E.t(E.e.tensor_add(sacc[:, 0:1], sacc[:, 0:1], stat[:, 0:1]))
            E.w(t3)
            t4 = E.t(E.e.tensor_add(sacc[:, 1:2], sacc[:, 1:2], stat[:, 1:2]))
            E.w(t4)
        pr.sec("vector", f_gn2)

    fz = [None]

    def f_ftr(E, u=u, fz=fz):
        E.w(self_tok(pr, "vector"))
        E.e.transpose(P[4][0:1, 0:64], sacc[:, 0:1], pr.sb["eye"][0:64, 0:64])
        i = E.e.transpose(P[4][0:1, 64:128], sacc[:, 1:2], pr.sb["eye"][0:64, 0:64])
        fz[0] = E.t(i)
    pr.sec("tensor", f_ftr)

    def f_fred(E, u=u, fz=fz):
        E.w(fz[0])
        t1 = E.t(E.e.tensor_copy(srow[:, 0:64], P[4][0:1, 0:64]))
        E.w(t1)
        t2 = E.t(E.e.reduce_sum(mrr[:, 0:1], srow[:, 0:64], axis=AX.X))
        E.w(t2)
        fz[0] = t2
    pr.sec("vector", f_fred)

    def f_fred2(E, u=u, fz=fz):
        E.w(fz[0])
        t1 = E.t(E.e.tensor_copy(srow[:, 0:64], P[4][0:1, 64:128]))
        E.w(t1)
        t2 = E.t(E.e.reduce_sum(mrr[:, 1:2], srow[:, 0:64], axis=AX.X))
        E.w(t2)
        t3 = E.t(E.e.tensor_scalar_mul(mrr[:, 0:1], mrr[:, 0:1], 1.0 / (NKV * 64.0)))
        E.w(t3)
        t4 = E.t(E.e.tensor_scalar_mul(mrr[:, 1:2], mrr[:, 1:2], 1.0 / (NKV * 64.0)))
        E.w(t4)
        t5 = E.t(E.e.tensor_mul(mrr[:, 2:3], mrr[:, 0:1], mrr[:, 0:1]))
        E.w(t5)
        t6 = E.t(E.e.tensor_sub(mrr[:, 1:2], mrr[:, 1:2], mrr[:, 2:3]))
        E.w(t6)
        t7 = E.t(E.e.tensor_scalar(mrr[:, 1:2], mrr[:, 1:2], 1e-5, None, ALU.add))
        fz[0] = t7
    pr.sec("vector", f_fred2)

    def f_fs(E, u=u, fz=fz):
        E.w(fz[0])
        fz[0] = E.t(E.e.activation(mrr[:, 1:2], mrr[:, 1:2], AF.Sqrt))
    pr.sec("scalar", f_fs)

    def f_fr(E, u=u, fz=fz):
        E.w(fz[0])
        t1 = E.t(E.e.reciprocal(mrr[:, 1:2], mrr[:, 1:2]))
        fz[0] = t1
    pr.sec("vector", f_fr)

    def f_fbc(E, u=u, fz=fz):
        E.w(fz[0])
        E.e.matmul(P[5][0:64, 0:1], pr.sb["ones1r"][:, 0:64], mrr[:, 0:1],
                   start=True, stop=True)
        i = E.e.matmul(P[5][0:64, 1:2], pr.sb["ones1r"][:, 0:64], mrr[:, 1:2],
                       start=True, stop=True)
        fz[0] = E.t(i)
    pr.sec("tensor", f_fbc)

    def f_fcp(E, u=u, fz=fz):
        E.w(fz[0])
        t1 = E.t(E.e.tensor_copy(stat[:, 2:3], P[5][0:64, 0:1]))
        E.w(t1)
        t2 = E.t(E.e.tensor_copy(stat[:, 3:4], P[5][0:64, 1:2]))
        E.w(t2)
        fz[0] = t2
    pr.sec("vector", f_fcp)

    for j in range(32):
        def f_nap(E, u=u, j=j, fz=fz):
            E.w(fz[0])
            t1 = E.t(E.e.tensor_scalar(yb[:, j * 128:(j + 1) * 128],
                                       yb[:, j * 128:(j + 1) * 128],
                                       stat[:, 2:3], stat[:, 3:4],
                                       ALU.subtract, ALU.mult))
            E.w(t1)
            t2 = E.t(E.e.tensor_scalar(yb[:, j * 128:(j + 1) * 128],
                                       yb[:, j * 128:(j + 1) * 128],
                                       gnw[:, u:u + 1], gnb[:, u:u + 1],
                                       ALU.mult, ALU.add))
            E.w(t2)
        pr.sec("vector", f_nap)

    def f_st(E, u=u):
        E.w(self_tok(pr, "vector"))
        E.dma(yun[u, :, :], yb[:, :])
    pr.sec("sync", f_st)


def _ret_out_phase(pr, io, yloc, gtok2, ctoks):
    xT_d = io["xT"]
    woT_d = io["woT"]
    out_d = io["yT"]

    xT = pr.sbuf("sxT", [128, KCH * 512], F32)
    yt = pr.sbuf("yt", [128, KCH * TOK], BF16)
    wbuf = pr.sbuf("wbuf", [128, KCH * D], BF16)
    outsb = pr.sbuf("outsb", [128, 512], F32)
    P = pr.ps
    wt = [None]
    olt = [None]

    def f_ldw(E):
        wt[0] = E.dma(sb3(wbuf[:, :], KCH), rearr_kp(woT_d), sem="dG")
    pr.sec("gpsimd", f_ldw)

    def f_ldo(E):
        E.w(gtok2)
        olt[0] = E.dma(yt[:, :].rearrange("p (k t) -> p k t", k=KCH),
                       yloc.rearrange("(k hh) d t -> (hh d) k t", k=KCH))
    pr.sec("sync", f_ldo)

    for g in range(4):
        xg = [None]

        def f_ldx(E, g=g):
            E.w(("dS", pr.cnt["dS"]))
            xg[0] = E.dma(sb3(xT[:, :], KCH), rearr_kp(xT_d)[:, :, g * 512:(g + 1) * 512])
        pr.sec("sync", f_ldx)

        for o in range(KCH):
            om = [None]

            def f_omm(E, g=g, o=o, om=om):
                E.w(wt[0], olt[0], self_tok(pr, "vector"))
                for k in range(KCH):
                    i = E.e.matmul(P[4 + o % 2][:, :],
                                   wbuf[:, k * D + o * 128: k * D + o * 128 + 128],
                                   yt[:, k * TOK + g * 512: k * TOK + (g + 1) * 512],
                                   start=(k == 0), stop=(k == KCH - 1))
                om[0] = E.t(i)
            pr.sec("tensor", f_omm)

            def f_oadd(E, g=g, o=o, om=om):
                E.w(om[0], xg[0])
                E.w(("dS", pr.cnt["dS"]))
                E.t(E.e.tensor_tensor(outsb[:, :], xT[:, o * 512:(o + 1) * 512],
                                      P[4 + o % 2][:, :], ALU.add))
            pr.sec("vector", f_oadd)

            def f_ost(E, g=g, o=o):
                E.w(self_tok(pr, "vector"))
                E.dma(rearr_kp(out_d)[:, o, g * 512:(g + 1) * 512], outsb[:, :])
            pr.sec("sync", f_ost)




def build_fused(nph=9):
    pr = Pr(num_devices=8)
    nc = pr.nc
    io = {}

    def ext(n, shape, dt):
        io[n] = nc.dram_tensor(n, list(shape), dt, kind="ExternalInput").ap()
        return io[n]

    x0T = ext("x0T", [D, TOK], F32)
    yT = nc.dram_tensor("yT", [D, TOK], F32, kind="ExternalOutput").ap()
    xa = pr.dram("xa", [D, TOK], F32).ap()
    xb = pr.dram("xb", [D, TOK], F32).ap()

    for n in ("a_wqT", "a_wkT", "a_wvT", "a_woT"):
        ext(n, [D, D], F32)
    ext("mrow", [1, NKV], BF16)
    ext("f0_gwT", [D, 8], F32)
    for n in ("f0_w1T", "f0_w2T", "f0_w3T"):
        ext(n, [8, D, D], F32)
    for n in ("f0_b1r", "f0_b2r", "f0_b3r"):
        ext(n, [128, 64], F32)
    ext("f0_sw1T", [D, 2048], F32)
    ext("f0_sw2T", [D, 2048], F32)
    ext("f0_sw3T", [2048, D], F32)
    for n in ("p_wqT", "p_wkT", "p_wvT", "p_woT"):
        ext(n, [D, D], F32)
    ext("p_projT", [HEADS, 64, PM], F32)
    ext("p_bor", [128, KCH], F32)
    for n in ("g_wqn", "g_wkn", "g_wvn", "g_wan"):
        ext(n, [D, DI], F32)
    ext("g_bar", [128, 4], F32)
    ext("g_gr", [128, KCH], F32)
    ext("g_br", [128, KCH], F32)
    ext("g_won", [DI, D], F32)
    l1wg_h = nc.dram_tensor("l1wg", [FI, D], F32, kind="ExternalInput")
    l1wu_h = nc.dram_tensor("l1wu", [FI, D], F32, kind="ExternalInput")
    l1wd_h = nc.dram_tensor("l1wd", [D, FI], F32, kind="ExternalInput")
    io["l1wg"] = l1wg_h.ap(); io["l1wu"] = l1wu_h.ap(); io["l1wd"] = l1wd_h.ap()
    io["l1wg_h"] = l1wg_h; io["l1wu_h"] = l1wu_h; io["l1wd_h"] = l1wd_h
    for n in ("r_wqT", "r_wkT", "r_wvT", "r_woT"):
        ext(n, [D, D], F32)
    ext("r_dmask", [128, 8 * 128], F32)
    ext("r_qgrow", [64, 8 * 128], F32)
    ext("r_kgcol", [128, 8], F32)
    ext("r_g128col", [64, 8], F32)
    ext("r_gnwcol", [64, 8], F32)
    ext("r_gnbcol", [64, 8], F32)

    pr.start()
    pr.init_parity(("sync", "gpsimd"))
    ctoks = emit_consts(pr)

    wq3 = phase_l1wq(pr, io, ctoks)
    chain = []
    chain.append(("atta", lambda cur, nxt: phase_atta(pr, {
        "xT": cur, "wqT": io["a_wqT"], "wkT": io["a_wkT"], "wvT": io["a_wvT"],
        "woT": io["a_woT"], "mrow": io["mrow"], "yT": nxt}, ctoks)))
    chain.append(("ffn0a", lambda cur, nxt: phase_ffn0(pr, {
        "xT": cur, "gwT": io["f0_gwT"], "w1T": io["f0_w1T"], "w2T": io["f0_w2T"],
        "w3T": io["f0_w3T"], "b1r": io["f0_b1r"], "b2r": io["f0_b2r"],
        "b3r": io["f0_b3r"], "sw1T": io["f0_sw1T"], "sw2T": io["f0_sw2T"],
        "sw3T": io["f0_sw3T"], "yT": nxt}, ctoks)))
    chain.append(("perf", lambda cur, nxt: phase_perf(pr, {
        "xT": cur, "wqT": io["p_wqT"], "wkT": io["p_wkT"], "wvT": io["p_wvT"],
        "woT": io["p_woT"], "projT": io["p_projT"], "bor": io["p_bor"],
        "yT": nxt}, ctoks)))
    chain.append(("ffn0b", chain[1][1]))
    chain.append(("gla", lambda cur, nxt: phase_gla(pr, {
        "xT": cur, "wqn": io["g_wqn"], "wkn": io["g_wkn"], "wvn": io["g_wvn"],
        "wan": io["g_wan"], "bar": io["g_bar"], "gr": io["g_gr"], "br": io["g_br"],
        "won": io["g_won"], "yT": nxt}, ctoks)))
    chain.append(("ffn1a", lambda cur, nxt: phase_ffn1(pr, {
        "xT": cur, "yT": nxt}, wq3, ctoks)))
    chain.append(("ret", lambda cur, nxt: phase_ret(pr, {
        "xT": cur, "wqT": io["r_wqT"], "wkT": io["r_wkT"], "wvT": io["r_wvT"],
        "woT": io["r_woT"], "yT": nxt, "dmask": io["r_dmask"],
        "qgrow": io["r_qgrow"], "kgcol": io["r_kgcol"], "g128col": io["r_g128col"],
        "gnwcol": io["r_gnwcol"], "gnbcol": io["r_gnbcol"]}, ctoks)))
    chain.append(("ffn1b", chain[5][1]))

    nph = min(nph, len(chain))
    cur = x0T
    bufs = [xa, xb]
    for i in range(nph):
        nxt = yT if i == nph - 1 else bufs[i % 2]
        chain[i][1](cur, nxt)
        cur = nxt
    if nph == 0:
        def f_cp(E):
            E.dma(yT, x0T)
        pr.sec("sync", f_cp)
    return pr.finish()


_FUSE_NPH = 8


def _fused_inmaps(ins):
    import ml_dtypes
    x = np.ascontiguousarray(ins["L0_attn_a_x"].astype(np.float32).reshape(-1, D))
    shards = _shard_T(x)
    gam = _gammas_np()
    gnw_full = ins["L1_attn_b_gn_w"].reshape(RH, 64)
    gnb_full = ins["L1_attn_b_gn_b"].reshape(RH, 64)
    base = {
        "a_wqT": np.ascontiguousarray(ins["L0_attn_a_Wq"].T),
        "a_wkT": np.ascontiguousarray(ins["L0_attn_a_Wk"].T),
        "a_wvT": np.ascontiguousarray(ins["L0_attn_a_Wv"].T),
        "a_woT": np.ascontiguousarray(ins["L0_attn_a_Wo"].T),
        "f0_gwT": np.ascontiguousarray(ins["L0_ffn_gate_w"].T),
        "f0_w1T": np.ascontiguousarray(ins["L0_ffn_w1"].transpose(0, 2, 1)),
        "f0_w2T": np.ascontiguousarray(ins["L0_ffn_w2"].transpose(0, 2, 1)),
        "f0_w3T": np.ascontiguousarray(ins["L0_ffn_w3"].transpose(0, 2, 1)),
        "f0_b1r": _col_bias(ins["L0_ffn_b1"]),
        "f0_b2r": _col_bias(ins["L0_ffn_b2"]),
        "f0_b3r": _col_bias(ins["L0_ffn_b3"]),
        "f0_sw1T": np.ascontiguousarray(ins["L0_ffn_sw1"].T),
        "f0_sw2T": np.ascontiguousarray(ins["L0_ffn_sw2"].T),
        "f0_sw3T": np.ascontiguousarray(ins["L0_ffn_sw3"].T),
        "p_wqT": np.ascontiguousarray(ins["L0_attn_b_Wq"].T),
        "p_wkT": np.ascontiguousarray(ins["L0_attn_b_Wk"].T),
        "p_wvT": np.ascontiguousarray(ins["L0_attn_b_Wv"].T),
        "p_woT": np.ascontiguousarray(ins["L0_attn_b_Wo"].T),
        "p_projT": np.ascontiguousarray(ins["L0_attn_b_proj"].transpose(0, 2, 1)),
        "p_bor": np.ascontiguousarray(ins["L0_attn_b_bo"].reshape(8, 128).T),
        "g_wqn": np.ascontiguousarray(ins["L1_attn_a_Wq"]),
        "g_wkn": np.ascontiguousarray(ins["L1_attn_a_Wk"]),
        "g_wvn": np.ascontiguousarray(ins["L1_attn_a_Wv"]),
        "g_wan": np.ascontiguousarray(ins["L1_attn_a_Wa"]),
        "g_bar": np.ascontiguousarray(ins["L1_attn_a_ba"].reshape(4, 128).T),
        "g_gr": _colchunks(ins["L1_attn_a_gamma"]),
        "g_br": _colchunks(ins["L1_attn_a_beta"]),
        "g_won": np.ascontiguousarray(ins["L1_attn_a_Wo"]),
        "l1wg": np.ascontiguousarray(ins["L1_ffn_w_gate"]),
        "l1wu": np.ascontiguousarray(ins["L1_ffn_w_up"]),
        "l1wd": np.ascontiguousarray(ins["L1_ffn_w_down"]),
        "r_wqT": np.ascontiguousarray(ins["L1_attn_b_wq"].T),
        "r_wkT": np.ascontiguousarray(ins["L1_attn_b_wk"].T),
        "r_wvT": np.ascontiguousarray(ins["L1_attn_b_wv"].T),
        "r_woT": np.ascontiguousarray(ins["L1_attn_b_wo"].T),
    }
    in_maps = []
    for c in range(NCORES):
        half = c % 2
        if half == 0:
            mrow = np.concatenate([np.full((1, TOK), -1e9, np.float32),
                                   np.zeros((1, TOK), np.float32)], axis=1)
        else:
            mrow = np.zeros((1, NKV), np.float32)
        dmask = np.zeros((128, 8 * 128), np.float32)
        qgrow = np.zeros((64, 8 * 128), np.float32)
        kgcol = np.zeros((128, 8), np.float32)
        g128c = np.zeros((64, 8), np.float32)
        gnwc = np.zeros((64, 8), np.float32)
        gnbc = np.zeros((64, 8), np.float32)
        idx = np.arange(128)
        for i in range(8):
            h = half * 8 + i
            g = gam[h]
            dmask[:, i * 128:(i + 1) * 128] = np.where(
                idx[None, :] >= idx[:, None], g ** (idx[None, :] - idx[:, None]), 0.0)
            qgrow[:, i * 128:(i + 1) * 128] = (g ** (idx + 1.0))[None, :]
            kgcol[:, i] = g ** (127.0 - idx)
            g128c[:, i] = g ** 128.0
            gnwc[:, i] = gnw_full[h]
            gnbc[:, i] = gnb_full[h]
        in_maps.append(dict(
            base, x0T=shards[c],
            mrow=mrow.astype(ml_dtypes.bfloat16),
            r_dmask=dmask, r_qgrow=qgrow, r_kgcol=kgcol,
            r_g128col=g128c, r_gnwcol=gnwc, r_gnbcol=gnbc))
    return in_maps


def kernel(**inputs):
    ins = {k: np.asarray(v) for k, v in inputs.items()}
    nc = _get(f"fused{_FUSE_NPH}", lambda: build_fused(_FUSE_NPH))
    in_maps = _fused_inmaps(ins)
    res = run_bass_kernel_spmd(nc, in_maps, core_ids=list(range(NCORES)))
    y = _unshard_T([res.results[c]["yT"] for c in range(NCORES)])
    return y.reshape(4, NKV, D).astype(np.float32)
